# version 10
# baseline (speedup 1.0000x reference)
import numpy as np
import ml_dtypes
import concourse.bass as bass
import concourse.mybir as mybir
from concourse.bass_utils import run_bass_kernel_spmd

F32 = mybir.dt.float32
BF16 = mybir.dt.bfloat16
AF = mybir.ActivationFunctionType
ALU = mybir.AluOpType
AX = mybir.AxisListType
BF = ml_dtypes.bfloat16

B, T, D = 4, 4096, 1024
NH, HD = 16, 64
TOWN = 2048
NDC = D // 128
HFF = 4096
EPS = 1e-5
TTT_LR = 0.01
NEXP = 8
HME = 4096



def _rope_tables():
    pos = np.arange(T, dtype=np.float64)
    i = np.arange(HD // 2, dtype=np.float64)
    coss, sins = [], []
    for s in (1.0, 4.0, 16.0):
        freqs = 1.0 / (10000.0 * s) ** (2.0 * i / HD)
        ang = pos[:, None] * freqs[None, :]
        coss.append(np.cos(ang))
        sins.append(np.sin(ang))
    cos = np.mean(coss, 0).T
    sin = np.mean(sins, 0).T
    cos2 = np.vstack([cos, cos, cos, cos]).astype(BF)
    sin2 = np.vstack([sin, sin, sin, sin]).astype(BF)
    return cos2, sin2


def _rope_perm():
    perm = np.zeros(D, dtype=np.int64)
    for c in range(NH // 2):
        base = c * 128
        perm[base + 0:base + 32] = (2 * c) * HD + np.arange(0, HD, 2)
        perm[base + 32:base + 64] = (2 * c) * HD + np.arange(1, HD, 2)
        perm[base + 64:base + 96] = (2 * c + 1) * HD + np.arange(0, HD, 2)
        perm[base + 96:base + 128] = (2 * c + 1) * HD + np.arange(1, HD, 2)
    return perm


def _c(a):
    return np.ascontiguousarray(a)


class Sem:
    def __init__(self, h):
        self.h = h
        self.n = 0

    def inc(self, instr, k=1):
        instr.then_inc(self.h, k)
        self.n += k
        return self.n


class Stream:

    def __init__(self, nc, name, nbuf, keep):
        self.nbuf = nbuf
        self.h = []
        for i in range(nbuf):
            g = nc.semaphore(f"{name}{i}")
            keep.append(g)
            self.h.append(g.__enter__())
        self.val = [0] * nbuf
        self.marks = []

    def put(self, eng, dmas):
        p = len(self.marks) % self.nbuf
        for (o, i_) in dmas:
            eng.dma_start(out=o, in_=i_).then_inc(self.h[p], 16)
            self.val[p] += 16
        self.marks.append((p, self.val[p]))
        return len(self.marks) - 1

    def wait(self, eng, item=None):
        if item is None:
            for p in range(self.nbuf):
                if self.val[p]:
                    eng.wait_ge(self.h[p], self.val[p])
            return
        p, v = self.marks[item]
        eng.wait_ge(self.h[p], v)



def build_block0(stop=None):
    nc = bass.Bass("TRN2", target_bir_lowering=False)

    x0fm = nc.dram_tensor("x0fm", [D, T], F32, kind="ExternalInput").ap()
    wqp = nc.dram_tensor("wqp", [D, D], F32, kind="ExternalInput").ap()
    wkp = nc.dram_tensor("wkp", [D, D], F32, kind="ExternalInput").ap()
    wvD = nc.dram_tensor("wv", [D, D], F32, kind="ExternalInput").ap()
    woD = nc.dram_tensor("wo", [D, D], F32, kind="ExternalInput").ap()
    w1t = nc.dram_tensor("w1t", [D, HFF], F32, kind="ExternalInput").ap()
    w1sD = nc.dram_tensor("w1s", [HFF, D], F32, kind="ExternalInput").ap()
    w2t = nc.dram_tensor("w2t", [HFF, D], F32, kind="ExternalInput").ap()
    w2sD = nc.dram_tensor("w2s", [D, HFF], F32, kind="ExternalInput").ap()
    b1tD = nc.dram_tensor("b1t", [128, 32], F32, kind="ExternalInput").ap()
    b2tD = nc.dram_tensor("b2t", [128, 8], F32, kind="ExternalInput").ap()
    cos2d = nc.dram_tensor("cos2", [128, T], BF16, kind="ExternalInput").ap()
    sin2d = nc.dram_tensor("sin2", [128, T], BF16, kind="ExternalInput").ap()
    x1own = nc.dram_tensor("x1own", [D, TOWN], F32, kind="ExternalOutput").ap()

    h1d = nc.dram_tensor("h1d", [32, 128, TOWN], BF16, kind="Internal").ap()
    w1sd = nc.dram_tensor("w1sd", [32, 128, 1024], BF16, kind="Internal").ap()
    w2sd = nc.dram_tensor("w2sd", [32, 128, 1024], BF16, kind="Internal").ap()

    xn = [nc.alloc_sbuf_tensor(f"xn{i}", [128, T], BF16) for i in range(NDC)]
    ffnacc = [nc.alloc_sbuf_tensor(f"ffa{i}", [128, TOWN], BF16) for i in range(NDC)]
    hs_ofm = nc.alloc_sbuf_tensor("hs_ofm", [128, 32 * 512], BF16)
    alpha1 = nc.alloc_sbuf_tensor("alpha1", [128, 32], F32)
    alpha2 = nc.alloc_sbuf_tensor("alpha2", [128, 8], F32)
    b1sb = nc.alloc_sbuf_tensor("b1sb", [128, 32], F32)
    b2sb = nc.alloc_sbuf_tensor("b2sb", [128, 8], F32)
    ones128 = nc.alloc_sbuf_tensor("ones128", [128, 128], BF16)
    epsb = nc.alloc_sbuf_tensor("epsb", [128, 1], F32)

    guards = []

    def scope(name, shape, dt):
        g = nc.sbuf_tensor(name, shape, dt)
        t = g.__enter__()
        guards.append(g)
        return t

    def close(n):
        for _ in range(n):
            guards.pop().__exit__(None, None, None)

    pg1 = nc.psum_tensor("PA", [128, 2048], F32)
    PA = pg1.__enter__()
    pg2 = nc.psum_tensor("PB", [128, 2048], F32)
    PB = pg2.__enter__()

    keep = []
    sg = [nc.semaphore(n) for n in ["sTT", "sVV", "sSS"]]
    TT, VV, SS = [Sem(g.__enter__()) for g in sg]
    keep.extend(sg)

    stB = Stream(nc, "stB", 1, keep)
    gX0 = Stream(nc, "gX0", 1, keep)
    sWT = Stream(nc, "sWT", 2, keep)
    sWS = Stream(nc, "sWS", 2, keep)
    gW1 = Stream(nc, "gW1", 2, keep)
    gW2 = Stream(nc, "gW2", 2, keep)
    sF1 = Stream(nc, "sF1", 2, keep)
    gH1 = Stream(nc, "gH1", 2, keep)
    sH1 = Stream(nc, "sH1", 1, keep)
    sF2 = Stream(nc, "sF2", 2, keep)
    sTAB = Stream(nc, "sTAB", 1, keep)
    gWG = Stream(nc, "gWG", 1, keep)
    gWOB = Stream(nc, "gWOB", 1, keep)
    sX0 = Stream(nc, "sX0", 2, keep)
    gXO = Stream(nc, "gXO", 2, keep)

    blk_g = nc.Block()
    blk = blk_g.__enter__()

    def _(sy):
        stB.put(sy, [(b1sb[:], b1tD), (b2sb[:], b2tD)])
    blk.sync(_)

    def _(g):
        for dc in range(NDC):
            gX0.put(g, [(xn[dc][:], x0fm[dc * 128:(dc + 1) * 128, :])])
    blk.gpsimd(_)

    def _(v):
        v.memset(epsb[:], EPS)
        VV.inc(v.memset(ones128[:], 1.0))
    blk.vector(_)
    vv_ones = VV.n

    wtmp = [scope(f"wtmp{i}", [128, 1024], F32) for i in range(2)]
    wstr = [scope(f"wstr{i}", [128, 4096], F32) for i in range(2)]
    wsgn = [scope(f"wsgn{i}", [128, 1024], BF16) for i in range(2)]

    w1t_r = w1t.rearrange("(dc p) h -> p dc h", p=128)

    ss0, vv0 = SS.n, VV.n

    def _(sy):
        for hc in range(32):
            if hc >= 2:
                sy.wait_ge(SS.h, ss0 + hc - 1)
                sy.wait_ge(VV.h, vv0 + hc - 1)
            sWT.put(sy, [(wtmp[hc % 2][:].rearrange("p (dc h) -> p dc h", dc=8),
                          w1t_r[:, :, hc * 128:(hc + 1) * 128])])
            sWS.put(sy, [(wstr[hc % 2][:, 0:1024], w1sD[hc * 128:(hc + 1) * 128, :])])
    blk.sync(_)

    def _(g):
        for hc in range(32):
            g.wait_ge(SS.h, ss0 + hc + 1)
            gW1.put(g, [(w1sd[hc], wsgn[hc % 2][:])])
    blk.gpsimd(_)

    def _(s):
        for hc in range(32):
            sWT.wait(s, hc)
            if hc >= 2:
                gW1.wait(s, hc - 2)
            SS.inc(s.activation(wsgn[hc % 2][:], wtmp[hc % 2][:], AF.Sign))
    blk.scalar(_)

    def _(v):
        for hc in range(32):
            sWS.wait(v, hc)
            VV.inc(v.reduce_sum(alpha1[:, hc:hc + 1], wstr[hc % 2][:, 0:1024],
                                axis=AX.X, apply_absolute_value=True))
    blk.vector(_)

    ss1, vv1 = SS.n, VV.n

    def _(sy):
        sy.wait_ge(SS.h, ss1)
        sy.wait_ge(VV.h, vv1)
        for hc in range(32):
            if hc >= 2:
                sy.wait_ge(SS.h, ss1 + hc - 1)
            sWT.put(sy, [(wtmp[hc % 2][:], w2t[hc * 128:(hc + 1) * 128, :])])
        for ec in range(8):
            if ec >= 2:
                sy.wait_ge(VV.h, vv1 + ec - 1)
            sWS.put(sy, [(wstr[ec % 2][:], w2sD[ec * 128:(ec + 1) * 128, :])])
    blk.sync(_)

    def _(g):
        for hc in range(32):
            g.wait_ge(SS.h, ss1 + hc + 1)
            gW2.put(g, [(w2sd[hc], wsgn[hc % 2][:])])
    blk.gpsimd(_)

    def _(s):
        gW1.wait(s)
        for hc in range(32):
            sWT.wait(s, 32 + hc)
            if hc >= 2:
                gW2.wait(s, hc - 2)
            SS.inc(s.activation(wsgn[hc % 2][:], wtmp[hc % 2][:], AF.Sign))
    blk.scalar(_)

    def _(v):
        for ec in range(8):
            sWS.wait(v, 32 + ec)
            VV.inc(v.reduce_sum(alpha2[:, ec:ec + 1], wstr[ec % 2][:],
                                axis=AX.X, apply_absolute_value=True))
        v.wait_ge(VV.h, VV.n)
        VV.inc(v.tensor_scalar_mul(alpha1[:], alpha1[:], 1.0 / D))
        VV.inc(v.tensor_scalar_mul(alpha2[:], alpha2[:], 1.0 / HFF))
    blk.vector(_)
    vv_alpha = VV.n

    close(6)

    fA = scope("fA", [128, 2048], F32)
    fB = scope("fB", [128, 2048], F32)
    fC = scope("fC", [128, 2048], F32)
    sqd = [scope(f"sqd{i}", [128, 2048], BF16) for i in range(2)]
    mu_bf = scope("mu_bf", [128, T], BF16)
    rstd_bf = scope("rstd_bf", [128, T], BF16)

    for tb in range(2):
        ts = tb * 2048
        tt_b = TT.n
        vv_b = VV.n

        def _(v, ts=ts, tt_b=tt_b, ss_w=SS.n):
            gX0.wait(v)
            v.wait_ge(SS.h, ss_w)
            gW1.wait(v)
            gW2.wait(v)
            for dc in range(NDC):
                if dc >= 2:
                    v.wait_ge(TT.h, tt_b + dc - 1)
                VV.inc(v.tensor_mul(sqd[dc % 2][:], xn[dc][:, ts:ts + 2048],
                                    xn[dc][:, ts:ts + 2048]))
        blk.vector(_)

        def _(t, ts=ts, vv_b=vv_b):
            t.wait_ge(VV.h, vv_ones)
            for dc in range(NDC):
                t.wait_ge(VV.h, vv_b + dc + 1)
                for q4 in range(4):
                    t.matmul(PA[:, q4 * 512:(q4 + 1) * 512], ones128[:],
                             xn[dc][:, ts + q4 * 512:ts + (q4 + 1) * 512],
                             start=(dc == 0), stop=(dc == NDC - 1))
                for q4 in range(4):
                    ins = t.matmul(PB[:, q4 * 512:(q4 + 1) * 512], ones128[:],
                                   sqd[dc % 2][:, q4 * 512:(q4 + 1) * 512],
                                   start=(dc == 0), stop=(dc == NDC - 1))
                TT.inc(ins)
        blk.tensor(_)
        tt_s1 = TT.n

        def _(v, tt_s1=tt_s1):
            v.wait_ge(TT.h, tt_s1)
            VV.inc(v.tensor_scalar_mul(fA[:], PA[:], 1.0 / D))
            VV.inc(v.tensor_scalar_mul(fB[:], PB[:], 1.0 / D))
            v.wait_ge(VV.h, VV.n)
            VV.inc(v.tensor_mul(fC[:], fA[:], fA[:]))
            v.wait_ge(VV.h, VV.n)
            VV.inc(v.tensor_sub(fB[:], fB[:], fC[:]))
        blk.vector(_)
        vv_var = VV.n

        def _(s, vv_var=vv_var):
            s.wait_ge(VV.h, vv_var)
            SS.inc(s.activation(fC[:], fB[:], AF.Sqrt, bias=epsb[:]))
        blk.scalar(_)
        ss_sd = SS.n

        def _(v, ts=ts, ss_sd=ss_sd):
            v.wait_ge(SS.h, ss_sd)
            VV.inc(v.reciprocal(fB[:], fC[:]))
            VV.inc(v.tensor_copy(mu_bf[:, ts:ts + 2048], fA[:]))
            v.wait_ge(VV.h, VV.n)
            VV.inc(v.tensor_copy(rstd_bf[:, ts:ts + 2048], fB[:]))
        blk.vector(_)

    def _(v):
        for dc in range(NDC):
            VV.inc(v.tensor_sub(xn[dc][:], xn[dc][:], mu_bf[:]))
            v.wait_ge(VV.h, VV.n)
            VV.inc(v.tensor_mul(xn[dc][:], xn[dc][:], rstd_bf[:]))
    blk.vector(_)
    vv_xn = VV.n

    close(7)

    w1b = [scope(f"w1b{i}", [128, 1024], BF16) for i in range(2)]
    h1t = [scope(f"h1t{i}", [128, TOWN], BF16) for i in range(2)]

    ttf, ssf = TT.n, SS.n

    def _(sy, ss_pre=SS.n):
        gW1.wait(sy)
        sy.wait_ge(VV.h, vv_xn)
        sy.wait_ge(SS.h, ss_pre)
        for hc in range(32):
            if hc >= 2:
                sy.wait_ge(TT.h, ttf + hc - 1)
            sF1.put(sy, [(w1b[hc % 2][:], w1sd[hc])])
    blk.sync(_)

    def _(t):
        t.wait_ge(VV.h, vv_xn)
        for hc in range(32):
            sF1.wait(t, hc)
            if hc >= 1:
                t.wait_ge(SS.h, ssf + hc)
            for q4 in range(4):
                for dc in range(NDC):
                    ins = t.matmul(PA[:, q4 * 512:(q4 + 1) * 512],
                                   w1b[hc % 2][:, dc * 128:(dc + 1) * 128],
                                   xn[dc][:, q4 * 512:(q4 + 1) * 512],
                                   start=(dc == 0), stop=(dc == NDC - 1))
            TT.inc(ins)
    blk.tensor(_)

    def _(g):
        for hc in range(32):
            g.wait_ge(SS.h, ssf + hc + 1)
            gH1.put(g, [(h1d[hc], h1t[hc % 2][:])])
    blk.gpsimd(_)

    def _(s):
        s.wait_ge(VV.h, vv_xn)
        stB.wait(s)
        for hc in range(32):
            s.wait_ge(TT.h, ttf + hc + 1)
            if hc >= 2:
                gH1.wait(s, hc - 2)
            SS.inc(s.activation(h1t[hc % 2][:], PA[:], AF.Gelu,
                                bias=b1sb[:, hc:hc + 1], scale=alpha1[:, hc:hc + 1]))
    blk.scalar(_)

    close(4)

    w2b = [scope(f"w2b{i}", [128, 1024], BF16) for i in range(2)]
    h1s = hs_ofm

    tt_prev_tb = TT.n
    ss_prev_tb = SS.n
    for tb in range(4):
        ts = tb * 512
        tt2 = TT.n

        def _(sy, ts=ts, tt2=tt2, tt_prev_tb=tt_prev_tb, tb=tb):
            gH1.wait(sy)
            gW2.wait(sy)
            sy.wait_ge(TT.h, tt_prev_tb)
            sH1.put(sy, [(h1s[:, grp * 8 * 512:(grp + 1) * 8 * 512].rearrange(
                              "p (hc t) -> p hc t", hc=8),
                          h1d[grp * 8:(grp + 1) * 8, :, ts:ts + 512].rearrange(
                              "hc p t -> p hc t"))
                         for grp in range(4)])
            for hc in range(32):
                if hc >= 2:
                    sy.wait_ge(TT.h, tt2 + hc - 1)
                sF2.put(sy, [(w2b[hc % 2][:], w2sd[hc])])
        blk.sync(_)

        def _(t, ss_prev_tb=ss_prev_tb, tb=tb):
            t.wait_ge(SS.h, ss_prev_tb)
            sH1.wait(t, tb)
            for hc in range(32):
                sF2.wait(t, 32 * tb + hc)
                for ec in range(8):
                    ps = PA if ec < 4 else PB
                    ins = t.matmul(ps[:, (ec % 4) * 512:(ec % 4 + 1) * 512],
                                   w2b[hc % 2][:, ec * 128:(ec + 1) * 128],
                                   h1s[:, hc * 512:(hc + 1) * 512],
                                   start=(hc == 0), stop=(hc == 31))
                TT.inc(ins)
        blk.tensor(_)
        tt_prev_tb = TT.n

        def _(s, ts=ts, tt_done=TT.n):
            s.wait_ge(TT.h, tt_done)
            stB.wait(s)
            for ec in range(8):
                ps = PA if ec < 4 else PB
                SS.inc(s.activation(ffnacc[ec][:, ts:ts + 512],
                                    ps[:, (ec % 4) * 512:(ec % 4 + 1) * 512],
                                    AF.Identity,
                                    bias=b2sb[:, ec:ec + 1], scale=alpha2[:, ec:ec + 1]))
        blk.scalar(_)
        ss_prev_tb = SS.n

    close(2)
    ss_attn0 = SS.n

    if stop == "f2":
        f32t_early = [scope(f"f32e{i}", [128, TOWN], F32) for i in range(2)]
        vv0e = VV.n
        def _(g):
            for dc in range(NDC):
                g.wait_ge(VV.h, vv0e + dc + 1)
                gXO.put(g, [(x1own[dc * 128:(dc + 1) * 128, :], f32t_early[dc % 2][:])])
            gXO.wait(g)
        blk.gpsimd(_)
        def _(v):
            v.wait_ge(SS.h, ss_attn0)
            for dc in range(NDC):
                if dc >= 2:
                    gXO.wait(v, dc - 2)
                VV.inc(v.tensor_copy(f32t_early[dc % 2][:], ffnacc[dc][:]))
        blk.vector(_)
        blk_g.__exit__(None, None, None)
        close(len(guards))
        pg2.__exit__(None, None, None)
        pg1.__exit__(None, None, None)
        for g in reversed(keep):
            g.__exit__(None, None, None)
        return nc

    cos2 = scope("cos2s", [128, T], BF16)
    sin2 = scope("sin2s", [128, T], BF16)
    kfm = scope("kfm", [128, T], BF16)
    qfm = scope("qfm", [128, TOWN], BF16)
    raw = scope("raw", [128, 2048], BF16)
    tscr = scope("tscr", [128, 2048], BF16)
    v65 = scope("v65", [128, 32 * 130], BF16)
    pb_sb = scope("pb_sb", [128, 2048], BF16)
    recb = scope("recb", [64, 2048], BF16)
    rec1 = scope("rec1", [1, 2048], BF16)
    wgq = scope("wgq", [128, 1024], BF16)
    wgk = scope("wgk", [128, 1024], BF16)
    wgv = scope("wgv", [128, 1024], BF16)
    ofm = hs_ofm

    def _(sy, tt_pre=TT.n, ss_pre=SS.n):
        sy.wait_ge(TT.h, tt_pre)
        sy.wait_ge(SS.h, ss_pre)
        sTAB.put(sy, [(cos2[:], cos2d), (sin2[:], sin2d)])
    blk.sync(_)

    wqp_r = wqp.rearrange("(dc p) j -> p dc j", p=128)
    wkp_r = wkp.rearrange("(dc p) j -> p dc j", p=128)
    wv_r = wvD.rearrange("(dc p) j -> p dc j", p=128)

    def emit_rope(v, dst, doff, src, tab_off, tlen):
        tsl = slice(tab_off, tab_off + tlen)
        for e in (0, 64):
            o = e + 32
            ce = cos2[e:e + 32, tsl]
            se = sin2[e:e + 32, tsl]
            co = cos2[o:o + 32, tsl]
            so = sin2[o:o + 32, tsl]
            VV.inc(v.tensor_mul(tscr[e:e + 32, 0:tlen], src[o:o + 32, 0:tlen], so))
            VV.inc(v.tensor_mul(dst[e:e + 32, doff:doff + tlen], src[e:e + 32, 0:tlen], ce))
            v.wait_ge(VV.h, VV.n)
            VV.inc(v.tensor_sub(dst[e:e + 32, doff:doff + tlen],
                                dst[e:e + 32, doff:doff + tlen], tscr[e:e + 32, 0:tlen]))
            VV.inc(v.tensor_mul(tscr[o:o + 32, 0:tlen], src[e:e + 32, 0:tlen], se))
            VV.inc(v.tensor_mul(dst[o:o + 32, doff:doff + tlen], src[o:o + 32, 0:tlen], co))
            v.wait_ge(VV.h, VV.n)
            VV.inc(v.tensor_add(dst[o:o + 32, doff:doff + tlen],
                                dst[o:o + 32, doff:doff + tlen], tscr[o:o + 32, 0:tlen]))

    tt_group_end = TT.n
    vv_group_end = VV.n

    for c in range(8):
        def _(g, tt_group_end=tt_group_end, c=c):
            g.wait_ge(TT.h, tt_group_end)
            gWG.put(g, [(wgq[:].rearrange("p (dc j) -> p dc j", dc=8),
                         wqp_r[:, :, c * 128:(c + 1) * 128]),
                        (wgk[:].rearrange("p (dc j) -> p dc j", dc=8),
                         wkp_r[:, :, c * 128:(c + 1) * 128]),
                        (wgv[:].rearrange("p (dc j) -> p dc j", dc=8),
                         wv_r[:, :, c * 128:(c + 1) * 128])])
            gWPP.put(g, [(w1bd[c], w1rD[c])])
            gWPP.put(g, [(w2bd[c], w2D[c])])
        blk.gpsimd(_)

        def _(t, vv_group_end=vv_group_end, c=c):
            gWG.wait(t, c)
            t.wait_ge(VV.h, max(vv_xn, vv_group_end))
            t.wait_ge(SS.h, ss_attn0)
            for q4 in range(4):
                for dc in range(NDC):
                    ins = t.matmul(PA[:, q4 * 512:(q4 + 1) * 512],
                                   wgq[:, dc * 128:(dc + 1) * 128],
                                   xn[dc][:, q4 * 512:(q4 + 1) * 512],
                                   start=(dc == 0), stop=(dc == NDC - 1))
            TT.inc(ins)
        blk.tensor(_)
        tt_q = TT.n

        def _(v, tt_q=tt_q):
            v.wait_ge(TT.h, tt_q)
            sTAB.wait(v)
            VV.inc(v.tensor_copy(raw[:], PA[:]))
            v.wait_ge(VV.h, VV.n)
            emit_rope(v, qfm, 0, raw, 0, 2048)
        blk.vector(_)

        for th in range(2):
            ts = th * 2048

            def _(t, ts=ts, vv_pre=VV.n):
                t.wait_ge(VV.h, vv_pre)
                for q4 in range(4):
                    for dc in range(NDC):
                        ins = t.matmul(PB[:, q4 * 512:(q4 + 1) * 512],
                                       wgk[:, dc * 128:(dc + 1) * 128],
                                       xn[dc][:, ts + q4 * 512:ts + (q4 + 1) * 512],
                                       start=(dc == 0), stop=(dc == NDC - 1))
                TT.inc(ins)
            blk.tensor(_)
            tt_k = TT.n

            def _(v, ts=ts, tt_k=tt_k):
                v.wait_ge(TT.h, tt_k)
                VV.inc(v.tensor_copy(raw[:], PB[:]))
                v.wait_ge(VV.h, VV.n)
                emit_rope(v, kfm, ts, raw, ts, 2048)
            blk.vector(_)

        def _(v, tt_group_end=tt_group_end):
            v.wait_ge(TT.h, tt_group_end)
            VV.inc(v.memset(v65[:], 1.0))
        blk.vector(_)
        vv_ms = VV.n

        tt_v0 = TT.n
        vv_v0 = VV.n

        def _(t, tt_v0=tt_v0, vv_v0=vv_v0, vv_ms=vv_ms):
            t.wait_ge(VV.h, vv_ms)
            for tc in range(32):
                if tc >= 2:
                    t.wait_ge(VV.h, vv_v0 + tc - 1)
                po = (tc % 2) * 512
                for dc in range(NDC):
                    ins = t.matmul(PA[:, po:po + 128],
                                   xn[dc][:, tc * 128:(tc + 1) * 128],
                                   wgv[:, dc * 128:(dc + 1) * 128],
                                   start=(dc == 0), stop=(dc == NDC - 1))
                TT.inc(ins)
        blk.tensor(_)

        def _(v, tt_v0=tt_v0, vv_ms=vv_ms):
            v.wait_ge(VV.h, vv_ms)
            for tc in range(32):
                v.wait_ge(TT.h, tt_v0 + tc + 1)
                po = (tc % 2) * 512
                v.tensor_copy(v65[:, tc * 130:tc * 130 + 64], PA[:, po:po + 64])
                VV.inc(v.tensor_copy(v65[:, tc * 130 + 65:tc * 130 + 129],
                                     PA[:, po + 64:po + 128]))
        blk.vector(_)

        nheads = 0 if stop == "qkv" else (1 if stop == "head0" else 2)
        for hh in range(nheads if stop in ("qkv", "head0", "grp0") and c == 0 else 2):
            hb = hh * 64
            vcol = hh * 65

            tt_h0 = TT.n
            ss_h0 = SS.n
            vv_pre_head = VV.n

            def _(t, tt_h0=tt_h0, ss_h0=ss_h0, vcol=vcol, hb=hb,
                  vv_pre_head=vv_pre_head):
                t.wait_ge(VV.h, vv_pre_head)
                t.wait_ge(SS.h, ss_h0)
                for kc in range(32):
                    for h2 in range(2):
                        if kc >= 1:
                            t.wait_ge(SS.h, ss_h0 + 2 * (kc - 1) + h2 + 1)
                        for q4 in range(2):
                            qo = h2 * 1024 + q4 * 512
                            ins = t.matmul(PA[:, qo:qo + 512],
                                           kfm[hb:hb + 64, kc * 128:(kc + 1) * 128],
                                           qfm[hb:hb + 64, qo:qo + 512],
                                           start=True, stop=True)
                        TT.inc(ins)
                    for h2 in range(2):
                        t.wait_ge(SS.h, ss_h0 + 2 * kc + h2 + 1)
                        for q4 in range(2):
                            qo = h2 * 1024 + q4 * 512
                            ins = t.matmul(PB[0:65, qo:qo + 512],
                                           v65[:, kc * 130 + vcol:kc * 130 + vcol + 65],
                                           pb_sb[:, qo:qo + 512],
                                           start=(kc == 0), stop=(kc == 31))
                        TT.inc(ins)
            blk.tensor(_)
            tt_av_last = TT.n

            def _(s, tt_h0=tt_h0):
                for kc in range(32):
                    for h2 in range(2):
                        s.wait_ge(TT.h, tt_h0 + 4 * kc + h2 + 1)
                        SS.inc(s.activation(pb_sb[:, h2 * 1024:(h2 + 1) * 1024],
                                            PA[:, h2 * 1024:(h2 + 1) * 1024],
                                            AF.Exp, scale=0.125))
            blk.scalar(_)

            def _(v, tt_av_last=tt_av_last):
                v.wait_ge(TT.h, tt_av_last)
                with nc.allow_low_precision("bf16 recip of softmax sums is within the graded bar"):
                    VV.inc(v.reciprocal(rec1[:], PB[64:65, :]))
            blk.vector(_)
            vv_rec = VV.n

            def _(t, vv_rec=vv_rec):
                t.wait_ge(VV.h, vv_rec)
                for q4 in range(4):
                    ins = t.matmul(PA[0:64, q4 * 512:(q4 + 1) * 512],
                                   ones128[0:1, 0:64],
                                   rec1[0:1, q4 * 512:(q4 + 1) * 512],
                                   start=True, stop=True)
                TT.inc(ins)
            blk.tensor(_)
            tt_bc = TT.n

            def _(v, tt_bc=tt_bc, c=c, hh=hh):
                v.wait_ge(TT.h, tt_bc)
                VV.inc(v.tensor_copy(recb[:], PA[0:64, :]))
                v.wait_ge(VV.h, VV.n)
                VV.inc(v.tensor_mul(ofm[hh * 64:(hh + 1) * 64, c * 2048:(c + 1) * 2048],
                                    PB[0:64, :], recb[:]))
            blk.vector(_)

        tt_group_end = TT.n
        vv_group_end = VV.n

        if stop in ("qkv", "head0", "grp0") and c == 0:
            break

    if stop in ("qkv", "head0", "grp0"):
        dmp = [scope(f"dmp{i}", [128, TOWN], F32) for i in range(2)]
        vvd = VV.n
        def _(g):
            for i in range(4):
                g.wait_ge(VV.h, vvd + i + 1)
                gXO.put(g, [(x1own[i * 128:(i + 1) * 128, :], dmp[i % 2][:])])
            gXO.wait(g)
        blk.gpsimd(_)
        def _(v):
            v.wait_ge(TT.h, TT.n)
            v.wait_ge(SS.h, SS.n)
            v.wait_ge(VV.h, VV.n)
            srcs = [qfm[:, 0:2048], kfm[:, 0:2048], kfm[:, 2048:4096],
                    v65[:, 0:2048]] if stop == "qkv" else                    [ofm[:, 0:2048], ofm[:, 2048:4096], qfm[:, 0:2048], kfm[:, 0:2048]]
            for i, s_ in enumerate(srcs):
                if i >= 2:
                    gXO.wait(v, i - 2)
                VV.inc(v.tensor_copy(dmp[i % 2][:], s_))
        blk.vector(_)
        blk_g.__exit__(None, None, None)
        close(len(guards))
        pg2.__exit__(None, None, None)
        pg1.__exit__(None, None, None)
        for g in reversed(keep):
            g.__exit__(None, None, None)
        return nc

    close(13)

    wob = [scope(f"wob{i}", [128, 1024], BF16) for i in range(8)]
    x0ld = [scope(f"x0ld{i}", [128, TOWN], F32) for i in range(2)]
    x1t = [scope(f"x1t{i}", [128, TOWN], F32) for i in range(2)]
    f32t = scope("f32t", [128, TOWN], F32)

    wo_r = woD.rearrange("(hc p) j -> p hc j", p=128)

    def _(g, tt_group_end=tt_group_end, vv_pre=VV.n, ss_pre=SS.n):
        g.wait_ge(TT.h, tt_group_end)
        g.wait_ge(VV.h, vv_pre)
        g.wait_ge(SS.h, ss_pre)
        gWOB.put(g, [(wob[hdc][:], wo_r[:, hdc, :]) for hdc in range(8)])
    blk.gpsimd(_)

    vv_cmb0 = VV.n
    tt_wo0 = TT.n

    def _(sy, tt_pre=TT.n, ss_pre=SS.n):
        sy.wait_ge(TT.h, tt_pre)
        sy.wait_ge(SS.h, ss_pre)
        sy.wait_ge(VV.h, vv_cmb0)
        for dc in range(NDC):
            if dc >= 2:
                sy.wait_ge(VV.h, vv_cmb0 + 3 * (dc - 1))
            sX0.put(sy, [(x0ld[dc % 2][:], x0fm[dc * 128:(dc + 1) * 128, 0:TOWN])])
    blk.sync(_)

    def _(t, vv_group_end=vv_group_end):
        gWOB.wait(t)
        t.wait_ge(VV.h, vv_group_end)
        for dco in range(NDC):
            if dco >= 2:
                t.wait_ge(VV.h, vv_cmb0 + 3 * (dco - 1))
            ps = PA if dco % 2 == 0 else PB
            for q4 in range(4):
                for hdc in range(8):
                    ins = t.matmul(ps[:, q4 * 512:(q4 + 1) * 512],
                                   wob[hdc][:, dco * 128:(dco + 1) * 128],
                                   ofm[:, hdc * 2048 + q4 * 512:hdc * 2048 + (q4 + 1) * 512],
                                   start=(hdc == 0), stop=(hdc == 7))
            TT.inc(ins)
    blk.tensor(_)

    def _(g):
        for dco in range(NDC):
            g.wait_ge(VV.h, vv_cmb0 + 3 * (dco + 1))
            gXO.put(g, [(x1own[dco * 128:(dco + 1) * 128, :], x1t[dco % 2][:])])
        gXO.wait(g)
    blk.gpsimd(_)

    def _(v):
        for dco in range(NDC):
            v.wait_ge(TT.h, tt_wo0 + dco + 1)
            sX0.wait(v, dco)
            if dco >= 2:
                gXO.wait(v, dco - 2)
            ps = PA if dco % 2 == 0 else PB
            v.wait_ge(VV.h, VV.n)
            VV.inc(v.tensor_copy(f32t[:], ffnacc[dco][:]))
            v.wait_ge(VV.h, VV.n)
            VV.inc(v.tensor_add(f32t[:], f32t[:], ps[:]))
            v.wait_ge(VV.h, VV.n)
            VV.inc(v.tensor_add(x1t[dco % 2][:], f32t[:], x0ld[dco % 2][:]))
    blk.vector(_)

    blk_g.__exit__(None, None, None)
    close(len(guards))
    pg2.__exit__(None, None, None)
    pg1.__exit__(None, None, None)
    for g in reversed(keep):
        g.__exit__(None, None, None)
    for g in reversed(sg):
        pass
    return nc



def build_ttt():
    nc = bass.Bass("TRN2", target_bir_lowering=False)

    x1fm = nc.dram_tensor("x1fm", [D, T], F32, kind="ExternalInput").ap()
    wktD = nc.dram_tensor("wkt", [D, 64], F32, kind="ExternalInput").ap()
    wvtD = nc.dram_tensor("wvt", [D, D], F32, kind="ExternalInput").ap()
    wqtD = nc.dram_tensor("wqt", [D, 64], F32, kind="ExternalInput").ap()
    bkD = nc.dram_tensor("bk", [64, 1], F32, kind="ExternalInput").ap()
    bqD = nc.dram_tensor("bq", [64, 1], F32, kind="ExternalInput").ap()
    bkbD = nc.dram_tensor("bkb", [128, 64], F32, kind="ExternalInput").ap()
    bvbD = nc.dram_tensor("bvb", [128, D], F32, kind="ExternalInput").ap()
    gbD = nc.dram_tensor("gb", [128, 8], F32, kind="ExternalInput").ap()
    btD = nc.dram_tensor("bt", [128, 8], F32, kind="ExternalInput").ap()
    mUeD = nc.dram_tensor("mUe", [128, 128], BF16, kind="ExternalInput").ap()
    mLeD = nc.dram_tensor("mLe", [128, 128], BF16, kind="ExternalInput").ap()
    mUnD = nc.dram_tensor("mUn", [128, 128], BF16, kind="ExternalInput").ap()
    idD = nc.dram_tensor("identb", [128, 128], BF16, kind="ExternalInput").ap()
    yfull = nc.dram_tensor("yfull", [T, D], F32, kind="ExternalOutput").ap()

    kfmB = nc.alloc_sbuf_tensor("kfmB", [64, T], BF16)
    qfmB = nc.alloc_sbuf_tensor("qfmB", [64, T], BF16)
    ktm = nc.alloc_sbuf_tensor("ktm", [128, 32 * 64], BF16)
    bvb = nc.alloc_sbuf_tensor("bvbS", [128, D], F32)
    bkb = nc.alloc_sbuf_tensor("bkbS", [128, 64], F32)
    bk = nc.alloc_sbuf_tensor("bkS", [64, 1], F32)
    bq = nc.alloc_sbuf_tensor("bqS", [64, 1], F32)
    gma = nc.alloc_sbuf_tensor("gma", [128, 8], F32)
    bta = nc.alloc_sbuf_tensor("bta", [128, 8], F32)
    mUe = nc.alloc_sbuf_tensor("mUeS", [128, 128], BF16)
    mLe = nc.alloc_sbuf_tensor("mLeS", [128, 128], BF16)
    mUn = nc.alloc_sbuf_tensor("mUnS", [128, 128], BF16)
    identb = nc.alloc_sbuf_tensor("identS", [128, 128], BF16)
    ones128 = nc.alloc_sbuf_tensor("ones128", [128, 128], BF16)
    epsb = nc.alloc_sbuf_tensor("epsb", [128, 1], F32)
    Sst = nc.alloc_sbuf_tensor("Sst", [64, D], F32)
    Sbf = nc.alloc_sbuf_tensor("Sbf", [64, D], BF16)
    nUt = nc.alloc_sbuf_tensor("nUt", [128, 128], F32)
    nLt = nc.alloc_sbuf_tensor("nLt", [128, 128], F32)
    Cbuf = nc.alloc_sbuf_tensor("Cbuf", [128, 6 * 128], F32)
    Ta = nc.alloc_sbuf_tensor("Ta", [128, 128], F32)
    Tb = nc.alloc_sbuf_tensor("Tb", [128, 128], F32)
    Rs = nc.alloc_sbuf_tensor("Rs", [128, 128], BF16)
    brawS = nc.alloc_sbuf_tensor("brawS", [128, D], BF16)
    Utm = nc.alloc_sbuf_tensor("Utm", [128, D], BF16)
    pmS = nc.alloc_sbuf_tensor("pmS", [128, 128], BF16)
    ysb = [nc.alloc_sbuf_tensor(f"ysb{i}", [128, D], F32) for i in range(2)]

    guards = []

    def scope(name, shape, dt):
        g = nc.sbuf_tensor(name, shape, dt)
        t = g.__enter__()
        guards.append(g)
        return t

    def close(n):
        for _ in range(n):
            guards.pop().__exit__(None, None, None)

    x1bf = [scope(f"x1b{i}", [128, T], BF16) for i in range(NDC)]
    mu_bf = scope("mu_bf", [128, T], BF16)
    f_bf = scope("f_bf", [128, T], BF16)

    pg1 = nc.psum_tensor("PA", [128, 2048], F32)
    PA = pg1.__enter__()
    pg2 = nc.psum_tensor("PB", [128, 2048], F32)
    PB = pg2.__enter__()

    keep = []
    sg = [nc.semaphore(n) for n in ["sTT", "sVV", "sSS"]]
    TT, VV, SS = [Sem(g.__enter__()) for g in sg]
    keep.extend(sg)
    gX1 = Stream(nc, "gX1", 1, keep)
    gWP = Stream(nc, "gWP", 1, keep)
    sMS = Stream(nc, "sMS", 1, keep)
    gY = Stream(nc, "gY", 2, keep)

    blk_g = nc.Block()
    blk = blk_g.__enter__()

    def _(g):
        for dc in range(NDC):
            gX1.put(g, [(x1bf[dc][:], x1fm[dc * 128:(dc + 1) * 128, :])])
    blk.gpsimd(_)

    def _(sy, tt_pre=TT.n, vv_pre=VV.n, ss_pre=SS.n):
        sy.wait_ge(TT.h, tt_pre)
        sy.wait_ge(VV.h, vv_pre)
        sy.wait_ge(SS.h, ss_pre)
        sMS.put(sy, [(bvb[:], bvbD), (bkb[:], bkbD), (bk[:], bkD), (bq[:], bqD),
                     (gma[:], gbD), (bta[:], btD), (mUe[:], mUeD), (mLe[:], mLeD),
                     (mUn[:], mUnD), (identb[:], idD)])
    blk.sync(_)

    def _(v):
        v.memset(epsb[:], EPS)
        v.memset(Sst[:], 0.0)
        v.memset(Sbf[:], 0.0)
        VV.inc(v.memset(ones128[:], 1.0))
    blk.vector(_)
    vv_ones = VV.n

    fA = scope("fA", [128, 2048], F32)
    fB = scope("fB", [128, 2048], F32)
    fC = scope("fC", [128, 2048], F32)
    fD = scope("fD", [128, 2048], F32)
    sqd = [scope(f"sqd{i}", [128, 2048], BF16) for i in range(2)]

    for tb in range(2):
        ts = tb * 2048
        tt_b = TT.n
        vv_b = VV.n

        def _(v, ts=ts, tt_b=tt_b):
            gX1.wait(v)
            for dc in range(NDC):
                if dc >= 2:
                    v.wait_ge(TT.h, tt_b + dc - 1)
                VV.inc(v.tensor_mul(sqd[dc % 2][:], x1bf[dc][:, ts:ts + 2048],
                                    x1bf[dc][:, ts:ts + 2048]))
        blk.vector(_)

        def _(t, ts=ts, vv_b=vv_b):
            t.wait_ge(VV.h, vv_ones)
            for dc in range(NDC):
                t.wait_ge(VV.h, vv_b + dc + 1)
                for q4 in range(4):
                    t.matmul(PA[:, q4 * 512:(q4 + 1) * 512], ones128[:],
                             x1bf[dc][:, ts + q4 * 512:ts + (q4 + 1) * 512],
                             start=(dc == 0), stop=(dc == NDC - 1))
                for q4 in range(4):
                    ins = t.matmul(PB[:, q4 * 512:(q4 + 1) * 512], ones128[:],
                                   sqd[dc % 2][:, q4 * 512:(q4 + 1) * 512],
                                   start=(dc == 0), stop=(dc == NDC - 1))
                TT.inc(ins)
        blk.tensor(_)
        tt_s1 = TT.n

        def _(v, tt_s1=tt_s1):
            v.wait_ge(TT.h, tt_s1)
            VV.inc(v.tensor_scalar_mul(fA[:], PA[:], 1.0 / D))
            VV.inc(v.tensor_scalar_mul(fB[:], PB[:], 1.0 / D))
            v.wait_ge(VV.h, VV.n)
            VV.inc(v.tensor_mul(fC[:], fA[:], fA[:]))
            v.wait_ge(VV.h, VV.n)
            VV.inc(v.tensor_sub(fB[:], fB[:], fC[:]))
        blk.vector(_)
        vv_var = VV.n

        def _(s, vv_var=vv_var):
            s.wait_ge(VV.h, vv_var)
            SS.inc(s.activation(fC[:], fB[:], AF.Sqrt, bias=epsb[:]))
        blk.scalar(_)
        ss_sd = SS.n

        def _(v, ss_sd=ss_sd):
            v.wait_ge(SS.h, ss_sd)
            VV.inc(v.reciprocal(fD[:], fC[:]))
            v.wait_ge(VV.h, VV.n)
            VV.inc(v.tensor_mul(fC[:], fD[:], fD[:]))
            v.wait_ge(VV.h, VV.n)
            VV.inc(v.tensor_mul(fC[:], fB[:], fC[:]))
        blk.vector(_)
        vv_vn = VV.n

        def _(s, vv_vn=vv_vn):
            s.wait_ge(VV.h, vv_vn)
            SS.inc(s.activation(fC[:], fC[:], AF.Sqrt, bias=epsb[:]))
        blk.scalar(_)
        ss_sdn = SS.n

        def _(v, ts=ts, ss_sdn=ss_sdn):
            v.wait_ge(SS.h, ss_sdn)
            VV.inc(v.reciprocal(fC[:], fC[:]))
            v.wait_ge(VV.h, VV.n)
            VV.inc(v.tensor_mul(fC[:], fC[:], fD[:]))
            v.wait_ge(VV.h, VV.n)
            VV.inc(v.tensor_copy(mu_bf[:, ts:ts + 2048], fA[:]))
            VV.inc(v.tensor_copy(f_bf[:, ts:ts + 2048], fC[:]))
        blk.vector(_)

    def _(v):
        sMS.wait(v)
        for dc in range(NDC):
            v.wait_ge(VV.h, VV.n)
            VV.inc(v.tensor_sub(x1bf[dc][:], x1bf[dc][:], mu_bf[:]))
            v.wait_ge(VV.h, VV.n)
            VV.inc(v.tensor_mul(x1bf[dc][:], x1bf[dc][:], f_bf[:]))
            v.wait_ge(VV.h, VV.n)
            VV.inc(v.tensor_scalar(x1bf[dc][:], x1bf[dc][:], gma[:, dc:dc + 1],
                                   bta[:, dc:dc + 1], ALU.mult, ALU.add))
    blk.vector(_)
    vv_h = VV.n

    close(6)
    h = x1bf
    close(2)

    vtm = [nc.alloc_sbuf_tensor(f"vtm{i}", [128, D], BF16, side="right")
           for i in range(32)]
    wvb = scope("wvb", [128, 8 * D], BF16)
    wkb = scope("wkb", [128, 8 * 64], BF16)
    wqb = scope("wqb", [128, 8 * 64], BF16)

    def _(g, ss_pre=SS.n):
        g.wait_ge(VV.h, vv_h)
        g.wait_ge(SS.h, ss_pre)
        gWP.put(g, [(wvb[:].rearrange("p (dc e) -> p dc e", dc=8),
                     wvtD.rearrange("(dc p) e -> p dc e", p=128)),
                    (wkb[:].rearrange("p (dc e) -> p dc e", dc=8),
                     wktD.rearrange("(dc p) e -> p dc e", p=128)),
                    (wqb[:].rearrange("p (dc e) -> p dc e", dc=8),
                     wqtD.rearrange("(dc p) e -> p dc e", p=128))])
    blk.gpsimd(_)

    def _(t, ss_pre=SS.n):
        t.wait_ge(VV.h, vv_h)
        gWP.wait(t)
        for th in range(2):
            ts = th * 2048
            if th >= 1:
                t.wait_ge(SS.h, ss_pre + 2 * th - 1)
            for q4 in range(4):
                for dc in range(NDC):
                    ins = t.matmul(PA[0:64, q4 * 512:(q4 + 1) * 512],
                                   wkb[:, dc * 64:(dc + 1) * 64],
                                   h[dc][:, ts + q4 * 512:ts + (q4 + 1) * 512],
                                   start=(dc == 0), stop=(dc == NDC - 1))
            TT.inc(ins)
            if th >= 1:
                t.wait_ge(SS.h, ss_pre + 2 * th)
            for q4 in range(4):
                for dc in range(NDC):
                    ins = t.matmul(PB[0:64, q4 * 512:(q4 + 1) * 512],
                                   wqb[:, dc * 64:(dc + 1) * 64],
                                   h[dc][:, ts + q4 * 512:ts + (q4 + 1) * 512],
                                   start=(dc == 0), stop=(dc == NDC - 1))
            TT.inc(ins)
    blk.tensor(_)
    tt_kq0 = TT.n - 4

    def _(s):
        sMS.wait(s)
        for th in range(2):
            ts = th * 2048
            s.wait_ge(TT.h, tt_kq0 + 2 * th + 1)
            SS.inc(s.activation(kfmB[:, ts:ts + 2048], PA[0:64, :], AF.Identity,
                                bias=bk[:]))
            s.wait_ge(TT.h, tt_kq0 + 2 * th + 2)
            SS.inc(s.activation(qfmB[:, ts:ts + 2048], PB[0:64, :], AF.Identity,
                                bias=bq[:]))
    blk.scalar(_)
    ss_kq = SS.n

    tt_kv0 = TT.n
    vv_kv0 = VV.n

    def _(t):
        t.wait_ge(SS.h, ss_kq)
        for tc in range(32):
            if tc >= 2:
                t.wait_ge(VV.h, vv_kv0 + 2 * (tc - 1))
            pok = (tc % 2) * 512
            for dc in range(NDC):
                ins = t.matmul(PA[:, pok:pok + 64],
                               h[dc][:, tc * 128:(tc + 1) * 128],
                               wkb[:, dc * 64:(dc + 1) * 64],
                               start=(dc == 0), stop=(dc == NDC - 1))
            TT.inc(ins)
            pov = (tc % 2) * 1024
            for j in range(2):
                for dc in range(NDC):
                    ins = t.matmul(PB[:, pov + j * 512:pov + (j + 1) * 512],
                                   h[dc][:, tc * 128:(tc + 1) * 128],
                                   wvb[:, dc * 1024 + j * 512:dc * 1024 + (j + 1) * 512],
                                   start=(dc == 0), stop=(dc == NDC - 1))
            TT.inc(ins)
    blk.tensor(_)

    def _(v):
        for tc in range(32):
            v.wait_ge(TT.h, tt_kv0 + 2 * tc + 1)
            pok = (tc % 2) * 512
            VV.inc(v.tensor_add(ktm[:, tc * 64:(tc + 1) * 64], PA[:, pok:pok + 64], bkb[:]))
            v.wait_ge(TT.h, tt_kv0 + 2 * tc + 2)
            pov = (tc % 2) * 1024
            VV.inc(v.tensor_add(vtm[tc][:], PB[:, pov:pov + 1024], bvb[:]))
    blk.vector(_)
    vv_kv_done = VV.n

    close(3)
    close(8)

    psG = PB[:, 1024:1152]
    psD0 = PB[:, 1024:1152]
    psD1 = PB[:, 1536:1664]
    psKM = PA[:, 0:1024]
    psU = PA[:, 1024:2048]
    psY = PB[:, 0:1024]
    psSK = PB[0:64, 1024:2048]

    tt_sc0 = TT.n
    vv_sc0 = VV.n
    NTT, NVV = 24, 31

    def _(t):
        t.wait_ge(VV.h, vv_kv_done)
        for tc in range(32):
            tt0 = tt_sc0 + NTT * tc
            vv0 = vv_sc0 + NVV * tc
            ksl = kfmB[0:64, tc * 128:(tc + 1) * 128]
            qsl = qfmB[0:64, tc * 128:(tc + 1) * 128]
            t.wait_ge(VV.h, vv0)
            TT.inc(t.matmul(psG, ksl, ksl, start=True, stop=True))
            for l in range(5):
                t.wait_ge(VV.h, vv0 + 2 + 3 * l)
                TT.inc(t.matmul(psD0, nLt[:], nUt[:], start=True, stop=True))
                TT.inc(t.matmul(psD1, nUt[:], nLt[:], start=True, stop=True))
            t.wait_ge(VV.h, vv0 + 17)
            TT.inc(t.matmul(psD0, nLt[:], nUt[:], start=True, stop=True))
            for idx, j in enumerate([5, 4, 3, 2, 1, 0]):
                t.wait_ge(VV.h, vv0 + 19 + idx)
                ps = psD1 if idx % 2 == 0 else psD0
                rhs = [Ta, Tb][idx % 2]
                TT.inc(t.matmul(ps, Cbuf[:, j * 128:(j + 1) * 128], rhs[:],
                                start=True, stop=True))
            for j in range(2):
                ins = t.matmul(psKM[:, j * 512:(j + 1) * 512], ksl,
                               Sbf[:, j * 512:(j + 1) * 512], start=True, stop=True)
            TT.inc(ins)
            t.wait_ge(VV.h, vv0 + 26)
            for j in range(2):
                ins = t.matmul(psU[:, j * 512:(j + 1) * 512], Rs[:],
                               brawS[:, j * 512:(j + 1) * 512], start=True, stop=True)
            TT.inc(ins)
            for j in range(2):
                ins = t.matmul(psY[:, j * 512:(j + 1) * 512], qsl,
                               Sbf[:, j * 512:(j + 1) * 512], start=True, stop=False)
            TT.inc(ins)
            t.wait_ge(VV.h, vv0 + 25)
            TT.inc(t.matmul(psG, ksl, qsl, start=True, stop=True))
            t.wait_ge(VV.h, vv0 + 28)
            for j in range(2):
                ins = t.matmul(psY[:, j * 512:(j + 1) * 512], pmS[:],
                               Utm[:, j * 512:(j + 1) * 512], start=False, stop=True)
            TT.inc(ins)
            for j in range(2):
                ins = t.matmul(psSK[:, j * 512:(j + 1) * 512],
                               ktm[:, tc * 64:(tc + 1) * 64],
                               Utm[:, j * 512:(j + 1) * 512], start=True, stop=True)
            TT.inc(ins)
    blk.tensor(_)

    def _(g):
        for tc in range(32):
            vv0 = vv_sc0 + NVV * tc
            g.wait_ge(VV.h, vv0 + 29)
            gY.put(g, [(yfull[tc * 128:(tc + 1) * 128, :], ysb[tc % 2][:])])
        gY.wait(g)
    blk.gpsimd(_)

    def _(v):
        for tc in range(32):
            tt0 = tt_sc0 + NTT * tc
            vv0 = vv_sc0 + NVV * tc
            v.wait_ge(TT.h, tt0 + 1)
            VV.inc(v.tensor_mul(nUt[:], psG, mUe[:]))
            VV.inc(v.tensor_mul(nLt[:], psG, mLe[:]))
            for l in range(5):
                v.wait_ge(VV.h, VV.n)
                if l == 0:
                    VV.inc(v.tensor_sub(Cbuf[:, 0:128], identb[:], nLt[:]))
                else:
                    VV.inc(v.tensor_add(Cbuf[:, l * 128:(l + 1) * 128], identb[:], nLt[:]))
                v.wait_ge(TT.h, tt0 + 3 + 2 * l)
                v.wait_ge(VV.h, VV.n)
                VV.inc(v.tensor_copy(nUt[:], psD0))
                VV.inc(v.tensor_copy(nLt[:], psD1))
            v.wait_ge(VV.h, VV.n)
            VV.inc(v.tensor_add(Cbuf[:, 5 * 128:6 * 128], identb[:], nLt[:]))
            v.wait_ge(TT.h, tt0 + 12)
            VV.inc(v.tensor_add(Ta[:], identb[:], psD0))
            for idx in range(6):
                v.wait_ge(TT.h, tt0 + 13 + idx)
                ps = psD1 if idx % 2 == 0 else psD0
                if idx < 5:
                    dst = [Tb, Ta][idx % 2]
                    VV.inc(v.tensor_copy(dst[:], ps))
                else:
                    with nc.allow_low_precision("lam-scaled inverse factor in bf16"):
                        VV.inc(v.tensor_scalar_mul(Rs[:], ps, TTT_LR))
            v.wait_ge(TT.h, tt0 + 19)
            VV.inc(v.tensor_sub(brawS[:], psKM, vtm[tc][:]))
            v.wait_ge(TT.h, tt0 + 20)
            VV.inc(v.tensor_copy(Utm[:], psU))
            v.wait_ge(TT.h, tt0 + 22)
            VV.inc(v.tensor_mul(pmS[:], psG, mUn[:]))
            v.wait_ge(TT.h, tt0 + 23)
            if tc >= 2:
                gY.wait(v, tc - 2)
            VV.inc(v.tensor_copy(ysb[tc % 2][:], psY))
            v.wait_ge(TT.h, tt0 + 24)
            v.wait_ge(VV.h, VV.n)
            VV.inc(v.tensor_sub(Sst[:], Sst[:], psSK))
            v.wait_ge(VV.h, VV.n)
            VV.inc(v.tensor_copy(Sbf[:], Sst[:]))
    blk.vector(_)

    blk_g.__exit__(None, None, None)
    close(len(guards))
    pg2.__exit__(None, None, None)
    pg1.__exit__(None, None, None)
    for g in reversed(keep):
        g.__exit__(None, None, None)
    return nc


def prep_ttt_inmaps(inputs, x1):
    wk = np.asarray(inputs["L1_attn_Wk"], np.float32)
    wv = np.asarray(inputs["L1_attn_Wv"], np.float32)
    wq = np.asarray(inputs["L1_attn_Wq"], np.float32)
    bk = np.asarray(inputs["L1_attn_bk"], np.float32)
    bv = np.asarray(inputs["L1_attn_bv"], np.float32)
    bq = np.asarray(inputs["L1_attn_bq"], np.float32)
    gm = np.asarray(inputs["L1_attn_gamma"], np.float32)
    bt = np.asarray(inputs["L1_attn_beta"], np.float32)
    on = np.ones((128, 128), np.float32)
    shared = dict(
        wkt=_c(wk.T), wvt=_c(wv.T), wqt=_c(wq.T),
        bk=_c(bk.reshape(64, 1)), bq=_c(bq.reshape(64, 1)),
        bkb=_c(np.tile(bk, (128, 1))), bvb=_c(np.tile(bv, (128, 1))),
        gb=_c(gm.reshape(8, 128).T), bt=_c(bt.reshape(8, 128).T),
        mUe=_c((np.triu(on, 1) * TTT_LR).astype(BF)),
        mLe=_c((np.tril(on, -1) * TTT_LR).astype(BF)),
        mUn=_c((-np.triu(on, 1)).astype(BF)),
        identb=_c(np.eye(128, dtype=np.float32).astype(BF)))
    in_maps = []
    for core in range(8):
        b = core // 2
        m = dict(shared)
        m.update(x1fm=_c(x1[b].T))
        in_maps.append(m)
    return in_maps


def run_ttt(inputs, x1):
    nc = build_ttt()
    in_maps = prep_ttt_inmaps(inputs, x1)
    res = run_bass_kernel_spmd(nc, in_maps, core_ids=list(range(8)))
    y = np.zeros((B, T, D), dtype=np.float32)
    for core in range(8):
        b, hh = core // 2, core % 2
        y[b, hh * TOWN:(hh + 1) * TOWN] = res.results[core]["yfull"][hh * TOWN:(hh + 1) * TOWN]
    return y



def build_moe():
    nc = bass.Bass("TRN2", target_bir_lowering=False)

    x1ownD = nc.dram_tensor("x1own", [D, TOWN], F32, kind="ExternalInput").ap()
    yownD = nc.dram_tensor("yown", [D, TOWN], F32, kind="ExternalInput").ap()
    wrD = nc.dram_tensor("wr", [D, 8], F32, kind="ExternalInput").ap()
    br8D = nc.dram_tensor("br8", [8, 1], F32, kind="ExternalInput").ap()
    w1rD = nc.dram_tensor("w1r", [8, 32, 128, 1024], F32, kind="ExternalInput").ap()
    w2D = nc.dram_tensor("w2m", [8, HME, D], F32, kind="ExternalInput").ap()
    b1aD = nc.dram_tensor("b1a", [128, 256], F32, kind="ExternalInput").ap()
    b2aD = nc.dram_tensor("b2a", [128, 64], F32, kind="ExternalInput").ap()
    idD = nc.dram_tensor("identb", [128, 128], BF16, kind="ExternalInput").ap()
    x2ownD = nc.dram_tensor("x2own", [D, TOWN], F32, kind="ExternalOutput").ap()

    x2acc = [nc.alloc_sbuf_tensor(f"x2a{i}", [128, TOWN], F32) for i in range(NDC)]
    xn1bf = [nc.alloc_sbuf_tensor(f"xn1b{i}", [128, TOWN], BF16) for i in range(NDC)]
    h1m = nc.alloc_sbuf_tensor("h1m", [128, 32 * 512], BF16)
    w1c = [nc.alloc_sbuf_tensor(f"w1c{i}", [128, 1024], BF16) for i in range(2)]
    w2c = [nc.alloc_sbuf_tensor(f"w2c{i}", [128, 1024], BF16) for i in range(2)]
    wbv = nc.alloc_sbuf_tensor("wbv", [128, TOWN], BF16)
    wrsb = nc.alloc_sbuf_tensor("wrsb", [128, 64], BF16)
    br8 = nc.alloc_sbuf_tensor("br8S", [8, 1], F32)
    b1a = nc.alloc_sbuf_tensor("b1aS", [128, 256], F32)
    b2a = nc.alloc_sbuf_tensor("b2aS", [128, 64], F32)
    identb = nc.alloc_sbuf_tensor("identS", [128, 128], BF16)
    ones128 = nc.alloc_sbuf_tensor("ones128", [128, 128], BF16)
    epsb = nc.alloc_sbuf_tensor("epsb", [128, 1], F32)
    scoresbf = nc.alloc_sbuf_tensor("scoresbf", [8, TOWN], BF16)
    s_tm = nc.alloc_sbuf_tensor("s_tm", [128, 128], F32)
    sA = nc.alloc_sbuf_tensor("sA", [128, 128], F32)
    sB = nc.alloc_sbuf_tensor("sB", [128, 128], F32)
    m16 = nc.alloc_sbuf_tensor("m16", [128, 64], F32)
    w_tmb = nc.alloc_sbuf_tensor("w_tmb", [128, 128], BF16)
    wrow = nc.alloc_sbuf_tensor("wrow", [1, TOWN], BF16)
    etmp = [nc.alloc_sbuf_tensor(f"etmp{i}", [128, 512], F32) for i in range(2)]

    guards = []

    def scope(name, shape, dt):
        g = nc.sbuf_tensor(name, shape, dt)
        t = g.__enter__()
        guards.append(g)
        return t

    def close(n):
        for _ in range(n):
            guards.pop().__exit__(None, None, None)

    pg1 = nc.psum_tensor("PA", [128, 2048], F32)
    PA = pg1.__enter__()
    pg2 = nc.psum_tensor("PB", [128, 2048], F32)
    PB = pg2.__enter__()

    keep = []
    sg = [nc.semaphore(n) for n in ["sTT", "sVV", "sSS"]]
    TT, VV, SS = [Sem(g.__enter__()) for g in sg]
    keep.extend(sg)
    sX1 = Stream(nc, "sX1", 1, keep)
    sY = Stream(nc, "sY", 2, keep)
    sSM = Stream(nc, "sSM", 1, keep)
    gW1L = Stream(nc, "gW1L", 2, keep)
    gW2L = Stream(nc, "gW2L", 2, keep)
    gWR = Stream(nc, "gWR", 1, keep)
    sXC = Stream(nc, "sXC", 1, keep)
    gWPP = Stream(nc, "gWPP", 1, keep)
    sW1L = Stream(nc, "sW1L", 2, keep)
    sW2L = Stream(nc, "sW2L", 2, keep)
    gXO = Stream(nc, "gXO", 2, keep)

    blk_g = nc.Block()
    blk = blk_g.__enter__()

    def _(sy):
        for dc in range(NDC):
            sX1.put(sy, [(x2acc[dc][:], x1ownD[dc * 128:(dc + 1) * 128, :])])
        sSM.put(sy, [(br8[:], br8D), (b1a[:], b1aD), (b2a[:], b2aD), (identb[:], idD)])
    blk.sync(_)

    def _(g):
        gWR.put(g, [(wrsb[:].rearrange("p (dc e) -> p dc e", dc=8),
                     wrD.rearrange("(dc p) e -> p dc e", p=128))])
    blk.gpsimd(_)

    def _(v):
        v.memset(epsb[:], EPS)
        VV.inc(v.memset(ones128[:], 1.0))
    blk.vector(_)
    vv_ones = VV.n

    def _(v):
        sX1.wait(v)
        for dc in range(NDC):
            VV.inc(v.tensor_copy(xn1bf[dc][:], x2acc[dc][:]))
    blk.vector(_)
    vv_x1b = VV.n

    fA = scope("fA", [128, 2048], F32)
    fB = scope("fB", [128, 2048], F32)
    fC = scope("fC", [128, 2048], F32)
    sqd = [scope(f"sqd{i}", [128, 2048], BF16) for i in range(2)]
    mu_bf = scope("mu_bf", [128, 2048], BF16)
    rstd_bf = scope("rstd_bf", [128, 2048], BF16)
    yld = [fC]

    tt_b = TT.n
    vv_b = VV.n

    def _(v):
        for dc in range(NDC):
            if dc >= 2:
                v.wait_ge(TT.h, tt_b + dc - 1)
            v.wait_ge(VV.h, VV.n)
            VV.inc(v.tensor_mul(sqd[dc % 2][:], xn1bf[dc][:], xn1bf[dc][:]))
    blk.vector(_)

    def _(t):
        t.wait_ge(VV.h, vv_ones)
        for dc in range(NDC):
            t.wait_ge(VV.h, vv_b + dc + 1)
            for q4 in range(4):
                t.matmul(PA[:, q4 * 512:(q4 + 1) * 512], ones128[:],
                         xn1bf[dc][:, q4 * 512:(q4 + 1) * 512],
                         start=(dc == 0), stop=(dc == NDC - 1))
            for q4 in range(4):
                ins = t.matmul(PB[:, q4 * 512:(q4 + 1) * 512], ones128[:],
                               sqd[dc % 2][:, q4 * 512:(q4 + 1) * 512],
                               start=(dc == 0), stop=(dc == NDC - 1))
            TT.inc(ins)
    blk.tensor(_)
    tt_s1 = TT.n

    def _(v):
        v.wait_ge(TT.h, tt_s1)
        VV.inc(v.tensor_scalar_mul(fA[:], PA[:], 1.0 / D))
        VV.inc(v.tensor_scalar_mul(fB[:], PB[:], 1.0 / D))
        v.wait_ge(VV.h, VV.n)
        VV.inc(v.tensor_mul(fC[:], fA[:], fA[:]))
        v.wait_ge(VV.h, VV.n)
        VV.inc(v.tensor_sub(fB[:], fB[:], fC[:]))
    blk.vector(_)
    vv_var = VV.n

    def _(s):
        s.wait_ge(VV.h, vv_var)
        SS.inc(s.activation(fC[:], fB[:], AF.Sqrt, bias=epsb[:]))
    blk.scalar(_)
    ss_sd = SS.n

    def _(v):
        v.wait_ge(SS.h, ss_sd)
        VV.inc(v.reciprocal(fB[:], fC[:]))
        VV.inc(v.tensor_copy(mu_bf[:], fA[:]))
        v.wait_ge(VV.h, VV.n)
        VV.inc(v.tensor_copy(rstd_bf[:], fB[:]))
    blk.vector(_)

    def _(sy, vv_pre=VV.n):
        sy.wait_ge(VV.h, vv_pre)
        sy.wait_ge(SS.h, SS.n)
        for dc in range(NDC):
            if dc >= 1:
                sy.wait_ge(VV.h, vv_pre + 5 * dc)
            sY.put(sy, [(yld[0][:], yownD[dc * 128:(dc + 1) * 128, :])])
    blk.sync(_)

    def _(v):
        for dc in range(NDC):
            v.wait_ge(VV.h, VV.n)
            VV.inc(v.tensor_sub(xn1bf[dc][:], xn1bf[dc][:], mu_bf[:]))
            v.wait_ge(VV.h, VV.n)
            VV.inc(v.tensor_mul(xn1bf[dc][:], xn1bf[dc][:], rstd_bf[:]))
            v.wait_ge(VV.h, VV.n)
            VV.inc(v.tensor_add(x2acc[dc][:], x2acc[dc][:], xn1bf[dc][:]))
            sY.wait(v, dc)
            v.wait_ge(VV.h, VV.n)
            VV.inc(v.tensor_add(x2acc[dc][:], x2acc[dc][:], yld[0][:]))
            VV.inc(v.engine_nop())
    blk.vector(_)
    vv_xny = VV.n

    close(7)

    def _(t):
        gWR.wait(t)
        t.wait_ge(VV.h, vv_xny)
        for q4 in range(4):
            for dc in range(NDC):
                ins = t.matmul(PA[0:8, q4 * 512:(q4 + 1) * 512],
                               wrsb[:, dc * 8:(dc + 1) * 8],
                               xn1bf[dc][:, q4 * 512:(q4 + 1) * 512],
                               start=(dc == 0), stop=(dc == NDC - 1))
        TT.inc(ins)
    blk.tensor(_)
    tt_sc = TT.n

    def _(s):
        s.wait_ge(TT.h, tt_sc)
        sSM.wait(s)
        SS.inc(s.activation(scoresbf[:], PA[0:8, :], AF.Identity, bias=br8[:]))
    blk.scalar(_)
    ss_sc = SS.n

    def _(t):
        t.wait_ge(SS.h, ss_sc)
        for tcw in range(16):
            ins = t.matmul(PB[:, tcw * 128:tcw * 128 + 8],
                           scoresbf[:, tcw * 128:(tcw + 1) * 128],
                           identb[0:8, 0:8], start=True, stop=True)
        TT.inc(ins)
    blk.tensor(_)
    tt_tr = TT.n

    def _(v):
        v.wait_ge(TT.h, tt_tr)
        for tcw in range(16):
            sl = slice(tcw * 8, (tcw + 1) * 8)
            ps = PB[:, tcw * 128:tcw * 128 + 8]
            VV.inc(v.tensor_copy(s_tm[:, sl], ps))
        for tcw in range(16):
            sl = slice(tcw * 8, (tcw + 1) * 8)
            v.wait_ge(VV.h, VV.n)
            VV.inc(v.reduce_max(m16[:, tcw * 4:tcw * 4 + 1], s_tm[:, sl], axis=AX.X))
            v.wait_ge(VV.h, VV.n)
            VV.inc(v.tensor_scalar(sA[:, sl], s_tm[:, sl], m16[:, tcw * 4:tcw * 4 + 1],
                                   None, ALU.is_ge))
            v.wait_ge(VV.h, VV.n)
            VV.inc(v.scalar_tensor_tensor(sB[:, sl], sA[:, sl], -1e30, s_tm[:, sl],
                                          ALU.mult, ALU.add))
            v.wait_ge(VV.h, VV.n)
            VV.inc(v.reduce_max(m16[:, tcw * 4 + 1:tcw * 4 + 2], sB[:, sl], axis=AX.X))
            v.wait_ge(VV.h, VV.n)
            VV.inc(v.tensor_scalar(sA[:, sl], s_tm[:, sl], m16[:, tcw * 4 + 1:tcw * 4 + 2],
                                   None, ALU.is_ge))
            v.wait_ge(VV.h, VV.n)
            VV.inc(v.tensor_scalar(sB[:, sl], s_tm[:, sl], m16[:, tcw * 4:tcw * 4 + 1],
                                   None, ALU.subtract))
    blk.vector(_)
    vv_t2 = VV.n

    def _(s):
        s.wait_ge(VV.h, vv_t2)
        SS.inc(s.activation(sB[:], sB[:], AF.Exp))
    blk.scalar(_)
    ss_exp = SS.n

    def _(v):
        v.wait_ge(SS.h, ss_exp)
        for tcw in range(16):
            sl = slice(tcw * 8, (tcw + 1) * 8)
            v.wait_ge(VV.h, VV.n)
            VV.inc(v.tensor_mul(sB[:, sl], sB[:, sl], sA[:, sl]))
            v.wait_ge(VV.h, VV.n)
            VV.inc(v.reduce_sum(m16[:, tcw * 4 + 2:tcw * 4 + 3], sB[:, sl], axis=AX.X))
            v.wait_ge(VV.h, VV.n)
            VV.inc(v.reciprocal(m16[:, tcw * 4 + 3:tcw * 4 + 4], m16[:, tcw * 4 + 2:tcw * 4 + 3]))
            v.wait_ge(VV.h, VV.n)
            VV.inc(v.tensor_scalar(w_tmb[:, sl], sB[:, sl], m16[:, tcw * 4 + 3:tcw * 4 + 4],
                                   None, ALU.mult))
    blk.vector(_)
    vv_w = VV.n

    w1r_r = w1rD
    ss_prev_evac = SS.n
    vv_prev = VV.n
    tt_prev = TT.n

    for i in range(NEXP):
        def _(t, i=i, vv_w=vv_w, vv_prev=vv_prev):
            t.wait_ge(VV.h, max(vv_w, vv_prev))
            for tcw in range(16):
                ins = t.matmul(PA[0:1, tcw * 128:(tcw + 1) * 128],
                               w_tmb[:, tcw * 8 + i:tcw * 8 + i + 1], identb[:],
                               start=True, stop=True)
            TT.inc(ins)
        blk.tensor(_)
        tt_row = TT.n

        def _(v, tt_row=tt_row):
            v.wait_ge(TT.h, tt_row)
            VV.inc(v.tensor_copy(wrow, PA[0:1, 0:2048]))
        blk.vector(_)
        vv_row = VV.n

        def _(t, vv_row=vv_row):
            t.wait_ge(VV.h, vv_row)
            for q4 in range(4):
                ins = t.matmul(PB[0:128, q4 * 512:(q4 + 1) * 512],
                               ones128[0:1, 0:128],
                               h1m[0:1, q4 * 512:(q4 + 1) * 512],
                               start=True, stop=True)
            TT.inc(ins)
        blk.tensor(_)
        tt_bc = TT.n

        def _(v, i=i, tt_bc=tt_bc):
            v.wait_ge(TT.h, tt_bc)
            VV.inc(v.tensor_copy(wbv[:], PB[:, 0:2048]))
        blk.vector(_)
        vv_bc = VV.n

        for tb in range(4):
            ts = tb * 512
            tt_f1 = TT.n
            ss_f1 = SS.n

            w1base = len(sW1L.marks)

            def _(g, i=i, tt_f1=tt_f1, vv_pre=VV.n):
                g.wait_ge(VV.h, min(vv_pre, vv_w))
                for hc in range(32):
                    if hc >= 2:
                        g.wait_ge(TT.h, tt_f1 + hc - 1)
                    gW1L.put(g, [(w1c[hc % 2][:], w1r_r[i, hc])])
            blk.gpsimd(_)

            def _(t, i=i, ts=ts, ss_f1=ss_f1, vv_bc=vv_bc, w1base=w1base,
                  ss_pe=ss_prev_evac):
                t.wait_ge(VV.h, vv_bc)
                t.wait_ge(SS.h, ss_pe)
                for hc in range(32):
                    gW1L.wait(t, w1base + hc)
                    if hc >= 1:
                        t.wait_ge(SS.h, ss_f1 + hc)
                    for dc in range(NDC):
                        ins = t.matmul(PA[:, 0:512],
                                       w1c[hc % 2][:, dc * 128:(dc + 1) * 128],
                                       xn1bf[dc][:, ts:ts + 512],
                                       start=(dc == 0), stop=(dc == NDC - 1))
                    TT.inc(ins)
            blk.tensor(_)

            def _(s, i=i, tt_f1=tt_f1):
                for hc in range(32):
                    s.wait_ge(TT.h, tt_f1 + hc + 1)
                    SS.inc(s.activation(h1m[:, hc * 512:(hc + 1) * 512], PA[:, 0:512],
                                        AF.Relu, bias=b1a[:, i * 32 + hc:i * 32 + hc + 1]))
            blk.scalar(_)
            ss_f1_done = SS.n

            tt_f2 = TT.n

            w2base = len(sW2L.marks)

            def _(g, i=i, tt_f2=tt_f2):
                for hc in range(32):
                    if hc >= 2:
                        g.wait_ge(TT.h, tt_f2 + hc - 1)
                    gW2L.put(g, [(w2c[hc % 2][:], w2D[i, hc * 128:(hc + 1) * 128, :])])
            blk.gpsimd(_)

            def _(t, i=i, ss_f1_done=ss_f1_done, ss_prev_evac=ss_prev_evac,
                  w2base=w2base):
                t.wait_ge(SS.h, max(ss_f1_done, ss_prev_evac))
                for hc in range(32):
                    gW2L.wait(t, w2base + hc)
                    for ec in range(8):
                        ps = PA if ec < 4 else PB
                        ins = t.matmul(ps[:, (ec % 4) * 512:(ec % 4 + 1) * 512],
                                       w2c[hc % 2][:, ec * 128:(ec + 1) * 128],
                                       h1m[:, hc * 512:(hc + 1) * 512],
                                       start=(hc == 0), stop=(hc == 31))
                    TT.inc(ins)
            blk.tensor(_)
            tt_f2_done = TT.n

            def _(s, i=i, tt_f2_done=tt_f2_done, vv_pre=VV.n):
                s.wait_ge(TT.h, tt_f2_done)
                s.wait_ge(VV.h, vv_pre)
                for ec in range(8):
                    if ec >= 2:
                        s.wait_ge(VV.h, vv_pre + 2 * (ec - 1))
                    ps = PA if ec < 4 else PB
                    SS.inc(s.activation(etmp[ec % 2][:],
                                        ps[:, (ec % 4) * 512:(ec % 4 + 1) * 512],
                                        AF.Identity,
                                        bias=b2a[:, i * 8 + ec:i * 8 + ec + 1]))
            blk.scalar(_)
            ss_evac0 = SS.n - 8

            def _(v, i=i, ts=ts, ss_evac0=ss_evac0):
                for ec in range(8):
                    v.wait_ge(SS.h, ss_evac0 + ec + 1)
                    v.wait_ge(VV.h, VV.n)
                    VV.inc(v.tensor_mul(etmp[ec % 2][:], etmp[ec % 2][:], wbv[:, ts:ts + 512]))
                    v.wait_ge(VV.h, VV.n)
                    VV.inc(v.tensor_add(x2acc[ec][:, ts:ts + 512],
                                        x2acc[ec][:, ts:ts + 512], etmp[ec % 2][:]))
            blk.vector(_)
            ss_prev_evac = SS.n
            vv_prev = VV.n

        tt_prev = TT.n

    vv_fin = VV.n

    def _(g):
        g.wait_ge(VV.h, vv_fin)
        for dc in range(NDC):
            gXO.put(g, [(x2ownD[dc * 128:(dc + 1) * 128, :], x2acc[dc][:])])
        gXO.wait(g)
    blk.gpsimd(_)

    blk_g.__exit__(None, None, None)
    close(len(guards))
    pg2.__exit__(None, None, None)
    pg1.__exit__(None, None, None)
    for g in reversed(keep):
        g.__exit__(None, None, None)
    return nc


def prep_moe_inmaps(inputs, x1, y):
    wr = np.asarray(inputs["L1_ffn_Wr"], np.float32)
    br = np.asarray(inputs["L1_ffn_br"], np.float32)
    w1 = np.asarray(inputs["L1_ffn_W1"], np.float32)
    b1 = np.asarray(inputs["L1_ffn_b1"], np.float32)
    w2 = np.asarray(inputs["L1_ffn_W2"], np.float32)
    b2 = np.asarray(inputs["L1_ffn_b2"], np.float32)
    w1r = _c(w1.reshape(8, 8, 128, 32, 128).transpose(0, 3, 2, 1, 4)
             .reshape(8, 32, 128, 1024))
    b1a = _c(np.concatenate([b1[i].reshape(32, 128).T for i in range(8)], axis=1))
    b2a = _c(np.concatenate([b2[i].reshape(8, 128).T for i in range(8)], axis=1))
    shared = dict(wr=_c(wr), br8=_c(br.reshape(8, 1)), w1r=w1r, w2m=_c(w2),
                  b1a=b1a, b2a=b2a,
                  identb=_c(np.eye(128, dtype=np.float32).astype(BF)))
    in_maps = []
    for core in range(8):
        b, hh = core // 2, core % 2
        own = slice(hh * TOWN, (hh + 1) * TOWN)
        m = dict(shared)
        m.update(x1own=_c(x1[b, own].T), yown=_c(y[b, own].T))
        in_maps.append(m)
    return in_maps


def run_moe(inputs, x1, y):
    nc = build_moe()
    in_maps = prep_moe_inmaps(inputs, x1, y)
    res = run_bass_kernel_spmd(nc, in_maps, core_ids=list(range(8)))
    x2 = np.zeros((B, T, D), dtype=np.float32)
    for core in range(8):
        b, hh = core // 2, core % 2
        x2[b, hh * TOWN:(hh + 1) * TOWN] = res.results[core]["x2own"].T
    return x2



def build_fused():
    nc = bass.Bass("TRN2", target_bir_lowering=False)

    x0fm = nc.dram_tensor("x0fm", [D, T], F32, kind="ExternalInput").ap()
    wqp = nc.dram_tensor("wqp", [D, D], F32, kind="ExternalInput").ap()
    wkp = nc.dram_tensor("wkp", [D, D], F32, kind="ExternalInput").ap()
    wvD = nc.dram_tensor("wv", [D, D], F32, kind="ExternalInput").ap()
    woD = nc.dram_tensor("wo", [D, D], F32, kind="ExternalInput").ap()
    w1t = nc.dram_tensor("w1t", [D, HFF], F32, kind="ExternalInput").ap()
    w1sD = nc.dram_tensor("w1s", [HFF, D], F32, kind="ExternalInput").ap()
    w2t = nc.dram_tensor("w2t", [HFF, D], F32, kind="ExternalInput").ap()
    w2sD = nc.dram_tensor("w2s", [D, HFF], F32, kind="ExternalInput").ap()
    b1tD = nc.dram_tensor("b1t", [128, 32], F32, kind="ExternalInput").ap()
    b2tD = nc.dram_tensor("b2t", [128, 8], F32, kind="ExternalInput").ap()
    cos2d = nc.dram_tensor("cos2", [128, T], BF16, kind="ExternalInput").ap()
    sin2d = nc.dram_tensor("sin2", [128, T], BF16, kind="ExternalInput").ap()
    wktD = nc.dram_tensor("wkt", [D, 64], F32, kind="ExternalInput").ap()
    wvtD = nc.dram_tensor("wvt", [D, D], F32, kind="ExternalInput").ap()
    wqtD = nc.dram_tensor("wqt", [D, 64], F32, kind="ExternalInput").ap()
    bkD = nc.dram_tensor("bk", [64, 1], F32, kind="ExternalInput").ap()
    bqD = nc.dram_tensor("bq", [64, 1], F32, kind="ExternalInput").ap()
    bkbD = nc.dram_tensor("bkb", [128, 64], F32, kind="ExternalInput").ap()
    bvbD = nc.dram_tensor("bvb", [128, D], F32, kind="ExternalInput").ap()
    gbD = nc.dram_tensor("gb", [128, 8], F32, kind="ExternalInput").ap()
    btD = nc.dram_tensor("bt", [128, 8], F32, kind="ExternalInput").ap()
    mUeD = nc.dram_tensor("mUe", [128, 128], BF16, kind="ExternalInput").ap()
    mLeD = nc.dram_tensor("mLe", [128, 128], BF16, kind="ExternalInput").ap()
    mUnD = nc.dram_tensor("mUn", [128, 128], BF16, kind="ExternalInput").ap()
    idD = nc.dram_tensor("identb", [128, 128], BF16, kind="ExternalInput").ap()
    wrD = nc.dram_tensor("wr", [D, 8], F32, kind="ExternalInput").ap()
    br8D = nc.dram_tensor("br8", [8, 1], F32, kind="ExternalInput").ap()
    w1rD = nc.dram_tensor("w1r", [8, 32, 128, 1024], F32, kind="ExternalInput").ap()
    w2D = nc.dram_tensor("w2m", [8, HME, D], F32, kind="ExternalInput").ap()
    b1aD = nc.dram_tensor("b1a", [128, 256], F32, kind="ExternalInput").ap()
    b2aD = nc.dram_tensor("b2a", [128, 64], F32, kind="ExternalInput").ap()
    sel0D = nc.dram_tensor("sel0", [128, 1], F32, kind="ExternalInput").ap()
    sel1D = nc.dram_tensor("sel1", [128, 1], F32, kind="ExternalInput").ap()
    x2ownD = nc.dram_tensor("x2own", [D, TOWN], F32, kind="ExternalOutput").ap()

    h1d = nc.dram_tensor("h1d", [32, 128, TOWN], BF16, kind="Internal").ap()
    w1sd = nc.dram_tensor("w1sd", [32, 128, 1024], BF16, kind="Internal").ap()
    w2sd = nc.dram_tensor("w2sd", [32, 128, 1024], BF16, kind="Internal").ap()
    x1d = nc.dram_tensor("x1d", [D, TOWN], F32, kind="Internal").ap()
    x1g = nc.dram_tensor("x1g", [2, D, TOWN], F32, kind="Internal").ap()
    vd = nc.dram_tensor("vd", [32, 128, D], BF16, kind="Internal").ap()
    w1bd = nc.dram_tensor("w1bd", [8, 32, 128, 1024], BF16, kind="Internal").ap()
    w2bd = nc.dram_tensor("w2bd", [8, HME, D], BF16, kind="Internal").ap()
    xn1d = nc.dram_tensor("xn1d", [NDC, 128, TOWN], BF16, kind="Internal").ap()
    wbv8d = nc.dram_tensor("wbv8d", [128, 8 * TOWN], BF16, kind="Internal").ap()

    ones128 = nc.alloc_sbuf_tensor("ones128", [128, 128], BF16)
    epsb = nc.alloc_sbuf_tensor("epsb", [128, 1], F32)
    identb = nc.alloc_sbuf_tensor("identS", [128, 128], BF16)

    guards = []

    def scope(name, shape, dt, side=None):
        g = nc.sbuf_tensor(name, shape, dt, side=side) if side else nc.sbuf_tensor(name, shape, dt)
        t = g.__enter__()
        guards.append(g)
        return t

    def close(n):
        for _ in range(n):
            guards.pop().__exit__(None, None, None)

    rguards = []

    def rscope(name, shape, dt):
        g = nc.sbuf_tensor(name, shape, dt, side="right")
        t = g.__enter__()
        rguards.append(g)
        return t

    def rclose(n):
        for _ in range(n):
            rguards.pop().__exit__(None, None, None)

    pg1 = nc.psum_tensor("PA", [128, 2048], F32)
    PA = pg1.__enter__()
    pg2 = nc.psum_tensor("PB", [128, 2048], F32)
    PB = pg2.__enter__()

    keep = []
    sg = [nc.semaphore(n) for n in ["sTT", "sVV", "sSS", "sCC"]]
    TT, VV, SS, CC = [Sem(g.__enter__()) for g in sg]
    keep.extend(sg)

    stB = Stream(nc, "stB", 1, keep)
    gX0 = Stream(nc, "gX0", 1, keep)
    sWT = Stream(nc, "sWT", 2, keep)
    sWS = Stream(nc, "sWS", 2, keep)
    gW1 = Stream(nc, "gW1", 2, keep)
    gW2 = Stream(nc, "gW2", 2, keep)
    sF1 = Stream(nc, "sF1", 2, keep)
    gH1 = Stream(nc, "gH1", 2, keep)
    sH1 = Stream(nc, "sH1", 1, keep)
    sF2 = Stream(nc, "sF2", 2, keep)
    sTAB = Stream(nc, "sTAB", 1, keep)
    gWG = Stream(nc, "gWG", 1, keep)
    gWOB = Stream(nc, "gWOB", 1, keep)
    sX0 = Stream(nc, "sX0", 2, keep)
    gX1D = Stream(nc, "gX1D", 2, keep)
    gX1 = Stream(nc, "gX1", 1, keep)
    gWP = Stream(nc, "gWP", 1, keep)
    sMS = Stream(nc, "sMS", 1, keep)
    gVD = Stream(nc, "gVD", 2, keep)
    sVT = Stream(nc, "sVT", 2, keep)
    gW1L = Stream(nc, "gW1L", 2, keep)
    gW2L = Stream(nc, "gW2L", 2, keep)
    gWR = Stream(nc, "gWR", 1, keep)
    sXC = Stream(nc, "sXC", 1, keep)
    gWPP = Stream(nc, "gWPP", 1, keep)
    sW1L = Stream(nc, "sW1L", 2, keep)
    sW2L = Stream(nc, "sW2L", 2, keep)
    gXO = Stream(nc, "gXO", 2, keep)
    sX1R = Stream(nc, "sX1R", 2, keep)
    sWRL = Stream(nc, "sWRL", 1, keep)
    sXNS = Stream(nc, "sXNS", 1, keep)
    sXNL = Stream(nc, "sXNL", 1, keep)
    sWBS = Stream(nc, "sWBS", 1, keep)
    sWBL = Stream(nc, "sWBL", 1, keep)

    blk_g = nc.Block()
    blk = blk_g.__enter__()

    xn = [rscope(f"xn{i}", [128, T], BF16) for i in range(NDC)]
    ffnacc = [scope(f"ffa{i}", [128, TOWN], BF16) for i in range(NDC)]
    hs_ofm = scope("hs_ofm", [128, 32 * 512], BF16)
    alpha1 = scope("alpha1", [128, 32], F32)
    alpha2 = scope("alpha2", [128, 8], F32)
    b1sb = scope("b1sb", [128, 32], F32)
    b2sb = scope("b2sb", [128, 8], F32)
    nA_base = len(guards)

    def _(sy):
        stB.put(sy, [(b1sb[:], b1tD), (b2sb[:], b2tD)])
    blk.sync(_)

    def _(g):
        for dc in range(NDC):
            gX0.put(g, [(xn[dc][:], x0fm[dc * 128:(dc + 1) * 128, :])])
    blk.gpsimd(_)

    def _(v):
        v.memset(epsb[:], EPS)
        VV.inc(v.memset(ones128[:], 1.0))
    blk.vector(_)
    vv_ones = VV.n

    wtmp = [scope(f"wtmp{i}", [128, 1024], F32) for i in range(2)]
    wstr = [scope(f"wstr{i}", [128, 4096], F32) for i in range(2)]
    wsgn = [scope(f"wsgn{i}", [128, 1024], BF16) for i in range(2)]

    w1t_r = w1t.rearrange("(dc p) h -> p dc h", p=128)

    ss0, vv0 = SS.n, VV.n

    def _(sy):
        for hc in range(32):
            if hc >= 2:
                sy.wait_ge(SS.h, ss0 + hc - 1)
                sy.wait_ge(VV.h, vv0 + hc - 1)
            sWT.put(sy, [(wtmp[hc % 2][:].rearrange("p (dc h) -> p dc h", dc=8),
                          w1t_r[:, :, hc * 128:(hc + 1) * 128])])
            sWS.put(sy, [(wstr[hc % 2][:, 0:1024], w1sD[hc * 128:(hc + 1) * 128, :])])
    blk.sync(_)

    def _(g):
        for hc in range(32):
            g.wait_ge(SS.h, ss0 + hc + 1)
            gW1.put(g, [(w1sd[hc], wsgn[hc % 2][:])])
    blk.gpsimd(_)

    def _(s):
        for hc in range(32):
            sWT.wait(s, hc)
            if hc >= 2:
                gW1.wait(s, hc - 2)
            SS.inc(s.activation(wsgn[hc % 2][:], wtmp[hc % 2][:], AF.Sign))
    blk.scalar(_)

    def _(v):
        for hc in range(32):
            sWS.wait(v, hc)
            VV.inc(v.reduce_sum(alpha1[:, hc:hc + 1], wstr[hc % 2][:, 0:1024],
                                axis=AX.X, apply_absolute_value=True))
    blk.vector(_)

    ss1, vv1 = SS.n, VV.n

    def _(sy):
        sy.wait_ge(SS.h, ss1)
        sy.wait_ge(VV.h, vv1)
        for hc in range(32):
            if hc >= 2:
                sy.wait_ge(SS.h, ss1 + hc - 1)
            sWT.put(sy, [(wtmp[hc % 2][:], w2t[hc * 128:(hc + 1) * 128, :])])
        for ec in range(8):
            if ec >= 2:
                sy.wait_ge(VV.h, vv1 + ec - 1)
            sWS.put(sy, [(wstr[ec % 2][:], w2sD[ec * 128:(ec + 1) * 128, :])])
    blk.sync(_)

    def _(g):
        for hc in range(32):
            g.wait_ge(SS.h, ss1 + hc + 1)
            gW2.put(g, [(w2sd[hc], wsgn[hc % 2][:])])
    blk.gpsimd(_)

    def _(s):
        gW1.wait(s)
        for hc in range(32):
            sWT.wait(s, 32 + hc)
            if hc >= 2:
                gW2.wait(s, hc - 2)
            SS.inc(s.activation(wsgn[hc % 2][:], wtmp[hc % 2][:], AF.Sign))
    blk.scalar(_)

    def _(v):
        for ec in range(8):
            sWS.wait(v, 32 + ec)
            VV.inc(v.reduce_sum(alpha2[:, ec:ec + 1], wstr[ec % 2][:],
                                axis=AX.X, apply_absolute_value=True))
        v.wait_ge(VV.h, VV.n)
        VV.inc(v.tensor_scalar_mul(alpha1[:], alpha1[:], 1.0 / D))
        VV.inc(v.tensor_scalar_mul(alpha2[:], alpha2[:], 1.0 / HFF))
    blk.vector(_)
    vv_alpha = VV.n

    close(len(guards) - nA_base)

    fA = scope("fA", [128, 2048], F32)
    fB = scope("fB", [128, 2048], F32)
    fC = scope("fC", [128, 2048], F32)
    sqd = [scope(f"sqd{i}", [128, 2048], BF16) for i in range(2)]
    mu_bf = scope("mu_bf", [128, T], BF16)
    rstd_bf = scope("rstd_bf", [128, T], BF16)

    for tb in range(2):
        ts = tb * 2048
        tt_b = TT.n
        vv_b = VV.n

        def _(v, ts=ts, tt_b=tt_b, ss_w=SS.n):
            gX0.wait(v)
            v.wait_ge(SS.h, ss_w)
            gW1.wait(v)
            gW2.wait(v)
            for dc in range(NDC):
                if dc >= 2:
                    v.wait_ge(TT.h, tt_b + dc - 1)
                VV.inc(v.tensor_mul(sqd[dc % 2][:], xn[dc][:, ts:ts + 2048],
                                    xn[dc][:, ts:ts + 2048]))
        blk.vector(_)

        def _(t, ts=ts, vv_b=vv_b):
            t.wait_ge(VV.h, vv_ones)
            for dc in range(NDC):
                t.wait_ge(VV.h, vv_b + dc + 1)
                for q4 in range(4):
                    t.matmul(PA[:, q4 * 512:(q4 + 1) * 512], ones128[:],
                             xn[dc][:, ts + q4 * 512:ts + (q4 + 1) * 512],
                             start=(dc == 0), stop=(dc == NDC - 1))
                for q4 in range(4):
                    ins = t.matmul(PB[:, q4 * 512:(q4 + 1) * 512], ones128[:],
                                   sqd[dc % 2][:, q4 * 512:(q4 + 1) * 512],
                                   start=(dc == 0), stop=(dc == NDC - 1))
                TT.inc(ins)
        blk.tensor(_)
        tt_s1 = TT.n

        def _(v, tt_s1=tt_s1):
            v.wait_ge(TT.h, tt_s1)
            VV.inc(v.tensor_scalar_mul(fA[:], PA[:], 1.0 / D))
            VV.inc(v.tensor_scalar_mul(fB[:], PB[:], 1.0 / D))
            v.wait_ge(VV.h, VV.n)
            VV.inc(v.tensor_mul(fC[:], fA[:], fA[:]))
            v.wait_ge(VV.h, VV.n)
            VV.inc(v.tensor_sub(fB[:], fB[:], fC[:]))
        blk.vector(_)
        vv_var = VV.n

        def _(s, vv_var=vv_var):
            s.wait_ge(VV.h, vv_var)
            SS.inc(s.activation(fC[:], fB[:], AF.Sqrt, bias=epsb[:]))
        blk.scalar(_)
        ss_sd = SS.n

        def _(v, ts=ts, ss_sd=ss_sd):
            v.wait_ge(SS.h, ss_sd)
            VV.inc(v.reciprocal(fB[:], fC[:]))
            VV.inc(v.tensor_copy(mu_bf[:, ts:ts + 2048], fA[:]))
            v.wait_ge(VV.h, VV.n)
            VV.inc(v.tensor_copy(rstd_bf[:, ts:ts + 2048], fB[:]))
        blk.vector(_)

    def _(v):
        for dc in range(NDC):
            VV.inc(v.tensor_sub(xn[dc][:], xn[dc][:], mu_bf[:]))
            v.wait_ge(VV.h, VV.n)
            VV.inc(v.tensor_mul(xn[dc][:], xn[dc][:], rstd_bf[:]))
    blk.vector(_)
    vv_xn = VV.n

    close(7)

    w1b = [scope(f"w1b{i}", [128, 1024], BF16) for i in range(2)]
    h1t = [scope(f"h1t{i}", [128, TOWN], BF16) for i in range(2)]

    ttf, ssf = TT.n, SS.n

    def _(sy, ss_pre=SS.n):
        gW1.wait(sy)
        sy.wait_ge(VV.h, vv_xn)
        sy.wait_ge(SS.h, ss_pre)
        for hc in range(32):
            if hc >= 2:
                sy.wait_ge(TT.h, ttf + hc - 1)
            sF1.put(sy, [(w1b[hc % 2][:], w1sd[hc])])
    blk.sync(_)

    def _(t):
        t.wait_ge(VV.h, vv_xn)
        for hc in range(32):
            sF1.wait(t, hc)
            if hc >= 1:
                t.wait_ge(SS.h, ssf + hc)
            for q4 in range(4):
                for dc in range(NDC):
                    ins = t.matmul(PA[:, q4 * 512:(q4 + 1) * 512],
                                   w1b[hc % 2][:, dc * 128:(dc + 1) * 128],
                                   xn[dc][:, q4 * 512:(q4 + 1) * 512],
                                   start=(dc == 0), stop=(dc == NDC - 1))
            TT.inc(ins)
    blk.tensor(_)

    def _(g):
        for hc in range(32):
            g.wait_ge(SS.h, ssf + hc + 1)
            gH1.put(g, [(h1d[hc], h1t[hc % 2][:])])
    blk.gpsimd(_)

    def _(s):
        s.wait_ge(VV.h, vv_xn)
        stB.wait(s)
        s.wait_ge(VV.h, vv_alpha)
        for hc in range(32):
            s.wait_ge(TT.h, ttf + hc + 1)
            if hc >= 2:
                gH1.wait(s, hc - 2)
            SS.inc(s.activation(h1t[hc % 2][:], PA[:], AF.Gelu,
                                bias=b1sb[:, hc:hc + 1], scale=alpha1[:, hc:hc + 1]))
    blk.scalar(_)

    close(4)

    w2b = [scope(f"w2b{i}", [128, 1024], BF16) for i in range(2)]
    h1s = hs_ofm

    tt_prev_tb = TT.n
    ss_prev_tb = SS.n
    for tb in range(4):
        ts = tb * 512
        tt2 = TT.n

        def _(sy, ts=ts, tt2=tt2, tt_prev_tb=tt_prev_tb, tb=tb):
            gH1.wait(sy)
            gW2.wait(sy)
            sy.wait_ge(TT.h, tt_prev_tb)
            sH1.put(sy, [(h1s[:, grp * 8 * 512:(grp + 1) * 8 * 512].rearrange(
                              "p (hc t) -> p hc t", hc=8),
                          h1d[grp * 8:(grp + 1) * 8, :, ts:ts + 512].rearrange(
                              "hc p t -> p hc t"))
                         for grp in range(4)])
            for hc in range(32):
                if hc >= 2:
                    sy.wait_ge(TT.h, tt2 + hc - 1)
                sF2.put(sy, [(w2b[hc % 2][:], w2sd[hc])])
        blk.sync(_)

        def _(t, ss_prev_tb=ss_prev_tb, tb=tb):
            t.wait_ge(SS.h, ss_prev_tb)
            sH1.wait(t, tb)
            for hc in range(32):
                sF2.wait(t, 32 * tb + hc)
                for ec in range(8):
                    ps = PA if ec < 4 else PB
                    ins = t.matmul(ps[:, (ec % 4) * 512:(ec % 4 + 1) * 512],
                                   w2b[hc % 2][:, ec * 128:(ec + 1) * 128],
                                   h1s[:, hc * 512:(hc + 1) * 512],
                                   start=(hc == 0), stop=(hc == 31))
                TT.inc(ins)
        blk.tensor(_)
        tt_prev_tb = TT.n

        def _(s, ts=ts, tt_done=TT.n):
            s.wait_ge(TT.h, tt_done)
            stB.wait(s)
            for ec in range(8):
                ps = PA if ec < 4 else PB
                SS.inc(s.activation(ffnacc[ec][:, ts:ts + 512],
                                    ps[:, (ec % 4) * 512:(ec % 4 + 1) * 512],
                                    AF.Identity,
                                    bias=b2sb[:, ec:ec + 1], scale=alpha2[:, ec:ec + 1]))
        blk.scalar(_)
        ss_prev_tb = SS.n

    close(2)
    ss_attn0 = SS.n

    cos2 = scope("cos2s", [128, T], BF16)
    sin2 = scope("sin2s", [128, T], BF16)
    kfm = scope("kfm", [128, T], BF16)
    qfm = scope("qfm", [128, TOWN], BF16)
    raw = scope("raw", [128, 2048], BF16)
    tscr = scope("tscr", [128, 2048], BF16)
    v65 = scope("v65", [128, 32 * 130], BF16)
    pb_sb = scope("pb_sb", [128, 2048], BF16)
    recb = scope("recb", [64, 2048], BF16)
    rec1 = scope("rec1", [1, 2048], BF16)
    wgq = scope("wgq", [128, 1024], BF16)
    wgk = scope("wgk", [128, 1024], BF16)
    wgv = scope("wgv", [128, 1024], BF16)
    ofm = hs_ofm

    def _(sy, tt_pre=TT.n, ss_pre=SS.n):
        sy.wait_ge(TT.h, tt_pre)
        sy.wait_ge(SS.h, ss_pre)
        sTAB.put(sy, [(cos2[:], cos2d), (sin2[:], sin2d)])
    blk.sync(_)

    wqp_r = wqp.rearrange("(dc p) j -> p dc j", p=128)
    wkp_r = wkp.rearrange("(dc p) j -> p dc j", p=128)
    wv_r = wvD.rearrange("(dc p) j -> p dc j", p=128)

    def emit_rope(v, dst, doff, src, tab_off, tlen):
        tsl = slice(tab_off, tab_off + tlen)
        for e in (0, 64):
            o = e + 32
            ce = cos2[e:e + 32, tsl]
            se = sin2[e:e + 32, tsl]
            co = cos2[o:o + 32, tsl]
            so = sin2[o:o + 32, tsl]
            VV.inc(v.tensor_mul(tscr[e:e + 32, 0:tlen], src[o:o + 32, 0:tlen], so))
            VV.inc(v.tensor_mul(dst[e:e + 32, doff:doff + tlen], src[e:e + 32, 0:tlen], ce))
            v.wait_ge(VV.h, VV.n)
            VV.inc(v.tensor_sub(dst[e:e + 32, doff:doff + tlen],
                                dst[e:e + 32, doff:doff + tlen], tscr[e:e + 32, 0:tlen]))
            VV.inc(v.tensor_mul(tscr[o:o + 32, 0:tlen], src[e:e + 32, 0:tlen], se))
            VV.inc(v.tensor_mul(dst[o:o + 32, doff:doff + tlen], src[o:o + 32, 0:tlen], co))
            v.wait_ge(VV.h, VV.n)
            VV.inc(v.tensor_add(dst[o:o + 32, doff:doff + tlen],
                                dst[o:o + 32, doff:doff + tlen], tscr[o:o + 32, 0:tlen]))

    tt_group_end = TT.n
    vv_group_end = VV.n

    for c in range(8):
        def _(g, tt_group_end=tt_group_end, c=c):
            g.wait_ge(TT.h, tt_group_end)
            gWG.put(g, [(wgq[:].rearrange("p (dc j) -> p dc j", dc=8),
                         wqp_r[:, :, c * 128:(c + 1) * 128]),
                        (wgk[:].rearrange("p (dc j) -> p dc j", dc=8),
                         wkp_r[:, :, c * 128:(c + 1) * 128]),
                        (wgv[:].rearrange("p (dc j) -> p dc j", dc=8),
                         wv_r[:, :, c * 128:(c + 1) * 128])])
            gWPP.put(g, [(w1bd[c], w1rD[c])])
            gWPP.put(g, [(w2bd[c], w2D[c])])
        blk.gpsimd(_)

        def _(t, vv_group_end=vv_group_end, c=c):
            gWG.wait(t, c)
            t.wait_ge(VV.h, max(vv_xn, vv_group_end))
            t.wait_ge(SS.h, ss_attn0)
            for q4 in range(4):
                for dc in range(NDC):
                    ins = t.matmul(PA[:, q4 * 512:(q4 + 1) * 512],
                                   wgq[:, dc * 128:(dc + 1) * 128],
                                   xn[dc][:, q4 * 512:(q4 + 1) * 512],
                                   start=(dc == 0), stop=(dc == NDC - 1))
            TT.inc(ins)
        blk.tensor(_)
        tt_q = TT.n

        def _(v, tt_q=tt_q):
            v.wait_ge(TT.h, tt_q)
            sTAB.wait(v)
            VV.inc(v.tensor_copy(raw[:], PA[:]))
            v.wait_ge(VV.h, VV.n)
            emit_rope(v, qfm, 0, raw, 0, 2048)
        blk.vector(_)

        for th in range(2):
            ts = th * 2048

            def _(t, ts=ts, vv_pre=VV.n):
                t.wait_ge(VV.h, vv_pre)
                for q4 in range(4):
                    for dc in range(NDC):
                        ins = t.matmul(PB[:, q4 * 512:(q4 + 1) * 512],
                                       wgk[:, dc * 128:(dc + 1) * 128],
                                       xn[dc][:, ts + q4 * 512:ts + (q4 + 1) * 512],
                                       start=(dc == 0), stop=(dc == NDC - 1))
                TT.inc(ins)
            blk.tensor(_)
            tt_k = TT.n

            def _(v, ts=ts, tt_k=tt_k):
                v.wait_ge(TT.h, tt_k)
                VV.inc(v.tensor_copy(raw[:], PB[:]))
                v.wait_ge(VV.h, VV.n)
                emit_rope(v, kfm, ts, raw, ts, 2048)
            blk.vector(_)

        def _(v, tt_group_end=tt_group_end):
            v.wait_ge(TT.h, tt_group_end)
            VV.inc(v.memset(v65[:], 1.0))
        blk.vector(_)
        vv_ms = VV.n

        tt_v0 = TT.n
        vv_v0 = VV.n

        def _(t, tt_v0=tt_v0, vv_v0=vv_v0, vv_ms=vv_ms):
            t.wait_ge(VV.h, vv_ms)
            for tc in range(32):
                if tc >= 2:
                    t.wait_ge(VV.h, vv_v0 + tc - 1)
                po = (tc % 2) * 512
                for dc in range(NDC):
                    ins = t.matmul(PA[:, po:po + 128],
                                   xn[dc][:, tc * 128:(tc + 1) * 128],
                                   wgv[:, dc * 128:(dc + 1) * 128],
                                   start=(dc == 0), stop=(dc == NDC - 1))
                TT.inc(ins)
        blk.tensor(_)

        def _(v, tt_v0=tt_v0, vv_ms=vv_ms):
            v.wait_ge(VV.h, vv_ms)
            for tc in range(32):
                v.wait_ge(TT.h, tt_v0 + tc + 1)
                po = (tc % 2) * 512
                v.tensor_copy(v65[:, tc * 130:tc * 130 + 64], PA[:, po:po + 64])
                VV.inc(v.tensor_copy(v65[:, tc * 130 + 65:tc * 130 + 129],
                                     PA[:, po + 64:po + 128]))
        blk.vector(_)

        for hh in range(2):
            hb = hh * 64
            vcol = hh * 65

            tt_h0 = TT.n
            ss_h0 = SS.n
            vv_pre_head = VV.n

            def _(t, tt_h0=tt_h0, ss_h0=ss_h0, vcol=vcol, hb=hb,
                  vv_pre_head=vv_pre_head):
                t.wait_ge(VV.h, vv_pre_head)
                t.wait_ge(SS.h, ss_h0)
                for kc in range(32):
                    for h2 in range(2):
                        if kc >= 1:
                            t.wait_ge(SS.h, ss_h0 + 2 * (kc - 1) + h2 + 1)
                        for q4 in range(2):
                            qo = h2 * 1024 + q4 * 512
                            ins = t.matmul(PA[:, qo:qo + 512],
                                           kfm[hb:hb + 64, kc * 128:(kc + 1) * 128],
                                           qfm[hb:hb + 64, qo:qo + 512],
                                           start=True, stop=True)
                        TT.inc(ins)
                    for h2 in range(2):
                        t.wait_ge(SS.h, ss_h0 + 2 * kc + h2 + 1)
                        for q4 in range(2):
                            qo = h2 * 1024 + q4 * 512
                            ins = t.matmul(PB[0:65, qo:qo + 512],
                                           v65[:, kc * 130 + vcol:kc * 130 + vcol + 65],
                                           pb_sb[:, qo:qo + 512],
                                           start=(kc == 0), stop=(kc == 31))
                        TT.inc(ins)
            blk.tensor(_)
            tt_av_last = TT.n

            def _(s, tt_h0=tt_h0):
                for kc in range(32):
                    for h2 in range(2):
                        s.wait_ge(TT.h, tt_h0 + 4 * kc + h2 + 1)
                        SS.inc(s.activation(pb_sb[:, h2 * 1024:(h2 + 1) * 1024],
                                            PA[:, h2 * 1024:(h2 + 1) * 1024],
                                            AF.Exp, scale=0.125))
            blk.scalar(_)

            def _(v, tt_av_last=tt_av_last):
                v.wait_ge(TT.h, tt_av_last)
                with nc.allow_low_precision("bf16 recip of softmax sums"):
                    VV.inc(v.reciprocal(rec1[:], PB[64:65, :]))
            blk.vector(_)
            vv_rec = VV.n

            def _(t, vv_rec=vv_rec):
                t.wait_ge(VV.h, vv_rec)
                for q4 in range(4):
                    ins = t.matmul(PA[0:64, q4 * 512:(q4 + 1) * 512],
                                   ones128[0:1, 0:64],
                                   rec1[0:1, q4 * 512:(q4 + 1) * 512],
                                   start=True, stop=True)
                TT.inc(ins)
            blk.tensor(_)
            tt_bc = TT.n

            def _(v, tt_bc=tt_bc, c=c, hh=hh):
                v.wait_ge(TT.h, tt_bc)
                VV.inc(v.tensor_copy(recb[:], PA[0:64, :]))
                v.wait_ge(VV.h, VV.n)
                VV.inc(v.tensor_mul(ofm[hh * 64:(hh + 1) * 64, c * 2048:(c + 1) * 2048],
                                    PB[0:64, :], recb[:]))
            blk.vector(_)

        tt_group_end = TT.n
        vv_group_end = VV.n

    close(13)

    wob = [scope(f"wob{i}", [128, 1024], BF16) for i in range(8)]
    x0ld = [scope(f"x0ld{i}", [128, TOWN], F32) for i in range(2)]
    x1t = [scope(f"x1t{i}", [128, TOWN], F32) for i in range(2)]
    f32t = scope("f32t", [128, TOWN], F32)

    rclose(8)

    wo_r = woD.rearrange("(hc p) j -> p hc j", p=128)

    def _(g, tt_group_end=tt_group_end, vv_pre=VV.n, ss_pre=SS.n):
        g.wait_ge(TT.h, tt_group_end)
        g.wait_ge(VV.h, vv_pre)
        g.wait_ge(SS.h, ss_pre)
        gWOB.put(g, [(wob[hdc][:], wo_r[:, hdc, :]) for hdc in range(8)])
    blk.gpsimd(_)

    vv_cmb0 = VV.n
    tt_wo0 = TT.n

    def _(sy, tt_pre=TT.n, ss_pre=SS.n):
        sy.wait_ge(TT.h, tt_pre)
        sy.wait_ge(SS.h, ss_pre)
        sy.wait_ge(VV.h, vv_cmb0)
        for dc in range(NDC):
            if dc >= 2:
                sy.wait_ge(VV.h, vv_cmb0 + 3 * (dc - 1))
            sX0.put(sy, [(x0ld[dc % 2][:], x0fm[dc * 128:(dc + 1) * 128, 0:TOWN])])
    blk.sync(_)

    def _(t, vv_group_end=vv_group_end):
        gWOB.wait(t)
        t.wait_ge(VV.h, vv_group_end)
        for dco in range(NDC):
            if dco >= 2:
                t.wait_ge(VV.h, vv_cmb0 + 3 * (dco - 1))
            ps = PA if dco % 2 == 0 else PB
            for q4 in range(4):
                for hdc in range(8):
                    ins = t.matmul(ps[:, q4 * 512:(q4 + 1) * 512],
                                   wob[hdc][:, dco * 128:(dco + 1) * 128],
                                   ofm[:, hdc * 2048 + q4 * 512:hdc * 2048 + (q4 + 1) * 512],
                                   start=(hdc == 0), stop=(hdc == 7))
            TT.inc(ins)
    blk.tensor(_)

    def _(g):
        for dco in range(NDC):
            g.wait_ge(VV.h, vv_cmb0 + 3 * (dco + 1))
            gX1D.put(g, [(x1d[dco * 128:(dco + 1) * 128, :], x1t[dco % 2][:])])
        gX1D.wait(g)
        CC.inc(g.collective_compute(
            "AllGather", mybir.AluOpType.bypass,
            replica_groups=[[0, 1], [2, 3], [4, 5], [6, 7]],
            ins=[x1d.opt()], outs=[x1g.opt()]))
    blk.gpsimd(_)

    def _(v):
        for dco in range(NDC):
            v.wait_ge(TT.h, tt_wo0 + dco + 1)
            sX0.wait(v, dco)
            if dco >= 2:
                gX1D.wait(v, dco - 2)
            ps = PA if dco % 2 == 0 else PB
            v.wait_ge(VV.h, VV.n)
            VV.inc(v.tensor_copy(f32t[:], ffnacc[dco][:]))
            v.wait_ge(VV.h, VV.n)
            VV.inc(v.tensor_add(f32t[:], f32t[:], ps[:]))
            v.wait_ge(VV.h, VV.n)
            VV.inc(v.tensor_add(x1t[dco % 2][:], f32t[:], x0ld[dco % 2][:]))
    blk.vector(_)
    vv_x1done = VV.n

    close(len(guards))

    nPre = len(guards)
    xnp = [scope(f"xnp{i}", [128, TOWN], BF16) for i in range(NDC)]
    xf32 = [scope(f"xf32{i}", [128, TOWN], F32) for i in range(2)]
    wrF32 = scope("wrF32", [128, 64], F32)
    fAP = scope("fAP", [128, 2048], F32)
    fBP = scope("fBP", [128, 2048], F32)
    fCP = scope("fCP", [128, 2048], F32)
    sqdP = [scope(f"sqdP{i}", [128, 2048], BF16) for i in range(2)]
    muP = scope("muP", [128, 2048], BF16)
    rsP = scope("rsP", [128, 2048], BF16)
    wrsbP = scope("wrsbP", [128, 64], BF16)
    br8P = scope("br8P", [8, 1], F32)
    scbfP = scope("scbfP", [8, TOWN], BF16)
    s_tmP = scope("s_tmP", [128, 128], F32)
    sAP = scope("sAP", [128, 128], F32)
    sBP = scope("sBP", [128, 128], F32)
    m16P = scope("m16P", [128, 64], F32)
    w_tmP = scope("w_tmP", [128, 128], BF16)
    wbv8P = scope("wbv8P", [128, 8 * TOWN], BF16)
    wrowP = scope("wrowP", [1, TOWN], BF16)

    vv_cp0 = VV.n

    def _(sy, tt_pre=TT.n, vv_pre=VV.n, ss_pre=SS.n):
        sy.wait_ge(TT.h, tt_pre)
        sy.wait_ge(VV.h, vv_pre)
        sy.wait_ge(SS.h, ss_pre)
        gX1D.wait(sy)
        sMS.put(sy, [(br8P[:], br8D), (identb[:], idD)])
        sWRL.put(sy, [(wrF32[:].rearrange("p (dc e) -> p dc e", dc=8),
                       wrD.rearrange("(dc p) e -> p dc e", p=128))])
        for dc in range(NDC):
            if dc >= 2:
                sy.wait_ge(VV.h, vv_cp0 + dc - 1)
            sX1R.put(sy, [(xf32[dc % 2][:], x1d[dc * 128:(dc + 1) * 128, :])])
    blk.sync(_)

    def _(v):
        for dc in range(NDC):
            sX1R.wait(v, dc)
            v.wait_ge(VV.h, VV.n)
            VV.inc(v.tensor_copy(xnp[dc][:], xf32[dc % 2][:]))
        sWRL.wait(v)
        v.wait_ge(VV.h, VV.n)
        VV.inc(v.tensor_copy(wrsbP[:], wrF32[:]))
    blk.vector(_)

    tt_bP = TT.n
    vv_bP = VV.n

    def _(v, tt_bP=tt_bP):
        for dc in range(NDC):
            if dc >= 2:
                v.wait_ge(TT.h, tt_bP + dc - 1)
            v.wait_ge(VV.h, VV.n)
            VV.inc(v.tensor_mul(sqdP[dc % 2][:], xnp[dc][:], xnp[dc][:]))
    blk.vector(_)

    def _(t, vv_bP=vv_bP, ss_pre=SS.n):
        t.wait_ge(SS.h, ss_pre)
        for dc in range(NDC):
            t.wait_ge(VV.h, vv_bP + dc + 1)
            for q4 in range(4):
                t.matmul(PA[:, q4 * 512:(q4 + 1) * 512], ones128[:],
                         xnp[dc][:, q4 * 512:(q4 + 1) * 512],
                         start=(dc == 0), stop=(dc == NDC - 1))
            for q4 in range(4):
                ins = t.matmul(PB[:, q4 * 512:(q4 + 1) * 512], ones128[:],
                               sqdP[dc % 2][:, q4 * 512:(q4 + 1) * 512],
                               start=(dc == 0), stop=(dc == NDC - 1))
            TT.inc(ins)
    blk.tensor(_)
    tt_s1P = TT.n

    def _(v, tt_s1P=tt_s1P):
        v.wait_ge(TT.h, tt_s1P)
        VV.inc(v.tensor_scalar_mul(fAP[:], PA[:], 1.0 / D))
        VV.inc(v.tensor_scalar_mul(fBP[:], PB[:], 1.0 / D))
        v.wait_ge(VV.h, VV.n)
        VV.inc(v.tensor_mul(fCP[:], fAP[:], fAP[:]))
        v.wait_ge(VV.h, VV.n)
        VV.inc(v.tensor_sub(fBP[:], fBP[:], fCP[:]))
    blk.vector(_)
    vv_varP = VV.n

    def _(s, vv_varP=vv_varP):
        s.wait_ge(VV.h, vv_varP)
        SS.inc(s.activation(fCP[:], fBP[:], AF.Sqrt, bias=epsb[:]))
    blk.scalar(_)
    ss_sdP = SS.n

    def _(v, ss_sdP=ss_sdP):
        v.wait_ge(SS.h, ss_sdP)
        VV.inc(v.reciprocal(fBP[:], fCP[:]))
        VV.inc(v.tensor_copy(muP[:], fAP[:]))
        v.wait_ge(VV.h, VV.n)
        VV.inc(v.tensor_copy(rsP[:], fBP[:]))
    blk.vector(_)

    vv_apP = VV.n

    def _(v):
        for dc in range(NDC):
            v.wait_ge(VV.h, VV.n)
            VV.inc(v.tensor_sub(xnp[dc][:], xnp[dc][:], muP[:]))
            v.wait_ge(VV.h, VV.n)
            VV.inc(v.tensor_mul(xnp[dc][:], xnp[dc][:], rsP[:]))
    blk.vector(_)
    vv_xnP = VV.n

    def _(sy, vv_apP=vv_apP):
        for dc in range(NDC):
            sy.wait_ge(VV.h, vv_apP + 2 * dc + 2)
            sXNS.put(sy, [(xn1d[dc], xnp[dc][:])])
    blk.sync(_)

    def _(t, vv_xnP=vv_xnP):
        t.wait_ge(VV.h, vv_xnP)
        for q4 in range(4):
            for dc in range(NDC):
                ins = t.matmul(PA[0:8, q4 * 512:(q4 + 1) * 512],
                               wrsbP[:, dc * 8:(dc + 1) * 8],
                               xnp[dc][:, q4 * 512:(q4 + 1) * 512],
                               start=(dc == 0), stop=(dc == NDC - 1))
        TT.inc(ins)
    blk.tensor(_)
    tt_scP = TT.n

    def _(s, tt_scP=tt_scP):
        s.wait_ge(TT.h, tt_scP)
        sMS.wait(s)
        SS.inc(s.activation(scbfP[:], PA[0:8, :], AF.Identity, bias=br8P[:]))
    blk.scalar(_)
    ss_scP = SS.n

    def _(t, ss_scP=ss_scP):
        t.wait_ge(SS.h, ss_scP)
        for tcw in range(16):
            ins = t.matmul(PB[:, tcw * 128:tcw * 128 + 8],
                           scbfP[:, tcw * 128:(tcw + 1) * 128],
                           identb[0:8, 0:8], start=True, stop=True)
        TT.inc(ins)
    blk.tensor(_)
    tt_trP = TT.n

    def _(v, tt_trP=tt_trP):
        v.wait_ge(TT.h, tt_trP)
        for tcw in range(16):
            sl = slice(tcw * 8, (tcw + 1) * 8)
            VV.inc(v.tensor_copy(s_tmP[:, sl], PB[:, tcw * 128:tcw * 128 + 8]))
        for tcw in range(16):
            sl = slice(tcw * 8, (tcw + 1) * 8)
            v.wait_ge(VV.h, VV.n)
            VV.inc(v.reduce_max(m16P[:, tcw * 4:tcw * 4 + 1], s_tmP[:, sl], axis=AX.X))
            v.wait_ge(VV.h, VV.n)
            VV.inc(v.tensor_scalar(sAP[:, sl], s_tmP[:, sl], m16P[:, tcw * 4:tcw * 4 + 1],
                                   None, ALU.is_ge))
            v.wait_ge(VV.h, VV.n)
            VV.inc(v.scalar_tensor_tensor(sBP[:, sl], sAP[:, sl], -1e30, s_tmP[:, sl],
                                          ALU.mult, ALU.add))
            v.wait_ge(VV.h, VV.n)
            VV.inc(v.reduce_max(m16P[:, tcw * 4 + 1:tcw * 4 + 2], sBP[:, sl], axis=AX.X))
            v.wait_ge(VV.h, VV.n)
            VV.inc(v.tensor_scalar(sAP[:, sl], s_tmP[:, sl],
                                   m16P[:, tcw * 4 + 1:tcw * 4 + 2], None, ALU.is_ge))
            v.wait_ge(VV.h, VV.n)
            VV.inc(v.tensor_scalar(sBP[:, sl], s_tmP[:, sl], m16P[:, tcw * 4:tcw * 4 + 1],
                                   None, ALU.subtract))
    blk.vector(_)
    vv_t2P = VV.n

    def _(s, vv_t2P=vv_t2P):
        s.wait_ge(VV.h, vv_t2P)
        SS.inc(s.activation(sBP[:], sBP[:], AF.Exp))
    blk.scalar(_)
    ss_expP = SS.n

    def _(v, ss_expP=ss_expP):
        v.wait_ge(SS.h, ss_expP)
        for tcw in range(16):
            sl = slice(tcw * 8, (tcw + 1) * 8)
            v.wait_ge(VV.h, VV.n)
            VV.inc(v.tensor_mul(sBP[:, sl], sBP[:, sl], sAP[:, sl]))
            v.wait_ge(VV.h, VV.n)
            VV.inc(v.reduce_sum(m16P[:, tcw * 4 + 2:tcw * 4 + 3], sBP[:, sl], axis=AX.X))
            v.wait_ge(VV.h, VV.n)
            VV.inc(v.reciprocal(m16P[:, tcw * 4 + 3:tcw * 4 + 4],
                                m16P[:, tcw * 4 + 2:tcw * 4 + 3]))
            v.wait_ge(VV.h, VV.n)
            VV.inc(v.tensor_scalar(w_tmP[:, sl], sBP[:, sl],
                                   m16P[:, tcw * 4 + 3:tcw * 4 + 4], None, ALU.mult))
    blk.vector(_)
    vv_wP = VV.n

    tt_r0 = TT.n
    vv_r0 = VV.n

    def _(t, vv_wP=vv_wP, tt_r0=tt_r0, vv_r0=vv_r0):
        t.wait_ge(VV.h, vv_wP)
        for i in range(NEXP):
            if i >= 1:
                t.wait_ge(VV.h, vv_r0 + 2 * i - 1)
            for tcw in range(16):
                ins = t.matmul(PA[0:1, tcw * 128:(tcw + 1) * 128],
                               w_tmP[:, tcw * 8 + i:tcw * 8 + i + 1], identb[:],
                               start=True, stop=True)
            TT.inc(ins)
            t.wait_ge(VV.h, vv_r0 + 2 * i + 1)
            for q4 in range(4):
                ins = t.matmul(PB[0:128, q4 * 512:(q4 + 1) * 512],
                               ones128[0:1, 0:128],
                               wrowP[0:1, q4 * 512:(q4 + 1) * 512],
                               start=True, stop=True)
            TT.inc(ins)
    blk.tensor(_)

    def _(v, tt_r0=tt_r0):
        for i in range(NEXP):
            v.wait_ge(TT.h, tt_r0 + 2 * i + 1)
            VV.inc(v.tensor_copy(wrowP[:], PA[0:1, 0:2048]))
            v.wait_ge(TT.h, tt_r0 + 2 * i + 2)
            VV.inc(v.tensor_copy(wbv8P[:, i * TOWN:(i + 1) * TOWN], PB[:, 0:2048]))
    blk.vector(_)

    def _(sy, vv_r0=vv_r0):
        for i in range(NEXP):
            sy.wait_ge(VV.h, vv_r0 + 2 * i + 2)
            sWBS.put(sy, [(wbv8d[:, i * TOWN:(i + 1) * TOWN],
                           wbv8P[:, i * TOWN:(i + 1) * TOWN])])
    blk.sync(_)

    close(len(guards) - nPre)

    kfmB = scope("kfmB", [64, T], BF16)
    qfmB = scope("qfmB", [64, T], BF16)
    ktm = scope("ktm", [128, 32 * 64], BF16)
    bvb = scope("bvbS", [128, D], F32)
    bkb = scope("bkbS", [128, 64], F32)
    bk = scope("bkS", [64, 1], F32)
    bq = scope("bqS", [64, 1], F32)
    gma = scope("gma", [128, 8], F32)
    bta = scope("bta", [128, 8], F32)
    mUe = scope("mUeS", [128, 128], BF16)
    mLe = scope("mLeS", [128, 128], BF16)
    mUn = scope("mUnS", [128, 128], BF16)
    Sst = scope("Sst", [64, D], F32)
    Sbf = scope("Sbf", [64, D], BF16)
    nUt = scope("nUt", [128, 128], F32)
    nLt = scope("nLt", [128, 128], F32)
    Cbuf = scope("Cbuf", [128, 6 * 128], F32)
    Ta = scope("Ta", [128, 128], F32)
    Tb = scope("Tb", [128, 128], F32)
    Rs = scope("Rs", [128, 128], BF16)
    brawS = scope("brawS", [128, D], BF16)
    Utm = scope("Utm", [128, D], BF16)
    pmS = scope("pmS", [128, 128], BF16)
    vld = [scope(f"vld{i}", [128, D], BF16) for i in range(2)]
    vev = [scope(f"vev{i}", [128, D], BF16) for i in range(2)]
    nB_base = len(guards)

    def _(sy, tt_pre=TT.n, vv_pre=VV.n, ss_pre=SS.n):
        sy.wait_ge(TT.h, tt_pre)
        sy.wait_ge(VV.h, vv_pre)
        sy.wait_ge(SS.h, ss_pre)
        sXNS.wait(sy)
        sWBS.wait(sy)
        sMS.put(sy, [(bvb[:], bvbD), (bkb[:], bkbD), (bk[:], bkD), (bq[:], bqD),
                     (gma[:], gbD), (bta[:], btD), (mUe[:], mUeD), (mLe[:], mLeD),
                     (mUn[:], mUnD), (identb[:], idD)])
    blk.sync(_)

    def _(v):
        v.wait_ge(VV.h, VV.n)
        sXNS.wait(v)
        sWBS.wait(v)
        v.memset(Sst[:], 0.0)
        VV.inc(v.memset(Sbf[:], 0.0))
    blk.vector(_)

    x1bf = [scope(f"x1b{i}", [128, T], BF16) for i in range(NDC)]
    mu_bfB = scope("mu_bfB", [128, T], BF16)
    f_bfB = scope("f_bfB", [128, T], BF16)

    def _(g, tt_pre=TT.n, vv_pre=VV.n, ss_pre=SS.n):
        g.wait_ge(CC.h, 1)
        sXNS.wait(g)
        sWBS.wait(g)
        g.wait_ge(TT.h, tt_pre)
        g.wait_ge(VV.h, vv_pre)
        g.wait_ge(SS.h, ss_pre)
        for dc in range(NDC):
            gX1.put(g, [(x1bf[dc][:, 0:TOWN], x1g[0, dc * 128:(dc + 1) * 128, :]),
                        (x1bf[dc][:, TOWN:T], x1g[1, dc * 128:(dc + 1) * 128, :])])
    blk.gpsimd(_)

    fAB = scope("fAB", [128, 2048], F32)
    fBB = scope("fBB", [128, 2048], F32)
    fCB = scope("fCB", [128, 2048], F32)
    fDB = scope("fDB", [128, 2048], F32)
    sqdB = [scope(f"sqdB{i}", [128, 2048], BF16) for i in range(2)]

    for tb in range(2):
        ts = tb * 2048
        tt_b = TT.n
        vv_b = VV.n

        def _(v, ts=ts, tt_b=tt_b):
            gX1.wait(v)
            for dc in range(NDC):
                if dc >= 2:
                    v.wait_ge(TT.h, tt_b + dc - 1)
                v.wait_ge(VV.h, VV.n)
                VV.inc(v.tensor_mul(sqdB[dc % 2][:], x1bf[dc][:, ts:ts + 2048],
                                    x1bf[dc][:, ts:ts + 2048]))
        blk.vector(_)

        def _(t, ts=ts, vv_b=vv_b, ss_pre=SS.n):
            t.wait_ge(SS.h, ss_pre)
            for dc in range(NDC):
                t.wait_ge(VV.h, vv_b + dc + 1)
                for q4 in range(4):
                    t.matmul(PA[:, q4 * 512:(q4 + 1) * 512], ones128[:],
                             x1bf[dc][:, ts + q4 * 512:ts + (q4 + 1) * 512],
                             start=(dc == 0), stop=(dc == NDC - 1))
                for q4 in range(4):
                    ins = t.matmul(PB[:, q4 * 512:(q4 + 1) * 512], ones128[:],
                                   sqdB[dc % 2][:, q4 * 512:(q4 + 1) * 512],
                                   start=(dc == 0), stop=(dc == NDC - 1))
                TT.inc(ins)
        blk.tensor(_)
        tt_s1 = TT.n

        def _(v, tt_s1=tt_s1):
            v.wait_ge(TT.h, tt_s1)
            VV.inc(v.tensor_scalar_mul(fAB[:], PA[:], 1.0 / D))
            VV.inc(v.tensor_scalar_mul(fBB[:], PB[:], 1.0 / D))
            v.wait_ge(VV.h, VV.n)
            VV.inc(v.tensor_mul(fCB[:], fAB[:], fAB[:]))
            v.wait_ge(VV.h, VV.n)
            VV.inc(v.tensor_sub(fBB[:], fBB[:], fCB[:]))
        blk.vector(_)
        vv_var = VV.n

        def _(s, vv_var=vv_var):
            s.wait_ge(VV.h, vv_var)
            SS.inc(s.activation(fCB[:], fBB[:], AF.Sqrt, bias=epsb[:]))
        blk.scalar(_)
        ss_sd = SS.n

        def _(v, ss_sd=ss_sd):
            v.wait_ge(SS.h, ss_sd)
            VV.inc(v.reciprocal(fDB[:], fCB[:]))
            v.wait_ge(VV.h, VV.n)
            VV.inc(v.tensor_mul(fCB[:], fDB[:], fDB[:]))
            v.wait_ge(VV.h, VV.n)
            VV.inc(v.tensor_mul(fCB[:], fBB[:], fCB[:]))
        blk.vector(_)
        vv_vn = VV.n

        def _(s, vv_vn=vv_vn):
            s.wait_ge(VV.h, vv_vn)
            SS.inc(s.activation(fCB[:], fCB[:], AF.Sqrt, bias=epsb[:]))
        blk.scalar(_)
        ss_sdn = SS.n

        def _(v, ts=ts, ss_sdn=ss_sdn):
            v.wait_ge(SS.h, ss_sdn)
            VV.inc(v.reciprocal(fCB[:], fCB[:]))
            v.wait_ge(VV.h, VV.n)
            VV.inc(v.tensor_mul(fCB[:], fCB[:], fDB[:]))
            v.wait_ge(VV.h, VV.n)
            VV.inc(v.tensor_copy(mu_bfB[:, ts:ts + 2048], fAB[:]))
            v.wait_ge(VV.h, VV.n)
            VV.inc(v.tensor_copy(f_bfB[:, ts:ts + 2048], fCB[:]))
        blk.vector(_)

    def _(v):
        sMS.wait(v)
        for dc in range(NDC):
            v.wait_ge(VV.h, VV.n)
            VV.inc(v.tensor_sub(x1bf[dc][:], x1bf[dc][:], mu_bfB[:]))
            v.wait_ge(VV.h, VV.n)
            VV.inc(v.tensor_mul(x1bf[dc][:], x1bf[dc][:], f_bfB[:]))
            v.wait_ge(VV.h, VV.n)
            VV.inc(v.tensor_scalar(x1bf[dc][:], x1bf[dc][:], gma[:, dc:dc + 1],
                                   bta[:, dc:dc + 1], ALU.mult, ALU.add))
    blk.vector(_)
    vv_h = VV.n

    close(6)
    h = x1bf

    wvb = scope("wvb", [128, 8 * D], BF16)
    wkb = scope("wkb", [128, 8 * 64], BF16)
    wqb = scope("wqb", [128, 8 * 64], BF16)

    def _(g, ss_pre=SS.n):
        g.wait_ge(VV.h, vv_h)
        g.wait_ge(SS.h, ss_pre)
        gWP.put(g, [(wvb[:].rearrange("p (dc e) -> p dc e", dc=8),
                     wvtD.rearrange("(dc p) e -> p dc e", p=128)),
                    (wkb[:].rearrange("p (dc e) -> p dc e", dc=8),
                     wktD.rearrange("(dc p) e -> p dc e", p=128)),
                    (wqb[:].rearrange("p (dc e) -> p dc e", dc=8),
                     wqtD.rearrange("(dc p) e -> p dc e", p=128))])
    blk.gpsimd(_)

    def _(t, ss_pre=SS.n):
        t.wait_ge(VV.h, vv_h)
        gWP.wait(t)
        for th in range(2):
            ts = th * 2048
            if th >= 1:
                t.wait_ge(SS.h, ss_pre + 2 * th - 1)
            for q4 in range(4):
                for dc in range(NDC):
                    ins = t.matmul(PA[0:64, q4 * 512:(q4 + 1) * 512],
                                   wkb[:, dc * 64:(dc + 1) * 64],
                                   h[dc][:, ts + q4 * 512:ts + (q4 + 1) * 512],
                                   start=(dc == 0), stop=(dc == NDC - 1))
            TT.inc(ins)
            if th >= 1:
                t.wait_ge(SS.h, ss_pre + 2 * th)
            for q4 in range(4):
                for dc in range(NDC):
                    ins = t.matmul(PB[0:64, q4 * 512:(q4 + 1) * 512],
                                   wqb[:, dc * 64:(dc + 1) * 64],
                                   h[dc][:, ts + q4 * 512:ts + (q4 + 1) * 512],
                                   start=(dc == 0), stop=(dc == NDC - 1))
            TT.inc(ins)
    blk.tensor(_)
    tt_kq0 = TT.n - 4

    def _(s):
        sMS.wait(s)
        for th in range(2):
            ts = th * 2048
            s.wait_ge(TT.h, tt_kq0 + 2 * th + 1)
            SS.inc(s.activation(kfmB[:, ts:ts + 2048], PA[0:64, :], AF.Identity,
                                bias=bk[:]))
            s.wait_ge(TT.h, tt_kq0 + 2 * th + 2)
            SS.inc(s.activation(qfmB[:, ts:ts + 2048], PB[0:64, :], AF.Identity,
                                bias=bq[:]))
    blk.scalar(_)
    ss_kq = SS.n

    tt_kv0 = TT.n
    vv_kv0 = VV.n

    def _(t):
        t.wait_ge(SS.h, ss_kq)
        for tc in range(32):
            if tc >= 2:
                t.wait_ge(VV.h, vv_kv0 + 2 * (tc - 1))
            pok = (tc % 2) * 512
            for dc in range(NDC):
                ins = t.matmul(PA[:, pok:pok + 64],
                               h[dc][:, tc * 128:(tc + 1) * 128],
                               wkb[:, dc * 64:(dc + 1) * 64],
                               start=(dc == 0), stop=(dc == NDC - 1))
            TT.inc(ins)
            pov = (tc % 2) * 1024
            for j in range(2):
                for dc in range(NDC):
                    ins = t.matmul(PB[:, pov + j * 512:pov + (j + 1) * 512],
                                   h[dc][:, tc * 128:(tc + 1) * 128],
                                   wvb[:, dc * 1024 + j * 512:dc * 1024 + (j + 1) * 512],
                                   start=(dc == 0), stop=(dc == NDC - 1))
            TT.inc(ins)
    blk.tensor(_)

    def _(g, vv_kv0=vv_kv0):
        for tc in range(32):
            g.wait_ge(VV.h, vv_kv0 + 2 * (tc + 1))
            gVD.put(g, [(vd[tc], vev[tc % 2][:])])
    blk.gpsimd(_)

    def _(v):
        for tc in range(32):
            v.wait_ge(TT.h, tt_kv0 + 2 * tc + 1)
            VV.inc(v.tensor_add(ktm[:, tc * 64:(tc + 1) * 64], PA[:, (tc % 2) * 512:(tc % 2) * 512 + 64], bkb[:]))
            v.wait_ge(TT.h, tt_kv0 + 2 * tc + 2)
            if tc >= 2:
                gVD.wait(v, tc - 2)
            pov = (tc % 2) * 1024
            VV.inc(v.tensor_add(vev[tc % 2][:], PB[:, pov:pov + 1024], bvb[:]))
    blk.vector(_)
    vv_kv_done = VV.n

    close(3)
    close(10)

    yfm = [rscope(f"yfm{i}", [128, T], BF16) for i in range(NDC)]

    Rs32 = rscope("Rs32", [128, 32 * 128], BF16)
    nUt2 = [scope(f"nUt2{i}", [128, 128], F32) for i in range(2)]
    nLt2 = [scope(f"nLt2{i}", [128, 128], F32) for i in range(2)]
    Cb2 = [scope(f"Cb2{i}", [128, 6 * 128], F32) for i in range(2)]
    Ta2 = [scope(f"Ta2{i}", [128, 128], F32) for i in range(2)]
    Tb2 = [scope(f"Tb2{i}", [128, 128], F32) for i in range(2)]

    pBA = [PA[:, 0:128], PA[:, 1024:1152]]
    pBB = [PA[:, 512:640], PA[:, 1536:1664]]

    tt_p1 = TT.n
    vv_p1 = VV.n
    NT1, NV1 = 18, 25

    def _(t, vv_kv_done=vv_kv_done):
        t.wait_ge(VV.h, vv_kv_done)
        for tc in range(32):
            p = tc % 2
            tt0 = tt_p1 + NT1 * tc
            vv0 = vv_p1 + NV1 * tc
            ksl = kfmB[0:64, tc * 128:(tc + 1) * 128]
            nU, nL = nUt2[p], nLt2[p]
            Cb, TA, TB = Cb2[p], Ta2[p], Tb2[p]
            if tc >= 2:
                t.wait_ge(VV.h, vv_p1 + NV1 * (tc - 1))
            TT.inc(t.matmul(pBA[p], ksl, ksl, start=True, stop=True))
            for l in range(5):
                t.wait_ge(VV.h, vv0 + 2 + 3 * l)
                TT.inc(t.matmul(pBA[p], nL[:], nU[:], start=True, stop=True))
                TT.inc(t.matmul(pBB[p], nU[:], nL[:], start=True, stop=True))
            t.wait_ge(VV.h, vv0 + 17)
            TT.inc(t.matmul(pBA[p], nL[:], nU[:], start=True, stop=True))
            for idx, j in enumerate([5, 4, 3, 2, 1, 0]):
                t.wait_ge(VV.h, vv0 + 19 + idx)
                ps = pBB[p] if idx % 2 == 0 else pBA[p]
                rhs = [TA, TB][idx % 2]
                TT.inc(t.matmul(ps, Cb[:, j * 128:(j + 1) * 128], rhs[:],
                                start=True, stop=True))
    blk.tensor(_)

    def _(v):
        for tc in range(32):
            p = tc % 2
            tt0 = tt_p1 + NT1 * tc
            vv0 = vv_p1 + NV1 * tc
            nU, nL = nUt2[p], nLt2[p]
            Cb, TA, TB = Cb2[p], Ta2[p], Tb2[p]
            v.wait_ge(TT.h, tt0 + 1)
            VV.inc(v.tensor_mul(nU[:], pBA[p], mUe[:]))
            VV.inc(v.tensor_mul(nL[:], pBA[p], mLe[:]))
            for l in range(5):
                v.wait_ge(VV.h, VV.n)
                if l == 0:
                    VV.inc(v.tensor_sub(Cb[:, 0:128], identb[:], nL[:]))
                else:
                    VV.inc(v.tensor_add(Cb[:, l * 128:(l + 1) * 128], identb[:], nL[:]))
                v.wait_ge(TT.h, tt0 + 3 + 2 * l)
                v.wait_ge(VV.h, VV.n)
                VV.inc(v.tensor_copy(nU[:], pBA[p]))
                VV.inc(v.tensor_copy(nL[:], pBB[p]))
            v.wait_ge(VV.h, VV.n)
            VV.inc(v.tensor_add(Cb[:, 5 * 128:6 * 128], identb[:], nL[:]))
            v.wait_ge(TT.h, tt0 + 12)
            VV.inc(v.tensor_add(TA[:], identb[:], pBA[p]))
            for idx in range(6):
                v.wait_ge(TT.h, tt0 + 13 + idx)
                ps = pBB[p] if idx % 2 == 0 else pBA[p]
                if idx < 5:
                    dst = [TB, TA][idx % 2]
                    VV.inc(v.tensor_copy(dst[:], ps))
                else:
                    with nc.allow_low_precision("lam-scaled inverse factor"):
                        VV.inc(v.tensor_scalar_mul(Rs32[:, tc * 128:(tc + 1) * 128],
                                                   ps, TTT_LR))
    blk.vector(_)
    vv_p1_end = VV.n

    psKM = PA[:, 0:1024]
    psU = PA[:, 1024:2048]
    psY = PB[:, 0:1024]
    psP = PB[:, 1024:1152]
    psSK = PB[0:64, 1024:2048]

    tt_p2 = TT.n
    vv_p2 = VV.n
    NT2, NV2 = 6, 6

    def _(sy):
        for tc in range(32):
            if tc >= 2:
                sy.wait_ge(VV.h, vv_p2 + NV2 * (tc - 2) + 1)
            gVD.wait(sy, tc)
            sVT.put(sy, [(vld[tc % 2][:], vd[tc])])
    blk.sync(_)

    def _(t):
        t.wait_ge(VV.h, vv_p1_end)
        for tc in range(32):
            tt0 = tt_p2 + NT2 * tc
            vv0 = vv_p2 + NV2 * tc
            ksl = kfmB[0:64, tc * 128:(tc + 1) * 128]
            qsl = qfmB[0:64, tc * 128:(tc + 1) * 128]
            t.wait_ge(VV.h, vv0)
            for j in range(2):
                ins = t.matmul(psKM[:, j * 512:(j + 1) * 512], ksl,
                               Sbf[:, j * 512:(j + 1) * 512], start=True, stop=True)
            TT.inc(ins)
            t.wait_ge(VV.h, vv0 + 1)
            for j in range(2):
                ins = t.matmul(psU[:, j * 512:(j + 1) * 512],
                               Rs32[:, tc * 128:(tc + 1) * 128],
                               brawS[:, j * 512:(j + 1) * 512], start=True, stop=True)
            TT.inc(ins)
            for dc in range(NDC):
                ins = t.matmul(psY[:, dc * 128:(dc + 1) * 128],
                               Sbf[:, dc * 128:(dc + 1) * 128], qsl,
                               start=(dc % 4 == 0), stop=False)
            TT.inc(ins)
            TT.inc(t.matmul(psP, ksl, qsl, start=True, stop=True))
            t.wait_ge(VV.h, vv0 + 3)
            for dc in range(NDC):
                ins = t.matmul(psY[:, dc * 128:(dc + 1) * 128],
                               Utm[:, dc * 128:(dc + 1) * 128], pmS[:],
                               start=False, stop=(dc % 4 == 3))
            TT.inc(ins)
            for j in range(2):
                ins = t.matmul(psSK[:, j * 512:(j + 1) * 512],
                               ktm[:, tc * 64:(tc + 1) * 64],
                               Utm[:, j * 512:(j + 1) * 512], start=True, stop=True)
            TT.inc(ins)
    blk.tensor(_)

    def _(v):
        for tc in range(32):
            tt0 = tt_p2 + NT2 * tc
            vv0 = vv_p2 + NV2 * tc
            v.wait_ge(TT.h, tt0 + 1)
            sVT.wait(v, tc)
            VV.inc(v.tensor_sub(brawS[:], psKM, vld[tc % 2][:]))
            v.wait_ge(TT.h, tt0 + 2)
            VV.inc(v.tensor_copy(Utm[:], psU))
            v.wait_ge(TT.h, tt0 + 4)
            VV.inc(v.tensor_mul(pmS[:], psP, mUn[:]))
            v.wait_ge(TT.h, tt0 + 5)
            for dc in range(NDC):
                ins = v.tensor_copy(yfm[dc][:, tc * 128:(tc + 1) * 128],
                                    psY[:, dc * 128:(dc + 1) * 128])
            VV.inc(ins)
            v.wait_ge(TT.h, tt0 + 6)
            v.wait_ge(VV.h, VV.n)
            VV.inc(v.tensor_sub(Sst[:], Sst[:], psSK))
            v.wait_ge(VV.h, VV.n)
            VV.inc(v.tensor_copy(Sbf[:], Sst[:]))
    blk.vector(_)
    vv_scan_done = VV.n

    rclose(1)

    close(len(guards) - 0)
    x2acc = [scope(f"x2a{i}", [128, TOWN], F32) for i in range(NDC)]
    xn1bf = [scope(f"xn1b{i}", [128, TOWN], BF16) for i in range(NDC)]
    sel0 = scope("sel0S", [128, 1], F32)
    sel1 = scope("sel1S", [128, 1], F32)
    w1c = [scope(f"w1c{i}", [128, 4096], BF16) for i in range(2)]
    w2c = [scope(f"w2c{i}", [128, 4096], BF16) for i in range(2)]
    b1a = scope("b1aS", [128, 256], F32)
    b2a = scope("b2aS", [128, 64], F32)
    etmp = [scope(f"etmp{i}", [128, 512], F32) for i in range(2)]

    def _(sy, tt_pre=TT.n, vv_pre=VV.n, ss_pre=SS.n):
        sy.wait_ge(TT.h, tt_pre)
        sy.wait_ge(VV.h, vv_pre)
        sy.wait_ge(SS.h, ss_pre)
        sXNS.wait(sy)
        for dc in range(NDC):
            sXNL.put(sy, [(xn1bf[dc][:], xn1d[dc])])
        sMS.put(sy, [(sel0[:], sel0D), (sel1[:], sel1D),
                     (b1a[:], b1aD), (b2a[:], b2aD)])
    blk.sync(_)

    def _(g, tt_pre=TT.n, vv_pre=VV.n, ss_pre=SS.n):
        g.wait_ge(TT.h, tt_pre)
        g.wait_ge(VV.h, vv_pre)
        g.wait_ge(SS.h, ss_pre)
        gX1D.wait(g)
        for dc in range(NDC):
            sXC.put(g, [(x2acc[dc][:], x1d[dc * 128:(dc + 1) * 128, :])])
    blk.gpsimd(_)

    def _(t):
        sXNL.wait(t)
    blk.tensor(_)

    def _(s):
        sMS.wait(s)
    blk.scalar(_)

    def _(v, vv_scan_done=vv_scan_done):
        v.wait_ge(VV.h, vv_scan_done)
        sMS.wait(v)
        sXC.wait(v)
        sXNL.wait(v)
        for dc in range(NDC):
            v.wait_ge(VV.h, VV.n)
            VV.inc(v.tensor_add(x2acc[dc][:], x2acc[dc][:], xn1bf[dc][:]))
            v.wait_ge(VV.h, VV.n)
            VV.inc(v.scalar_tensor_tensor(x2acc[dc][:], yfm[dc][:, 0:TOWN], sel0[:],
                                          x2acc[dc][:], ALU.mult, ALU.add))
            v.wait_ge(VV.h, VV.n)
            VV.inc(v.scalar_tensor_tensor(x2acc[dc][:], yfm[dc][:, TOWN:T], sel1[:],
                                          x2acc[dc][:], ALU.mult, ALU.add))
    blk.vector(_)
    vv_xny = VV.n

    rclose(8)
    h1m = scope("h1m", [128, 32 * 512], BF16)
    wbv8 = scope("wbv8", [128, 8 * TOWN], BF16)

    def _(g):
        g.wait_ge(VV.h, vv_xny)
        sWBS.wait(g)
        for i in range(NEXP):
            sWBL.put(g, [(wbv8[:, i * TOWN:(i + 1) * TOWN],
                          wbv8d[:, i * TOWN:(i + 1) * TOWN])])
    blk.gpsimd(_)

    def _(v):
        sWBL.wait(v)
    blk.vector(_)

    vv_w = vv_scan_done
    vv_bc = vv_scan_done

    ss_prev_evac = SS.n
    vv_prev = VV.n
    tt_prev = TT.n

    for i in range(NEXP):
        for tb in range(4):
            ts = tb * 512
            tt_f1 = TT.n
            ss_f1 = SS.n
            w1base = len(sW1L.marks)

            def _(sy, i=i, tt_f1=tt_f1, vv_pre=VV.n):
                sy.wait_ge(VV.h, min(vv_pre, vv_w))
                gWPP.wait(sy)
                for gr in range(8):
                    if gr >= 2:
                        sy.wait_ge(TT.h, tt_f1 + 4 * (gr - 1))
                    sW1L.put(sy, [(w1c[gr % 2][:].rearrange("p (c e) -> p c e", c=4),
                                   w1bd[i, 4 * gr:4 * gr + 4].rearrange("c p e -> p c e"))])
            blk.sync(_)

            def _(t, i=i, ts=ts, ss_f1=ss_f1, vv_bc=vv_bc, w1base=w1base,
                  ss_pe=ss_prev_evac):
                t.wait_ge(VV.h, vv_bc)
                for hc in range(32):
                    sW1L.wait(t, w1base + hc // 4)
                    if hc < 2:
                        t.wait_ge(SS.h, max(0, ss_pe - 7 + hc))
                    if hc >= 2:
                        t.wait_ge(SS.h, ss_f1 + hc - 1)
                    po = (hc % 2) * 512
                    for dc in range(NDC):
                        ins = t.matmul(PA[:, po:po + 512],
                                       w1c[(hc // 4) % 2][:, (hc % 4) * 1024 + dc * 128:
                                                          (hc % 4) * 1024 + (dc + 1) * 128],
                                       xn1bf[dc][:, ts:ts + 512],
                                       start=(dc == 0), stop=(dc == NDC - 1))
                    TT.inc(ins)
            blk.tensor(_)

            def _(s, i=i, tt_f1=tt_f1, first=(i == 0 and tb == 0)):
                if first:
                    s.wait_ge(VV.h, vv_xny)
                for hc in range(32):
                    s.wait_ge(TT.h, tt_f1 + hc + 1)
                    po = (hc % 2) * 512
                    SS.inc(s.activation(h1m[:, hc * 512:(hc + 1) * 512], PA[:, po:po + 512],
                                        AF.Relu, bias=b1a[:, i * 32 + hc:i * 32 + hc + 1]))
            blk.scalar(_)
            ss_f1_done = SS.n

            tt_f2 = TT.n
            w2base = len(sW2L.marks)

            def _(sy, i=i, tt_f2=tt_f2):
                gWPP.wait(sy)
                for gr in range(8):
                    if gr >= 2:
                        sy.wait_ge(TT.h, tt_f2 + 4 * (gr - 1))
                    sW2L.put(sy, [(w2c[gr % 2][:].rearrange("p (c e) -> p c e", c=4),
                                   w2bd[i, 4 * gr * 128:(4 * gr + 4) * 128, :].rearrange(
                                       "(c p) e -> p c e", p=128))])
            blk.sync(_)

            def _(t, i=i, ss_f1_done=ss_f1_done, ss_prev_evac=ss_prev_evac,
                  w2base=w2base, tt_f1=tt_f1, ss_f1=ss_f1):
                t.wait_ge(SS.h, max(ss_f1_done, ss_prev_evac))
                for hc in range(32):
                    sW2L.wait(t, w2base + hc // 4)
                    for ec in range(8):
                        ps = PA if ec < 4 else PB
                        ins = t.matmul(ps[:, (ec % 4) * 512:(ec % 4 + 1) * 512],
                                       w2c[(hc // 4) % 2][:, (hc % 4) * 1024 + ec * 128:
                                                          (hc % 4) * 1024 + (ec + 1) * 128],
                                       h1m[:, hc * 512:(hc + 1) * 512],
                                       start=(hc == 0), stop=(hc == 31))
                    TT.inc(ins)
            blk.tensor(_)
            tt_f2_done = TT.n

            def _(s, i=i, tt_f2_done=tt_f2_done, vv_pre=VV.n):
                s.wait_ge(TT.h, tt_f2_done)
                s.wait_ge(VV.h, vv_pre)
                for ec in range(8):
                    if ec >= 2:
                        s.wait_ge(VV.h, vv_pre + 2 * (ec - 1))
                    ps = PA if ec < 4 else PB
                    SS.inc(s.activation(etmp[ec % 2][:],
                                        ps[:, (ec % 4) * 512:(ec % 4 + 1) * 512],
                                        AF.Identity,
                                        bias=b2a[:, i * 8 + ec:i * 8 + ec + 1]))
            blk.scalar(_)
            ss_evac0 = SS.n - 8

            def _(v, i=i, ts=ts, ss_evac0=ss_evac0):
                for ec in range(8):
                    v.wait_ge(SS.h, ss_evac0 + ec + 1)
                    v.wait_ge(VV.h, VV.n)
                    VV.inc(v.tensor_mul(etmp[ec % 2][:], etmp[ec % 2][:],
                                        wbv8[:, i * TOWN + ts:i * TOWN + ts + 512]))
                    v.wait_ge(VV.h, VV.n)
                    VV.inc(v.tensor_add(x2acc[ec][:, ts:ts + 512],
                                        x2acc[ec][:, ts:ts + 512], etmp[ec % 2][:]))
            blk.vector(_)
            ss_prev_evac = SS.n
            vv_prev = VV.n

        tt_prev = TT.n

    vv_fin = VV.n

    def _(g):
        g.wait_ge(VV.h, vv_fin)
        for dc in range(NDC):
            gXO.put(g, [(x2ownD[dc * 128:(dc + 1) * 128, :], x2acc[dc][:])])
        gXO.wait(g)
    blk.gpsimd(_)

    blk_g.__exit__(None, None, None)
    close(len(guards))
    rclose(len(rguards))
    pg2.__exit__(None, None, None)
    pg1.__exit__(None, None, None)
    for g in reversed(keep):
        g.__exit__(None, None, None)
    return nc


def prep_fused_inmaps(inputs):
    a_maps = prep_block0_inmaps(inputs)
    wk = np.asarray(inputs["L1_attn_Wk"], np.float32)
    wv = np.asarray(inputs["L1_attn_Wv"], np.float32)
    wq = np.asarray(inputs["L1_attn_Wq"], np.float32)
    bk = np.asarray(inputs["L1_attn_bk"], np.float32)
    bv = np.asarray(inputs["L1_attn_bv"], np.float32)
    bq = np.asarray(inputs["L1_attn_bq"], np.float32)
    gm = np.asarray(inputs["L1_attn_gamma"], np.float32)
    bt = np.asarray(inputs["L1_attn_beta"], np.float32)
    wr = np.asarray(inputs["L1_ffn_Wr"], np.float32)
    br = np.asarray(inputs["L1_ffn_br"], np.float32)
    w1 = np.asarray(inputs["L1_ffn_W1"], np.float32)
    b1 = np.asarray(inputs["L1_ffn_b1"], np.float32)
    w2 = np.asarray(inputs["L1_ffn_W2"], np.float32)
    b2 = np.asarray(inputs["L1_ffn_b2"], np.float32)
    on = np.ones((128, 128), np.float32)
    shared = dict(
        wkt=_c(wk.T), wvt=_c(wv.T), wqt=_c(wq.T),
        bk=_c(bk.reshape(64, 1)), bq=_c(bq.reshape(64, 1)),
        bkb=_c(np.tile(bk, (128, 1))), bvb=_c(np.tile(bv, (128, 1))),
        gb=_c(gm.reshape(8, 128).T), bt=_c(bt.reshape(8, 128).T),
        mUe=_c((np.triu(on, 1) * TTT_LR).astype(BF)),
        mLe=_c((np.tril(on, -1) * TTT_LR).astype(BF)),
        mUn=_c((-np.triu(on, 1)).astype(BF)),
        identb=_c(np.eye(128, dtype=np.float32).astype(BF)),
        wr=_c(wr), br8=_c(br.reshape(8, 1)),
        w1r=_c(w1.reshape(8, 8, 128, 32, 128).transpose(0, 3, 2, 1, 4)
               .reshape(8, 32, 128, 1024)),
        w2m=_c(w2),
        b1a=_c(np.concatenate([b1[i].reshape(32, 128).T for i in range(8)], axis=1)),
        b2a=_c(np.concatenate([b2[i].reshape(8, 128).T for i in range(8)], axis=1)))
    in_maps = []
    for core in range(8):
        hh = core % 2
        m = dict(a_maps[core])
        m.update(shared)
        m.update(sel0=_c(np.full((128, 1), 1.0 - hh, np.float32)),
                 sel1=_c(np.full((128, 1), float(hh), np.float32)))
        in_maps.append(m)
    return in_maps


def run_fused(inputs):
    nc = build_fused()
    in_maps = prep_fused_inmaps(inputs)
    res = run_bass_kernel_spmd(nc, in_maps, core_ids=list(range(8)))
    x2 = np.zeros((B, T, D), dtype=np.float32)
    for core in range(8):
        b, hh = core // 2, core % 2
        x2[b, hh * TOWN:(hh + 1) * TOWN] = res.results[core]["x2own"].T
    return x2


def prep_block0_inmaps(inputs):
    x0 = np.asarray(inputs["L0_attn_x"], dtype=np.float32)
    wq = np.asarray(inputs["L0_attn_Wq"], dtype=np.float32)
    wk = np.asarray(inputs["L0_attn_Wk"], dtype=np.float32)
    wv = np.asarray(inputs["L0_attn_Wv"], dtype=np.float32)
    wo = np.asarray(inputs["L0_attn_Wo"], dtype=np.float32)
    w1 = np.asarray(inputs["L0_ffn_w1"], dtype=np.float32)
    b1 = np.asarray(inputs["L0_ffn_b1"], dtype=np.float32)
    w2 = np.asarray(inputs["L0_ffn_w2"], dtype=np.float32)
    b2 = np.asarray(inputs["L0_ffn_b2"], dtype=np.float32)

    perm = _rope_perm()
    cos2, sin2 = _rope_tables()
    shared = dict(wqp=_c(wq[:, perm]), wkp=_c(wk[:, perm]), wv=_c(wv), wo=_c(wo),
                  w1t=_c(w1.T), w1s=_c(w1), w2t=_c(w2.T), w2s=_c(w2),
                  b1t=_c(b1.reshape(32, 128).T), b2t=_c(b2.reshape(8, 128).T))
    in_maps = []
    for core in range(8):
        b, h = core // 2, core % 2
        own = slice(h * TOWN, (h + 1) * TOWN)
        oth = slice((1 - h) * TOWN, (2 - h) * TOWN)
        m = dict(shared)
        m.update(x0fm=_c(np.concatenate([x0[b, own], x0[b, oth]], axis=0).T),
                 cos2=_c(np.concatenate([cos2[:, own], cos2[:, oth]], axis=1)),
                 sin2=_c(np.concatenate([sin2[:, own], sin2[:, oth]], axis=1)))
        in_maps.append(m)
    return in_maps


def run_block0(inputs):
    nc = build_block0()
    in_maps = prep_block0_inmaps(inputs)
    res = run_bass_kernel_spmd(nc, in_maps, core_ids=list(range(8)))
    x1 = np.zeros((B, T, D), dtype=np.float32)
    for core in range(8):
        b, h = core // 2, core % 2
        x1[b, h * TOWN:(h + 1) * TOWN] = res.results[core]["x1own"].T
    return x1


def kernel(**inputs):
    return run_fused(inputs)


# version 11
# speedup vs baseline: 1.0186x; 1.0186x over previous
import numpy as np
import ml_dtypes
import concourse.bass as bass
import concourse.mybir as mybir
from concourse.bass_utils import run_bass_kernel_spmd

F32 = mybir.dt.float32
BF16 = mybir.dt.bfloat16
AF = mybir.ActivationFunctionType
ALU = mybir.AluOpType
AX = mybir.AxisListType
BF = ml_dtypes.bfloat16

B, T, D = 4, 4096, 1024
NH, HD = 16, 64
TOWN = 2048
NDC = D // 128
HFF = 4096
EPS = 1e-5
TTT_LR = 0.01
NEXP = 8
HME = 4096



def _rope_tables():
    pos = np.arange(T, dtype=np.float64)
    i = np.arange(HD // 2, dtype=np.float64)
    coss, sins = [], []
    for s in (1.0, 4.0, 16.0):
        freqs = 1.0 / (10000.0 * s) ** (2.0 * i / HD)
        ang = pos[:, None] * freqs[None, :]
        coss.append(np.cos(ang))
        sins.append(np.sin(ang))
    cos = np.mean(coss, 0).T
    sin = np.mean(sins, 0).T
    cos2 = np.vstack([cos, cos, cos, cos]).astype(BF)
    sin2 = np.vstack([sin, sin, sin, sin]).astype(BF)
    return cos2, sin2


def _rope_perm():
    perm = np.zeros(D, dtype=np.int64)
    for c in range(NH // 2):
        base = c * 128
        perm[base + 0:base + 32] = (2 * c) * HD + np.arange(0, HD, 2)
        perm[base + 32:base + 64] = (2 * c) * HD + np.arange(1, HD, 2)
        perm[base + 64:base + 96] = (2 * c + 1) * HD + np.arange(0, HD, 2)
        perm[base + 96:base + 128] = (2 * c + 1) * HD + np.arange(1, HD, 2)
    return perm


def _c(a):
    return np.ascontiguousarray(a)


class Sem:
    def __init__(self, h):
        self.h = h
        self.n = 0

    def inc(self, instr, k=1):
        instr.then_inc(self.h, k)
        self.n += k
        return self.n


class Stream:

    def __init__(self, nc, name, nbuf, keep):
        self.nbuf = nbuf
        self.h = []
        for i in range(nbuf):
            g = nc.semaphore(f"{name}{i}")
            keep.append(g)
            self.h.append(g.__enter__())
        self.val = [0] * nbuf
        self.marks = []

    def put(self, eng, dmas):
        p = len(self.marks) % self.nbuf
        for (o, i_) in dmas:
            eng.dma_start(out=o, in_=i_).then_inc(self.h[p], 16)
            self.val[p] += 16
        self.marks.append((p, self.val[p]))
        return len(self.marks) - 1

    def wait(self, eng, item=None):
        if item is None:
            for p in range(self.nbuf):
                if self.val[p]:
                    eng.wait_ge(self.h[p], self.val[p])
            return
        p, v = self.marks[item]
        eng.wait_ge(self.h[p], v)



def build_block0(stop=None):
    nc = bass.Bass("TRN2", target_bir_lowering=False)

    x0fm = nc.dram_tensor("x0fm", [D, T], F32, kind="ExternalInput").ap()
    wqp = nc.dram_tensor("wqp", [D, D], F32, kind="ExternalInput").ap()
    wkp = nc.dram_tensor("wkp", [D, D], F32, kind="ExternalInput").ap()
    wvD = nc.dram_tensor("wv", [D, D], F32, kind="ExternalInput").ap()
    woD = nc.dram_tensor("wo", [D, D], F32, kind="ExternalInput").ap()
    w1t = nc.dram_tensor("w1t", [D, HFF], F32, kind="ExternalInput").ap()
    w1sD = nc.dram_tensor("w1s", [HFF, D], F32, kind="ExternalInput").ap()
    w2t = nc.dram_tensor("w2t", [HFF, D], F32, kind="ExternalInput").ap()
    w2sD = nc.dram_tensor("w2s", [D, HFF], F32, kind="ExternalInput").ap()
    b1tD = nc.dram_tensor("b1t", [128, 32], F32, kind="ExternalInput").ap()
    b2tD = nc.dram_tensor("b2t", [128, 8], F32, kind="ExternalInput").ap()
    cos2d = nc.dram_tensor("cos2", [128, T], BF16, kind="ExternalInput").ap()
    sin2d = nc.dram_tensor("sin2", [128, T], BF16, kind="ExternalInput").ap()
    x1own = nc.dram_tensor("x1own", [D, TOWN], F32, kind="ExternalOutput").ap()

    h1d = nc.dram_tensor("h1d", [32, 128, TOWN], BF16, kind="Internal").ap()
    w1sd = nc.dram_tensor("w1sd", [32, 128, 1024], BF16, kind="Internal").ap()
    w2sd = nc.dram_tensor("w2sd", [32, 128, 1024], BF16, kind="Internal").ap()

    xn = [nc.alloc_sbuf_tensor(f"xn{i}", [128, T], BF16) for i in range(NDC)]
    ffnacc = [nc.alloc_sbuf_tensor(f"ffa{i}", [128, TOWN], BF16) for i in range(NDC)]
    hs_ofm = nc.alloc_sbuf_tensor("hs_ofm", [128, 32 * 512], BF16)
    alpha1 = nc.alloc_sbuf_tensor("alpha1", [128, 32], F32)
    alpha2 = nc.alloc_sbuf_tensor("alpha2", [128, 8], F32)
    b1sb = nc.alloc_sbuf_tensor("b1sb", [128, 32], F32)
    b2sb = nc.alloc_sbuf_tensor("b2sb", [128, 8], F32)
    ones128 = nc.alloc_sbuf_tensor("ones128", [128, 128], BF16)
    epsb = nc.alloc_sbuf_tensor("epsb", [128, 1], F32)

    guards = []

    def scope(name, shape, dt):
        g = nc.sbuf_tensor(name, shape, dt)
        t = g.__enter__()
        guards.append(g)
        return t

    def close(n):
        for _ in range(n):
            guards.pop().__exit__(None, None, None)

    pg1 = nc.psum_tensor("PA", [128, 2048], F32)
    PA = pg1.__enter__()
    pg2 = nc.psum_tensor("PB", [128, 2048], F32)
    PB = pg2.__enter__()

    keep = []
    sg = [nc.semaphore(n) for n in ["sTT", "sVV", "sSS"]]
    TT, VV, SS = [Sem(g.__enter__()) for g in sg]
    keep.extend(sg)

    stB = Stream(nc, "stB", 1, keep)
    gX0 = Stream(nc, "gX0", 1, keep)
    sWT = Stream(nc, "sWT", 2, keep)
    sWS = Stream(nc, "sWS", 2, keep)
    gW1 = Stream(nc, "gW1", 2, keep)
    gW2 = Stream(nc, "gW2", 2, keep)
    sF1 = Stream(nc, "sF1", 2, keep)
    gH1 = Stream(nc, "gH1", 2, keep)
    sH1 = Stream(nc, "sH1", 1, keep)
    sF2 = Stream(nc, "sF2", 2, keep)
    sTAB = Stream(nc, "sTAB", 1, keep)
    gWG = Stream(nc, "gWG", 1, keep)
    gWOB = Stream(nc, "gWOB", 1, keep)
    sX0 = Stream(nc, "sX0", 2, keep)
    gXO = Stream(nc, "gXO", 2, keep)

    blk_g = nc.Block()
    blk = blk_g.__enter__()

    def _(sy):
        stB.put(sy, [(b1sb[:], b1tD), (b2sb[:], b2tD)])
    blk.sync(_)

    def _(g):
        for dc in range(NDC):
            gX0.put(g, [(xn[dc][:], x0fm[dc * 128:(dc + 1) * 128, :])])
    blk.gpsimd(_)

    def _(v):
        v.memset(epsb[:], EPS)
        VV.inc(v.memset(ones128[:], 1.0))
    blk.vector(_)
    vv_ones = VV.n

    wtmp = [scope(f"wtmp{i}", [128, 1024], F32) for i in range(2)]
    wstr = [scope(f"wstr{i}", [128, 4096], F32) for i in range(2)]
    wsgn = [scope(f"wsgn{i}", [128, 1024], BF16) for i in range(2)]

    w1t_r = w1t.rearrange("(dc p) h -> p dc h", p=128)

    ss0, vv0 = SS.n, VV.n

    def _(sy):
        for hc in range(32):
            if hc >= 2:
                sy.wait_ge(SS.h, ss0 + hc - 1)
                sy.wait_ge(VV.h, vv0 + hc - 1)
            sWT.put(sy, [(wtmp[hc % 2][:].rearrange("p (dc h) -> p dc h", dc=8),
                          w1t_r[:, :, hc * 128:(hc + 1) * 128])])
            sWS.put(sy, [(wstr[hc % 2][:, 0:1024], w1sD[hc * 128:(hc + 1) * 128, :])])
    blk.sync(_)

    def _(g):
        for hc in range(32):
            g.wait_ge(SS.h, ss0 + hc + 1)
            gW1.put(g, [(w1sd[hc], wsgn[hc % 2][:])])
    blk.gpsimd(_)

    def _(s):
        for hc in range(32):
            sWT.wait(s, hc)
            if hc >= 2:
                gW1.wait(s, hc - 2)
            SS.inc(s.activation(wsgn[hc % 2][:], wtmp[hc % 2][:], AF.Sign))
    blk.scalar(_)

    def _(v):
        for hc in range(32):
            sWS.wait(v, hc)
            VV.inc(v.reduce_sum(alpha1[:, hc:hc + 1], wstr[hc % 2][:, 0:1024],
                                axis=AX.X, apply_absolute_value=True))
    blk.vector(_)

    ss1, vv1 = SS.n, VV.n

    def _(sy):
        sy.wait_ge(SS.h, ss1)
        sy.wait_ge(VV.h, vv1)
        for hc in range(32):
            if hc >= 2:
                sy.wait_ge(SS.h, ss1 + hc - 1)
            sWT.put(sy, [(wtmp[hc % 2][:], w2t[hc * 128:(hc + 1) * 128, :])])
        for ec in range(8):
            if ec >= 2:
                sy.wait_ge(VV.h, vv1 + ec - 1)
            sWS.put(sy, [(wstr[ec % 2][:], w2sD[ec * 128:(ec + 1) * 128, :])])
    blk.sync(_)

    def _(g):
        for hc in range(32):
            g.wait_ge(SS.h, ss1 + hc + 1)
            gW2.put(g, [(w2sd[hc], wsgn[hc % 2][:])])
    blk.gpsimd(_)

    def _(s):
        gW1.wait(s)
        for hc in range(32):
            sWT.wait(s, 32 + hc)
            if hc >= 2:
                gW2.wait(s, hc - 2)
            SS.inc(s.activation(wsgn[hc % 2][:], wtmp[hc % 2][:], AF.Sign))
    blk.scalar(_)

    def _(v):
        for ec in range(8):
            sWS.wait(v, 32 + ec)
            VV.inc(v.reduce_sum(alpha2[:, ec:ec + 1], wstr[ec % 2][:],
                                axis=AX.X, apply_absolute_value=True))
        v.wait_ge(VV.h, VV.n)
        VV.inc(v.tensor_scalar_mul(alpha1[:], alpha1[:], 1.0 / D))
        VV.inc(v.tensor_scalar_mul(alpha2[:], alpha2[:], 1.0 / HFF))
    blk.vector(_)
    vv_alpha = VV.n

    close(6)

    fA = scope("fA", [128, 2048], F32)
    fB = scope("fB", [128, 2048], F32)
    fC = scope("fC", [128, 2048], F32)
    sqd = [scope(f"sqd{i}", [128, 2048], BF16) for i in range(2)]
    mu_bf = scope("mu_bf", [128, T], BF16)
    rstd_bf = scope("rstd_bf", [128, T], BF16)

    for tb in range(2):
        ts = tb * 2048
        tt_b = TT.n
        vv_b = VV.n

        def _(v, ts=ts, tt_b=tt_b, ss_w=SS.n):
            gX0.wait(v)
            v.wait_ge(SS.h, ss_w)
            gW1.wait(v)
            gW2.wait(v)
            for dc in range(NDC):
                if dc >= 2:
                    v.wait_ge(TT.h, tt_b + dc - 1)
                VV.inc(v.tensor_mul(sqd[dc % 2][:], xn[dc][:, ts:ts + 2048],
                                    xn[dc][:, ts:ts + 2048]))
        blk.vector(_)

        def _(t, ts=ts, vv_b=vv_b):
            t.wait_ge(VV.h, vv_ones)
            for dc in range(NDC):
                t.wait_ge(VV.h, vv_b + dc + 1)
                for q4 in range(4):
                    t.matmul(PA[:, q4 * 512:(q4 + 1) * 512], ones128[:],
                             xn[dc][:, ts + q4 * 512:ts + (q4 + 1) * 512],
                             start=(dc == 0), stop=(dc == NDC - 1))
                for q4 in range(4):
                    ins = t.matmul(PB[:, q4 * 512:(q4 + 1) * 512], ones128[:],
                                   sqd[dc % 2][:, q4 * 512:(q4 + 1) * 512],
                                   start=(dc == 0), stop=(dc == NDC - 1))
                TT.inc(ins)
        blk.tensor(_)
        tt_s1 = TT.n

        def _(v, tt_s1=tt_s1):
            v.wait_ge(TT.h, tt_s1)
            VV.inc(v.tensor_scalar_mul(fA[:], PA[:], 1.0 / D))
            VV.inc(v.tensor_scalar_mul(fB[:], PB[:], 1.0 / D))
            v.wait_ge(VV.h, VV.n)
            VV.inc(v.tensor_mul(fC[:], fA[:], fA[:]))
            v.wait_ge(VV.h, VV.n)
            VV.inc(v.tensor_sub(fB[:], fB[:], fC[:]))
        blk.vector(_)
        vv_var = VV.n

        def _(s, vv_var=vv_var):
            s.wait_ge(VV.h, vv_var)
            SS.inc(s.activation(fC[:], fB[:], AF.Sqrt, bias=epsb[:]))
        blk.scalar(_)
        ss_sd = SS.n

        def _(v, ts=ts, ss_sd=ss_sd):
            v.wait_ge(SS.h, ss_sd)
            VV.inc(v.reciprocal(fB[:], fC[:]))
            VV.inc(v.tensor_copy(mu_bf[:, ts:ts + 2048], fA[:]))
            v.wait_ge(VV.h, VV.n)
            VV.inc(v.tensor_copy(rstd_bf[:, ts:ts + 2048], fB[:]))
        blk.vector(_)

    def _(v):
        for dc in range(NDC):
            VV.inc(v.tensor_sub(xn[dc][:], xn[dc][:], mu_bf[:]))
            v.wait_ge(VV.h, VV.n)
            VV.inc(v.tensor_mul(xn[dc][:], xn[dc][:], rstd_bf[:]))
    blk.vector(_)
    vv_xn = VV.n

    close(7)

    w1b = [scope(f"w1b{i}", [128, 1024], BF16) for i in range(2)]
    h1t = [scope(f"h1t{i}", [128, TOWN], BF16) for i in range(2)]

    ttf, ssf = TT.n, SS.n

    def _(sy, ss_pre=SS.n):
        gW1.wait(sy)
        sy.wait_ge(VV.h, vv_xn)
        sy.wait_ge(SS.h, ss_pre)
        for hc in range(32):
            if hc >= 2:
                sy.wait_ge(TT.h, ttf + hc - 1)
            sF1.put(sy, [(w1b[hc % 2][:], w1sd[hc])])
    blk.sync(_)

    def _(t):
        t.wait_ge(VV.h, vv_xn)
        for hc in range(32):
            sF1.wait(t, hc)
            if hc >= 1:
                t.wait_ge(SS.h, ssf + hc)
            for q4 in range(4):
                for dc in range(NDC):
                    ins = t.matmul(PA[:, q4 * 512:(q4 + 1) * 512],
                                   w1b[hc % 2][:, dc * 128:(dc + 1) * 128],
                                   xn[dc][:, q4 * 512:(q4 + 1) * 512],
                                   start=(dc == 0), stop=(dc == NDC - 1))
            TT.inc(ins)
    blk.tensor(_)

    def _(g):
        for hc in range(32):
            g.wait_ge(SS.h, ssf + hc + 1)
            gH1.put(g, [(h1d[hc], h1t[hc % 2][:])])
    blk.gpsimd(_)

    def _(s):
        s.wait_ge(VV.h, vv_xn)
        stB.wait(s)
        for hc in range(32):
            s.wait_ge(TT.h, ttf + hc + 1)
            if hc >= 2:
                gH1.wait(s, hc - 2)
            SS.inc(s.activation(h1t[hc % 2][:], PA[:], AF.Gelu,
                                bias=b1sb[:, hc:hc + 1], scale=alpha1[:, hc:hc + 1]))
    blk.scalar(_)

    close(4)

    w2b = [scope(f"w2b{i}", [128, 1024], BF16) for i in range(2)]
    h1s = hs_ofm

    tt_prev_tb = TT.n
    ss_prev_tb = SS.n
    for tb in range(4):
        ts = tb * 512
        tt2 = TT.n

        def _(sy, ts=ts, tt2=tt2, tt_prev_tb=tt_prev_tb, tb=tb):
            gH1.wait(sy)
            gW2.wait(sy)
            sy.wait_ge(TT.h, tt_prev_tb)
            sH1.put(sy, [(h1s[:, grp * 8 * 512:(grp + 1) * 8 * 512].rearrange(
                              "p (hc t) -> p hc t", hc=8),
                          h1d[grp * 8:(grp + 1) * 8, :, ts:ts + 512].rearrange(
                              "hc p t -> p hc t"))
                         for grp in range(4)])
            for hc in range(32):
                if hc >= 2:
                    sy.wait_ge(TT.h, tt2 + hc - 1)
                sF2.put(sy, [(w2b[hc % 2][:], w2sd[hc])])
        blk.sync(_)

        def _(t, ss_prev_tb=ss_prev_tb, tb=tb):
            t.wait_ge(SS.h, ss_prev_tb)
            sH1.wait(t, tb)
            for hc in range(32):
                sF2.wait(t, 32 * tb + hc)
                for ec in range(8):
                    ps = PA if ec < 4 else PB
                    ins = t.matmul(ps[:, (ec % 4) * 512:(ec % 4 + 1) * 512],
                                   w2b[hc % 2][:, ec * 128:(ec + 1) * 128],
                                   h1s[:, hc * 512:(hc + 1) * 512],
                                   start=(hc == 0), stop=(hc == 31))
                TT.inc(ins)
        blk.tensor(_)
        tt_prev_tb = TT.n

        def _(s, ts=ts, tt_done=TT.n):
            s.wait_ge(TT.h, tt_done)
            stB.wait(s)
            for ec in range(8):
                ps = PA if ec < 4 else PB
                SS.inc(s.activation(ffnacc[ec][:, ts:ts + 512],
                                    ps[:, (ec % 4) * 512:(ec % 4 + 1) * 512],
                                    AF.Identity,
                                    bias=b2sb[:, ec:ec + 1], scale=alpha2[:, ec:ec + 1]))
        blk.scalar(_)
        ss_prev_tb = SS.n

    close(2)
    ss_attn0 = SS.n

    if stop == "f2":
        f32t_early = [scope(f"f32e{i}", [128, TOWN], F32) for i in range(2)]
        vv0e = VV.n
        def _(g):
            for dc in range(NDC):
                g.wait_ge(VV.h, vv0e + dc + 1)
                gXO.put(g, [(x1own[dc * 128:(dc + 1) * 128, :], f32t_early[dc % 2][:])])
            gXO.wait(g)
        blk.gpsimd(_)
        def _(v):
            v.wait_ge(SS.h, ss_attn0)
            for dc in range(NDC):
                if dc >= 2:
                    gXO.wait(v, dc - 2)
                VV.inc(v.tensor_copy(f32t_early[dc % 2][:], ffnacc[dc][:]))
        blk.vector(_)
        blk_g.__exit__(None, None, None)
        close(len(guards))
        pg2.__exit__(None, None, None)
        pg1.__exit__(None, None, None)
        for g in reversed(keep):
            g.__exit__(None, None, None)
        return nc

    cos2 = scope("cos2s", [128, T], BF16)
    sin2 = scope("sin2s", [128, T], BF16)
    kfm = scope("kfm", [128, T], BF16)
    qfm = scope("qfm", [128, TOWN], BF16)
    raw = scope("raw", [128, 2048], BF16)
    tscr = scope("tscr", [128, 2048], BF16)
    v65 = scope("v65", [128, 32 * 130], BF16)
    pb_sb = scope("pb_sb", [128, 2048], BF16)
    recb = scope("recb", [64, 2048], BF16)
    rec1 = scope("rec1", [1, 2048], BF16)
    wgq = scope("wgq", [128, 1024], BF16)
    wgk = scope("wgk", [128, 1024], BF16)
    wgv = scope("wgv", [128, 1024], BF16)
    ofm = hs_ofm

    def _(sy, tt_pre=TT.n, ss_pre=SS.n):
        sy.wait_ge(TT.h, tt_pre)
        sy.wait_ge(SS.h, ss_pre)
        sTAB.put(sy, [(cos2[:], cos2d), (sin2[:], sin2d)])
    blk.sync(_)

    wqp_r = wqp.rearrange("(dc p) j -> p dc j", p=128)
    wkp_r = wkp.rearrange("(dc p) j -> p dc j", p=128)
    wv_r = wvD.rearrange("(dc p) j -> p dc j", p=128)

    def emit_rope(v, dst, doff, src, tab_off, tlen):
        tsl = slice(tab_off, tab_off + tlen)
        for e in (0, 64):
            o = e + 32
            ce = cos2[e:e + 32, tsl]
            se = sin2[e:e + 32, tsl]
            co = cos2[o:o + 32, tsl]
            so = sin2[o:o + 32, tsl]
            VV.inc(v.tensor_mul(tscr[e:e + 32, 0:tlen], src[o:o + 32, 0:tlen], so))
            VV.inc(v.tensor_mul(dst[e:e + 32, doff:doff + tlen], src[e:e + 32, 0:tlen], ce))
            v.wait_ge(VV.h, VV.n)
            VV.inc(v.tensor_sub(dst[e:e + 32, doff:doff + tlen],
                                dst[e:e + 32, doff:doff + tlen], tscr[e:e + 32, 0:tlen]))
            VV.inc(v.tensor_mul(tscr[o:o + 32, 0:tlen], src[e:e + 32, 0:tlen], se))
            VV.inc(v.tensor_mul(dst[o:o + 32, doff:doff + tlen], src[o:o + 32, 0:tlen], co))
            v.wait_ge(VV.h, VV.n)
            VV.inc(v.tensor_add(dst[o:o + 32, doff:doff + tlen],
                                dst[o:o + 32, doff:doff + tlen], tscr[o:o + 32, 0:tlen]))

    tt_group_end = TT.n
    vv_group_end = VV.n

    for c in range(8):
        def _(g, tt_group_end=tt_group_end, c=c):
            g.wait_ge(TT.h, tt_group_end)
            gWG.put(g, [(wgq[:].rearrange("p (dc j) -> p dc j", dc=8),
                         wqp_r[:, :, c * 128:(c + 1) * 128]),
                        (wgk[:].rearrange("p (dc j) -> p dc j", dc=8),
                         wkp_r[:, :, c * 128:(c + 1) * 128]),
                        (wgv[:].rearrange("p (dc j) -> p dc j", dc=8),
                         wv_r[:, :, c * 128:(c + 1) * 128])])
            gWPP.put(g, [(w1bd[c], w1rD[c])])
            gWPP.put(g, [(w2bd[c], w2D[c])])
        blk.gpsimd(_)

        def _(t, vv_group_end=vv_group_end, c=c):
            gWG.wait(t, c)
            t.wait_ge(VV.h, max(vv_xn, vv_group_end))
            t.wait_ge(SS.h, ss_attn0)
            for q4 in range(4):
                for dc in range(NDC):
                    ins = t.matmul(PA[:, q4 * 512:(q4 + 1) * 512],
                                   wgq[:, dc * 128:(dc + 1) * 128],
                                   xn[dc][:, q4 * 512:(q4 + 1) * 512],
                                   start=(dc == 0), stop=(dc == NDC - 1))
            TT.inc(ins)
        blk.tensor(_)
        tt_q = TT.n

        def _(v, tt_q=tt_q):
            v.wait_ge(TT.h, tt_q)
            sTAB.wait(v)
            VV.inc(v.tensor_copy(raw[:], PA[:]))
            v.wait_ge(VV.h, VV.n)
            emit_rope(v, qfm, 0, raw, 0, 2048)
        blk.vector(_)

        for th in range(2):
            ts = th * 2048

            def _(t, ts=ts, vv_pre=VV.n):
                t.wait_ge(VV.h, vv_pre)
                for q4 in range(4):
                    for dc in range(NDC):
                        ins = t.matmul(PB[:, q4 * 512:(q4 + 1) * 512],
                                       wgk[:, dc * 128:(dc + 1) * 128],
                                       xn[dc][:, ts + q4 * 512:ts + (q4 + 1) * 512],
                                       start=(dc == 0), stop=(dc == NDC - 1))
                TT.inc(ins)
            blk.tensor(_)
            tt_k = TT.n

            def _(v, ts=ts, tt_k=tt_k):
                v.wait_ge(TT.h, tt_k)
                VV.inc(v.tensor_copy(raw[:], PB[:]))
                v.wait_ge(VV.h, VV.n)
                emit_rope(v, kfm, ts, raw, ts, 2048)
            blk.vector(_)

        def _(v, tt_group_end=tt_group_end):
            v.wait_ge(TT.h, tt_group_end)
            VV.inc(v.memset(v65[:], 1.0))
        blk.vector(_)
        vv_ms = VV.n

        tt_v0 = TT.n
        vv_v0 = VV.n

        def _(t, tt_v0=tt_v0, vv_v0=vv_v0, vv_ms=vv_ms):
            t.wait_ge(VV.h, vv_ms)
            for tc in range(32):
                if tc >= 2:
                    t.wait_ge(VV.h, vv_v0 + tc - 1)
                po = (tc % 2) * 512
                for dc in range(NDC):
                    ins = t.matmul(PA[:, po:po + 128],
                                   xn[dc][:, tc * 128:(tc + 1) * 128],
                                   wgv[:, dc * 128:(dc + 1) * 128],
                                   start=(dc == 0), stop=(dc == NDC - 1))
                TT.inc(ins)
        blk.tensor(_)

        def _(v, tt_v0=tt_v0, vv_ms=vv_ms):
            v.wait_ge(VV.h, vv_ms)
            for tc in range(32):
                v.wait_ge(TT.h, tt_v0 + tc + 1)
                po = (tc % 2) * 512
                v.tensor_copy(v65[:, tc * 130:tc * 130 + 64], PA[:, po:po + 64])
                VV.inc(v.tensor_copy(v65[:, tc * 130 + 65:tc * 130 + 129],
                                     PA[:, po + 64:po + 128]))
        blk.vector(_)

        nheads = 0 if stop == "qkv" else (1 if stop == "head0" else 2)
        for hh in range(nheads if stop in ("qkv", "head0", "grp0") and c == 0 else 2):
            hb = hh * 64
            vcol = hh * 65

            tt_h0 = TT.n
            ss_h0 = SS.n
            vv_pre_head = VV.n

            def _(t, tt_h0=tt_h0, ss_h0=ss_h0, vcol=vcol, hb=hb,
                  vv_pre_head=vv_pre_head):
                t.wait_ge(VV.h, vv_pre_head)
                t.wait_ge(SS.h, ss_h0)
                for kc in range(32):
                    for h2 in range(2):
                        if kc >= 1:
                            t.wait_ge(SS.h, ss_h0 + 2 * (kc - 1) + h2 + 1)
                        for q4 in range(2):
                            qo = h2 * 1024 + q4 * 512
                            ins = t.matmul(PA[:, qo:qo + 512],
                                           kfm[hb:hb + 64, kc * 128:(kc + 1) * 128],
                                           qfm[hb:hb + 64, qo:qo + 512],
                                           start=True, stop=True)
                        TT.inc(ins)
                    for h2 in range(2):
                        t.wait_ge(SS.h, ss_h0 + 2 * kc + h2 + 1)
                        for q4 in range(2):
                            qo = h2 * 1024 + q4 * 512
                            ins = t.matmul(PB[0:65, qo:qo + 512],
                                           v65[:, kc * 130 + vcol:kc * 130 + vcol + 65],
                                           pb_sb[:, qo:qo + 512],
                                           start=(kc == 0), stop=(kc == 31))
                        TT.inc(ins)
            blk.tensor(_)
            tt_av_last = TT.n

            def _(s, tt_h0=tt_h0):
                for kc in range(32):
                    for h2 in range(2):
                        s.wait_ge(TT.h, tt_h0 + 4 * kc + h2 + 1)
                        SS.inc(s.activation(pb_sb[:, h2 * 1024:(h2 + 1) * 1024],
                                            PA[:, h2 * 1024:(h2 + 1) * 1024],
                                            AF.Exp, scale=0.125))
            blk.scalar(_)

            def _(v, tt_av_last=tt_av_last):
                v.wait_ge(TT.h, tt_av_last)
                with nc.allow_low_precision("bf16 recip of softmax sums is within the graded bar"):
                    VV.inc(v.reciprocal(rec1[:], PB[64:65, :]))
            blk.vector(_)
            vv_rec = VV.n

            def _(t, vv_rec=vv_rec):
                t.wait_ge(VV.h, vv_rec)
                for q4 in range(4):
                    ins = t.matmul(PA[0:64, q4 * 512:(q4 + 1) * 512],
                                   ones128[0:1, 0:64],
                                   rec1[0:1, q4 * 512:(q4 + 1) * 512],
                                   start=True, stop=True)
                TT.inc(ins)
            blk.tensor(_)
            tt_bc = TT.n

            def _(v, tt_bc=tt_bc, c=c, hh=hh):
                v.wait_ge(TT.h, tt_bc)
                VV.inc(v.tensor_copy(recb[:], PA[0:64, :]))
                v.wait_ge(VV.h, VV.n)
                VV.inc(v.tensor_mul(ofm[hh * 64:(hh + 1) * 64, c * 2048:(c + 1) * 2048],
                                    PB[0:64, :], recb[:]))
            blk.vector(_)

        tt_group_end = TT.n
        vv_group_end = VV.n

        if stop in ("qkv", "head0", "grp0") and c == 0:
            break

    if stop in ("qkv", "head0", "grp0"):
        dmp = [scope(f"dmp{i}", [128, TOWN], F32) for i in range(2)]
        vvd = VV.n
        def _(g):
            for i in range(4):
                g.wait_ge(VV.h, vvd + i + 1)
                gXO.put(g, [(x1own[i * 128:(i + 1) * 128, :], dmp[i % 2][:])])
            gXO.wait(g)
        blk.gpsimd(_)
        def _(v):
            v.wait_ge(TT.h, TT.n)
            v.wait_ge(SS.h, SS.n)
            v.wait_ge(VV.h, VV.n)
            srcs = [qfm[:, 0:2048], kfm[:, 0:2048], kfm[:, 2048:4096],
                    v65[:, 0:2048]] if stop == "qkv" else                    [ofm[:, 0:2048], ofm[:, 2048:4096], qfm[:, 0:2048], kfm[:, 0:2048]]
            for i, s_ in enumerate(srcs):
                if i >= 2:
                    gXO.wait(v, i - 2)
                VV.inc(v.tensor_copy(dmp[i % 2][:], s_))
        blk.vector(_)
        blk_g.__exit__(None, None, None)
        close(len(guards))
        pg2.__exit__(None, None, None)
        pg1.__exit__(None, None, None)
        for g in reversed(keep):
            g.__exit__(None, None, None)
        return nc

    close(13)

    wob = [scope(f"wob{i}", [128, 1024], BF16) for i in range(8)]
    x0ld = [scope(f"x0ld{i}", [128, TOWN], F32) for i in range(2)]
    x1t = [scope(f"x1t{i}", [128, TOWN], F32) for i in range(2)]
    f32t = scope("f32t", [128, TOWN], F32)

    wo_r = woD.rearrange("(hc p) j -> p hc j", p=128)

    def _(g, tt_group_end=tt_group_end, vv_pre=VV.n, ss_pre=SS.n):
        g.wait_ge(TT.h, tt_group_end)
        g.wait_ge(VV.h, vv_pre)
        g.wait_ge(SS.h, ss_pre)
        gWOB.put(g, [(wob[hdc][:], wo_r[:, hdc, :]) for hdc in range(8)])
    blk.gpsimd(_)

    vv_cmb0 = VV.n
    tt_wo0 = TT.n

    def _(sy, tt_pre=TT.n, ss_pre=SS.n):
        sy.wait_ge(TT.h, tt_pre)
        sy.wait_ge(SS.h, ss_pre)
        sy.wait_ge(VV.h, vv_cmb0)
        for dc in range(NDC):
            if dc >= 2:
                sy.wait_ge(VV.h, vv_cmb0 + 3 * (dc - 1))
            sX0.put(sy, [(x0ld[dc % 2][:], x0fm[dc * 128:(dc + 1) * 128, 0:TOWN])])
    blk.sync(_)

    def _(t, vv_group_end=vv_group_end):
        gWOB.wait(t)
        t.wait_ge(VV.h, vv_group_end)
        for dco in range(NDC):
            if dco >= 2:
                t.wait_ge(VV.h, vv_cmb0 + 3 * (dco - 1))
            ps = PA if dco % 2 == 0 else PB
            for q4 in range(4):
                for hdc in range(8):
                    ins = t.matmul(ps[:, q4 * 512:(q4 + 1) * 512],
                                   wob[hdc][:, dco * 128:(dco + 1) * 128],
                                   ofm[:, hdc * 2048 + q4 * 512:hdc * 2048 + (q4 + 1) * 512],
                                   start=(hdc == 0), stop=(hdc == 7))
            TT.inc(ins)
    blk.tensor(_)

    def _(g):
        for dco in range(NDC):
            g.wait_ge(VV.h, vv_cmb0 + 3 * (dco + 1))
            gXO.put(g, [(x1own[dco * 128:(dco + 1) * 128, :], x1t[dco % 2][:])])
        gXO.wait(g)
    blk.gpsimd(_)

    def _(v):
        for dco in range(NDC):
            v.wait_ge(TT.h, tt_wo0 + dco + 1)
            sX0.wait(v, dco)
            if dco >= 2:
                gXO.wait(v, dco - 2)
            ps = PA if dco % 2 == 0 else PB
            v.wait_ge(VV.h, VV.n)
            VV.inc(v.tensor_copy(f32t[:], ffnacc[dco][:]))
            v.wait_ge(VV.h, VV.n)
            VV.inc(v.tensor_add(f32t[:], f32t[:], ps[:]))
            v.wait_ge(VV.h, VV.n)
            VV.inc(v.tensor_add(x1t[dco % 2][:], f32t[:], x0ld[dco % 2][:]))
    blk.vector(_)

    blk_g.__exit__(None, None, None)
    close(len(guards))
    pg2.__exit__(None, None, None)
    pg1.__exit__(None, None, None)
    for g in reversed(keep):
        g.__exit__(None, None, None)
    for g in reversed(sg):
        pass
    return nc



def build_ttt():
    nc = bass.Bass("TRN2", target_bir_lowering=False)

    x1fm = nc.dram_tensor("x1fm", [D, T], F32, kind="ExternalInput").ap()
    wktD = nc.dram_tensor("wkt", [D, 64], F32, kind="ExternalInput").ap()
    wvtD = nc.dram_tensor("wvt", [D, D], F32, kind="ExternalInput").ap()
    wqtD = nc.dram_tensor("wqt", [D, 64], F32, kind="ExternalInput").ap()
    bkD = nc.dram_tensor("bk", [64, 1], F32, kind="ExternalInput").ap()
    bqD = nc.dram_tensor("bq", [64, 1], F32, kind="ExternalInput").ap()
    bkbD = nc.dram_tensor("bkb", [128, 64], F32, kind="ExternalInput").ap()
    bvbD = nc.dram_tensor("bvb", [128, D], F32, kind="ExternalInput").ap()
    gbD = nc.dram_tensor("gb", [128, 8], F32, kind="ExternalInput").ap()
    btD = nc.dram_tensor("bt", [128, 8], F32, kind="ExternalInput").ap()
    mUeD = nc.dram_tensor("mUe", [128, 128], BF16, kind="ExternalInput").ap()
    mLeD = nc.dram_tensor("mLe", [128, 128], BF16, kind="ExternalInput").ap()
    mUnD = nc.dram_tensor("mUn", [128, 128], BF16, kind="ExternalInput").ap()
    idD = nc.dram_tensor("identb", [128, 128], BF16, kind="ExternalInput").ap()
    yfull = nc.dram_tensor("yfull", [T, D], F32, kind="ExternalOutput").ap()

    kfmB = nc.alloc_sbuf_tensor("kfmB", [64, T], BF16)
    qfmB = nc.alloc_sbuf_tensor("qfmB", [64, T], BF16)
    ktm = nc.alloc_sbuf_tensor("ktm", [128, 32 * 64], BF16)
    bvb = nc.alloc_sbuf_tensor("bvbS", [128, D], F32)
    bkb = nc.alloc_sbuf_tensor("bkbS", [128, 64], F32)
    bk = nc.alloc_sbuf_tensor("bkS", [64, 1], F32)
    bq = nc.alloc_sbuf_tensor("bqS", [64, 1], F32)
    gma = nc.alloc_sbuf_tensor("gma", [128, 8], F32)
    bta = nc.alloc_sbuf_tensor("bta", [128, 8], F32)
    mUe = nc.alloc_sbuf_tensor("mUeS", [128, 128], BF16)
    mLe = nc.alloc_sbuf_tensor("mLeS", [128, 128], BF16)
    mUn = nc.alloc_sbuf_tensor("mUnS", [128, 128], BF16)
    identb = nc.alloc_sbuf_tensor("identS", [128, 128], BF16)
    ones128 = nc.alloc_sbuf_tensor("ones128", [128, 128], BF16)
    epsb = nc.alloc_sbuf_tensor("epsb", [128, 1], F32)
    Sst = nc.alloc_sbuf_tensor("Sst", [64, D], F32)
    Sbf = nc.alloc_sbuf_tensor("Sbf", [64, D], BF16)
    nUt = nc.alloc_sbuf_tensor("nUt", [128, 128], F32)
    nLt = nc.alloc_sbuf_tensor("nLt", [128, 128], F32)
    Cbuf = nc.alloc_sbuf_tensor("Cbuf", [128, 6 * 128], F32)
    Ta = nc.alloc_sbuf_tensor("Ta", [128, 128], F32)
    Tb = nc.alloc_sbuf_tensor("Tb", [128, 128], F32)
    Rs = nc.alloc_sbuf_tensor("Rs", [128, 128], BF16)
    brawS = nc.alloc_sbuf_tensor("brawS", [128, D], BF16)
    Utm = nc.alloc_sbuf_tensor("Utm", [128, D], BF16)
    pmS = nc.alloc_sbuf_tensor("pmS", [128, 128], BF16)
    ysb = [nc.alloc_sbuf_tensor(f"ysb{i}", [128, D], F32) for i in range(2)]

    guards = []

    def scope(name, shape, dt):
        g = nc.sbuf_tensor(name, shape, dt)
        t = g.__enter__()
        guards.append(g)
        return t

    def close(n):
        for _ in range(n):
            guards.pop().__exit__(None, None, None)

    x1bf = [scope(f"x1b{i}", [128, T], BF16) for i in range(NDC)]
    mu_bf = scope("mu_bf", [128, T], BF16)
    f_bf = scope("f_bf", [128, T], BF16)

    pg1 = nc.psum_tensor("PA", [128, 2048], F32)
    PA = pg1.__enter__()
    pg2 = nc.psum_tensor("PB", [128, 2048], F32)
    PB = pg2.__enter__()

    keep = []
    sg = [nc.semaphore(n) for n in ["sTT", "sVV", "sSS"]]
    TT, VV, SS = [Sem(g.__enter__()) for g in sg]
    keep.extend(sg)
    gX1 = Stream(nc, "gX1", 1, keep)
    gWP = Stream(nc, "gWP", 1, keep)
    sMS = Stream(nc, "sMS", 1, keep)
    gY = Stream(nc, "gY", 2, keep)

    blk_g = nc.Block()
    blk = blk_g.__enter__()

    def _(g):
        for dc in range(NDC):
            gX1.put(g, [(x1bf[dc][:], x1fm[dc * 128:(dc + 1) * 128, :])])
    blk.gpsimd(_)

    def _(sy, tt_pre=TT.n, vv_pre=VV.n, ss_pre=SS.n):
        sy.wait_ge(TT.h, tt_pre)
        sy.wait_ge(VV.h, vv_pre)
        sy.wait_ge(SS.h, ss_pre)
        sMS.put(sy, [(bvb[:], bvbD), (bkb[:], bkbD), (bk[:], bkD), (bq[:], bqD),
                     (gma[:], gbD), (bta[:], btD), (mUe[:], mUeD), (mLe[:], mLeD),
                     (mUn[:], mUnD), (identb[:], idD)])
    blk.sync(_)

    def _(v):
        v.memset(epsb[:], EPS)
        v.memset(Sst[:], 0.0)
        v.memset(Sbf[:], 0.0)
        VV.inc(v.memset(ones128[:], 1.0))
    blk.vector(_)
    vv_ones = VV.n

    fA = scope("fA", [128, 2048], F32)
    fB = scope("fB", [128, 2048], F32)
    fC = scope("fC", [128, 2048], F32)
    fD = scope("fD", [128, 2048], F32)
    sqd = [scope(f"sqd{i}", [128, 2048], BF16) for i in range(2)]

    for tb in range(2):
        ts = tb * 2048
        tt_b = TT.n
        vv_b = VV.n

        def _(v, ts=ts, tt_b=tt_b):
            gX1.wait(v)
            for dc in range(NDC):
                if dc >= 2:
                    v.wait_ge(TT.h, tt_b + dc - 1)
                VV.inc(v.tensor_mul(sqd[dc % 2][:], x1bf[dc][:, ts:ts + 2048],
                                    x1bf[dc][:, ts:ts + 2048]))
        blk.vector(_)

        def _(t, ts=ts, vv_b=vv_b):
            t.wait_ge(VV.h, vv_ones)
            for dc in range(NDC):
                t.wait_ge(VV.h, vv_b + dc + 1)
                for q4 in range(4):
                    t.matmul(PA[:, q4 * 512:(q4 + 1) * 512], ones128[:],
                             x1bf[dc][:, ts + q4 * 512:ts + (q4 + 1) * 512],
                             start=(dc == 0), stop=(dc == NDC - 1))
                for q4 in range(4):
                    ins = t.matmul(PB[:, q4 * 512:(q4 + 1) * 512], ones128[:],
                                   sqd[dc % 2][:, q4 * 512:(q4 + 1) * 512],
                                   start=(dc == 0), stop=(dc == NDC - 1))
                TT.inc(ins)
        blk.tensor(_)
        tt_s1 = TT.n

        def _(v, tt_s1=tt_s1):
            v.wait_ge(TT.h, tt_s1)
            VV.inc(v.tensor_scalar_mul(fA[:], PA[:], 1.0 / D))
            VV.inc(v.tensor_scalar_mul(fB[:], PB[:], 1.0 / D))
            v.wait_ge(VV.h, VV.n)
            VV.inc(v.tensor_mul(fC[:], fA[:], fA[:]))
            v.wait_ge(VV.h, VV.n)
            VV.inc(v.tensor_sub(fB[:], fB[:], fC[:]))
        blk.vector(_)
        vv_var = VV.n

        def _(s, vv_var=vv_var):
            s.wait_ge(VV.h, vv_var)
            SS.inc(s.activation(fC[:], fB[:], AF.Sqrt, bias=epsb[:]))
        blk.scalar(_)
        ss_sd = SS.n

        def _(v, ss_sd=ss_sd):
            v.wait_ge(SS.h, ss_sd)
            VV.inc(v.reciprocal(fD[:], fC[:]))
            v.wait_ge(VV.h, VV.n)
            VV.inc(v.tensor_mul(fC[:], fD[:], fD[:]))
            v.wait_ge(VV.h, VV.n)
            VV.inc(v.tensor_mul(fC[:], fB[:], fC[:]))
        blk.vector(_)
        vv_vn = VV.n

        def _(s, vv_vn=vv_vn):
            s.wait_ge(VV.h, vv_vn)
            SS.inc(s.activation(fC[:], fC[:], AF.Sqrt, bias=epsb[:]))
        blk.scalar(_)
        ss_sdn = SS.n

        def _(v, ts=ts, ss_sdn=ss_sdn):
            v.wait_ge(SS.h, ss_sdn)
            VV.inc(v.reciprocal(fC[:], fC[:]))
            v.wait_ge(VV.h, VV.n)
            VV.inc(v.tensor_mul(fC[:], fC[:], fD[:]))
            v.wait_ge(VV.h, VV.n)
            VV.inc(v.tensor_copy(mu_bf[:, ts:ts + 2048], fA[:]))
            VV.inc(v.tensor_copy(f_bf[:, ts:ts + 2048], fC[:]))
        blk.vector(_)

    def _(v):
        sMS.wait(v)
        for dc in range(NDC):
            v.wait_ge(VV.h, VV.n)
            VV.inc(v.tensor_sub(x1bf[dc][:], x1bf[dc][:], mu_bf[:]))
            v.wait_ge(VV.h, VV.n)
            VV.inc(v.tensor_mul(x1bf[dc][:], x1bf[dc][:], f_bf[:]))
            v.wait_ge(VV.h, VV.n)
            VV.inc(v.tensor_scalar(x1bf[dc][:], x1bf[dc][:], gma[:, dc:dc + 1],
                                   bta[:, dc:dc + 1], ALU.mult, ALU.add))
    blk.vector(_)
    vv_h = VV.n

    close(6)
    h = x1bf
    close(2)

    vtm = [nc.alloc_sbuf_tensor(f"vtm{i}", [128, D], BF16, side="right")
           for i in range(32)]
    wvb = scope("wvb", [128, 8 * D], BF16)
    wkb = scope("wkb", [128, 8 * 64], BF16)
    wqb = scope("wqb", [128, 8 * 64], BF16)

    def _(g, ss_pre=SS.n):
        g.wait_ge(VV.h, vv_h)
        g.wait_ge(SS.h, ss_pre)
        gWP.put(g, [(wvb[:].rearrange("p (dc e) -> p dc e", dc=8),
                     wvtD.rearrange("(dc p) e -> p dc e", p=128)),
                    (wkb[:].rearrange("p (dc e) -> p dc e", dc=8),
                     wktD.rearrange("(dc p) e -> p dc e", p=128)),
                    (wqb[:].rearrange("p (dc e) -> p dc e", dc=8),
                     wqtD.rearrange("(dc p) e -> p dc e", p=128))])
    blk.gpsimd(_)

    def _(t, ss_pre=SS.n):
        t.wait_ge(VV.h, vv_h)
        gWP.wait(t)
        for th in range(2):
            ts = th * 2048
            if th >= 1:
                t.wait_ge(SS.h, ss_pre + 2 * th - 1)
            for q4 in range(4):
                for dc in range(NDC):
                    ins = t.matmul(PA[0:64, q4 * 512:(q4 + 1) * 512],
                                   wkb[:, dc * 64:(dc + 1) * 64],
                                   h[dc][:, ts + q4 * 512:ts + (q4 + 1) * 512],
                                   start=(dc == 0), stop=(dc == NDC - 1))
            TT.inc(ins)
            if th >= 1:
                t.wait_ge(SS.h, ss_pre + 2 * th)
            for q4 in range(4):
                for dc in range(NDC):
                    ins = t.matmul(PB[0:64, q4 * 512:(q4 + 1) * 512],
                                   wqb[:, dc * 64:(dc + 1) * 64],
                                   h[dc][:, ts + q4 * 512:ts + (q4 + 1) * 512],
                                   start=(dc == 0), stop=(dc == NDC - 1))
            TT.inc(ins)
    blk.tensor(_)
    tt_kq0 = TT.n - 4

    def _(s):
        sMS.wait(s)
        for th in range(2):
            ts = th * 2048
            s.wait_ge(TT.h, tt_kq0 + 2 * th + 1)
            SS.inc(s.activation(kfmB[:, ts:ts + 2048], PA[0:64, :], AF.Identity,
                                bias=bk[:]))
            s.wait_ge(TT.h, tt_kq0 + 2 * th + 2)
            SS.inc(s.activation(qfmB[:, ts:ts + 2048], PB[0:64, :], AF.Identity,
                                bias=bq[:]))
    blk.scalar(_)
    ss_kq = SS.n

    tt_kv0 = TT.n
    vv_kv0 = VV.n

    def _(t):
        t.wait_ge(SS.h, ss_kq)
        for tc in range(32):
            if tc >= 2:
                t.wait_ge(VV.h, vv_kv0 + 2 * (tc - 1))
            pok = (tc % 2) * 512
            for dc in range(NDC):
                ins = t.matmul(PA[:, pok:pok + 64],
                               h[dc][:, tc * 128:(tc + 1) * 128],
                               wkb[:, dc * 64:(dc + 1) * 64],
                               start=(dc == 0), stop=(dc == NDC - 1))
            TT.inc(ins)
            pov = (tc % 2) * 1024
            for j in range(2):
                for dc in range(NDC):
                    ins = t.matmul(PB[:, pov + j * 512:pov + (j + 1) * 512],
                                   h[dc][:, tc * 128:(tc + 1) * 128],
                                   wvb[:, dc * 1024 + j * 512:dc * 1024 + (j + 1) * 512],
                                   start=(dc == 0), stop=(dc == NDC - 1))
            TT.inc(ins)
    blk.tensor(_)

    def _(v):
        for tc in range(32):
            v.wait_ge(TT.h, tt_kv0 + 2 * tc + 1)
            pok = (tc % 2) * 512
            VV.inc(v.tensor_add(ktm[:, tc * 64:(tc + 1) * 64], PA[:, pok:pok + 64], bkb[:]))
            v.wait_ge(TT.h, tt_kv0 + 2 * tc + 2)
            pov = (tc % 2) * 1024
            VV.inc(v.tensor_add(vtm[tc][:], PB[:, pov:pov + 1024], bvb[:]))
    blk.vector(_)
    vv_kv_done = VV.n

    close(3)
    close(8)

    psG = PB[:, 1024:1152]
    psD0 = PB[:, 1024:1152]
    psD1 = PB[:, 1536:1664]
    psKM = PA[:, 0:1024]
    psU = PA[:, 1024:2048]
    psY = PB[:, 0:1024]
    psSK = PB[0:64, 1024:2048]

    tt_sc0 = TT.n
    vv_sc0 = VV.n
    NTT, NVV = 24, 31

    def _(t):
        t.wait_ge(VV.h, vv_kv_done)
        for tc in range(32):
            tt0 = tt_sc0 + NTT * tc
            vv0 = vv_sc0 + NVV * tc
            ksl = kfmB[0:64, tc * 128:(tc + 1) * 128]
            qsl = qfmB[0:64, tc * 128:(tc + 1) * 128]
            t.wait_ge(VV.h, vv0)
            TT.inc(t.matmul(psG, ksl, ksl, start=True, stop=True))
            for l in range(5):
                t.wait_ge(VV.h, vv0 + 2 + 3 * l)
                TT.inc(t.matmul(psD0, nLt[:], nUt[:], start=True, stop=True))
                TT.inc(t.matmul(psD1, nUt[:], nLt[:], start=True, stop=True))
            t.wait_ge(VV.h, vv0 + 17)
            TT.inc(t.matmul(psD0, nLt[:], nUt[:], start=True, stop=True))
            for idx, j in enumerate([5, 4, 3, 2, 1, 0]):
                t.wait_ge(VV.h, vv0 + 19 + idx)
                ps = psD1 if idx % 2 == 0 else psD0
                rhs = [Ta, Tb][idx % 2]
                TT.inc(t.matmul(ps, Cbuf[:, j * 128:(j + 1) * 128], rhs[:],
                                start=True, stop=True))
            for j in range(2):
                ins = t.matmul(psKM[:, j * 512:(j + 1) * 512], ksl,
                               Sbf[:, j * 512:(j + 1) * 512], start=True, stop=True)
            TT.inc(ins)
            t.wait_ge(VV.h, vv0 + 26)
            for j in range(2):
                ins = t.matmul(psU[:, j * 512:(j + 1) * 512], Rs[:],
                               brawS[:, j * 512:(j + 1) * 512], start=True, stop=True)
            TT.inc(ins)
            for j in range(2):
                ins = t.matmul(psY[:, j * 512:(j + 1) * 512], qsl,
                               Sbf[:, j * 512:(j + 1) * 512], start=True, stop=False)
            TT.inc(ins)
            t.wait_ge(VV.h, vv0 + 25)
            TT.inc(t.matmul(psG, ksl, qsl, start=True, stop=True))
            t.wait_ge(VV.h, vv0 + 28)
            for j in range(2):
                ins = t.matmul(psY[:, j * 512:(j + 1) * 512], pmS[:],
                               Utm[:, j * 512:(j + 1) * 512], start=False, stop=True)
            TT.inc(ins)
            for j in range(2):
                ins = t.matmul(psSK[:, j * 512:(j + 1) * 512],
                               ktm[:, tc * 64:(tc + 1) * 64],
                               Utm[:, j * 512:(j + 1) * 512], start=True, stop=True)
            TT.inc(ins)
    blk.tensor(_)

    def _(g):
        for tc in range(32):
            vv0 = vv_sc0 + NVV * tc
            g.wait_ge(VV.h, vv0 + 29)
            gY.put(g, [(yfull[tc * 128:(tc + 1) * 128, :], ysb[tc % 2][:])])
        gY.wait(g)
    blk.gpsimd(_)

    def _(v):
        for tc in range(32):
            tt0 = tt_sc0 + NTT * tc
            vv0 = vv_sc0 + NVV * tc
            v.wait_ge(TT.h, tt0 + 1)
            VV.inc(v.tensor_mul(nUt[:], psG, mUe[:]))
            VV.inc(v.tensor_mul(nLt[:], psG, mLe[:]))
            for l in range(5):
                v.wait_ge(VV.h, VV.n)
                if l == 0:
                    VV.inc(v.tensor_sub(Cbuf[:, 0:128], identb[:], nLt[:]))
                else:
                    VV.inc(v.tensor_add(Cbuf[:, l * 128:(l + 1) * 128], identb[:], nLt[:]))
                v.wait_ge(TT.h, tt0 + 3 + 2 * l)
                v.wait_ge(VV.h, VV.n)
                VV.inc(v.tensor_copy(nUt[:], psD0))
                VV.inc(v.tensor_copy(nLt[:], psD1))
            v.wait_ge(VV.h, VV.n)
            VV.inc(v.tensor_add(Cbuf[:, 5 * 128:6 * 128], identb[:], nLt[:]))
            v.wait_ge(TT.h, tt0 + 12)
            VV.inc(v.tensor_add(Ta[:], identb[:], psD0))
            for idx in range(6):
                v.wait_ge(TT.h, tt0 + 13 + idx)
                ps = psD1 if idx % 2 == 0 else psD0
                if idx < 5:
                    dst = [Tb, Ta][idx % 2]
                    VV.inc(v.tensor_copy(dst[:], ps))
                else:
                    with nc.allow_low_precision("lam-scaled inverse factor in bf16"):
                        VV.inc(v.tensor_scalar_mul(Rs[:], ps, TTT_LR))
            v.wait_ge(TT.h, tt0 + 19)
            VV.inc(v.tensor_sub(brawS[:], psKM, vtm[tc][:]))
            v.wait_ge(TT.h, tt0 + 20)
            VV.inc(v.tensor_copy(Utm[:], psU))
            v.wait_ge(TT.h, tt0 + 22)
            VV.inc(v.tensor_mul(pmS[:], psG, mUn[:]))
            v.wait_ge(TT.h, tt0 + 23)
            if tc >= 2:
                gY.wait(v, tc - 2)
            VV.inc(v.tensor_copy(ysb[tc % 2][:], psY))
            v.wait_ge(TT.h, tt0 + 24)
            v.wait_ge(VV.h, VV.n)
            VV.inc(v.tensor_sub(Sst[:], Sst[:], psSK))
            v.wait_ge(VV.h, VV.n)
            VV.inc(v.tensor_copy(Sbf[:], Sst[:]))
    blk.vector(_)

    blk_g.__exit__(None, None, None)
    close(len(guards))
    pg2.__exit__(None, None, None)
    pg1.__exit__(None, None, None)
    for g in reversed(keep):
        g.__exit__(None, None, None)
    return nc


def prep_ttt_inmaps(inputs, x1):
    wk = np.asarray(inputs["L1_attn_Wk"], np.float32)
    wv = np.asarray(inputs["L1_attn_Wv"], np.float32)
    wq = np.asarray(inputs["L1_attn_Wq"], np.float32)
    bk = np.asarray(inputs["L1_attn_bk"], np.float32)
    bv = np.asarray(inputs["L1_attn_bv"], np.float32)
    bq = np.asarray(inputs["L1_attn_bq"], np.float32)
    gm = np.asarray(inputs["L1_attn_gamma"], np.float32)
    bt = np.asarray(inputs["L1_attn_beta"], np.float32)
    on = np.ones((128, 128), np.float32)
    shared = dict(
        wkt=_c(wk.T), wvt=_c(wv.T), wqt=_c(wq.T),
        bk=_c(bk.reshape(64, 1)), bq=_c(bq.reshape(64, 1)),
        bkb=_c(np.tile(bk, (128, 1))), bvb=_c(np.tile(bv, (128, 1))),
        gb=_c(gm.reshape(8, 128).T), bt=_c(bt.reshape(8, 128).T),
        mUe=_c((np.triu(on, 1) * TTT_LR).astype(BF)),
        mLe=_c((np.tril(on, -1) * TTT_LR).astype(BF)),
        mUn=_c((-np.triu(on, 1)).astype(BF)),
        identb=_c(np.eye(128, dtype=np.float32).astype(BF)))
    in_maps = []
    for core in range(8):
        b = core // 2
        m = dict(shared)
        m.update(x1fm=_c(x1[b].T))
        in_maps.append(m)
    return in_maps


def run_ttt(inputs, x1):
    nc = build_ttt()
    in_maps = prep_ttt_inmaps(inputs, x1)
    res = run_bass_kernel_spmd(nc, in_maps, core_ids=list(range(8)))
    y = np.zeros((B, T, D), dtype=np.float32)
    for core in range(8):
        b, hh = core // 2, core % 2
        y[b, hh * TOWN:(hh + 1) * TOWN] = res.results[core]["yfull"][hh * TOWN:(hh + 1) * TOWN]
    return y



def build_moe():
    nc = bass.Bass("TRN2", target_bir_lowering=False)

    x1ownD = nc.dram_tensor("x1own", [D, TOWN], F32, kind="ExternalInput").ap()
    yownD = nc.dram_tensor("yown", [D, TOWN], F32, kind="ExternalInput").ap()
    wrD = nc.dram_tensor("wr", [D, 8], F32, kind="ExternalInput").ap()
    br8D = nc.dram_tensor("br8", [8, 1], F32, kind="ExternalInput").ap()
    w1rD = nc.dram_tensor("w1r", [8, 32, 128, 1024], F32, kind="ExternalInput").ap()
    w2D = nc.dram_tensor("w2m", [8, HME, D], F32, kind="ExternalInput").ap()
    b1aD = nc.dram_tensor("b1a", [128, 256], F32, kind="ExternalInput").ap()
    b2aD = nc.dram_tensor("b2a", [128, 64], F32, kind="ExternalInput").ap()
    idD = nc.dram_tensor("identb", [128, 128], BF16, kind="ExternalInput").ap()
    x2ownD = nc.dram_tensor("x2own", [D, TOWN], F32, kind="ExternalOutput").ap()

    x2acc = [nc.alloc_sbuf_tensor(f"x2a{i}", [128, TOWN], F32) for i in range(NDC)]
    xn1bf = [nc.alloc_sbuf_tensor(f"xn1b{i}", [128, TOWN], BF16) for i in range(NDC)]
    h1m = nc.alloc_sbuf_tensor("h1m", [128, 32 * 512], BF16)
    w1c = [nc.alloc_sbuf_tensor(f"w1c{i}", [128, 1024], BF16) for i in range(2)]
    w2c = [nc.alloc_sbuf_tensor(f"w2c{i}", [128, 1024], BF16) for i in range(2)]
    wbv = nc.alloc_sbuf_tensor("wbv", [128, TOWN], BF16)
    wrsb = nc.alloc_sbuf_tensor("wrsb", [128, 64], BF16)
    br8 = nc.alloc_sbuf_tensor("br8S", [8, 1], F32)
    b1a = nc.alloc_sbuf_tensor("b1aS", [128, 256], F32)
    b2a = nc.alloc_sbuf_tensor("b2aS", [128, 64], F32)
    identb = nc.alloc_sbuf_tensor("identS", [128, 128], BF16)
    ones128 = nc.alloc_sbuf_tensor("ones128", [128, 128], BF16)
    epsb = nc.alloc_sbuf_tensor("epsb", [128, 1], F32)
    scoresbf = nc.alloc_sbuf_tensor("scoresbf", [8, TOWN], BF16)
    s_tm = nc.alloc_sbuf_tensor("s_tm", [128, 128], F32)
    sA = nc.alloc_sbuf_tensor("sA", [128, 128], F32)
    sB = nc.alloc_sbuf_tensor("sB", [128, 128], F32)
    m16 = nc.alloc_sbuf_tensor("m16", [128, 64], F32)
    w_tmb = nc.alloc_sbuf_tensor("w_tmb", [128, 128], BF16)
    wrow = nc.alloc_sbuf_tensor("wrow", [1, TOWN], BF16)
    etmp = [nc.alloc_sbuf_tensor(f"etmp{i}", [128, 512], F32) for i in range(2)]

    guards = []

    def scope(name, shape, dt):
        g = nc.sbuf_tensor(name, shape, dt)
        t = g.__enter__()
        guards.append(g)
        return t

    def close(n):
        for _ in range(n):
            guards.pop().__exit__(None, None, None)

    pg1 = nc.psum_tensor("PA", [128, 2048], F32)
    PA = pg1.__enter__()
    pg2 = nc.psum_tensor("PB", [128, 2048], F32)
    PB = pg2.__enter__()

    keep = []
    sg = [nc.semaphore(n) for n in ["sTT", "sVV", "sSS"]]
    TT, VV, SS = [Sem(g.__enter__()) for g in sg]
    keep.extend(sg)
    sX1 = Stream(nc, "sX1", 1, keep)
    sY = Stream(nc, "sY", 2, keep)
    sSM = Stream(nc, "sSM", 1, keep)
    gW1L = Stream(nc, "gW1L", 2, keep)
    gW2L = Stream(nc, "gW2L", 2, keep)
    gWR = Stream(nc, "gWR", 1, keep)
    sXC = Stream(nc, "sXC", 1, keep)
    gWPP = Stream(nc, "gWPP", 1, keep)
    sW1L = Stream(nc, "sW1L", 2, keep)
    sW2L = Stream(nc, "sW2L", 2, keep)
    gXO = Stream(nc, "gXO", 2, keep)

    blk_g = nc.Block()
    blk = blk_g.__enter__()

    def _(sy):
        for dc in range(NDC):
            sX1.put(sy, [(x2acc[dc][:], x1ownD[dc * 128:(dc + 1) * 128, :])])
        sSM.put(sy, [(br8[:], br8D), (b1a[:], b1aD), (b2a[:], b2aD), (identb[:], idD)])
    blk.sync(_)

    def _(g):
        gWR.put(g, [(wrsb[:].rearrange("p (dc e) -> p dc e", dc=8),
                     wrD.rearrange("(dc p) e -> p dc e", p=128))])
    blk.gpsimd(_)

    def _(v):
        v.memset(epsb[:], EPS)
        VV.inc(v.memset(ones128[:], 1.0))
    blk.vector(_)
    vv_ones = VV.n

    def _(v):
        sX1.wait(v)
        for dc in range(NDC):
            VV.inc(v.tensor_copy(xn1bf[dc][:], x2acc[dc][:]))
    blk.vector(_)
    vv_x1b = VV.n

    fA = scope("fA", [128, 2048], F32)
    fB = scope("fB", [128, 2048], F32)
    fC = scope("fC", [128, 2048], F32)
    sqd = [scope(f"sqd{i}", [128, 2048], BF16) for i in range(2)]
    mu_bf = scope("mu_bf", [128, 2048], BF16)
    rstd_bf = scope("rstd_bf", [128, 2048], BF16)
    yld = [fC]

    tt_b = TT.n
    vv_b = VV.n

    def _(v):
        for dc in range(NDC):
            if dc >= 2:
                v.wait_ge(TT.h, tt_b + dc - 1)
            v.wait_ge(VV.h, VV.n)
            VV.inc(v.tensor_mul(sqd[dc % 2][:], xn1bf[dc][:], xn1bf[dc][:]))
    blk.vector(_)

    def _(t):
        t.wait_ge(VV.h, vv_ones)
        for dc in range(NDC):
            t.wait_ge(VV.h, vv_b + dc + 1)
            for q4 in range(4):
                t.matmul(PA[:, q4 * 512:(q4 + 1) * 512], ones128[:],
                         xn1bf[dc][:, q4 * 512:(q4 + 1) * 512],
                         start=(dc == 0), stop=(dc == NDC - 1))
            for q4 in range(4):
                ins = t.matmul(PB[:, q4 * 512:(q4 + 1) * 512], ones128[:],
                               sqd[dc % 2][:, q4 * 512:(q4 + 1) * 512],
                               start=(dc == 0), stop=(dc == NDC - 1))
            TT.inc(ins)
    blk.tensor(_)
    tt_s1 = TT.n

    def _(v):
        v.wait_ge(TT.h, tt_s1)
        VV.inc(v.tensor_scalar_mul(fA[:], PA[:], 1.0 / D))
        VV.inc(v.tensor_scalar_mul(fB[:], PB[:], 1.0 / D))
        v.wait_ge(VV.h, VV.n)
        VV.inc(v.tensor_mul(fC[:], fA[:], fA[:]))
        v.wait_ge(VV.h, VV.n)
        VV.inc(v.tensor_sub(fB[:], fB[:], fC[:]))
    blk.vector(_)
    vv_var = VV.n

    def _(s):
        s.wait_ge(VV.h, vv_var)
        SS.inc(s.activation(fC[:], fB[:], AF.Sqrt, bias=epsb[:]))
    blk.scalar(_)
    ss_sd = SS.n

    def _(v):
        v.wait_ge(SS.h, ss_sd)
        VV.inc(v.reciprocal(fB[:], fC[:]))
        VV.inc(v.tensor_copy(mu_bf[:], fA[:]))
        v.wait_ge(VV.h, VV.n)
        VV.inc(v.tensor_copy(rstd_bf[:], fB[:]))
    blk.vector(_)

    def _(sy, vv_pre=VV.n):
        sy.wait_ge(VV.h, vv_pre)
        sy.wait_ge(SS.h, SS.n)
        for dc in range(NDC):
            if dc >= 1:
                sy.wait_ge(VV.h, vv_pre + 5 * dc)
            sY.put(sy, [(yld[0][:], yownD[dc * 128:(dc + 1) * 128, :])])
    blk.sync(_)

    def _(v):
        for dc in range(NDC):
            v.wait_ge(VV.h, VV.n)
            VV.inc(v.tensor_sub(xn1bf[dc][:], xn1bf[dc][:], mu_bf[:]))
            v.wait_ge(VV.h, VV.n)
            VV.inc(v.tensor_mul(xn1bf[dc][:], xn1bf[dc][:], rstd_bf[:]))
            v.wait_ge(VV.h, VV.n)
            VV.inc(v.tensor_add(x2acc[dc][:], x2acc[dc][:], xn1bf[dc][:]))
            sY.wait(v, dc)
            v.wait_ge(VV.h, VV.n)
            VV.inc(v.tensor_add(x2acc[dc][:], x2acc[dc][:], yld[0][:]))
            VV.inc(v.engine_nop())
    blk.vector(_)
    vv_xny = VV.n

    close(7)

    def _(t):
        gWR.wait(t)
        t.wait_ge(VV.h, vv_xny)
        for q4 in range(4):
            for dc in range(NDC):
                ins = t.matmul(PA[0:8, q4 * 512:(q4 + 1) * 512],
                               wrsb[:, dc * 8:(dc + 1) * 8],
                               xn1bf[dc][:, q4 * 512:(q4 + 1) * 512],
                               start=(dc == 0), stop=(dc == NDC - 1))
        TT.inc(ins)
    blk.tensor(_)
    tt_sc = TT.n

    def _(s):
        s.wait_ge(TT.h, tt_sc)
        sSM.wait(s)
        SS.inc(s.activation(scoresbf[:], PA[0:8, :], AF.Identity, bias=br8[:]))
    blk.scalar(_)
    ss_sc = SS.n

    def _(t):
        t.wait_ge(SS.h, ss_sc)
        for tcw in range(16):
            ins = t.matmul(PB[:, tcw * 128:tcw * 128 + 8],
                           scoresbf[:, tcw * 128:(tcw + 1) * 128],
                           identb[0:8, 0:8], start=True, stop=True)
        TT.inc(ins)
    blk.tensor(_)
    tt_tr = TT.n

    def _(v):
        v.wait_ge(TT.h, tt_tr)
        for tcw in range(16):
            sl = slice(tcw * 8, (tcw + 1) * 8)
            ps = PB[:, tcw * 128:tcw * 128 + 8]
            VV.inc(v.tensor_copy(s_tm[:, sl], ps))
        for tcw in range(16):
            sl = slice(tcw * 8, (tcw + 1) * 8)
            v.wait_ge(VV.h, VV.n)
            VV.inc(v.reduce_max(m16[:, tcw * 4:tcw * 4 + 1], s_tm[:, sl], axis=AX.X))
            v.wait_ge(VV.h, VV.n)
            VV.inc(v.tensor_scalar(sA[:, sl], s_tm[:, sl], m16[:, tcw * 4:tcw * 4 + 1],
                                   None, ALU.is_ge))
            v.wait_ge(VV.h, VV.n)
            VV.inc(v.scalar_tensor_tensor(sB[:, sl], sA[:, sl], -1e30, s_tm[:, sl],
                                          ALU.mult, ALU.add))
            v.wait_ge(VV.h, VV.n)
            VV.inc(v.reduce_max(m16[:, tcw * 4 + 1:tcw * 4 + 2], sB[:, sl], axis=AX.X))
            v.wait_ge(VV.h, VV.n)
            VV.inc(v.tensor_scalar(sA[:, sl], s_tm[:, sl], m16[:, tcw * 4 + 1:tcw * 4 + 2],
                                   None, ALU.is_ge))
            v.wait_ge(VV.h, VV.n)
            VV.inc(v.tensor_scalar(sB[:, sl], s_tm[:, sl], m16[:, tcw * 4:tcw * 4 + 1],
                                   None, ALU.subtract))
    blk.vector(_)
    vv_t2 = VV.n

    def _(s):
        s.wait_ge(VV.h, vv_t2)
        SS.inc(s.activation(sB[:], sB[:], AF.Exp))
    blk.scalar(_)
    ss_exp = SS.n

    def _(v):
        v.wait_ge(SS.h, ss_exp)
        for tcw in range(16):
            sl = slice(tcw * 8, (tcw + 1) * 8)
            v.wait_ge(VV.h, VV.n)
            VV.inc(v.tensor_mul(sB[:, sl], sB[:, sl], sA[:, sl]))
            v.wait_ge(VV.h, VV.n)
            VV.inc(v.reduce_sum(m16[:, tcw * 4 + 2:tcw * 4 + 3], sB[:, sl], axis=AX.X))
            v.wait_ge(VV.h, VV.n)
            VV.inc(v.reciprocal(m16[:, tcw * 4 + 3:tcw * 4 + 4], m16[:, tcw * 4 + 2:tcw * 4 + 3]))
            v.wait_ge(VV.h, VV.n)
            VV.inc(v.tensor_scalar(w_tmb[:, sl], sB[:, sl], m16[:, tcw * 4 + 3:tcw * 4 + 4],
                                   None, ALU.mult))
    blk.vector(_)
    vv_w = VV.n

    w1r_r = w1rD
    ss_prev_evac = SS.n
    vv_prev = VV.n
    tt_prev = TT.n

    for i in range(NEXP):
        def _(t, i=i, vv_w=vv_w, vv_prev=vv_prev):
            t.wait_ge(VV.h, max(vv_w, vv_prev))
            for tcw in range(16):
                ins = t.matmul(PA[0:1, tcw * 128:(tcw + 1) * 128],
                               w_tmb[:, tcw * 8 + i:tcw * 8 + i + 1], identb[:],
                               start=True, stop=True)
            TT.inc(ins)
        blk.tensor(_)
        tt_row = TT.n

        def _(v, tt_row=tt_row):
            v.wait_ge(TT.h, tt_row)
            VV.inc(v.tensor_copy(wrow, PA[0:1, 0:2048]))
        blk.vector(_)
        vv_row = VV.n

        def _(t, vv_row=vv_row):
            t.wait_ge(VV.h, vv_row)
            for q4 in range(4):
                ins = t.matmul(PB[0:128, q4 * 512:(q4 + 1) * 512],
                               ones128[0:1, 0:128],
                               h1m[0:1, q4 * 512:(q4 + 1) * 512],
                               start=True, stop=True)
            TT.inc(ins)
        blk.tensor(_)
        tt_bc = TT.n

        def _(v, i=i, tt_bc=tt_bc):
            v.wait_ge(TT.h, tt_bc)
            VV.inc(v.tensor_copy(wbv[:], PB[:, 0:2048]))
        blk.vector(_)
        vv_bc = VV.n

        for tb in range(4):
            ts = tb * 512
            tt_f1 = TT.n
            ss_f1 = SS.n

            w1base = len(sW1L.marks)

            def _(g, i=i, tt_f1=tt_f1, vv_pre=VV.n):
                g.wait_ge(VV.h, min(vv_pre, vv_w))
                for hc in range(32):
                    if hc >= 2:
                        g.wait_ge(TT.h, tt_f1 + hc - 1)
                    gW1L.put(g, [(w1c[hc % 2][:], w1r_r[i, hc])])
            blk.gpsimd(_)

            def _(t, i=i, ts=ts, ss_f1=ss_f1, vv_bc=vv_bc, w1base=w1base,
                  ss_pe=ss_prev_evac):
                t.wait_ge(VV.h, vv_bc)
                t.wait_ge(SS.h, ss_pe)
                for hc in range(32):
                    gW1L.wait(t, w1base + hc)
                    if hc >= 1:
                        t.wait_ge(SS.h, ss_f1 + hc)
                    for dc in range(NDC):
                        ins = t.matmul(PA[:, 0:512],
                                       w1c[hc % 2][:, dc * 128:(dc + 1) * 128],
                                       xn1bf[dc][:, ts:ts + 512],
                                       start=(dc == 0), stop=(dc == NDC - 1))
                    TT.inc(ins)
            blk.tensor(_)

            def _(s, i=i, tt_f1=tt_f1):
                for hc in range(32):
                    s.wait_ge(TT.h, tt_f1 + hc + 1)
                    SS.inc(s.activation(h1m[:, hc * 512:(hc + 1) * 512], PA[:, 0:512],
                                        AF.Relu, bias=b1a[:, i * 32 + hc:i * 32 + hc + 1]))
            blk.scalar(_)
            ss_f1_done = SS.n

            tt_f2 = TT.n

            w2base = len(sW2L.marks)

            def _(g, i=i, tt_f2=tt_f2):
                for hc in range(32):
                    if hc >= 2:
                        g.wait_ge(TT.h, tt_f2 + hc - 1)
                    gW2L.put(g, [(w2c[hc % 2][:], w2D[i, hc * 128:(hc + 1) * 128, :])])
            blk.gpsimd(_)

            def _(t, i=i, ss_f1_done=ss_f1_done, ss_prev_evac=ss_prev_evac,
                  w2base=w2base):
                t.wait_ge(SS.h, max(ss_f1_done, ss_prev_evac))
                for hc in range(32):
                    gW2L.wait(t, w2base + hc)
                    for ec in range(8):
                        ps = PA if ec < 4 else PB
                        ins = t.matmul(ps[:, (ec % 4) * 512:(ec % 4 + 1) * 512],
                                       w2c[hc % 2][:, ec * 128:(ec + 1) * 128],
                                       h1m[:, hc * 512:(hc + 1) * 512],
                                       start=(hc == 0), stop=(hc == 31))
                    TT.inc(ins)
            blk.tensor(_)
            tt_f2_done = TT.n

            def _(s, i=i, tt_f2_done=tt_f2_done, vv_pre=VV.n):
                s.wait_ge(TT.h, tt_f2_done)
                s.wait_ge(VV.h, vv_pre)
                for ec in range(8):
                    if ec >= 2:
                        s.wait_ge(VV.h, vv_pre + 2 * (ec - 1))
                    ps = PA if ec < 4 else PB
                    SS.inc(s.activation(etmp[ec % 2][:],
                                        ps[:, (ec % 4) * 512:(ec % 4 + 1) * 512],
                                        AF.Identity,
                                        bias=b2a[:, i * 8 + ec:i * 8 + ec + 1]))
            blk.scalar(_)
            ss_evac0 = SS.n - 8

            def _(v, i=i, ts=ts, ss_evac0=ss_evac0):
                for ec in range(8):
                    v.wait_ge(SS.h, ss_evac0 + ec + 1)
                    v.wait_ge(VV.h, VV.n)
                    VV.inc(v.tensor_mul(etmp[ec % 2][:], etmp[ec % 2][:], wbv[:, ts:ts + 512]))
                    v.wait_ge(VV.h, VV.n)
                    VV.inc(v.tensor_add(x2acc[ec][:, ts:ts + 512],
                                        x2acc[ec][:, ts:ts + 512], etmp[ec % 2][:]))
            blk.vector(_)
            ss_prev_evac = SS.n
            vv_prev = VV.n

        tt_prev = TT.n

    vv_fin = VV.n

    def _(g):
        g.wait_ge(VV.h, vv_fin)
        for dc in range(NDC):
            gXO.put(g, [(x2ownD[dc * 128:(dc + 1) * 128, :], x2acc[dc][:])])
        gXO.wait(g)
    blk.gpsimd(_)

    blk_g.__exit__(None, None, None)
    close(len(guards))
    pg2.__exit__(None, None, None)
    pg1.__exit__(None, None, None)
    for g in reversed(keep):
        g.__exit__(None, None, None)
    return nc


def prep_moe_inmaps(inputs, x1, y):
    wr = np.asarray(inputs["L1_ffn_Wr"], np.float32)
    br = np.asarray(inputs["L1_ffn_br"], np.float32)
    w1 = np.asarray(inputs["L1_ffn_W1"], np.float32)
    b1 = np.asarray(inputs["L1_ffn_b1"], np.float32)
    w2 = np.asarray(inputs["L1_ffn_W2"], np.float32)
    b2 = np.asarray(inputs["L1_ffn_b2"], np.float32)
    w1r = _c(w1.reshape(8, 8, 128, 32, 128).transpose(0, 3, 2, 1, 4)
             .reshape(8, 32, 128, 1024))
    b1a = _c(np.concatenate([b1[i].reshape(32, 128).T for i in range(8)], axis=1))
    b2a = _c(np.concatenate([b2[i].reshape(8, 128).T for i in range(8)], axis=1))
    shared = dict(wr=_c(wr), br8=_c(br.reshape(8, 1)), w1r=w1r, w2m=_c(w2),
                  b1a=b1a, b2a=b2a,
                  identb=_c(np.eye(128, dtype=np.float32).astype(BF)))
    in_maps = []
    for core in range(8):
        b, hh = core // 2, core % 2
        own = slice(hh * TOWN, (hh + 1) * TOWN)
        m = dict(shared)
        m.update(x1own=_c(x1[b, own].T), yown=_c(y[b, own].T))
        in_maps.append(m)
    return in_maps


def run_moe(inputs, x1, y):
    nc = build_moe()
    in_maps = prep_moe_inmaps(inputs, x1, y)
    res = run_bass_kernel_spmd(nc, in_maps, core_ids=list(range(8)))
    x2 = np.zeros((B, T, D), dtype=np.float32)
    for core in range(8):
        b, hh = core // 2, core % 2
        x2[b, hh * TOWN:(hh + 1) * TOWN] = res.results[core]["x2own"].T
    return x2



def build_fused():
    nc = bass.Bass("TRN2", target_bir_lowering=False)

    x0fm = nc.dram_tensor("x0fm", [D, T], F32, kind="ExternalInput").ap()
    wqp = nc.dram_tensor("wqp", [D, D], F32, kind="ExternalInput").ap()
    wkp = nc.dram_tensor("wkp", [D, D], F32, kind="ExternalInput").ap()
    wvD = nc.dram_tensor("wv", [D, D], F32, kind="ExternalInput").ap()
    woD = nc.dram_tensor("wo", [D, D], F32, kind="ExternalInput").ap()
    w1t = nc.dram_tensor("w1t", [D, HFF], F32, kind="ExternalInput").ap()
    w1sD = nc.dram_tensor("w1s", [HFF, D], F32, kind="ExternalInput").ap()
    w2t = nc.dram_tensor("w2t", [HFF, D], F32, kind="ExternalInput").ap()
    w2sD = nc.dram_tensor("w2s", [D, HFF], F32, kind="ExternalInput").ap()
    b1tD = nc.dram_tensor("b1t", [128, 32], F32, kind="ExternalInput").ap()
    b2tD = nc.dram_tensor("b2t", [128, 8], F32, kind="ExternalInput").ap()
    cos2d = nc.dram_tensor("cos2", [128, T], BF16, kind="ExternalInput").ap()
    sin2d = nc.dram_tensor("sin2", [128, T], BF16, kind="ExternalInput").ap()
    wktD = nc.dram_tensor("wkt", [D, 64], F32, kind="ExternalInput").ap()
    wvtD = nc.dram_tensor("wvt", [D, D], F32, kind="ExternalInput").ap()
    wqtD = nc.dram_tensor("wqt", [D, 64], F32, kind="ExternalInput").ap()
    bkD = nc.dram_tensor("bk", [64, 1], F32, kind="ExternalInput").ap()
    bqD = nc.dram_tensor("bq", [64, 1], F32, kind="ExternalInput").ap()
    bkbD = nc.dram_tensor("bkb", [128, 64], F32, kind="ExternalInput").ap()
    bvbD = nc.dram_tensor("bvb", [128, D], F32, kind="ExternalInput").ap()
    gbD = nc.dram_tensor("gb", [128, 8], F32, kind="ExternalInput").ap()
    btD = nc.dram_tensor("bt", [128, 8], F32, kind="ExternalInput").ap()
    mUeD = nc.dram_tensor("mUe", [128, 128], BF16, kind="ExternalInput").ap()
    mLeD = nc.dram_tensor("mLe", [128, 128], BF16, kind="ExternalInput").ap()
    mUnD = nc.dram_tensor("mUn", [128, 128], BF16, kind="ExternalInput").ap()
    idD = nc.dram_tensor("identb", [128, 128], BF16, kind="ExternalInput").ap()
    wrD = nc.dram_tensor("wr", [D, 8], F32, kind="ExternalInput").ap()
    br8D = nc.dram_tensor("br8", [8, 1], F32, kind="ExternalInput").ap()
    w1rD = nc.dram_tensor("w1r", [8, 32, 128, 1024], F32, kind="ExternalInput").ap()
    w2D = nc.dram_tensor("w2m", [8, HME, D], F32, kind="ExternalInput").ap()
    b1aD = nc.dram_tensor("b1a", [128, 256], F32, kind="ExternalInput").ap()
    b2aD = nc.dram_tensor("b2a", [128, 64], F32, kind="ExternalInput").ap()
    sel0D = nc.dram_tensor("sel0", [128, 1], F32, kind="ExternalInput").ap()
    sel1D = nc.dram_tensor("sel1", [128, 1], F32, kind="ExternalInput").ap()
    x2ownD = nc.dram_tensor("x2own", [D, TOWN], F32, kind="ExternalOutput").ap()

    h1d = nc.dram_tensor("h1d", [32, 128, TOWN], BF16, kind="Internal").ap()
    w1sd = nc.dram_tensor("w1sd", [32, 128, 1024], BF16, kind="Internal").ap()
    w2sd = nc.dram_tensor("w2sd", [32, 128, 1024], BF16, kind="Internal").ap()
    x1d = nc.dram_tensor("x1d", [D, TOWN], F32, kind="Internal").ap()
    x1g = nc.dram_tensor("x1g", [2, D, TOWN], F32, kind="Internal").ap()
    vd = nc.dram_tensor("vd", [32, 128, D], BF16, kind="Internal").ap()
    w1bd = nc.dram_tensor("w1bd", [8, 32, 128, 1024], BF16, kind="Internal").ap()
    w2bd = nc.dram_tensor("w2bd", [8, HME, D], BF16, kind="Internal").ap()
    xn1d = nc.dram_tensor("xn1d", [NDC, 128, TOWN], BF16, kind="Internal").ap()
    wbv8d = nc.dram_tensor("wbv8d", [128, 8 * TOWN], BF16, kind="Internal").ap()
    e0d = nc.dram_tensor("e0d", [4, 8, 128, 512], BF16, kind="Internal").ap()

    ones128 = nc.alloc_sbuf_tensor("ones128", [128, 128], BF16)
    epsb = nc.alloc_sbuf_tensor("epsb", [128, 1], F32)
    identb = nc.alloc_sbuf_tensor("identS", [128, 128], BF16)

    guards = []

    def scope(name, shape, dt, side=None):
        g = nc.sbuf_tensor(name, shape, dt, side=side) if side else nc.sbuf_tensor(name, shape, dt)
        t = g.__enter__()
        guards.append(g)
        return t

    def close(n):
        for _ in range(n):
            guards.pop().__exit__(None, None, None)

    rguards = []

    def rscope(name, shape, dt):
        g = nc.sbuf_tensor(name, shape, dt, side="right")
        t = g.__enter__()
        rguards.append(g)
        return t

    def rclose(n):
        for _ in range(n):
            rguards.pop().__exit__(None, None, None)

    pg1 = nc.psum_tensor("PA", [128, 2048], F32)
    PA = pg1.__enter__()
    pg2 = nc.psum_tensor("PB", [128, 2048], F32)
    PB = pg2.__enter__()

    keep = []
    sg = [nc.semaphore(n) for n in ["sTT", "sVV", "sSS", "sCC"]]
    TT, VV, SS, CC = [Sem(g.__enter__()) for g in sg]
    keep.extend(sg)

    stB = Stream(nc, "stB", 1, keep)
    gX0 = Stream(nc, "gX0", 1, keep)
    sWT = Stream(nc, "sWT", 2, keep)
    sWS = Stream(nc, "sWS", 2, keep)
    gW1 = Stream(nc, "gW1", 2, keep)
    gW2 = Stream(nc, "gW2", 2, keep)
    sF1 = Stream(nc, "sF1", 2, keep)
    gH1 = Stream(nc, "gH1", 2, keep)
    sH1 = Stream(nc, "sH1", 1, keep)
    sF2 = Stream(nc, "sF2", 2, keep)
    sTAB = Stream(nc, "sTAB", 1, keep)
    gWG = Stream(nc, "gWG", 1, keep)
    gWOB = Stream(nc, "gWOB", 1, keep)
    sX0 = Stream(nc, "sX0", 2, keep)
    gX1D = Stream(nc, "gX1D", 2, keep)
    gX1 = Stream(nc, "gX1", 1, keep)
    gWP = Stream(nc, "gWP", 1, keep)
    sMS = Stream(nc, "sMS", 1, keep)
    gVD = Stream(nc, "gVD", 2, keep)
    sVT = Stream(nc, "sVT", 2, keep)
    gW1L = Stream(nc, "gW1L", 2, keep)
    gW2L = Stream(nc, "gW2L", 2, keep)
    gWR = Stream(nc, "gWR", 1, keep)
    sXC = Stream(nc, "sXC", 1, keep)
    gWPP = Stream(nc, "gWPP", 1, keep)
    sW1L = Stream(nc, "sW1L", 2, keep)
    sW2L = Stream(nc, "sW2L", 2, keep)
    gXO = Stream(nc, "gXO", 2, keep)
    sX1R = Stream(nc, "sX1R", 2, keep)
    sWRL = Stream(nc, "sWRL", 1, keep)
    sXNS = Stream(nc, "sXNS", 1, keep)
    sXNL = Stream(nc, "sXNL", 1, keep)
    sWBS = Stream(nc, "sWBS", 1, keep)
    sWBL = Stream(nc, "sWBL", 1, keep)
    sW1P = Stream(nc, "sW1P", 2, keep)
    sW2P = Stream(nc, "sW2P", 2, keep)
    gE0S = Stream(nc, "gE0S", 1, keep)
    sE0L = Stream(nc, "sE0L", 1, keep)

    blk_g = nc.Block()
    blk = blk_g.__enter__()

    xn = [rscope(f"xn{i}", [128, T], BF16) for i in range(NDC)]
    ffnacc = [scope(f"ffa{i}", [128, TOWN], BF16) for i in range(NDC)]
    hs_ofm = scope("hs_ofm", [128, 32 * 512], BF16)
    alpha1 = scope("alpha1", [128, 32], F32)
    alpha2 = scope("alpha2", [128, 8], F32)
    b1sb = scope("b1sb", [128, 32], F32)
    b2sb = scope("b2sb", [128, 8], F32)
    nA_base = len(guards)

    def _(sy):
        stB.put(sy, [(b1sb[:], b1tD), (b2sb[:], b2tD)])
    blk.sync(_)

    def _(g):
        for dc in range(NDC):
            gX0.put(g, [(xn[dc][:], x0fm[dc * 128:(dc + 1) * 128, :])])
    blk.gpsimd(_)

    def _(v):
        v.memset(epsb[:], EPS)
        VV.inc(v.memset(ones128[:], 1.0))
    blk.vector(_)
    vv_ones = VV.n

    wtmp = [scope(f"wtmp{i}", [128, 1024], F32) for i in range(2)]
    wstr = [scope(f"wstr{i}", [128, 4096], F32) for i in range(2)]
    wsgn = [scope(f"wsgn{i}", [128, 1024], BF16) for i in range(2)]

    w1t_r = w1t.rearrange("(dc p) h -> p dc h", p=128)

    ss0, vv0 = SS.n, VV.n

    def _(sy):
        for hc in range(32):
            if hc >= 2:
                sy.wait_ge(SS.h, ss0 + hc - 1)
                sy.wait_ge(VV.h, vv0 + hc - 1)
            sWT.put(sy, [(wtmp[hc % 2][:].rearrange("p (dc h) -> p dc h", dc=8),
                          w1t_r[:, :, hc * 128:(hc + 1) * 128])])
            sWS.put(sy, [(wstr[hc % 2][:, 0:1024], w1sD[hc * 128:(hc + 1) * 128, :])])
    blk.sync(_)

    def _(g):
        for hc in range(32):
            g.wait_ge(SS.h, ss0 + hc + 1)
            gW1.put(g, [(w1sd[hc], wsgn[hc % 2][:])])
    blk.gpsimd(_)

    def _(s):
        for hc in range(32):
            sWT.wait(s, hc)
            if hc >= 2:
                gW1.wait(s, hc - 2)
            SS.inc(s.activation(wsgn[hc % 2][:], wtmp[hc % 2][:], AF.Sign))
    blk.scalar(_)

    def _(v):
        for hc in range(32):
            sWS.wait(v, hc)
            VV.inc(v.reduce_sum(alpha1[:, hc:hc + 1], wstr[hc % 2][:, 0:1024],
                                axis=AX.X, apply_absolute_value=True))
    blk.vector(_)

    ss1, vv1 = SS.n, VV.n

    def _(sy):
        sy.wait_ge(SS.h, ss1)
        sy.wait_ge(VV.h, vv1)
        for hc in range(32):
            if hc >= 2:
                sy.wait_ge(SS.h, ss1 + hc - 1)
            sWT.put(sy, [(wtmp[hc % 2][:], w2t[hc * 128:(hc + 1) * 128, :])])
        for ec in range(8):
            if ec >= 2:
                sy.wait_ge(VV.h, vv1 + ec - 1)
            sWS.put(sy, [(wstr[ec % 2][:], w2sD[ec * 128:(ec + 1) * 128, :])])
    blk.sync(_)

    def _(g):
        for hc in range(32):
            g.wait_ge(SS.h, ss1 + hc + 1)
            gW2.put(g, [(w2sd[hc], wsgn[hc % 2][:])])
    blk.gpsimd(_)

    def _(s):
        gW1.wait(s)
        for hc in range(32):
            sWT.wait(s, 32 + hc)
            if hc >= 2:
                gW2.wait(s, hc - 2)
            SS.inc(s.activation(wsgn[hc % 2][:], wtmp[hc % 2][:], AF.Sign))
    blk.scalar(_)

    def _(v):
        for ec in range(8):
            sWS.wait(v, 32 + ec)
            VV.inc(v.reduce_sum(alpha2[:, ec:ec + 1], wstr[ec % 2][:],
                                axis=AX.X, apply_absolute_value=True))
        v.wait_ge(VV.h, VV.n)
        VV.inc(v.tensor_scalar_mul(alpha1[:], alpha1[:], 1.0 / D))
        VV.inc(v.tensor_scalar_mul(alpha2[:], alpha2[:], 1.0 / HFF))
    blk.vector(_)
    vv_alpha = VV.n

    close(len(guards) - nA_base)

    fA = scope("fA", [128, 2048], F32)
    fB = scope("fB", [128, 2048], F32)
    fC = scope("fC", [128, 2048], F32)
    sqd = [scope(f"sqd{i}", [128, 2048], BF16) for i in range(2)]
    mu_bf = scope("mu_bf", [128, T], BF16)
    rstd_bf = scope("rstd_bf", [128, T], BF16)

    for tb in range(2):
        ts = tb * 2048
        tt_b = TT.n
        vv_b = VV.n

        def _(v, ts=ts, tt_b=tt_b, ss_w=SS.n):
            gX0.wait(v)
            v.wait_ge(SS.h, ss_w)
            gW1.wait(v)
            gW2.wait(v)
            for dc in range(NDC):
                if dc >= 2:
                    v.wait_ge(TT.h, tt_b + dc - 1)
                VV.inc(v.tensor_mul(sqd[dc % 2][:], xn[dc][:, ts:ts + 2048],
                                    xn[dc][:, ts:ts + 2048]))
        blk.vector(_)

        def _(t, ts=ts, vv_b=vv_b):
            t.wait_ge(VV.h, vv_ones)
            for dc in range(NDC):
                t.wait_ge(VV.h, vv_b + dc + 1)
                for q4 in range(4):
                    t.matmul(PA[:, q4 * 512:(q4 + 1) * 512], ones128[:],
                             xn[dc][:, ts + q4 * 512:ts + (q4 + 1) * 512],
                             start=(dc == 0), stop=(dc == NDC - 1))
                for q4 in range(4):
                    ins = t.matmul(PB[:, q4 * 512:(q4 + 1) * 512], ones128[:],
                                   sqd[dc % 2][:, q4 * 512:(q4 + 1) * 512],
                                   start=(dc == 0), stop=(dc == NDC - 1))
                TT.inc(ins)
        blk.tensor(_)
        tt_s1 = TT.n

        def _(v, tt_s1=tt_s1):
            v.wait_ge(TT.h, tt_s1)
            VV.inc(v.tensor_scalar_mul(fA[:], PA[:], 1.0 / D))
            VV.inc(v.tensor_scalar_mul(fB[:], PB[:], 1.0 / D))
            v.wait_ge(VV.h, VV.n)
            VV.inc(v.tensor_mul(fC[:], fA[:], fA[:]))
            v.wait_ge(VV.h, VV.n)
            VV.inc(v.tensor_sub(fB[:], fB[:], fC[:]))
        blk.vector(_)
        vv_var = VV.n

        def _(s, vv_var=vv_var):
            s.wait_ge(VV.h, vv_var)
            SS.inc(s.activation(fC[:], fB[:], AF.Sqrt, bias=epsb[:]))
        blk.scalar(_)
        ss_sd = SS.n

        def _(v, ts=ts, ss_sd=ss_sd):
            v.wait_ge(SS.h, ss_sd)
            VV.inc(v.reciprocal(fB[:], fC[:]))
            VV.inc(v.tensor_copy(mu_bf[:, ts:ts + 2048], fA[:]))
            v.wait_ge(VV.h, VV.n)
            VV.inc(v.tensor_copy(rstd_bf[:, ts:ts + 2048], fB[:]))
        blk.vector(_)

    def _(v):
        for dc in range(NDC):
            VV.inc(v.tensor_sub(xn[dc][:], xn[dc][:], mu_bf[:]))
            v.wait_ge(VV.h, VV.n)
            VV.inc(v.tensor_mul(xn[dc][:], xn[dc][:], rstd_bf[:]))
    blk.vector(_)
    vv_xn = VV.n

    close(7)

    w1b = [scope(f"w1b{i}", [128, 1024], BF16) for i in range(2)]
    h1t = [scope(f"h1t{i}", [128, TOWN], BF16) for i in range(2)]

    ttf, ssf = TT.n, SS.n

    def _(sy, ss_pre=SS.n):
        gW1.wait(sy)
        sy.wait_ge(VV.h, vv_xn)
        sy.wait_ge(SS.h, ss_pre)
        for hc in range(32):
            if hc >= 2:
                sy.wait_ge(TT.h, ttf + hc - 1)
            sF1.put(sy, [(w1b[hc % 2][:], w1sd[hc])])
    blk.sync(_)

    def _(t):
        t.wait_ge(VV.h, vv_xn)
        for hc in range(32):
            sF1.wait(t, hc)
            if hc >= 1:
                t.wait_ge(SS.h, ssf + hc)
            for q4 in range(4):
                for dc in range(NDC):
                    ins = t.matmul(PA[:, q4 * 512:(q4 + 1) * 512],
                                   w1b[hc % 2][:, dc * 128:(dc + 1) * 128],
                                   xn[dc][:, q4 * 512:(q4 + 1) * 512],
                                   start=(dc == 0), stop=(dc == NDC - 1))
            TT.inc(ins)
    blk.tensor(_)

    def _(g):
        for hc in range(32):
            g.wait_ge(SS.h, ssf + hc + 1)
            gH1.put(g, [(h1d[hc], h1t[hc % 2][:])])
    blk.gpsimd(_)

    def _(s):
        s.wait_ge(VV.h, vv_xn)
        stB.wait(s)
        s.wait_ge(VV.h, vv_alpha)
        for hc in range(32):
            s.wait_ge(TT.h, ttf + hc + 1)
            if hc >= 2:
                gH1.wait(s, hc - 2)
            SS.inc(s.activation(h1t[hc % 2][:], PA[:], AF.Gelu,
                                bias=b1sb[:, hc:hc + 1], scale=alpha1[:, hc:hc + 1]))
    blk.scalar(_)

    close(4)

    w2b = [scope(f"w2b{i}", [128, 1024], BF16) for i in range(2)]
    h1s = hs_ofm

    tt_prev_tb = TT.n
    ss_prev_tb = SS.n
    for tb in range(4):
        ts = tb * 512
        tt2 = TT.n

        def _(sy, ts=ts, tt2=tt2, tt_prev_tb=tt_prev_tb, tb=tb):
            gH1.wait(sy)
            gW2.wait(sy)
            sy.wait_ge(TT.h, tt_prev_tb)
            sH1.put(sy, [(h1s[:, grp * 8 * 512:(grp + 1) * 8 * 512].rearrange(
                              "p (hc t) -> p hc t", hc=8),
                          h1d[grp * 8:(grp + 1) * 8, :, ts:ts + 512].rearrange(
                              "hc p t -> p hc t"))
                         for grp in range(4)])
            for hc in range(32):
                if hc >= 2:
                    sy.wait_ge(TT.h, tt2 + hc - 1)
                sF2.put(sy, [(w2b[hc % 2][:], w2sd[hc])])
        blk.sync(_)

        def _(t, ss_prev_tb=ss_prev_tb, tb=tb):
            t.wait_ge(SS.h, ss_prev_tb)
            sH1.wait(t, tb)
            for hc in range(32):
                sF2.wait(t, 32 * tb + hc)
                for ec in range(8):
                    ps = PA if ec < 4 else PB
                    ins = t.matmul(ps[:, (ec % 4) * 512:(ec % 4 + 1) * 512],
                                   w2b[hc % 2][:, ec * 128:(ec + 1) * 128],
                                   h1s[:, hc * 512:(hc + 1) * 512],
                                   start=(hc == 0), stop=(hc == 31))
                TT.inc(ins)
        blk.tensor(_)
        tt_prev_tb = TT.n

        def _(s, ts=ts, tt_done=TT.n):
            s.wait_ge(TT.h, tt_done)
            stB.wait(s)
            for ec in range(8):
                ps = PA if ec < 4 else PB
                SS.inc(s.activation(ffnacc[ec][:, ts:ts + 512],
                                    ps[:, (ec % 4) * 512:(ec % 4 + 1) * 512],
                                    AF.Identity,
                                    bias=b2sb[:, ec:ec + 1], scale=alpha2[:, ec:ec + 1]))
        blk.scalar(_)
        ss_prev_tb = SS.n

    close(2)
    ss_attn0 = SS.n

    cos2 = scope("cos2s", [128, T], BF16)
    sin2 = scope("sin2s", [128, T], BF16)
    kfm = scope("kfm", [128, T], BF16)
    qfm = scope("qfm", [128, TOWN], BF16)
    raw = scope("raw", [128, 2048], BF16)
    tscr = scope("tscr", [128, 2048], BF16)
    v65 = scope("v65", [128, 32 * 130], BF16)
    pb_sb = scope("pb_sb", [128, 2048], BF16)
    recb = scope("recb", [64, 2048], BF16)
    rec1 = scope("rec1", [1, 2048], BF16)
    wgq = scope("wgq", [128, 1024], BF16)
    wgk = scope("wgk", [128, 1024], BF16)
    wgv = scope("wgv", [128, 1024], BF16)
    ofm = hs_ofm

    def _(sy, tt_pre=TT.n, ss_pre=SS.n):
        sy.wait_ge(TT.h, tt_pre)
        sy.wait_ge(SS.h, ss_pre)
        sTAB.put(sy, [(cos2[:], cos2d), (sin2[:], sin2d)])
    blk.sync(_)

    wqp_r = wqp.rearrange("(dc p) j -> p dc j", p=128)
    wkp_r = wkp.rearrange("(dc p) j -> p dc j", p=128)
    wv_r = wvD.rearrange("(dc p) j -> p dc j", p=128)

    def emit_rope(v, dst, doff, src, tab_off, tlen):
        tsl = slice(tab_off, tab_off + tlen)
        for e in (0, 64):
            o = e + 32
            ce = cos2[e:e + 32, tsl]
            se = sin2[e:e + 32, tsl]
            co = cos2[o:o + 32, tsl]
            so = sin2[o:o + 32, tsl]
            VV.inc(v.tensor_mul(tscr[e:e + 32, 0:tlen], src[o:o + 32, 0:tlen], so))
            VV.inc(v.tensor_mul(dst[e:e + 32, doff:doff + tlen], src[e:e + 32, 0:tlen], ce))
            v.wait_ge(VV.h, VV.n)
            VV.inc(v.tensor_sub(dst[e:e + 32, doff:doff + tlen],
                                dst[e:e + 32, doff:doff + tlen], tscr[e:e + 32, 0:tlen]))
            VV.inc(v.tensor_mul(tscr[o:o + 32, 0:tlen], src[e:e + 32, 0:tlen], se))
            VV.inc(v.tensor_mul(dst[o:o + 32, doff:doff + tlen], src[o:o + 32, 0:tlen], co))
            v.wait_ge(VV.h, VV.n)
            VV.inc(v.tensor_add(dst[o:o + 32, doff:doff + tlen],
                                dst[o:o + 32, doff:doff + tlen], tscr[o:o + 32, 0:tlen]))

    tt_group_end = TT.n
    vv_group_end = VV.n

    for c in range(8):
        def _(g, tt_group_end=tt_group_end, c=c):
            g.wait_ge(TT.h, tt_group_end)
            gWG.put(g, [(wgq[:].rearrange("p (dc j) -> p dc j", dc=8),
                         wqp_r[:, :, c * 128:(c + 1) * 128]),
                        (wgk[:].rearrange("p (dc j) -> p dc j", dc=8),
                         wkp_r[:, :, c * 128:(c + 1) * 128]),
                        (wgv[:].rearrange("p (dc j) -> p dc j", dc=8),
                         wv_r[:, :, c * 128:(c + 1) * 128])])
            gWPP.put(g, [(w1bd[c], w1rD[c])])
            gWPP.put(g, [(w2bd[c], w2D[c])])
        blk.gpsimd(_)

        def _(t, vv_group_end=vv_group_end, c=c):
            gWG.wait(t, c)
            t.wait_ge(VV.h, max(vv_xn, vv_group_end))
            t.wait_ge(SS.h, ss_attn0)
            for q4 in range(4):
                for dc in range(NDC):
                    ins = t.matmul(PA[:, q4 * 512:(q4 + 1) * 512],
                                   wgq[:, dc * 128:(dc + 1) * 128],
                                   xn[dc][:, q4 * 512:(q4 + 1) * 512],
                                   start=(dc == 0), stop=(dc == NDC - 1))
            TT.inc(ins)
        blk.tensor(_)
        tt_q = TT.n

        def _(v, tt_q=tt_q):
            v.wait_ge(TT.h, tt_q)
            sTAB.wait(v)
            VV.inc(v.tensor_copy(raw[:], PA[:]))
            v.wait_ge(VV.h, VV.n)
            emit_rope(v, qfm, 0, raw, 0, 2048)
        blk.vector(_)

        for th in range(2):
            ts = th * 2048

            def _(t, ts=ts, vv_pre=VV.n):
                t.wait_ge(VV.h, vv_pre)
                for q4 in range(4):
                    for dc in range(NDC):
                        ins = t.matmul(PB[:, q4 * 512:(q4 + 1) * 512],
                                       wgk[:, dc * 128:(dc + 1) * 128],
                                       xn[dc][:, ts + q4 * 512:ts + (q4 + 1) * 512],
                                       start=(dc == 0), stop=(dc == NDC - 1))
                TT.inc(ins)
            blk.tensor(_)
            tt_k = TT.n

            def _(v, ts=ts, tt_k=tt_k):
                v.wait_ge(TT.h, tt_k)
                VV.inc(v.tensor_copy(raw[:], PB[:]))
                v.wait_ge(VV.h, VV.n)
                emit_rope(v, kfm, ts, raw, ts, 2048)
            blk.vector(_)

        def _(v, tt_group_end=tt_group_end):
            v.wait_ge(TT.h, tt_group_end)
            VV.inc(v.memset(v65[:], 1.0))
        blk.vector(_)
        vv_ms = VV.n

        tt_v0 = TT.n
        vv_v0 = VV.n

        def _(t, tt_v0=tt_v0, vv_v0=vv_v0, vv_ms=vv_ms):
            t.wait_ge(VV.h, vv_ms)
            for tc in range(32):
                if tc >= 2:
                    t.wait_ge(VV.h, vv_v0 + tc - 1)
                po = (tc % 2) * 512
                for dc in range(NDC):
                    ins = t.matmul(PA[:, po:po + 128],
                                   xn[dc][:, tc * 128:(tc + 1) * 128],
                                   wgv[:, dc * 128:(dc + 1) * 128],
                                   start=(dc == 0), stop=(dc == NDC - 1))
                TT.inc(ins)
        blk.tensor(_)

        def _(v, tt_v0=tt_v0, vv_ms=vv_ms):
            v.wait_ge(VV.h, vv_ms)
            for tc in range(32):
                v.wait_ge(TT.h, tt_v0 + tc + 1)
                po = (tc % 2) * 512
                v.tensor_copy(v65[:, tc * 130:tc * 130 + 64], PA[:, po:po + 64])
                VV.inc(v.tensor_copy(v65[:, tc * 130 + 65:tc * 130 + 129],
                                     PA[:, po + 64:po + 128]))
        blk.vector(_)

        for hh in range(2):
            hb = hh * 64
            vcol = hh * 65

            tt_h0 = TT.n
            ss_h0 = SS.n
            vv_pre_head = VV.n

            def _(t, tt_h0=tt_h0, ss_h0=ss_h0, vcol=vcol, hb=hb,
                  vv_pre_head=vv_pre_head):
                t.wait_ge(VV.h, vv_pre_head)
                t.wait_ge(SS.h, ss_h0)
                for kc in range(32):
                    for h2 in range(2):
                        if kc >= 1:
                            t.wait_ge(SS.h, ss_h0 + 2 * (kc - 1) + h2 + 1)
                        for q4 in range(2):
                            qo = h2 * 1024 + q4 * 512
                            ins = t.matmul(PA[:, qo:qo + 512],
                                           kfm[hb:hb + 64, kc * 128:(kc + 1) * 128],
                                           qfm[hb:hb + 64, qo:qo + 512],
                                           start=True, stop=True)
                        TT.inc(ins)
                    for h2 in range(2):
                        t.wait_ge(SS.h, ss_h0 + 2 * kc + h2 + 1)
                        for q4 in range(2):
                            qo = h2 * 1024 + q4 * 512
                            ins = t.matmul(PB[0:65, qo:qo + 512],
                                           v65[:, kc * 130 + vcol:kc * 130 + vcol + 65],
                                           pb_sb[:, qo:qo + 512],
                                           start=(kc == 0), stop=(kc == 31))
                        TT.inc(ins)
            blk.tensor(_)
            tt_av_last = TT.n

            def _(s, tt_h0=tt_h0):
                for kc in range(32):
                    for h2 in range(2):
                        s.wait_ge(TT.h, tt_h0 + 4 * kc + h2 + 1)
                        SS.inc(s.activation(pb_sb[:, h2 * 1024:(h2 + 1) * 1024],
                                            PA[:, h2 * 1024:(h2 + 1) * 1024],
                                            AF.Exp, scale=0.125))
            blk.scalar(_)

            def _(v, tt_av_last=tt_av_last):
                v.wait_ge(TT.h, tt_av_last)
                with nc.allow_low_precision("bf16 recip of softmax sums"):
                    VV.inc(v.reciprocal(rec1[:], PB[64:65, :]))
            blk.vector(_)
            vv_rec = VV.n

            def _(t, vv_rec=vv_rec):
                t.wait_ge(VV.h, vv_rec)
                for q4 in range(4):
                    ins = t.matmul(PA[0:64, q4 * 512:(q4 + 1) * 512],
                                   ones128[0:1, 0:64],
                                   rec1[0:1, q4 * 512:(q4 + 1) * 512],
                                   start=True, stop=True)
                TT.inc(ins)
            blk.tensor(_)
            tt_bc = TT.n

            def _(v, tt_bc=tt_bc, c=c, hh=hh):
                v.wait_ge(TT.h, tt_bc)
                VV.inc(v.tensor_copy(recb[:], PA[0:64, :]))
                v.wait_ge(VV.h, VV.n)
                VV.inc(v.tensor_mul(ofm[hh * 64:(hh + 1) * 64, c * 2048:(c + 1) * 2048],
                                    PB[0:64, :], recb[:]))
            blk.vector(_)

        tt_group_end = TT.n
        vv_group_end = VV.n

    close(13)

    wob = [scope(f"wob{i}", [128, 1024], BF16) for i in range(8)]
    x0ld = [scope(f"x0ld{i}", [128, TOWN], F32) for i in range(2)]
    x1t = [scope(f"x1t{i}", [128, TOWN], F32) for i in range(2)]
    f32t = scope("f32t", [128, TOWN], F32)

    rclose(8)

    wo_r = woD.rearrange("(hc p) j -> p hc j", p=128)

    def _(g, tt_group_end=tt_group_end, vv_pre=VV.n, ss_pre=SS.n):
        g.wait_ge(TT.h, tt_group_end)
        g.wait_ge(VV.h, vv_pre)
        g.wait_ge(SS.h, ss_pre)
        gWOB.put(g, [(wob[hdc][:], wo_r[:, hdc, :]) for hdc in range(8)])
    blk.gpsimd(_)

    vv_cmb0 = VV.n
    tt_wo0 = TT.n

    def _(sy, tt_pre=TT.n, ss_pre=SS.n):
        sy.wait_ge(TT.h, tt_pre)
        sy.wait_ge(SS.h, ss_pre)
        sy.wait_ge(VV.h, vv_cmb0)
        for dc in range(NDC):
            if dc >= 2:
                sy.wait_ge(VV.h, vv_cmb0 + 3 * (dc - 1))
            sX0.put(sy, [(x0ld[dc % 2][:], x0fm[dc * 128:(dc + 1) * 128, 0:TOWN])])
    blk.sync(_)

    def _(t, vv_group_end=vv_group_end):
        gWOB.wait(t)
        t.wait_ge(VV.h, vv_group_end)
        for dco in range(NDC):
            if dco >= 2:
                t.wait_ge(VV.h, vv_cmb0 + 3 * (dco - 1))
            ps = PA if dco % 2 == 0 else PB
            for q4 in range(4):
                for hdc in range(8):
                    ins = t.matmul(ps[:, q4 * 512:(q4 + 1) * 512],
                                   wob[hdc][:, dco * 128:(dco + 1) * 128],
                                   ofm[:, hdc * 2048 + q4 * 512:hdc * 2048 + (q4 + 1) * 512],
                                   start=(hdc == 0), stop=(hdc == 7))
            TT.inc(ins)
    blk.tensor(_)

    def _(g):
        for dco in range(NDC):
            g.wait_ge(VV.h, vv_cmb0 + 3 * (dco + 1))
            gX1D.put(g, [(x1d[dco * 128:(dco + 1) * 128, :], x1t[dco % 2][:])])
        gX1D.wait(g)
        CC.inc(g.collective_compute(
            "AllGather", mybir.AluOpType.bypass,
            replica_groups=[[0, 1], [2, 3], [4, 5], [6, 7]],
            ins=[x1d.opt()], outs=[x1g.opt()]))
    blk.gpsimd(_)

    def _(v):
        for dco in range(NDC):
            v.wait_ge(TT.h, tt_wo0 + dco + 1)
            sX0.wait(v, dco)
            if dco >= 2:
                gX1D.wait(v, dco - 2)
            ps = PA if dco % 2 == 0 else PB
            v.wait_ge(VV.h, VV.n)
            VV.inc(v.tensor_copy(f32t[:], ffnacc[dco][:]))
            v.wait_ge(VV.h, VV.n)
            VV.inc(v.tensor_add(f32t[:], f32t[:], ps[:]))
            v.wait_ge(VV.h, VV.n)
            VV.inc(v.tensor_add(x1t[dco % 2][:], f32t[:], x0ld[dco % 2][:]))
    blk.vector(_)
    vv_x1done = VV.n

    close(len(guards))

    nPre = len(guards)
    xnp = [scope(f"xnp{i}", [128, TOWN], BF16) for i in range(NDC)]
    w_tmP = scope("w_tmP", [128, 128], BF16)
    wbv8P = scope("wbv8P", [128, 8 * TOWN], BF16)
    xf32 = [scope(f"xf32{i}", [128, TOWN], F32) for i in range(2)]
    wrF32 = scope("wrF32", [128, 64], F32)
    fAP = scope("fAP", [128, 2048], F32)
    fBP = scope("fBP", [128, 2048], F32)
    fCP = scope("fCP", [128, 2048], F32)
    sqdP = [scope(f"sqdP{i}", [128, 2048], BF16) for i in range(2)]
    muP = scope("muP", [128, 2048], BF16)
    rsP = scope("rsP", [128, 2048], BF16)
    wrsbP = scope("wrsbP", [128, 64], BF16)
    br8P = scope("br8P", [8, 1], F32)
    scbfP = scope("scbfP", [8, TOWN], BF16)
    s_tmP = scope("s_tmP", [128, 128], F32)
    sAP = scope("sAP", [128, 128], F32)
    sBP = scope("sBP", [128, 128], F32)
    m16P = scope("m16P", [128, 64], F32)
    wrowP = scope("wrowP", [1, TOWN], BF16)

    vv_cp0 = VV.n

    def _(sy, tt_pre=TT.n, vv_pre=VV.n, ss_pre=SS.n):
        sy.wait_ge(TT.h, tt_pre)
        sy.wait_ge(VV.h, vv_pre)
        sy.wait_ge(SS.h, ss_pre)
        gX1D.wait(sy)
        sMS.put(sy, [(br8P[:], br8D), (identb[:], idD)])
        sWRL.put(sy, [(wrF32[:].rearrange("p (dc e) -> p dc e", dc=8),
                       wrD.rearrange("(dc p) e -> p dc e", p=128))])
        for dc in range(NDC):
            if dc >= 2:
                sy.wait_ge(VV.h, vv_cp0 + dc - 1)
            sX1R.put(sy, [(xf32[dc % 2][:], x1d[dc * 128:(dc + 1) * 128, :])])
    blk.sync(_)

    def _(v):
        for dc in range(NDC):
            sX1R.wait(v, dc)
            v.wait_ge(VV.h, VV.n)
            VV.inc(v.tensor_copy(xnp[dc][:], xf32[dc % 2][:]))
        sWRL.wait(v)
        v.wait_ge(VV.h, VV.n)
        VV.inc(v.tensor_copy(wrsbP[:], wrF32[:]))
    blk.vector(_)

    tt_bP = TT.n
    vv_bP = VV.n

    def _(v, tt_bP=tt_bP):
        for dc in range(NDC):
            if dc >= 2:
                v.wait_ge(TT.h, tt_bP + dc - 1)
            v.wait_ge(VV.h, VV.n)
            VV.inc(v.tensor_mul(sqdP[dc % 2][:], xnp[dc][:], xnp[dc][:]))
    blk.vector(_)

    def _(t, vv_bP=vv_bP, ss_pre=SS.n):
        t.wait_ge(SS.h, ss_pre)
        for dc in range(NDC):
            t.wait_ge(VV.h, vv_bP + dc + 1)
            for q4 in range(4):
                t.matmul(PA[:, q4 * 512:(q4 + 1) * 512], ones128[:],
                         xnp[dc][:, q4 * 512:(q4 + 1) * 512],
                         start=(dc == 0), stop=(dc == NDC - 1))
            for q4 in range(4):
                ins = t.matmul(PB[:, q4 * 512:(q4 + 1) * 512], ones128[:],
                               sqdP[dc % 2][:, q4 * 512:(q4 + 1) * 512],
                               start=(dc == 0), stop=(dc == NDC - 1))
            TT.inc(ins)
    blk.tensor(_)
    tt_s1P = TT.n

    def _(v, tt_s1P=tt_s1P):
        v.wait_ge(TT.h, tt_s1P)
        VV.inc(v.tensor_scalar_mul(fAP[:], PA[:], 1.0 / D))
        VV.inc(v.tensor_scalar_mul(fBP[:], PB[:], 1.0 / D))
        v.wait_ge(VV.h, VV.n)
        VV.inc(v.tensor_mul(fCP[:], fAP[:], fAP[:]))
        v.wait_ge(VV.h, VV.n)
        VV.inc(v.tensor_sub(fBP[:], fBP[:], fCP[:]))
    blk.vector(_)
    vv_varP = VV.n

    def _(s, vv_varP=vv_varP):
        s.wait_ge(VV.h, vv_varP)
        SS.inc(s.activation(fCP[:], fBP[:], AF.Sqrt, bias=epsb[:]))
    blk.scalar(_)
    ss_sdP = SS.n

    def _(v, ss_sdP=ss_sdP):
        v.wait_ge(SS.h, ss_sdP)
        VV.inc(v.reciprocal(fBP[:], fCP[:]))
        VV.inc(v.tensor_copy(muP[:], fAP[:]))
        v.wait_ge(VV.h, VV.n)
        VV.inc(v.tensor_copy(rsP[:], fBP[:]))
    blk.vector(_)

    vv_apP = VV.n

    def _(v):
        for dc in range(NDC):
            v.wait_ge(VV.h, VV.n)
            VV.inc(v.tensor_sub(xnp[dc][:], xnp[dc][:], muP[:]))
            v.wait_ge(VV.h, VV.n)
            VV.inc(v.tensor_mul(xnp[dc][:], xnp[dc][:], rsP[:]))
    blk.vector(_)
    vv_xnP = VV.n

    def _(sy, vv_apP=vv_apP):
        for dc in range(NDC):
            sy.wait_ge(VV.h, vv_apP + 2 * dc + 2)
            sXNS.put(sy, [(xn1d[dc], xnp[dc][:])])
    blk.sync(_)

    def _(t, vv_xnP=vv_xnP):
        t.wait_ge(VV.h, vv_xnP)
        for q4 in range(4):
            for dc in range(NDC):
                ins = t.matmul(PA[0:8, q4 * 512:(q4 + 1) * 512],
                               wrsbP[:, dc * 8:(dc + 1) * 8],
                               xnp[dc][:, q4 * 512:(q4 + 1) * 512],
                               start=(dc == 0), stop=(dc == NDC - 1))
        TT.inc(ins)
    blk.tensor(_)
    tt_scP = TT.n

    def _(s, tt_scP=tt_scP):
        s.wait_ge(TT.h, tt_scP)
        sMS.wait(s)
        SS.inc(s.activation(scbfP[:], PA[0:8, :], AF.Identity, bias=br8P[:]))
    blk.scalar(_)
    ss_scP = SS.n

    def _(t, ss_scP=ss_scP):
        t.wait_ge(SS.h, ss_scP)
        for tcw in range(16):
            ins = t.matmul(PB[:, tcw * 128:tcw * 128 + 8],
                           scbfP[:, tcw * 128:(tcw + 1) * 128],
                           identb[0:8, 0:8], start=True, stop=True)
        TT.inc(ins)
    blk.tensor(_)
    tt_trP = TT.n

    def _(v, tt_trP=tt_trP):
        v.wait_ge(TT.h, tt_trP)
        for tcw in range(16):
            sl = slice(tcw * 8, (tcw + 1) * 8)
            VV.inc(v.tensor_copy(s_tmP[:, sl], PB[:, tcw * 128:tcw * 128 + 8]))
        for tcw in range(16):
            sl = slice(tcw * 8, (tcw + 1) * 8)
            v.wait_ge(VV.h, VV.n)
            VV.inc(v.reduce_max(m16P[:, tcw * 4:tcw * 4 + 1], s_tmP[:, sl], axis=AX.X))
            v.wait_ge(VV.h, VV.n)
            VV.inc(v.tensor_scalar(sAP[:, sl], s_tmP[:, sl], m16P[:, tcw * 4:tcw * 4 + 1],
                                   None, ALU.is_ge))
            v.wait_ge(VV.h, VV.n)
            VV.inc(v.scalar_tensor_tensor(sBP[:, sl], sAP[:, sl], -1e30, s_tmP[:, sl],
                                          ALU.mult, ALU.add))
            v.wait_ge(VV.h, VV.n)
            VV.inc(v.reduce_max(m16P[:, tcw * 4 + 1:tcw * 4 + 2], sBP[:, sl], axis=AX.X))
            v.wait_ge(VV.h, VV.n)
            VV.inc(v.tensor_scalar(sAP[:, sl], s_tmP[:, sl],
                                   m16P[:, tcw * 4 + 1:tcw * 4 + 2], None, ALU.is_ge))
            v.wait_ge(VV.h, VV.n)
            VV.inc(v.tensor_scalar(sBP[:, sl], s_tmP[:, sl], m16P[:, tcw * 4:tcw * 4 + 1],
                                   None, ALU.subtract))
    blk.vector(_)
    vv_t2P = VV.n

    def _(s, vv_t2P=vv_t2P):
        s.wait_ge(VV.h, vv_t2P)
        SS.inc(s.activation(sBP[:], sBP[:], AF.Exp))
    blk.scalar(_)
    ss_expP = SS.n

    def _(v, ss_expP=ss_expP):
        v.wait_ge(SS.h, ss_expP)
        for tcw in range(16):
            sl = slice(tcw * 8, (tcw + 1) * 8)
            v.wait_ge(VV.h, VV.n)
            VV.inc(v.tensor_mul(sBP[:, sl], sBP[:, sl], sAP[:, sl]))
            v.wait_ge(VV.h, VV.n)
            VV.inc(v.reduce_sum(m16P[:, tcw * 4 + 2:tcw * 4 + 3], sBP[:, sl], axis=AX.X))
            v.wait_ge(VV.h, VV.n)
            VV.inc(v.reciprocal(m16P[:, tcw * 4 + 3:tcw * 4 + 4],
                                m16P[:, tcw * 4 + 2:tcw * 4 + 3]))
            v.wait_ge(VV.h, VV.n)
            VV.inc(v.tensor_scalar(w_tmP[:, sl], sBP[:, sl],
                                   m16P[:, tcw * 4 + 3:tcw * 4 + 4], None, ALU.mult))
    blk.vector(_)
    vv_wP = VV.n

    tt_r0 = TT.n
    vv_r0 = VV.n

    def _(t, vv_wP=vv_wP, tt_r0=tt_r0, vv_r0=vv_r0):
        t.wait_ge(VV.h, vv_wP)
        for i in range(NEXP):
            if i >= 1:
                t.wait_ge(VV.h, vv_r0 + 2 * i - 1)
            for tcw in range(16):
                ins = t.matmul(PA[0:1, tcw * 128:(tcw + 1) * 128],
                               w_tmP[:, tcw * 8 + i:tcw * 8 + i + 1], identb[:],
                               start=True, stop=True)
            TT.inc(ins)
            t.wait_ge(VV.h, vv_r0 + 2 * i + 1)
            for q4 in range(4):
                ins = t.matmul(PB[0:128, q4 * 512:(q4 + 1) * 512],
                               ones128[0:1, 0:128],
                               wrowP[0:1, q4 * 512:(q4 + 1) * 512],
                               start=True, stop=True)
            TT.inc(ins)
    blk.tensor(_)

    def _(v, tt_r0=tt_r0):
        for i in range(NEXP):
            v.wait_ge(TT.h, tt_r0 + 2 * i + 1)
            VV.inc(v.tensor_copy(wrowP[:], PA[0:1, 0:2048]))
            v.wait_ge(TT.h, tt_r0 + 2 * i + 2)
            VV.inc(v.tensor_copy(wbv8P[:, i * TOWN:(i + 1) * TOWN], PB[:, 0:2048]))
    blk.vector(_)

    tt_rowsE = TT.n
    vv_bcP = VV.n

    def _(sy, vv_r0=vv_r0):
        for i in range(NEXP):
            sy.wait_ge(VV.h, vv_r0 + 2 * i + 2)
            sWBS.put(sy, [(wbv8d[:, i * TOWN:(i + 1) * TOWN],
                           wbv8P[:, i * TOWN:(i + 1) * TOWN])])
    blk.sync(_)

    close(18)
    h1mP = scope("h1mP", [128, 32 * 512], BF16)
    w1cP = [scope(f"w1cP{i}", [128, 4096], BF16) for i in range(2)]
    w2cP = [scope(f"w2cP{i}", [128, 4096], BF16) for i in range(2)]
    etmpP = [scope(f"etmpP{i}", [128, 512], F32) for i in range(2)]
    e0bf = scope("e0bf", [128, 4096], BF16)
    b1aP = scope("b1aP", [128, 32], F32)
    b2aP = scope("b2aP", [128, 8], F32)

    def _(sy, tt_rE=tt_rowsE, vv_bcP=vv_bcP):
        sy.wait_ge(TT.h, tt_rE)
        sy.wait_ge(VV.h, vv_bcP)
        gWPP.wait(sy)
        sMS.put(sy, [(b1aP[:], b1aD[:, 0:32]), (b2aP[:], b2aD[:, 0:8])])
    blk.sync(_)

    ss_pevP = SS.n
    for tb in range(4):
        ts = tb * 512
        tt_f1 = TT.n
        ss_f1 = SS.n
        w1base = len(sW1P.marks)

        def _(sy, tt_f1=tt_f1):
            for gr in range(8):
                if gr >= 2:
                    sy.wait_ge(TT.h, tt_f1 + 4 * (gr - 1))
                sW1P.put(sy, [(w1cP[gr % 2][:].rearrange("p (c e) -> p c e", c=4),
                               w1bd[0, 4 * gr:4 * gr + 4].rearrange("c p e -> p c e"))])
        blk.sync(_)

        def _(t, ts=ts, ss_f1=ss_f1, w1base=w1base, ss_pe=ss_pevP,
              vv_bcP=vv_bcP, tb=tb):
            if tb == 0:
                t.wait_ge(VV.h, vv_bcP)
            for hc in range(32):
                sW1P.wait(t, w1base + hc // 4)
                if hc < 2:
                    t.wait_ge(SS.h, max(0, ss_pe - 7 + hc))
                if hc >= 2:
                    t.wait_ge(SS.h, ss_f1 + hc - 1)
                po = (hc % 2) * 512
                for dc in range(NDC):
                    ins = t.matmul(PA[:, po:po + 512],
                                   w1cP[(hc // 4) % 2][:, (hc % 4) * 1024 + dc * 128:
                                                       (hc % 4) * 1024 + (dc + 1) * 128],
                                   xnp[dc][:, ts:ts + 512],
                                   start=(dc == 0), stop=(dc == NDC - 1))
                TT.inc(ins)
        blk.tensor(_)

        def _(s, tt_f1=tt_f1, ss_f1=ss_f1, tb=tb):
            if tb == 0:
                sMS.wait(s)
            for hc in range(32):
                s.wait_ge(TT.h, tt_f1 + hc + 1)
                po = (hc % 2) * 512
                SS.inc(s.activation(h1mP[:, hc * 512:(hc + 1) * 512], PA[:, po:po + 512],
                                    AF.Relu, bias=b1aP[:, hc:hc + 1]))
        blk.scalar(_)
        ss_f1_done = SS.n

        tt_f2 = TT.n
        w2base = len(sW2P.marks)

        def _(sy, tt_f2=tt_f2):
            for gr in range(8):
                if gr >= 2:
                    sy.wait_ge(TT.h, tt_f2 + 4 * (gr - 1))
                sW2P.put(sy, [(w2cP[gr % 2][:].rearrange("p (c e) -> p c e", c=4),
                               w2bd[0, 4 * gr * 128:(4 * gr + 4) * 128, :].rearrange(
                                   "(c p) e -> p c e", p=128))])
        blk.sync(_)

        def _(t, ss_f1_done=ss_f1_done, ss_pe=ss_pevP, w2base=w2base):
            t.wait_ge(SS.h, max(ss_f1_done, ss_pe))
            for hc in range(32):
                sW2P.wait(t, w2base + hc // 4)
                for ec in range(8):
                    ps = PA if ec < 4 else PB
                    ins = t.matmul(ps[:, (ec % 4) * 512:(ec % 4 + 1) * 512],
                                   w2cP[(hc // 4) % 2][:, (hc % 4) * 1024 + ec * 128:
                                                       (hc % 4) * 1024 + (ec + 1) * 128],
                                   h1mP[:, hc * 512:(hc + 1) * 512],
                                   start=(hc == 0), stop=(hc == 31))
                TT.inc(ins)
        blk.tensor(_)
        tt_f2_done = TT.n

        def _(s, tt_f2_done=tt_f2_done, vv_pre=VV.n):
            s.wait_ge(TT.h, tt_f2_done)
            s.wait_ge(VV.h, vv_pre)
            for ec in range(8):
                if ec >= 2:
                    s.wait_ge(VV.h, vv_pre + (ec - 1))
                ps = PA if ec < 4 else PB
                SS.inc(s.activation(etmpP[ec % 2][:],
                                    ps[:, (ec % 4) * 512:(ec % 4 + 1) * 512],
                                    AF.Identity,
                                    bias=b2aP[:, ec:ec + 1]))
        blk.scalar(_)
        ss_ev0 = SS.n - 8

        def _(v, ss_ev0=ss_ev0, ts=ts, tb=tb):
            if tb >= 1:
                gE0S.wait(v, tb - 1)
            for ec in range(8):
                v.wait_ge(SS.h, ss_ev0 + ec + 1)
                v.wait_ge(VV.h, VV.n)
                VV.inc(v.tensor_mul(e0bf[:, ec * 512:(ec + 1) * 512], etmpP[ec % 2][:],
                                    wbv8P[:, ts:ts + 512]))
        blk.vector(_)
        vv_e0 = VV.n

        def _(g, vv_e0=vv_e0, tb=tb):
            g.wait_ge(VV.h, vv_e0)
            gE0S.put(g, [(e0d[tb].rearrange("ec p e -> p ec e"),
                          e0bf[:].rearrange("p (ec e) -> p ec e", ec=8))])
        blk.gpsimd(_)

        ss_pevP = SS.n

    close(len(guards) - nPre)

    kfmB = scope("kfmB", [64, T], BF16)
    qfmB = scope("qfmB", [64, T], BF16)
    ktm = scope("ktm", [128, 32 * 64], BF16)
    bvb = scope("bvbS", [128, D], F32)
    bkb = scope("bkbS", [128, 64], F32)
    bk = scope("bkS", [64, 1], F32)
    bq = scope("bqS", [64, 1], F32)
    gma = scope("gma", [128, 8], F32)
    bta = scope("bta", [128, 8], F32)
    mUe = scope("mUeS", [128, 128], BF16)
    mLe = scope("mLeS", [128, 128], BF16)
    mUn = scope("mUnS", [128, 128], BF16)
    Sst = scope("Sst", [64, D], F32)
    Sbf = scope("Sbf", [64, D], BF16)
    nUt = scope("nUt", [128, 128], F32)
    nLt = scope("nLt", [128, 128], F32)
    Cbuf = scope("Cbuf", [128, 6 * 128], F32)
    Ta = scope("Ta", [128, 128], F32)
    Tb = scope("Tb", [128, 128], F32)
    Rs = scope("Rs", [128, 128], BF16)
    brawS = scope("brawS", [128, D], BF16)
    Utm = scope("Utm", [128, D], BF16)
    pmS = scope("pmS", [128, 128], BF16)
    vld = [scope(f"vld{i}", [128, D], BF16) for i in range(2)]
    vev = [scope(f"vev{i}", [128, D], BF16) for i in range(2)]
    nB_base = len(guards)

    def _(sy, tt_pre=TT.n, vv_pre=VV.n, ss_pre=SS.n):
        sy.wait_ge(TT.h, tt_pre)
        sy.wait_ge(VV.h, vv_pre)
        sy.wait_ge(SS.h, ss_pre)
        sXNS.wait(sy)
        sWBS.wait(sy)
        sW1P.wait(sy)
        sW2P.wait(sy)
        gE0S.wait(sy)
        sMS.put(sy, [(bvb[:], bvbD), (bkb[:], bkbD), (bk[:], bkD), (bq[:], bqD),
                     (gma[:], gbD), (bta[:], btD), (mUe[:], mUeD), (mLe[:], mLeD),
                     (mUn[:], mUnD), (identb[:], idD)])
    blk.sync(_)

    def _(v):
        v.wait_ge(VV.h, VV.n)
        sXNS.wait(v)
        sWBS.wait(v)
        sW1P.wait(v)
        sW2P.wait(v)
        gE0S.wait(v)
        v.memset(Sst[:], 0.0)
        VV.inc(v.memset(Sbf[:], 0.0))
    blk.vector(_)

    x1bf = [scope(f"x1b{i}", [128, T], BF16) for i in range(NDC)]
    mu_bfB = scope("mu_bfB", [128, T], BF16)
    f_bfB = scope("f_bfB", [128, T], BF16)

    def _(g, tt_pre=TT.n, vv_pre=VV.n, ss_pre=SS.n):
        g.wait_ge(CC.h, 1)
        sXNS.wait(g)
        sWBS.wait(g)
        sW1P.wait(g)
        sW2P.wait(g)
        gE0S.wait(g)
        g.wait_ge(TT.h, tt_pre)
        g.wait_ge(VV.h, vv_pre)
        g.wait_ge(SS.h, ss_pre)
        for dc in range(NDC):
            gX1.put(g, [(x1bf[dc][:, 0:TOWN], x1g[0, dc * 128:(dc + 1) * 128, :]),
                        (x1bf[dc][:, TOWN:T], x1g[1, dc * 128:(dc + 1) * 128, :])])
    blk.gpsimd(_)

    fAB = scope("fAB", [128, 2048], F32)
    fBB = scope("fBB", [128, 2048], F32)
    fCB = scope("fCB", [128, 2048], F32)
    fDB = scope("fDB", [128, 2048], F32)
    sqdB = [scope(f"sqdB{i}", [128, 2048], BF16) for i in range(2)]

    for tb in range(2):
        ts = tb * 2048
        tt_b = TT.n
        vv_b = VV.n

        def _(v, ts=ts, tt_b=tt_b):
            gX1.wait(v)
            for dc in range(NDC):
                if dc >= 2:
                    v.wait_ge(TT.h, tt_b + dc - 1)
                v.wait_ge(VV.h, VV.n)
                VV.inc(v.tensor_mul(sqdB[dc % 2][:], x1bf[dc][:, ts:ts + 2048],
                                    x1bf[dc][:, ts:ts + 2048]))
        blk.vector(_)

        def _(t, ts=ts, vv_b=vv_b, ss_pre=SS.n):
            t.wait_ge(SS.h, ss_pre)
            for dc in range(NDC):
                t.wait_ge(VV.h, vv_b + dc + 1)
                for q4 in range(4):
                    t.matmul(PA[:, q4 * 512:(q4 + 1) * 512], ones128[:],
                             x1bf[dc][:, ts + q4 * 512:ts + (q4 + 1) * 512],
                             start=(dc == 0), stop=(dc == NDC - 1))
                for q4 in range(4):
                    ins = t.matmul(PB[:, q4 * 512:(q4 + 1) * 512], ones128[:],
                                   sqdB[dc % 2][:, q4 * 512:(q4 + 1) * 512],
                                   start=(dc == 0), stop=(dc == NDC - 1))
                TT.inc(ins)
        blk.tensor(_)
        tt_s1 = TT.n

        def _(v, tt_s1=tt_s1):
            v.wait_ge(TT.h, tt_s1)
            VV.inc(v.tensor_scalar_mul(fAB[:], PA[:], 1.0 / D))
            VV.inc(v.tensor_scalar_mul(fBB[:], PB[:], 1.0 / D))
            v.wait_ge(VV.h, VV.n)
            VV.inc(v.tensor_mul(fCB[:], fAB[:], fAB[:]))
            v.wait_ge(VV.h, VV.n)
            VV.inc(v.tensor_sub(fBB[:], fBB[:], fCB[:]))
        blk.vector(_)
        vv_var = VV.n

        def _(s, vv_var=vv_var):
            s.wait_ge(VV.h, vv_var)
            SS.inc(s.activation(fCB[:], fBB[:], AF.Sqrt, bias=epsb[:]))
        blk.scalar(_)
        ss_sd = SS.n

        def _(v, ss_sd=ss_sd):
            v.wait_ge(SS.h, ss_sd)
            VV.inc(v.reciprocal(fDB[:], fCB[:]))
            v.wait_ge(VV.h, VV.n)
            VV.inc(v.tensor_mul(fCB[:], fDB[:], fDB[:]))
            v.wait_ge(VV.h, VV.n)
            VV.inc(v.tensor_mul(fCB[:], fBB[:], fCB[:]))
        blk.vector(_)
        vv_vn = VV.n

        def _(s, vv_vn=vv_vn):
            s.wait_ge(VV.h, vv_vn)
            SS.inc(s.activation(fCB[:], fCB[:], AF.Sqrt, bias=epsb[:]))
        blk.scalar(_)
        ss_sdn = SS.n

        def _(v, ts=ts, ss_sdn=ss_sdn):
            v.wait_ge(SS.h, ss_sdn)
            VV.inc(v.reciprocal(fCB[:], fCB[:]))
            v.wait_ge(VV.h, VV.n)
            VV.inc(v.tensor_mul(fCB[:], fCB[:], fDB[:]))
            v.wait_ge(VV.h, VV.n)
            VV.inc(v.tensor_copy(mu_bfB[:, ts:ts + 2048], fAB[:]))
            v.wait_ge(VV.h, VV.n)
            VV.inc(v.tensor_copy(f_bfB[:, ts:ts + 2048], fCB[:]))
        blk.vector(_)

    def _(v):
        sMS.wait(v)
        for dc in range(NDC):
            v.wait_ge(VV.h, VV.n)
            VV.inc(v.tensor_sub(x1bf[dc][:], x1bf[dc][:], mu_bfB[:]))
            v.wait_ge(VV.h, VV.n)
            VV.inc(v.tensor_mul(x1bf[dc][:], x1bf[dc][:], f_bfB[:]))
            v.wait_ge(VV.h, VV.n)
            VV.inc(v.tensor_scalar(x1bf[dc][:], x1bf[dc][:], gma[:, dc:dc + 1],
                                   bta[:, dc:dc + 1], ALU.mult, ALU.add))
    blk.vector(_)
    vv_h = VV.n

    close(6)
    h = x1bf

    wvb = scope("wvb", [128, 8 * D], BF16)
    wkb = scope("wkb", [128, 8 * 64], BF16)
    wqb = scope("wqb", [128, 8 * 64], BF16)

    def _(g, ss_pre=SS.n):
        g.wait_ge(VV.h, vv_h)
        g.wait_ge(SS.h, ss_pre)
        gWP.put(g, [(wvb[:].rearrange("p (dc e) -> p dc e", dc=8),
                     wvtD.rearrange("(dc p) e -> p dc e", p=128)),
                    (wkb[:].rearrange("p (dc e) -> p dc e", dc=8),
                     wktD.rearrange("(dc p) e -> p dc e", p=128)),
                    (wqb[:].rearrange("p (dc e) -> p dc e", dc=8),
                     wqtD.rearrange("(dc p) e -> p dc e", p=128))])
    blk.gpsimd(_)

    def _(t, ss_pre=SS.n):
        t.wait_ge(VV.h, vv_h)
        gWP.wait(t)
        for th in range(2):
            ts = th * 2048
            if th >= 1:
                t.wait_ge(SS.h, ss_pre + 2 * th - 1)
            for q4 in range(4):
                for dc in range(NDC):
                    ins = t.matmul(PA[0:64, q4 * 512:(q4 + 1) * 512],
                                   wkb[:, dc * 64:(dc + 1) * 64],
                                   h[dc][:, ts + q4 * 512:ts + (q4 + 1) * 512],
                                   start=(dc == 0), stop=(dc == NDC - 1))
            TT.inc(ins)
            if th >= 1:
                t.wait_ge(SS.h, ss_pre + 2 * th)
            for q4 in range(4):
                for dc in range(NDC):
                    ins = t.matmul(PB[0:64, q4 * 512:(q4 + 1) * 512],
                                   wqb[:, dc * 64:(dc + 1) * 64],
                                   h[dc][:, ts + q4 * 512:ts + (q4 + 1) * 512],
                                   start=(dc == 0), stop=(dc == NDC - 1))
            TT.inc(ins)
    blk.tensor(_)
    tt_kq0 = TT.n - 4

    def _(s):
        sMS.wait(s)
        for th in range(2):
            ts = th * 2048
            s.wait_ge(TT.h, tt_kq0 + 2 * th + 1)
            SS.inc(s.activation(kfmB[:, ts:ts + 2048], PA[0:64, :], AF.Identity,
                                bias=bk[:]))
            s.wait_ge(TT.h, tt_kq0 + 2 * th + 2)
            SS.inc(s.activation(qfmB[:, ts:ts + 2048], PB[0:64, :], AF.Identity,
                                bias=bq[:]))
    blk.scalar(_)
    ss_kq = SS.n

    tt_kv0 = TT.n
    vv_kv0 = VV.n

    def _(t):
        t.wait_ge(SS.h, ss_kq)
        for tc in range(32):
            if tc >= 2:
                t.wait_ge(VV.h, vv_kv0 + 2 * (tc - 1))
            pok = (tc % 2) * 512
            for dc in range(NDC):
                ins = t.matmul(PA[:, pok:pok + 64],
                               h[dc][:, tc * 128:(tc + 1) * 128],
                               wkb[:, dc * 64:(dc + 1) * 64],
                               start=(dc == 0), stop=(dc == NDC - 1))
            TT.inc(ins)
            pov = (tc % 2) * 1024
            for j in range(2):
                for dc in range(NDC):
                    ins = t.matmul(PB[:, pov + j * 512:pov + (j + 1) * 512],
                                   h[dc][:, tc * 128:(tc + 1) * 128],
                                   wvb[:, dc * 1024 + j * 512:dc * 1024 + (j + 1) * 512],
                                   start=(dc == 0), stop=(dc == NDC - 1))
            TT.inc(ins)
    blk.tensor(_)

    def _(g, vv_kv0=vv_kv0):
        for tc in range(32):
            g.wait_ge(VV.h, vv_kv0 + 2 * (tc + 1))
            gVD.put(g, [(vd[tc], vev[tc % 2][:])])
    blk.gpsimd(_)

    def _(v):
        for tc in range(32):
            v.wait_ge(TT.h, tt_kv0 + 2 * tc + 1)
            VV.inc(v.tensor_add(ktm[:, tc * 64:(tc + 1) * 64], PA[:, (tc % 2) * 512:(tc % 2) * 512 + 64], bkb[:]))
            v.wait_ge(TT.h, tt_kv0 + 2 * tc + 2)
            if tc >= 2:
                gVD.wait(v, tc - 2)
            pov = (tc % 2) * 1024
            VV.inc(v.tensor_add(vev[tc % 2][:], PB[:, pov:pov + 1024], bvb[:]))
    blk.vector(_)
    vv_kv_done = VV.n

    close(3)
    close(10)

    yfm = [rscope(f"yfm{i}", [128, T], BF16) for i in range(NDC)]

    Rs32 = rscope("Rs32", [128, 32 * 128], BF16)
    nUt2 = [scope(f"nUt2{i}", [128, 128], F32) for i in range(2)]
    nLt2 = [scope(f"nLt2{i}", [128, 128], F32) for i in range(2)]
    Cb2 = [scope(f"Cb2{i}", [128, 6 * 128], F32) for i in range(2)]
    Ta2 = [scope(f"Ta2{i}", [128, 128], F32) for i in range(2)]
    Tb2 = [scope(f"Tb2{i}", [128, 128], F32) for i in range(2)]

    pBA = [PA[:, 0:128], PA[:, 1024:1152]]
    pBB = [PA[:, 512:640], PA[:, 1536:1664]]

    tt_p1 = TT.n
    vv_p1 = VV.n
    NT1, NV1 = 18, 25

    def _(t, vv_kv_done=vv_kv_done):
        t.wait_ge(VV.h, vv_kv_done)
        for tc in range(32):
            p = tc % 2
            tt0 = tt_p1 + NT1 * tc
            vv0 = vv_p1 + NV1 * tc
            ksl = kfmB[0:64, tc * 128:(tc + 1) * 128]
            nU, nL = nUt2[p], nLt2[p]
            Cb, TA, TB = Cb2[p], Ta2[p], Tb2[p]
            if tc >= 2:
                t.wait_ge(VV.h, vv_p1 + NV1 * (tc - 1))
            TT.inc(t.matmul(pBA[p], ksl, ksl, start=True, stop=True))
            for l in range(5):
                t.wait_ge(VV.h, vv0 + 2 + 3 * l)
                TT.inc(t.matmul(pBA[p], nL[:], nU[:], start=True, stop=True))
                TT.inc(t.matmul(pBB[p], nU[:], nL[:], start=True, stop=True))
            t.wait_ge(VV.h, vv0 + 17)
            TT.inc(t.matmul(pBA[p], nL[:], nU[:], start=True, stop=True))
            for idx, j in enumerate([5, 4, 3, 2, 1, 0]):
                t.wait_ge(VV.h, vv0 + 19 + idx)
                ps = pBB[p] if idx % 2 == 0 else pBA[p]
                rhs = [TA, TB][idx % 2]
                TT.inc(t.matmul(ps, Cb[:, j * 128:(j + 1) * 128], rhs[:],
                                start=True, stop=True))
    blk.tensor(_)

    def _(v):
        for tc in range(32):
            p = tc % 2
            tt0 = tt_p1 + NT1 * tc
            vv0 = vv_p1 + NV1 * tc
            nU, nL = nUt2[p], nLt2[p]
            Cb, TA, TB = Cb2[p], Ta2[p], Tb2[p]
            v.wait_ge(TT.h, tt0 + 1)
            VV.inc(v.tensor_mul(nU[:], pBA[p], mUe[:]))
            VV.inc(v.tensor_mul(nL[:], pBA[p], mLe[:]))
            for l in range(5):
                v.wait_ge(VV.h, VV.n)
                if l == 0:
                    VV.inc(v.tensor_sub(Cb[:, 0:128], identb[:], nL[:]))
                else:
                    VV.inc(v.tensor_add(Cb[:, l * 128:(l + 1) * 128], identb[:], nL[:]))
                v.wait_ge(TT.h, tt0 + 3 + 2 * l)
                v.wait_ge(VV.h, VV.n)
                VV.inc(v.tensor_copy(nU[:], pBA[p]))
                VV.inc(v.tensor_copy(nL[:], pBB[p]))
            v.wait_ge(VV.h, VV.n)
            VV.inc(v.tensor_add(Cb[:, 5 * 128:6 * 128], identb[:], nL[:]))
            v.wait_ge(TT.h, tt0 + 12)
            VV.inc(v.tensor_add(TA[:], identb[:], pBA[p]))
            for idx in range(6):
                v.wait_ge(TT.h, tt0 + 13 + idx)
                ps = pBB[p] if idx % 2 == 0 else pBA[p]
                if idx < 5:
                    dst = [TB, TA][idx % 2]
                    VV.inc(v.tensor_copy(dst[:], ps))
                else:
                    with nc.allow_low_precision("lam-scaled inverse factor"):
                        VV.inc(v.tensor_scalar_mul(Rs32[:, tc * 128:(tc + 1) * 128],
                                                   ps, TTT_LR))
    blk.vector(_)
    vv_p1_end = VV.n

    psKM = PA[:, 0:1024]
    psU = PA[:, 1024:2048]
    psY = PB[:, 0:1024]
    psP = PB[:, 1024:1152]
    psSK = PB[0:64, 1024:2048]

    tt_p2 = TT.n
    vv_p2 = VV.n
    NT2, NV2 = 6, 6

    def _(sy):
        for tc in range(32):
            if tc >= 2:
                sy.wait_ge(VV.h, vv_p2 + NV2 * (tc - 2) + 1)
            gVD.wait(sy, tc)
            sVT.put(sy, [(vld[tc % 2][:], vd[tc])])
    blk.sync(_)

    def _(t):
        t.wait_ge(VV.h, vv_p1_end)
        for tc in range(32):
            tt0 = tt_p2 + NT2 * tc
            vv0 = vv_p2 + NV2 * tc
            ksl = kfmB[0:64, tc * 128:(tc + 1) * 128]
            qsl = qfmB[0:64, tc * 128:(tc + 1) * 128]
            t.wait_ge(VV.h, vv0)
            for j in range(2):
                ins = t.matmul(psKM[:, j * 512:(j + 1) * 512], ksl,
                               Sbf[:, j * 512:(j + 1) * 512], start=True, stop=True)
            TT.inc(ins)
            t.wait_ge(VV.h, vv0 + 1)
            for j in range(2):
                ins = t.matmul(psU[:, j * 512:(j + 1) * 512],
                               Rs32[:, tc * 128:(tc + 1) * 128],
                               brawS[:, j * 512:(j + 1) * 512], start=True, stop=True)
            TT.inc(ins)
            for dc in range(NDC):
                ins = t.matmul(psY[:, dc * 128:(dc + 1) * 128],
                               Sbf[:, dc * 128:(dc + 1) * 128], qsl,
                               start=(dc % 4 == 0), stop=False)
            TT.inc(ins)
            TT.inc(t.matmul(psP, ksl, qsl, start=True, stop=True))
            t.wait_ge(VV.h, vv0 + 3)
            for dc in range(NDC):
                ins = t.matmul(psY[:, dc * 128:(dc + 1) * 128],
                               Utm[:, dc * 128:(dc + 1) * 128], pmS[:],
                               start=False, stop=(dc % 4 == 3))
            TT.inc(ins)
            for j in range(2):
                ins = t.matmul(psSK[:, j * 512:(j + 1) * 512],
                               ktm[:, tc * 64:(tc + 1) * 64],
                               Utm[:, j * 512:(j + 1) * 512], start=True, stop=True)
            TT.inc(ins)
    blk.tensor(_)

    def _(v):
        for tc in range(32):
            tt0 = tt_p2 + NT2 * tc
            vv0 = vv_p2 + NV2 * tc
            v.wait_ge(TT.h, tt0 + 1)
            sVT.wait(v, tc)
            VV.inc(v.tensor_sub(brawS[:], psKM, vld[tc % 2][:]))
            v.wait_ge(TT.h, tt0 + 2)
            VV.inc(v.tensor_copy(Utm[:], psU))
            v.wait_ge(TT.h, tt0 + 4)
            VV.inc(v.tensor_mul(pmS[:], psP, mUn[:]))
            v.wait_ge(TT.h, tt0 + 5)
            for dc in range(NDC):
                ins = v.tensor_copy(yfm[dc][:, tc * 128:(tc + 1) * 128],
                                    psY[:, dc * 128:(dc + 1) * 128])
            VV.inc(ins)
            v.wait_ge(TT.h, tt0 + 6)
            v.wait_ge(VV.h, VV.n)
            VV.inc(v.tensor_sub(Sst[:], Sst[:], psSK))
            v.wait_ge(VV.h, VV.n)
            VV.inc(v.tensor_copy(Sbf[:], Sst[:]))
    blk.vector(_)
    vv_scan_done = VV.n

    rclose(1)

    close(len(guards) - 0)
    x2acc = [scope(f"x2a{i}", [128, TOWN], F32) for i in range(NDC)]
    xn1bf = [scope(f"xn1b{i}", [128, TOWN], BF16) for i in range(NDC)]
    sel0 = scope("sel0S", [128, 1], F32)
    sel1 = scope("sel1S", [128, 1], F32)
    w1c = [scope(f"w1c{i}", [128, 4096], BF16) for i in range(2)]
    w2c = [scope(f"w2c{i}", [128, 4096], BF16) for i in range(2)]
    b1a = scope("b1aS", [128, 256], F32)
    b2a = scope("b2aS", [128, 64], F32)
    etmp = [scope(f"etmp{i}", [128, 512], F32) for i in range(2)]

    def _(sy, tt_pre=TT.n, vv_pre=VV.n, ss_pre=SS.n):
        sy.wait_ge(TT.h, tt_pre)
        sy.wait_ge(VV.h, vv_pre)
        sy.wait_ge(SS.h, ss_pre)
        sXNS.wait(sy)
        for dc in range(NDC):
            sXNL.put(sy, [(xn1bf[dc][:], xn1d[dc])])
        sMS.put(sy, [(sel0[:], sel0D), (sel1[:], sel1D),
                     (b1a[:], b1aD), (b2a[:], b2aD)])
    blk.sync(_)

    def _(g, tt_pre=TT.n, vv_pre=VV.n, ss_pre=SS.n):
        g.wait_ge(TT.h, tt_pre)
        g.wait_ge(VV.h, vv_pre)
        g.wait_ge(SS.h, ss_pre)
        gX1D.wait(g)
        for dc in range(NDC):
            sXC.put(g, [(x2acc[dc][:], x1d[dc * 128:(dc + 1) * 128, :])])
    blk.gpsimd(_)

    def _(t):
        sXNL.wait(t)
    blk.tensor(_)

    def _(s):
        sMS.wait(s)
    blk.scalar(_)

    def _(v, vv_scan_done=vv_scan_done):
        v.wait_ge(VV.h, vv_scan_done)
        sMS.wait(v)
        sXC.wait(v)
        sXNL.wait(v)
        for dc in range(NDC):
            v.wait_ge(VV.h, VV.n)
            VV.inc(v.tensor_add(x2acc[dc][:], x2acc[dc][:], xn1bf[dc][:]))
            v.wait_ge(VV.h, VV.n)
            VV.inc(v.scalar_tensor_tensor(x2acc[dc][:], yfm[dc][:, 0:TOWN], sel0[:],
                                          x2acc[dc][:], ALU.mult, ALU.add))
            v.wait_ge(VV.h, VV.n)
            VV.inc(v.scalar_tensor_tensor(x2acc[dc][:], yfm[dc][:, TOWN:T], sel1[:],
                                          x2acc[dc][:], ALU.mult, ALU.add))
    blk.vector(_)
    vv_xny = VV.n

    rclose(8)
    h1m = scope("h1m", [128, 32 * 512], BF16)
    wbv8 = scope("wbv8", [128, 8 * TOWN], BF16)

    e0ld = scope("e0ld", [128, 4096], BF16)

    def _(g):
        g.wait_ge(VV.h, vv_xny)
        sWBS.wait(g)
        for i in range(NEXP):
            sWBL.put(g, [(wbv8[:, i * TOWN:(i + 1) * TOWN],
                          wbv8d[:, i * TOWN:(i + 1) * TOWN])])
        gE0S.wait(g)
        for tb in range(4):
            if tb >= 1:
                g.wait_ge(VV.h, vv_xny + 8 * tb)
            sE0L.put(g, [(e0ld[:].rearrange("p (ec e) -> p ec e", ec=8),
                          e0d[tb].rearrange("ec p e -> p ec e"))])
    blk.gpsimd(_)

    def _(v):
        for tb in range(4):
            sE0L.wait(v, tb)
            for ec in range(8):
                v.wait_ge(VV.h, VV.n)
                VV.inc(v.tensor_add(x2acc[ec][:, tb * 512:(tb + 1) * 512],
                                    x2acc[ec][:, tb * 512:(tb + 1) * 512],
                                    e0ld[:, ec * 512:(ec + 1) * 512]))
    blk.vector(_)

    def _(v):
        sWBL.wait(v)
    blk.vector(_)

    vv_w = vv_scan_done
    vv_bc = vv_scan_done

    ss_prev_evac = SS.n
    vv_prev = VV.n
    tt_prev = TT.n

    for i in range(1, NEXP):
        for tb in range(4):
            ts = tb * 512
            tt_f1 = TT.n
            ss_f1 = SS.n
            w1base = len(sW1L.marks)

            def _(sy, i=i, tt_f1=tt_f1, vv_pre=VV.n):
                sy.wait_ge(VV.h, min(vv_pre, vv_w))
                gWPP.wait(sy)
                for gr in range(8):
                    if gr >= 2:
                        sy.wait_ge(TT.h, tt_f1 + 4 * (gr - 1))
                    sW1L.put(sy, [(w1c[gr % 2][:].rearrange("p (c e) -> p c e", c=4),
                                   w1bd[i, 4 * gr:4 * gr + 4].rearrange("c p e -> p c e"))])
            blk.sync(_)

            def _(t, i=i, ts=ts, ss_f1=ss_f1, vv_bc=vv_bc, w1base=w1base,
                  ss_pe=ss_prev_evac):
                t.wait_ge(VV.h, vv_bc)
                for hc in range(32):
                    sW1L.wait(t, w1base + hc // 4)
                    if hc < 2:
                        t.wait_ge(SS.h, max(0, ss_pe - 7 + hc))
                    if hc >= 2:
                        t.wait_ge(SS.h, ss_f1 + hc - 1)
                    po = (hc % 2) * 512
                    for dc in range(NDC):
                        ins = t.matmul(PA[:, po:po + 512],
                                       w1c[(hc // 4) % 2][:, (hc % 4) * 1024 + dc * 128:
                                                          (hc % 4) * 1024 + (dc + 1) * 128],
                                       xn1bf[dc][:, ts:ts + 512],
                                       start=(dc == 0), stop=(dc == NDC - 1))
                    TT.inc(ins)
            blk.tensor(_)

            def _(s, i=i, tt_f1=tt_f1, first=(i == 1 and tb == 0)):
                if first:
                    s.wait_ge(VV.h, vv_xny)
                for hc in range(32):
                    s.wait_ge(TT.h, tt_f1 + hc + 1)
                    po = (hc % 2) * 512
                    SS.inc(s.activation(h1m[:, hc * 512:(hc + 1) * 512], PA[:, po:po + 512],
                                        AF.Relu, bias=b1a[:, i * 32 + hc:i * 32 + hc + 1]))
            blk.scalar(_)
            ss_f1_done = SS.n

            tt_f2 = TT.n
            w2base = len(sW2L.marks)

            def _(sy, i=i, tt_f2=tt_f2):
                gWPP.wait(sy)
                for gr in range(8):
                    if gr >= 2:
                        sy.wait_ge(TT.h, tt_f2 + 4 * (gr - 1))
                    sW2L.put(sy, [(w2c[gr % 2][:].rearrange("p (c e) -> p c e", c=4),
                                   w2bd[i, 4 * gr * 128:(4 * gr + 4) * 128, :].rearrange(
                                       "(c p) e -> p c e", p=128))])
            blk.sync(_)

            def _(t, i=i, ss_f1_done=ss_f1_done, ss_prev_evac=ss_prev_evac,
                  w2base=w2base, tt_f1=tt_f1, ss_f1=ss_f1):
                t.wait_ge(SS.h, max(ss_f1_done, ss_prev_evac))
                for hc in range(32):
                    sW2L.wait(t, w2base + hc // 4)
                    for ec in range(8):
                        ps = PA if ec < 4 else PB
                        ins = t.matmul(ps[:, (ec % 4) * 512:(ec % 4 + 1) * 512],
                                       w2c[(hc // 4) % 2][:, (hc % 4) * 1024 + ec * 128:
                                                          (hc % 4) * 1024 + (ec + 1) * 128],
                                       h1m[:, hc * 512:(hc + 1) * 512],
                                       start=(hc == 0), stop=(hc == 31))
                    TT.inc(ins)
            blk.tensor(_)
            tt_f2_done = TT.n

            def _(s, i=i, tt_f2_done=tt_f2_done, vv_pre=VV.n):
                s.wait_ge(TT.h, tt_f2_done)
                s.wait_ge(VV.h, vv_pre)
                for ec in range(8):
                    if ec >= 2:
                        s.wait_ge(VV.h, vv_pre + 2 * (ec - 1))
                    ps = PA if ec < 4 else PB
                    SS.inc(s.activation(etmp[ec % 2][:],
                                        ps[:, (ec % 4) * 512:(ec % 4 + 1) * 512],
                                        AF.Identity,
                                        bias=b2a[:, i * 8 + ec:i * 8 + ec + 1]))
            blk.scalar(_)
            ss_evac0 = SS.n - 8

            def _(v, i=i, ts=ts, ss_evac0=ss_evac0):
                for ec in range(8):
                    v.wait_ge(SS.h, ss_evac0 + ec + 1)
                    v.wait_ge(VV.h, VV.n)
                    VV.inc(v.tensor_mul(etmp[ec % 2][:], etmp[ec % 2][:],
                                        wbv8[:, i * TOWN + ts:i * TOWN + ts + 512]))
                    v.wait_ge(VV.h, VV.n)
                    VV.inc(v.tensor_add(x2acc[ec][:, ts:ts + 512],
                                        x2acc[ec][:, ts:ts + 512], etmp[ec % 2][:]))
            blk.vector(_)
            ss_prev_evac = SS.n
            vv_prev = VV.n

        tt_prev = TT.n

    vv_fin = VV.n

    def _(g):
        g.wait_ge(VV.h, vv_fin)
        for dc in range(NDC):
            gXO.put(g, [(x2ownD[dc * 128:(dc + 1) * 128, :], x2acc[dc][:])])
        gXO.wait(g)
    blk.gpsimd(_)

    blk_g.__exit__(None, None, None)
    close(len(guards))
    rclose(len(rguards))
    pg2.__exit__(None, None, None)
    pg1.__exit__(None, None, None)
    for g in reversed(keep):
        g.__exit__(None, None, None)
    return nc


def prep_fused_inmaps(inputs):
    a_maps = prep_block0_inmaps(inputs)
    wk = np.asarray(inputs["L1_attn_Wk"], np.float32)
    wv = np.asarray(inputs["L1_attn_Wv"], np.float32)
    wq = np.asarray(inputs["L1_attn_Wq"], np.float32)
    bk = np.asarray(inputs["L1_attn_bk"], np.float32)
    bv = np.asarray(inputs["L1_attn_bv"], np.float32)
    bq = np.asarray(inputs["L1_attn_bq"], np.float32)
    gm = np.asarray(inputs["L1_attn_gamma"], np.float32)
    bt = np.asarray(inputs["L1_attn_beta"], np.float32)
    wr = np.asarray(inputs["L1_ffn_Wr"], np.float32)
    br = np.asarray(inputs["L1_ffn_br"], np.float32)
    w1 = np.asarray(inputs["L1_ffn_W1"], np.float32)
    b1 = np.asarray(inputs["L1_ffn_b1"], np.float32)
    w2 = np.asarray(inputs["L1_ffn_W2"], np.float32)
    b2 = np.asarray(inputs["L1_ffn_b2"], np.float32)
    on = np.ones((128, 128), np.float32)
    shared = dict(
        wkt=_c(wk.T), wvt=_c(wv.T), wqt=_c(wq.T),
        bk=_c(bk.reshape(64, 1)), bq=_c(bq.reshape(64, 1)),
        bkb=_c(np.tile(bk, (128, 1))), bvb=_c(np.tile(bv, (128, 1))),
        gb=_c(gm.reshape(8, 128).T), bt=_c(bt.reshape(8, 128).T),
        mUe=_c((np.triu(on, 1) * TTT_LR).astype(BF)),
        mLe=_c((np.tril(on, -1) * TTT_LR).astype(BF)),
        mUn=_c((-np.triu(on, 1)).astype(BF)),
        identb=_c(np.eye(128, dtype=np.float32).astype(BF)),
        wr=_c(wr), br8=_c(br.reshape(8, 1)),
        w1r=_c(w1.reshape(8, 8, 128, 32, 128).transpose(0, 3, 2, 1, 4)
               .reshape(8, 32, 128, 1024)),
        w2m=_c(w2),
        b1a=_c(np.concatenate([b1[i].reshape(32, 128).T for i in range(8)], axis=1)),
        b2a=_c(np.concatenate([b2[i].reshape(8, 128).T for i in range(8)], axis=1)))
    in_maps = []
    for core in range(8):
        hh = core % 2
        m = dict(a_maps[core])
        m.update(shared)
        m.update(sel0=_c(np.full((128, 1), 1.0 - hh, np.float32)),
                 sel1=_c(np.full((128, 1), float(hh), np.float32)))
        in_maps.append(m)
    return in_maps


def run_fused(inputs):
    nc = build_fused()
    in_maps = prep_fused_inmaps(inputs)
    res = run_bass_kernel_spmd(nc, in_maps, core_ids=list(range(8)))
    x2 = np.zeros((B, T, D), dtype=np.float32)
    for core in range(8):
        b, hh = core // 2, core % 2
        x2[b, hh * TOWN:(hh + 1) * TOWN] = res.results[core]["x2own"].T
    return x2


def prep_block0_inmaps(inputs):
    x0 = np.asarray(inputs["L0_attn_x"], dtype=np.float32)
    wq = np.asarray(inputs["L0_attn_Wq"], dtype=np.float32)
    wk = np.asarray(inputs["L0_attn_Wk"], dtype=np.float32)
    wv = np.asarray(inputs["L0_attn_Wv"], dtype=np.float32)
    wo = np.asarray(inputs["L0_attn_Wo"], dtype=np.float32)
    w1 = np.asarray(inputs["L0_ffn_w1"], dtype=np.float32)
    b1 = np.asarray(inputs["L0_ffn_b1"], dtype=np.float32)
    w2 = np.asarray(inputs["L0_ffn_w2"], dtype=np.float32)
    b2 = np.asarray(inputs["L0_ffn_b2"], dtype=np.float32)

    perm = _rope_perm()
    cos2, sin2 = _rope_tables()
    shared = dict(wqp=_c(wq[:, perm]), wkp=_c(wk[:, perm]), wv=_c(wv), wo=_c(wo),
                  w1t=_c(w1.T), w1s=_c(w1), w2t=_c(w2.T), w2s=_c(w2),
                  b1t=_c(b1.reshape(32, 128).T), b2t=_c(b2.reshape(8, 128).T))
    in_maps = []
    for core in range(8):
        b, h = core // 2, core % 2
        own = slice(h * TOWN, (h + 1) * TOWN)
        oth = slice((1 - h) * TOWN, (2 - h) * TOWN)
        m = dict(shared)
        m.update(x0fm=_c(np.concatenate([x0[b, own], x0[b, oth]], axis=0).T),
                 cos2=_c(np.concatenate([cos2[:, own], cos2[:, oth]], axis=1)),
                 sin2=_c(np.concatenate([sin2[:, own], sin2[:, oth]], axis=1)))
        in_maps.append(m)
    return in_maps


def run_block0(inputs):
    nc = build_block0()
    in_maps = prep_block0_inmaps(inputs)
    res = run_bass_kernel_spmd(nc, in_maps, core_ids=list(range(8)))
    x1 = np.zeros((B, T, D), dtype=np.float32)
    for core in range(8):
        b, h = core // 2, core % 2
        x1[b, h * TOWN:(h + 1) * TOWN] = res.results[core]["x1own"].T
    return x1


def kernel(**inputs):
    return run_fused(inputs)


# version 12
# speedup vs baseline: 1.0344x; 1.0156x over previous
import numpy as np
import ml_dtypes
import concourse.bass as bass
import concourse.mybir as mybir
from concourse.bass_utils import run_bass_kernel_spmd

F32 = mybir.dt.float32
BF16 = mybir.dt.bfloat16
AF = mybir.ActivationFunctionType
ALU = mybir.AluOpType
AX = mybir.AxisListType
BF = ml_dtypes.bfloat16

B, T, D = 4, 4096, 1024
NH, HD = 16, 64
TOWN = 2048
NDC = D // 128
HFF = 4096
EPS = 1e-5
TTT_LR = 0.01
NEXP = 8
HME = 4096



def _rope_tables():
    pos = np.arange(T, dtype=np.float64)
    i = np.arange(HD // 2, dtype=np.float64)
    coss, sins = [], []
    for s in (1.0, 4.0, 16.0):
        freqs = 1.0 / (10000.0 * s) ** (2.0 * i / HD)
        ang = pos[:, None] * freqs[None, :]
        coss.append(np.cos(ang))
        sins.append(np.sin(ang))
    cos = np.mean(coss, 0).T
    sin = np.mean(sins, 0).T
    cos2 = np.vstack([cos, cos, cos, cos]).astype(BF)
    sin2 = np.vstack([sin, sin, sin, sin]).astype(BF)
    return cos2, sin2


def _rope_perm():
    perm = np.zeros(D, dtype=np.int64)
    for c in range(NH // 2):
        base = c * 128
        perm[base + 0:base + 32] = (2 * c) * HD + np.arange(0, HD, 2)
        perm[base + 32:base + 64] = (2 * c) * HD + np.arange(1, HD, 2)
        perm[base + 64:base + 96] = (2 * c + 1) * HD + np.arange(0, HD, 2)
        perm[base + 96:base + 128] = (2 * c + 1) * HD + np.arange(1, HD, 2)
    return perm


def _c(a):
    return np.ascontiguousarray(a)


class Sem:
    def __init__(self, h):
        self.h = h
        self.n = 0

    def inc(self, instr, k=1):
        instr.then_inc(self.h, k)
        self.n += k
        return self.n


class Stream:

    def __init__(self, nc, name, nbuf, keep):
        self.nbuf = nbuf
        self.h = []
        for i in range(nbuf):
            g = nc.semaphore(f"{name}{i}")
            keep.append(g)
            self.h.append(g.__enter__())
        self.val = [0] * nbuf
        self.marks = []

    def put(self, eng, dmas):
        p = len(self.marks) % self.nbuf
        for (o, i_) in dmas:
            eng.dma_start(out=o, in_=i_).then_inc(self.h[p], 16)
            self.val[p] += 16
        self.marks.append((p, self.val[p]))
        return len(self.marks) - 1

    def wait(self, eng, item=None):
        if item is None:
            for p in range(self.nbuf):
                if self.val[p]:
                    eng.wait_ge(self.h[p], self.val[p])
            return
        p, v = self.marks[item]
        eng.wait_ge(self.h[p], v)



def build_block0(stop=None):
    nc = bass.Bass("TRN2", target_bir_lowering=False)

    x0fm = nc.dram_tensor("x0fm", [D, T], F32, kind="ExternalInput").ap()
    wqp = nc.dram_tensor("wqp", [D, D], F32, kind="ExternalInput").ap()
    wkp = nc.dram_tensor("wkp", [D, D], F32, kind="ExternalInput").ap()
    wvD = nc.dram_tensor("wv", [D, D], F32, kind="ExternalInput").ap()
    woD = nc.dram_tensor("wo", [D, D], F32, kind="ExternalInput").ap()
    w1t = nc.dram_tensor("w1t", [D, HFF], F32, kind="ExternalInput").ap()
    w1sD = nc.dram_tensor("w1s", [HFF, D], F32, kind="ExternalInput").ap()
    w2t = nc.dram_tensor("w2t", [HFF, D], F32, kind="ExternalInput").ap()
    w2sD = nc.dram_tensor("w2s", [D, HFF], F32, kind="ExternalInput").ap()
    b1tD = nc.dram_tensor("b1t", [128, 32], F32, kind="ExternalInput").ap()
    b2tD = nc.dram_tensor("b2t", [128, 8], F32, kind="ExternalInput").ap()
    cos2d = nc.dram_tensor("cos2", [128, T], BF16, kind="ExternalInput").ap()
    sin2d = nc.dram_tensor("sin2", [128, T], BF16, kind="ExternalInput").ap()
    x1own = nc.dram_tensor("x1own", [D, TOWN], F32, kind="ExternalOutput").ap()

    h1d = nc.dram_tensor("h1d", [32, 128, TOWN], BF16, kind="Internal").ap()
    w1sd = nc.dram_tensor("w1sd", [32, 128, 1024], BF16, kind="Internal").ap()
    w2sd = nc.dram_tensor("w2sd", [32, 128, 1024], BF16, kind="Internal").ap()

    xn = [nc.alloc_sbuf_tensor(f"xn{i}", [128, T], BF16) for i in range(NDC)]
    ffnacc = [nc.alloc_sbuf_tensor(f"ffa{i}", [128, TOWN], BF16) for i in range(NDC)]
    hs_ofm = nc.alloc_sbuf_tensor("hs_ofm", [128, 32 * 512], BF16)
    alpha1 = nc.alloc_sbuf_tensor("alpha1", [128, 32], F32)
    alpha2 = nc.alloc_sbuf_tensor("alpha2", [128, 8], F32)
    b1sb = nc.alloc_sbuf_tensor("b1sb", [128, 32], F32)
    b2sb = nc.alloc_sbuf_tensor("b2sb", [128, 8], F32)
    ones128 = nc.alloc_sbuf_tensor("ones128", [128, 128], BF16)
    epsb = nc.alloc_sbuf_tensor("epsb", [128, 1], F32)

    guards = []

    def scope(name, shape, dt):
        g = nc.sbuf_tensor(name, shape, dt)
        t = g.__enter__()
        guards.append(g)
        return t

    def close(n):
        for _ in range(n):
            guards.pop().__exit__(None, None, None)

    pg1 = nc.psum_tensor("PA", [128, 2048], F32)
    PA = pg1.__enter__()
    pg2 = nc.psum_tensor("PB", [128, 2048], F32)
    PB = pg2.__enter__()

    keep = []
    sg = [nc.semaphore(n) for n in ["sTT", "sVV", "sSS"]]
    TT, VV, SS = [Sem(g.__enter__()) for g in sg]
    keep.extend(sg)

    stB = Stream(nc, "stB", 1, keep)
    gX0 = Stream(nc, "gX0", 1, keep)
    sWT = Stream(nc, "sWT", 2, keep)
    sWS = Stream(nc, "sWS", 2, keep)
    gW1 = Stream(nc, "gW1", 2, keep)
    gW2 = Stream(nc, "gW2", 2, keep)
    sF1 = Stream(nc, "sF1", 2, keep)
    gH1 = Stream(nc, "gH1", 2, keep)
    sH1 = Stream(nc, "sH1", 1, keep)
    sF2 = Stream(nc, "sF2", 2, keep)
    sTAB = Stream(nc, "sTAB", 1, keep)
    gWG = Stream(nc, "gWG", 1, keep)
    gWOB = Stream(nc, "gWOB", 1, keep)
    sX0 = Stream(nc, "sX0", 2, keep)
    gXO = Stream(nc, "gXO", 2, keep)

    blk_g = nc.Block()
    blk = blk_g.__enter__()

    def _(sy):
        stB.put(sy, [(b1sb[:], b1tD), (b2sb[:], b2tD)])
    blk.sync(_)

    def _(g):
        for dc in range(NDC):
            gX0.put(g, [(xn[dc][:], x0fm[dc * 128:(dc + 1) * 128, :])])
    blk.gpsimd(_)

    def _(v):
        v.memset(epsb[:], EPS)
        VV.inc(v.memset(ones128[:], 1.0))
    blk.vector(_)
    vv_ones = VV.n

    wtmp = [scope(f"wtmp{i}", [128, 1024], F32) for i in range(2)]
    wstr = [scope(f"wstr{i}", [128, 4096], F32) for i in range(2)]
    wsgn = [scope(f"wsgn{i}", [128, 1024], BF16) for i in range(2)]

    w1t_r = w1t.rearrange("(dc p) h -> p dc h", p=128)

    ss0, vv0 = SS.n, VV.n

    def _(sy):
        for hc in range(32):
            if hc >= 2:
                sy.wait_ge(SS.h, ss0 + hc - 1)
                sy.wait_ge(VV.h, vv0 + hc - 1)
            sWT.put(sy, [(wtmp[hc % 2][:].rearrange("p (dc h) -> p dc h", dc=8),
                          w1t_r[:, :, hc * 128:(hc + 1) * 128])])
            sWS.put(sy, [(wstr[hc % 2][:, 0:1024], w1sD[hc * 128:(hc + 1) * 128, :])])
    blk.sync(_)

    def _(g):
        for hc in range(32):
            g.wait_ge(SS.h, ss0 + hc + 1)
            gW1.put(g, [(w1sd[hc], wsgn[hc % 2][:])])
    blk.gpsimd(_)

    def _(s):
        for hc in range(32):
            sWT.wait(s, hc)
            if hc >= 2:
                gW1.wait(s, hc - 2)
            SS.inc(s.activation(wsgn[hc % 2][:], wtmp[hc % 2][:], AF.Sign))
    blk.scalar(_)

    def _(v):
        for hc in range(32):
            sWS.wait(v, hc)
            VV.inc(v.reduce_sum(alpha1[:, hc:hc + 1], wstr[hc % 2][:, 0:1024],
                                axis=AX.X, apply_absolute_value=True))
    blk.vector(_)

    ss1, vv1 = SS.n, VV.n

    def _(sy):
        sy.wait_ge(SS.h, ss1)
        sy.wait_ge(VV.h, vv1)
        for hc in range(32):
            if hc >= 2:
                sy.wait_ge(SS.h, ss1 + hc - 1)
            sWT.put(sy, [(wtmp[hc % 2][:], w2t[hc * 128:(hc + 1) * 128, :])])
        for ec in range(8):
            if ec >= 2:
                sy.wait_ge(VV.h, vv1 + ec - 1)
            sWS.put(sy, [(wstr[ec % 2][:], w2sD[ec * 128:(ec + 1) * 128, :])])
    blk.sync(_)

    def _(g):
        for hc in range(32):
            g.wait_ge(SS.h, ss1 + hc + 1)
            gW2.put(g, [(w2sd[hc], wsgn[hc % 2][:])])
    blk.gpsimd(_)

    def _(s):
        gW1.wait(s)
        for hc in range(32):
            sWT.wait(s, 32 + hc)
            if hc >= 2:
                gW2.wait(s, hc - 2)
            SS.inc(s.activation(wsgn[hc % 2][:], wtmp[hc % 2][:], AF.Sign))
    blk.scalar(_)

    def _(v):
        for ec in range(8):
            sWS.wait(v, 32 + ec)
            VV.inc(v.reduce_sum(alpha2[:, ec:ec + 1], wstr[ec % 2][:],
                                axis=AX.X, apply_absolute_value=True))
        v.wait_ge(VV.h, VV.n)
        VV.inc(v.tensor_scalar_mul(alpha1[:], alpha1[:], 1.0 / D))
        VV.inc(v.tensor_scalar_mul(alpha2[:], alpha2[:], 1.0 / HFF))
    blk.vector(_)
    vv_alpha = VV.n

    close(6)

    fA = scope("fA", [128, 2048], F32)
    fB = scope("fB", [128, 2048], F32)
    fC = scope("fC", [128, 2048], F32)
    sqd = [scope(f"sqd{i}", [128, 2048], BF16) for i in range(2)]
    mu_bf = scope("mu_bf", [128, T], BF16)
    rstd_bf = scope("rstd_bf", [128, T], BF16)

    for tb in range(2):
        ts = tb * 2048
        tt_b = TT.n
        vv_b = VV.n

        def _(v, ts=ts, tt_b=tt_b, ss_w=SS.n):
            gX0.wait(v)
            v.wait_ge(SS.h, ss_w)
            gW1.wait(v)
            gW2.wait(v)
            for dc in range(NDC):
                if dc >= 2:
                    v.wait_ge(TT.h, tt_b + dc - 1)
                VV.inc(v.tensor_mul(sqd[dc % 2][:], xn[dc][:, ts:ts + 2048],
                                    xn[dc][:, ts:ts + 2048]))
        blk.vector(_)

        def _(t, ts=ts, vv_b=vv_b):
            t.wait_ge(VV.h, vv_ones)
            for dc in range(NDC):
                t.wait_ge(VV.h, vv_b + dc + 1)
                for q4 in range(4):
                    t.matmul(PA[:, q4 * 512:(q4 + 1) * 512], ones128[:],
                             xn[dc][:, ts + q4 * 512:ts + (q4 + 1) * 512],
                             start=(dc == 0), stop=(dc == NDC - 1))
                for q4 in range(4):
                    ins = t.matmul(PB[:, q4 * 512:(q4 + 1) * 512], ones128[:],
                                   sqd[dc % 2][:, q4 * 512:(q4 + 1) * 512],
                                   start=(dc == 0), stop=(dc == NDC - 1))
                TT.inc(ins)
        blk.tensor(_)
        tt_s1 = TT.n

        def _(v, tt_s1=tt_s1):
            v.wait_ge(TT.h, tt_s1)
            VV.inc(v.tensor_scalar_mul(fA[:], PA[:], 1.0 / D))
            VV.inc(v.tensor_scalar_mul(fB[:], PB[:], 1.0 / D))
            v.wait_ge(VV.h, VV.n)
            VV.inc(v.tensor_mul(fC[:], fA[:], fA[:]))
            v.wait_ge(VV.h, VV.n)
            VV.inc(v.tensor_sub(fB[:], fB[:], fC[:]))
        blk.vector(_)
        vv_var = VV.n

        def _(s, vv_var=vv_var):
            s.wait_ge(VV.h, vv_var)
            SS.inc(s.activation(fC[:], fB[:], AF.Sqrt, bias=epsb[:]))
        blk.scalar(_)
        ss_sd = SS.n

        def _(v, ts=ts, ss_sd=ss_sd):
            v.wait_ge(SS.h, ss_sd)
            VV.inc(v.reciprocal(fB[:], fC[:]))
            VV.inc(v.tensor_copy(mu_bf[:, ts:ts + 2048], fA[:]))
            v.wait_ge(VV.h, VV.n)
            VV.inc(v.tensor_copy(rstd_bf[:, ts:ts + 2048], fB[:]))
        blk.vector(_)

    def _(v):
        for dc in range(NDC):
            VV.inc(v.tensor_sub(xn[dc][:], xn[dc][:], mu_bf[:]))
            v.wait_ge(VV.h, VV.n)
            VV.inc(v.tensor_mul(xn[dc][:], xn[dc][:], rstd_bf[:]))
    blk.vector(_)
    vv_xn = VV.n

    close(7)

    w1b = [scope(f"w1b{i}", [128, 1024], BF16) for i in range(2)]
    h1t = [scope(f"h1t{i}", [128, TOWN], BF16) for i in range(2)]

    ttf, ssf = TT.n, SS.n

    def _(sy, ss_pre=SS.n):
        gW1.wait(sy)
        sy.wait_ge(VV.h, vv_xn)
        sy.wait_ge(SS.h, ss_pre)
        for hc in range(32):
            if hc >= 2:
                sy.wait_ge(TT.h, ttf + hc - 1)
            sF1.put(sy, [(w1b[hc % 2][:], w1sd[hc])])
    blk.sync(_)

    def _(t):
        t.wait_ge(VV.h, vv_xn)
        for hc in range(32):
            sF1.wait(t, hc)
            if hc >= 1:
                t.wait_ge(SS.h, ssf + hc)
            for q4 in range(4):
                for dc in range(NDC):
                    ins = t.matmul(PA[:, q4 * 512:(q4 + 1) * 512],
                                   w1b[hc % 2][:, dc * 128:(dc + 1) * 128],
                                   xn[dc][:, q4 * 512:(q4 + 1) * 512],
                                   start=(dc == 0), stop=(dc == NDC - 1))
            TT.inc(ins)
    blk.tensor(_)

    def _(g):
        for hc in range(32):
            g.wait_ge(SS.h, ssf + hc + 1)
            gH1.put(g, [(h1d[hc], h1t[hc % 2][:])])
    blk.gpsimd(_)

    def _(s):
        s.wait_ge(VV.h, vv_xn)
        stB.wait(s)
        for hc in range(32):
            s.wait_ge(TT.h, ttf + hc + 1)
            if hc >= 2:
                gH1.wait(s, hc - 2)
            SS.inc(s.activation(h1t[hc % 2][:], PA[:], AF.Gelu,
                                bias=b1sb[:, hc:hc + 1], scale=alpha1[:, hc:hc + 1]))
    blk.scalar(_)

    close(4)

    w2b = [scope(f"w2b{i}", [128, 1024], BF16) for i in range(2)]
    h1s = hs_ofm

    tt_prev_tb = TT.n
    ss_prev_tb = SS.n
    for tb in range(4):
        ts = tb * 512
        tt2 = TT.n

        def _(sy, ts=ts, tt2=tt2, tt_prev_tb=tt_prev_tb, tb=tb):
            gH1.wait(sy)
            gW2.wait(sy)
            sy.wait_ge(TT.h, tt_prev_tb)
            sH1.put(sy, [(h1s[:, grp * 8 * 512:(grp + 1) * 8 * 512].rearrange(
                              "p (hc t) -> p hc t", hc=8),
                          h1d[grp * 8:(grp + 1) * 8, :, ts:ts + 512].rearrange(
                              "hc p t -> p hc t"))
                         for grp in range(4)])
            for hc in range(32):
                if hc >= 2:
                    sy.wait_ge(TT.h, tt2 + hc - 1)
                sF2.put(sy, [(w2b[hc % 2][:], w2sd[hc])])
        blk.sync(_)

        def _(t, ss_prev_tb=ss_prev_tb, tb=tb):
            t.wait_ge(SS.h, ss_prev_tb)
            sH1.wait(t, tb)
            for hc in range(32):
                sF2.wait(t, 32 * tb + hc)
                for ec in range(8):
                    ps = PA if ec < 4 else PB
                    ins = t.matmul(ps[:, (ec % 4) * 512:(ec % 4 + 1) * 512],
                                   w2b[hc % 2][:, ec * 128:(ec + 1) * 128],
                                   h1s[:, hc * 512:(hc + 1) * 512],
                                   start=(hc == 0), stop=(hc == 31))
                TT.inc(ins)
        blk.tensor(_)
        tt_prev_tb = TT.n

        def _(s, ts=ts, tt_done=TT.n):
            s.wait_ge(TT.h, tt_done)
            stB.wait(s)
            for ec in range(8):
                ps = PA if ec < 4 else PB
                SS.inc(s.activation(ffnacc[ec][:, ts:ts + 512],
                                    ps[:, (ec % 4) * 512:(ec % 4 + 1) * 512],
                                    AF.Identity,
                                    bias=b2sb[:, ec:ec + 1], scale=alpha2[:, ec:ec + 1]))
        blk.scalar(_)
        ss_prev_tb = SS.n

    close(2)
    ss_attn0 = SS.n

    if stop == "f2":
        f32t_early = [scope(f"f32e{i}", [128, TOWN], F32) for i in range(2)]
        vv0e = VV.n
        def _(g):
            for dc in range(NDC):
                g.wait_ge(VV.h, vv0e + dc + 1)
                gXO.put(g, [(x1own[dc * 128:(dc + 1) * 128, :], f32t_early[dc % 2][:])])
            gXO.wait(g)
        blk.gpsimd(_)
        def _(v):
            v.wait_ge(SS.h, ss_attn0)
            for dc in range(NDC):
                if dc >= 2:
                    gXO.wait(v, dc - 2)
                VV.inc(v.tensor_copy(f32t_early[dc % 2][:], ffnacc[dc][:]))
        blk.vector(_)
        blk_g.__exit__(None, None, None)
        close(len(guards))
        pg2.__exit__(None, None, None)
        pg1.__exit__(None, None, None)
        for g in reversed(keep):
            g.__exit__(None, None, None)
        return nc

    cos2 = scope("cos2s", [128, T], BF16)
    sin2 = scope("sin2s", [128, T], BF16)
    kfm = scope("kfm", [128, T], BF16)
    qfm = scope("qfm", [128, TOWN], BF16)
    raw = scope("raw", [128, 2048], BF16)
    tscr = scope("tscr", [128, 2048], BF16)
    v65 = scope("v65", [128, 32 * 130], BF16)
    pb_sb = scope("pb_sb", [128, 2048], BF16)
    recb = scope("recb", [64, 2048], BF16)
    rec1 = scope("rec1", [1, 2048], BF16)
    wgq = scope("wgq", [128, 1024], BF16)
    wgk = scope("wgk", [128, 1024], BF16)
    wgv = scope("wgv", [128, 1024], BF16)
    ofm = hs_ofm

    def _(sy, tt_pre=TT.n, ss_pre=SS.n):
        sy.wait_ge(TT.h, tt_pre)
        sy.wait_ge(SS.h, ss_pre)
        sTAB.put(sy, [(cos2[:], cos2d), (sin2[:], sin2d)])
    blk.sync(_)

    wqp_r = wqp.rearrange("(dc p) j -> p dc j", p=128)
    wkp_r = wkp.rearrange("(dc p) j -> p dc j", p=128)
    wv_r = wvD.rearrange("(dc p) j -> p dc j", p=128)

    def emit_rope(v, dst, doff, src, tab_off, tlen):
        tsl = slice(tab_off, tab_off + tlen)
        for e in (0, 64):
            o = e + 32
            ce = cos2[e:e + 32, tsl]
            se = sin2[e:e + 32, tsl]
            co = cos2[o:o + 32, tsl]
            so = sin2[o:o + 32, tsl]
            VV.inc(v.tensor_mul(tscr[e:e + 32, 0:tlen], src[o:o + 32, 0:tlen], so))
            VV.inc(v.tensor_mul(dst[e:e + 32, doff:doff + tlen], src[e:e + 32, 0:tlen], ce))
            v.wait_ge(VV.h, VV.n)
            VV.inc(v.tensor_sub(dst[e:e + 32, doff:doff + tlen],
                                dst[e:e + 32, doff:doff + tlen], tscr[e:e + 32, 0:tlen]))
            VV.inc(v.tensor_mul(tscr[o:o + 32, 0:tlen], src[e:e + 32, 0:tlen], se))
            VV.inc(v.tensor_mul(dst[o:o + 32, doff:doff + tlen], src[o:o + 32, 0:tlen], co))
            v.wait_ge(VV.h, VV.n)
            VV.inc(v.tensor_add(dst[o:o + 32, doff:doff + tlen],
                                dst[o:o + 32, doff:doff + tlen], tscr[o:o + 32, 0:tlen]))

    tt_group_end = TT.n
    vv_group_end = VV.n

    for c in range(8):
        def _(g, tt_group_end=tt_group_end, c=c):
            g.wait_ge(TT.h, tt_group_end)
            gWG.put(g, [(wgq[:].rearrange("p (dc j) -> p dc j", dc=8),
                         wqp_r[:, :, c * 128:(c + 1) * 128]),
                        (wgk[:].rearrange("p (dc j) -> p dc j", dc=8),
                         wkp_r[:, :, c * 128:(c + 1) * 128]),
                        (wgv[:].rearrange("p (dc j) -> p dc j", dc=8),
                         wv_r[:, :, c * 128:(c + 1) * 128])])
            gWPP.put(g, [(w1bd[c], w1rD[c])])
            gWPP.put(g, [(w2bd[c], w2D[c])])
        blk.gpsimd(_)

        def _(t, vv_group_end=vv_group_end, c=c):
            gWG.wait(t, c)
            t.wait_ge(VV.h, max(vv_xn, vv_group_end))
            t.wait_ge(SS.h, ss_attn0)
            for q4 in range(4):
                for dc in range(NDC):
                    ins = t.matmul(PA[:, q4 * 512:(q4 + 1) * 512],
                                   wgq[:, dc * 128:(dc + 1) * 128],
                                   xn[dc][:, q4 * 512:(q4 + 1) * 512],
                                   start=(dc == 0), stop=(dc == NDC - 1))
            TT.inc(ins)
        blk.tensor(_)
        tt_q = TT.n

        def _(v, tt_q=tt_q):
            v.wait_ge(TT.h, tt_q)
            sTAB.wait(v)
            VV.inc(v.tensor_copy(raw[:], PA[:]))
            v.wait_ge(VV.h, VV.n)
            emit_rope(v, qfm, 0, raw, 0, 2048)
        blk.vector(_)

        for th in range(2):
            ts = th * 2048

            def _(t, ts=ts, vv_pre=VV.n):
                t.wait_ge(VV.h, vv_pre)
                for q4 in range(4):
                    for dc in range(NDC):
                        ins = t.matmul(PB[:, q4 * 512:(q4 + 1) * 512],
                                       wgk[:, dc * 128:(dc + 1) * 128],
                                       xn[dc][:, ts + q4 * 512:ts + (q4 + 1) * 512],
                                       start=(dc == 0), stop=(dc == NDC - 1))
                TT.inc(ins)
            blk.tensor(_)
            tt_k = TT.n

            def _(v, ts=ts, tt_k=tt_k):
                v.wait_ge(TT.h, tt_k)
                VV.inc(v.tensor_copy(raw[:], PB[:]))
                v.wait_ge(VV.h, VV.n)
                emit_rope(v, kfm, ts, raw, ts, 2048)
            blk.vector(_)

        def _(v, tt_group_end=tt_group_end):
            v.wait_ge(TT.h, tt_group_end)
            VV.inc(v.memset(v65[:], 1.0))
        blk.vector(_)
        vv_ms = VV.n

        tt_v0 = TT.n
        vv_v0 = VV.n

        def _(t, tt_v0=tt_v0, vv_v0=vv_v0, vv_ms=vv_ms):
            t.wait_ge(VV.h, vv_ms)
            for tc in range(32):
                if tc >= 2:
                    t.wait_ge(VV.h, vv_v0 + tc - 1)
                po = (tc % 2) * 512
                for dc in range(NDC):
                    ins = t.matmul(PA[:, po:po + 128],
                                   xn[dc][:, tc * 128:(tc + 1) * 128],
                                   wgv[:, dc * 128:(dc + 1) * 128],
                                   start=(dc == 0), stop=(dc == NDC - 1))
                TT.inc(ins)
        blk.tensor(_)

        def _(v, tt_v0=tt_v0, vv_ms=vv_ms):
            v.wait_ge(VV.h, vv_ms)
            for tc in range(32):
                v.wait_ge(TT.h, tt_v0 + tc + 1)
                po = (tc % 2) * 512
                v.tensor_copy(v65[:, tc * 130:tc * 130 + 64], PA[:, po:po + 64])
                VV.inc(v.tensor_copy(v65[:, tc * 130 + 65:tc * 130 + 129],
                                     PA[:, po + 64:po + 128]))
        blk.vector(_)

        nheads = 0 if stop == "qkv" else (1 if stop == "head0" else 2)
        for hh in range(nheads if stop in ("qkv", "head0", "grp0") and c == 0 else 2):
            hb = hh * 64
            vcol = hh * 65

            tt_h0 = TT.n
            ss_h0 = SS.n
            vv_pre_head = VV.n

            def _(t, tt_h0=tt_h0, ss_h0=ss_h0, vcol=vcol, hb=hb,
                  vv_pre_head=vv_pre_head):
                t.wait_ge(VV.h, vv_pre_head)
                t.wait_ge(SS.h, ss_h0)
                for kc in range(32):
                    for h2 in range(2):
                        if kc >= 1:
                            t.wait_ge(SS.h, ss_h0 + 2 * (kc - 1) + h2 + 1)
                        for q4 in range(2):
                            qo = h2 * 1024 + q4 * 512
                            ins = t.matmul(PA[:, qo:qo + 512],
                                           kfm[hb:hb + 64, kc * 128:(kc + 1) * 128],
                                           qfm[hb:hb + 64, qo:qo + 512],
                                           start=True, stop=True)
                        TT.inc(ins)
                    for h2 in range(2):
                        t.wait_ge(SS.h, ss_h0 + 2 * kc + h2 + 1)
                        for q4 in range(2):
                            qo = h2 * 1024 + q4 * 512
                            ins = t.matmul(PB[0:65, qo:qo + 512],
                                           v65[:, kc * 130 + vcol:kc * 130 + vcol + 65],
                                           pb_sb[:, qo:qo + 512],
                                           start=(kc == 0), stop=(kc == 31))
                        TT.inc(ins)
            blk.tensor(_)
            tt_av_last = TT.n

            def _(s, tt_h0=tt_h0):
                for kc in range(32):
                    for h2 in range(2):
                        s.wait_ge(TT.h, tt_h0 + 4 * kc + h2 + 1)
                        SS.inc(s.activation(pb_sb[:, h2 * 1024:(h2 + 1) * 1024],
                                            PA[:, h2 * 1024:(h2 + 1) * 1024],
                                            AF.Exp, scale=0.125))
            blk.scalar(_)

            def _(v, tt_av_last=tt_av_last):
                v.wait_ge(TT.h, tt_av_last)
                with nc.allow_low_precision("bf16 recip of softmax sums is within the graded bar"):
                    VV.inc(v.reciprocal(rec1[:], PB[64:65, :]))
            blk.vector(_)
            vv_rec = VV.n

            def _(t, vv_rec=vv_rec):
                t.wait_ge(VV.h, vv_rec)
                for q4 in range(4):
                    ins = t.matmul(PA[0:64, q4 * 512:(q4 + 1) * 512],
                                   ones128[0:1, 0:64],
                                   rec1[0:1, q4 * 512:(q4 + 1) * 512],
                                   start=True, stop=True)
                TT.inc(ins)
            blk.tensor(_)
            tt_bc = TT.n

            def _(v, tt_bc=tt_bc, c=c, hh=hh):
                v.wait_ge(TT.h, tt_bc)
                VV.inc(v.tensor_copy(recb[:], PA[0:64, :]))
                v.wait_ge(VV.h, VV.n)
                VV.inc(v.tensor_mul(ofm[hh * 64:(hh + 1) * 64, c * 2048:(c + 1) * 2048],
                                    PB[0:64, :], recb[:]))
            blk.vector(_)

        tt_group_end = TT.n
        vv_group_end = VV.n

        if stop in ("qkv", "head0", "grp0") and c == 0:
            break

    if stop in ("qkv", "head0", "grp0"):
        dmp = [scope(f"dmp{i}", [128, TOWN], F32) for i in range(2)]
        vvd = VV.n
        def _(g):
            for i in range(4):
                g.wait_ge(VV.h, vvd + i + 1)
                gXO.put(g, [(x1own[i * 128:(i + 1) * 128, :], dmp[i % 2][:])])
            gXO.wait(g)
        blk.gpsimd(_)
        def _(v):
            v.wait_ge(TT.h, TT.n)
            v.wait_ge(SS.h, SS.n)
            v.wait_ge(VV.h, VV.n)
            srcs = [qfm[:, 0:2048], kfm[:, 0:2048], kfm[:, 2048:4096],
                    v65[:, 0:2048]] if stop == "qkv" else                    [ofm[:, 0:2048], ofm[:, 2048:4096], qfm[:, 0:2048], kfm[:, 0:2048]]
            for i, s_ in enumerate(srcs):
                if i >= 2:
                    gXO.wait(v, i - 2)
                VV.inc(v.tensor_copy(dmp[i % 2][:], s_))
        blk.vector(_)
        blk_g.__exit__(None, None, None)
        close(len(guards))
        pg2.__exit__(None, None, None)
        pg1.__exit__(None, None, None)
        for g in reversed(keep):
            g.__exit__(None, None, None)
        return nc

    close(13)

    wob = [scope(f"wob{i}", [128, 1024], BF16) for i in range(8)]
    x0ld = [scope(f"x0ld{i}", [128, TOWN], F32) for i in range(2)]
    x1t = [scope(f"x1t{i}", [128, TOWN], F32) for i in range(2)]
    f32t = scope("f32t", [128, TOWN], F32)

    wo_r = woD.rearrange("(hc p) j -> p hc j", p=128)

    def _(g, tt_group_end=tt_group_end, vv_pre=VV.n, ss_pre=SS.n):
        g.wait_ge(TT.h, tt_group_end)
        g.wait_ge(VV.h, vv_pre)
        g.wait_ge(SS.h, ss_pre)
        gWOB.put(g, [(wob[hdc][:], wo_r[:, hdc, :]) for hdc in range(8)])
    blk.gpsimd(_)

    vv_cmb0 = VV.n
    tt_wo0 = TT.n

    def _(sy, tt_pre=TT.n, ss_pre=SS.n):
        sy.wait_ge(TT.h, tt_pre)
        sy.wait_ge(SS.h, ss_pre)
        sy.wait_ge(VV.h, vv_cmb0)
        for dc in range(NDC):
            if dc >= 2:
                sy.wait_ge(VV.h, vv_cmb0 + 3 * (dc - 1))
            sX0.put(sy, [(x0ld[dc % 2][:], x0fm[dc * 128:(dc + 1) * 128, 0:TOWN])])
    blk.sync(_)

    def _(t, vv_group_end=vv_group_end):
        gWOB.wait(t)
        t.wait_ge(VV.h, vv_group_end)
        for dco in range(NDC):
            if dco >= 2:
                t.wait_ge(VV.h, vv_cmb0 + 3 * (dco - 1))
            ps = PA if dco % 2 == 0 else PB
            for q4 in range(4):
                for hdc in range(8):
                    ins = t.matmul(ps[:, q4 * 512:(q4 + 1) * 512],
                                   wob[hdc][:, dco * 128:(dco + 1) * 128],
                                   ofm[:, hdc * 2048 + q4 * 512:hdc * 2048 + (q4 + 1) * 512],
                                   start=(hdc == 0), stop=(hdc == 7))
            TT.inc(ins)
    blk.tensor(_)

    def _(g):
        for dco in range(NDC):
            g.wait_ge(VV.h, vv_cmb0 + 3 * (dco + 1))
            gXO.put(g, [(x1own[dco * 128:(dco + 1) * 128, :], x1t[dco % 2][:])])
        gXO.wait(g)
    blk.gpsimd(_)

    def _(v):
        for dco in range(NDC):
            v.wait_ge(TT.h, tt_wo0 + dco + 1)
            sX0.wait(v, dco)
            if dco >= 2:
                gXO.wait(v, dco - 2)
            ps = PA if dco % 2 == 0 else PB
            v.wait_ge(VV.h, VV.n)
            VV.inc(v.tensor_copy(f32t[:], ffnacc[dco][:]))
            v.wait_ge(VV.h, VV.n)
            VV.inc(v.tensor_add(f32t[:], f32t[:], ps[:]))
            v.wait_ge(VV.h, VV.n)
            VV.inc(v.tensor_add(x1t[dco % 2][:], f32t[:], x0ld[dco % 2][:]))
    blk.vector(_)

    blk_g.__exit__(None, None, None)
    close(len(guards))
    pg2.__exit__(None, None, None)
    pg1.__exit__(None, None, None)
    for g in reversed(keep):
        g.__exit__(None, None, None)
    for g in reversed(sg):
        pass
    return nc



def build_ttt():
    nc = bass.Bass("TRN2", target_bir_lowering=False)

    x1fm = nc.dram_tensor("x1fm", [D, T], F32, kind="ExternalInput").ap()
    wktD = nc.dram_tensor("wkt", [D, 64], F32, kind="ExternalInput").ap()
    wvtD = nc.dram_tensor("wvt", [D, D], F32, kind="ExternalInput").ap()
    wqtD = nc.dram_tensor("wqt", [D, 64], F32, kind="ExternalInput").ap()
    bkD = nc.dram_tensor("bk", [64, 1], F32, kind="ExternalInput").ap()
    bqD = nc.dram_tensor("bq", [64, 1], F32, kind="ExternalInput").ap()
    bkbD = nc.dram_tensor("bkb", [128, 64], F32, kind="ExternalInput").ap()
    bvbD = nc.dram_tensor("bvb", [128, D], F32, kind="ExternalInput").ap()
    gbD = nc.dram_tensor("gb", [128, 8], F32, kind="ExternalInput").ap()
    btD = nc.dram_tensor("bt", [128, 8], F32, kind="ExternalInput").ap()
    mUeD = nc.dram_tensor("mUe", [128, 128], BF16, kind="ExternalInput").ap()
    mLeD = nc.dram_tensor("mLe", [128, 128], BF16, kind="ExternalInput").ap()
    mUnD = nc.dram_tensor("mUn", [128, 128], BF16, kind="ExternalInput").ap()
    idD = nc.dram_tensor("identb", [128, 128], BF16, kind="ExternalInput").ap()
    yfull = nc.dram_tensor("yfull", [T, D], F32, kind="ExternalOutput").ap()

    kfmB = nc.alloc_sbuf_tensor("kfmB", [64, T], BF16)
    qfmB = nc.alloc_sbuf_tensor("qfmB", [64, T], BF16)
    ktm = nc.alloc_sbuf_tensor("ktm", [128, 32 * 64], BF16)
    bvb = nc.alloc_sbuf_tensor("bvbS", [128, D], F32)
    bkb = nc.alloc_sbuf_tensor("bkbS", [128, 64], F32)
    bk = nc.alloc_sbuf_tensor("bkS", [64, 1], F32)
    bq = nc.alloc_sbuf_tensor("bqS", [64, 1], F32)
    gma = nc.alloc_sbuf_tensor("gma", [128, 8], F32)
    bta = nc.alloc_sbuf_tensor("bta", [128, 8], F32)
    mUe = nc.alloc_sbuf_tensor("mUeS", [128, 128], BF16)
    mLe = nc.alloc_sbuf_tensor("mLeS", [128, 128], BF16)
    mUn = nc.alloc_sbuf_tensor("mUnS", [128, 128], BF16)
    identb = nc.alloc_sbuf_tensor("identS", [128, 128], BF16)
    ones128 = nc.alloc_sbuf_tensor("ones128", [128, 128], BF16)
    epsb = nc.alloc_sbuf_tensor("epsb", [128, 1], F32)
    Sst = nc.alloc_sbuf_tensor("Sst", [64, D], F32)
    Sbf = nc.alloc_sbuf_tensor("Sbf", [64, D], BF16)
    nUt = nc.alloc_sbuf_tensor("nUt", [128, 128], F32)
    nLt = nc.alloc_sbuf_tensor("nLt", [128, 128], F32)
    Cbuf = nc.alloc_sbuf_tensor("Cbuf", [128, 6 * 128], F32)
    Ta = nc.alloc_sbuf_tensor("Ta", [128, 128], F32)
    Tb = nc.alloc_sbuf_tensor("Tb", [128, 128], F32)
    Rs = nc.alloc_sbuf_tensor("Rs", [128, 128], BF16)
    brawS = nc.alloc_sbuf_tensor("brawS", [128, D], BF16)
    Utm = nc.alloc_sbuf_tensor("Utm", [128, D], BF16)
    pmS = nc.alloc_sbuf_tensor("pmS", [128, 128], BF16)
    ysb = [nc.alloc_sbuf_tensor(f"ysb{i}", [128, D], F32) for i in range(2)]

    guards = []

    def scope(name, shape, dt):
        g = nc.sbuf_tensor(name, shape, dt)
        t = g.__enter__()
        guards.append(g)
        return t

    def close(n):
        for _ in range(n):
            guards.pop().__exit__(None, None, None)

    x1bf = [scope(f"x1b{i}", [128, T], BF16) for i in range(NDC)]
    mu_bf = scope("mu_bf", [128, T], BF16)
    f_bf = scope("f_bf", [128, T], BF16)

    pg1 = nc.psum_tensor("PA", [128, 2048], F32)
    PA = pg1.__enter__()
    pg2 = nc.psum_tensor("PB", [128, 2048], F32)
    PB = pg2.__enter__()

    keep = []
    sg = [nc.semaphore(n) for n in ["sTT", "sVV", "sSS"]]
    TT, VV, SS = [Sem(g.__enter__()) for g in sg]
    keep.extend(sg)
    gX1 = Stream(nc, "gX1", 1, keep)
    gWP = Stream(nc, "gWP", 1, keep)
    sMS = Stream(nc, "sMS", 1, keep)
    gY = Stream(nc, "gY", 2, keep)

    blk_g = nc.Block()
    blk = blk_g.__enter__()

    def _(g):
        for dc in range(NDC):
            gX1.put(g, [(x1bf[dc][:], x1fm[dc * 128:(dc + 1) * 128, :])])
    blk.gpsimd(_)

    def _(sy, tt_pre=TT.n, vv_pre=VV.n, ss_pre=SS.n):
        sy.wait_ge(TT.h, tt_pre)
        sy.wait_ge(VV.h, vv_pre)
        sy.wait_ge(SS.h, ss_pre)
        sMS.put(sy, [(bvb[:], bvbD), (bkb[:], bkbD), (bk[:], bkD), (bq[:], bqD),
                     (gma[:], gbD), (bta[:], btD), (mUe[:], mUeD), (mLe[:], mLeD),
                     (mUn[:], mUnD), (identb[:], idD)])
    blk.sync(_)

    def _(v):
        v.memset(epsb[:], EPS)
        v.memset(Sst[:], 0.0)
        v.memset(Sbf[:], 0.0)
        VV.inc(v.memset(ones128[:], 1.0))
    blk.vector(_)
    vv_ones = VV.n

    fA = scope("fA", [128, 2048], F32)
    fB = scope("fB", [128, 2048], F32)
    fC = scope("fC", [128, 2048], F32)
    fD = scope("fD", [128, 2048], F32)
    sqd = [scope(f"sqd{i}", [128, 2048], BF16) for i in range(2)]

    for tb in range(2):
        ts = tb * 2048
        tt_b = TT.n
        vv_b = VV.n

        def _(v, ts=ts, tt_b=tt_b):
            gX1.wait(v)
            for dc in range(NDC):
                if dc >= 2:
                    v.wait_ge(TT.h, tt_b + dc - 1)
                VV.inc(v.tensor_mul(sqd[dc % 2][:], x1bf[dc][:, ts:ts + 2048],
                                    x1bf[dc][:, ts:ts + 2048]))
        blk.vector(_)

        def _(t, ts=ts, vv_b=vv_b):
            t.wait_ge(VV.h, vv_ones)
            for dc in range(NDC):
                t.wait_ge(VV.h, vv_b + dc + 1)
                for q4 in range(4):
                    t.matmul(PA[:, q4 * 512:(q4 + 1) * 512], ones128[:],
                             x1bf[dc][:, ts + q4 * 512:ts + (q4 + 1) * 512],
                             start=(dc == 0), stop=(dc == NDC - 1))
                for q4 in range(4):
                    ins = t.matmul(PB[:, q4 * 512:(q4 + 1) * 512], ones128[:],
                                   sqd[dc % 2][:, q4 * 512:(q4 + 1) * 512],
                                   start=(dc == 0), stop=(dc == NDC - 1))
                TT.inc(ins)
        blk.tensor(_)
        tt_s1 = TT.n

        def _(v, tt_s1=tt_s1):
            v.wait_ge(TT.h, tt_s1)
            VV.inc(v.tensor_scalar_mul(fA[:], PA[:], 1.0 / D))
            VV.inc(v.tensor_scalar_mul(fB[:], PB[:], 1.0 / D))
            v.wait_ge(VV.h, VV.n)
            VV.inc(v.tensor_mul(fC[:], fA[:], fA[:]))
            v.wait_ge(VV.h, VV.n)
            VV.inc(v.tensor_sub(fB[:], fB[:], fC[:]))
        blk.vector(_)
        vv_var = VV.n

        def _(s, vv_var=vv_var):
            s.wait_ge(VV.h, vv_var)
            SS.inc(s.activation(fC[:], fB[:], AF.Sqrt, bias=epsb[:]))
        blk.scalar(_)
        ss_sd = SS.n

        def _(v, ss_sd=ss_sd):
            v.wait_ge(SS.h, ss_sd)
            VV.inc(v.reciprocal(fD[:], fC[:]))
            v.wait_ge(VV.h, VV.n)
            VV.inc(v.tensor_mul(fC[:], fD[:], fD[:]))
            v.wait_ge(VV.h, VV.n)
            VV.inc(v.tensor_mul(fC[:], fB[:], fC[:]))
        blk.vector(_)
        vv_vn = VV.n

        def _(s, vv_vn=vv_vn):
            s.wait_ge(VV.h, vv_vn)
            SS.inc(s.activation(fC[:], fC[:], AF.Sqrt, bias=epsb[:]))
        blk.scalar(_)
        ss_sdn = SS.n

        def _(v, ts=ts, ss_sdn=ss_sdn):
            v.wait_ge(SS.h, ss_sdn)
            VV.inc(v.reciprocal(fC[:], fC[:]))
            v.wait_ge(VV.h, VV.n)
            VV.inc(v.tensor_mul(fC[:], fC[:], fD[:]))
            v.wait_ge(VV.h, VV.n)
            VV.inc(v.tensor_copy(mu_bf[:, ts:ts + 2048], fA[:]))
            VV.inc(v.tensor_copy(f_bf[:, ts:ts + 2048], fC[:]))
        blk.vector(_)

    def _(v):
        sMS.wait(v)
        for dc in range(NDC):
            v.wait_ge(VV.h, VV.n)
            VV.inc(v.tensor_sub(x1bf[dc][:], x1bf[dc][:], mu_bf[:]))
            v.wait_ge(VV.h, VV.n)
            VV.inc(v.tensor_mul(x1bf[dc][:], x1bf[dc][:], f_bf[:]))
            v.wait_ge(VV.h, VV.n)
            VV.inc(v.tensor_scalar(x1bf[dc][:], x1bf[dc][:], gma[:, dc:dc + 1],
                                   bta[:, dc:dc + 1], ALU.mult, ALU.add))
    blk.vector(_)
    vv_h = VV.n

    close(6)
    h = x1bf
    close(2)

    vtm = [nc.alloc_sbuf_tensor(f"vtm{i}", [128, D], BF16, side="right")
           for i in range(32)]
    wvb = scope("wvb", [128, 8 * D], BF16)
    wkb = scope("wkb", [128, 8 * 64], BF16)
    wqb = scope("wqb", [128, 8 * 64], BF16)

    def _(g, ss_pre=SS.n):
        g.wait_ge(VV.h, vv_h)
        g.wait_ge(SS.h, ss_pre)
        gWP.put(g, [(wvb[:].rearrange("p (dc e) -> p dc e", dc=8),
                     wvtD.rearrange("(dc p) e -> p dc e", p=128)),
                    (wkb[:].rearrange("p (dc e) -> p dc e", dc=8),
                     wktD.rearrange("(dc p) e -> p dc e", p=128)),
                    (wqb[:].rearrange("p (dc e) -> p dc e", dc=8),
                     wqtD.rearrange("(dc p) e -> p dc e", p=128))])
    blk.gpsimd(_)

    def _(t, ss_pre=SS.n):
        t.wait_ge(VV.h, vv_h)
        gWP.wait(t)
        for th in range(2):
            ts = th * 2048
            if th >= 1:
                t.wait_ge(SS.h, ss_pre + 2 * th - 1)
            for q4 in range(4):
                for dc in range(NDC):
                    ins = t.matmul(PA[0:64, q4 * 512:(q4 + 1) * 512],
                                   wkb[:, dc * 64:(dc + 1) * 64],
                                   h[dc][:, ts + q4 * 512:ts + (q4 + 1) * 512],
                                   start=(dc == 0), stop=(dc == NDC - 1))
            TT.inc(ins)
            if th >= 1:
                t.wait_ge(SS.h, ss_pre + 2 * th)
            for q4 in range(4):
                for dc in range(NDC):
                    ins = t.matmul(PB[0:64, q4 * 512:(q4 + 1) * 512],
                                   wqb[:, dc * 64:(dc + 1) * 64],
                                   h[dc][:, ts + q4 * 512:ts + (q4 + 1) * 512],
                                   start=(dc == 0), stop=(dc == NDC - 1))
            TT.inc(ins)
    blk.tensor(_)
    tt_kq0 = TT.n - 4

    def _(s):
        sMS.wait(s)
        for th in range(2):
            ts = th * 2048
            s.wait_ge(TT.h, tt_kq0 + 2 * th + 1)
            SS.inc(s.activation(kfmB[:, ts:ts + 2048], PA[0:64, :], AF.Identity,
                                bias=bk[:]))
            s.wait_ge(TT.h, tt_kq0 + 2 * th + 2)
            SS.inc(s.activation(qfmB[:, ts:ts + 2048], PB[0:64, :], AF.Identity,
                                bias=bq[:]))
    blk.scalar(_)
    ss_kq = SS.n

    tt_kv0 = TT.n
    vv_kv0 = VV.n

    def _(t):
        t.wait_ge(SS.h, ss_kq)
        for tc in range(32):
            if tc >= 2:
                t.wait_ge(VV.h, vv_kv0 + 2 * (tc - 1))
            pok = (tc % 2) * 512
            for dc in range(NDC):
                ins = t.matmul(PA[:, pok:pok + 64],
                               h[dc][:, tc * 128:(tc + 1) * 128],
                               wkb[:, dc * 64:(dc + 1) * 64],
                               start=(dc == 0), stop=(dc == NDC - 1))
            TT.inc(ins)
            pov = (tc % 2) * 1024
            for j in range(2):
                for dc in range(NDC):
                    ins = t.matmul(PB[:, pov + j * 512:pov + (j + 1) * 512],
                                   h[dc][:, tc * 128:(tc + 1) * 128],
                                   wvb[:, dc * 1024 + j * 512:dc * 1024 + (j + 1) * 512],
                                   start=(dc == 0), stop=(dc == NDC - 1))
            TT.inc(ins)
    blk.tensor(_)

    def _(v):
        for tc in range(32):
            v.wait_ge(TT.h, tt_kv0 + 2 * tc + 1)
            pok = (tc % 2) * 512
            VV.inc(v.tensor_add(ktm[:, tc * 64:(tc + 1) * 64], PA[:, pok:pok + 64], bkb[:]))
            v.wait_ge(TT.h, tt_kv0 + 2 * tc + 2)
            pov = (tc % 2) * 1024
            VV.inc(v.tensor_add(vtm[tc][:], PB[:, pov:pov + 1024], bvb[:]))
    blk.vector(_)
    vv_kv_done = VV.n

    close(3)
    close(8)

    psG = PB[:, 1024:1152]
    psD0 = PB[:, 1024:1152]
    psD1 = PB[:, 1536:1664]
    psKM = PA[:, 0:1024]
    psU = PA[:, 1024:2048]
    psY = PB[:, 0:1024]
    psSK = PB[0:64, 1024:2048]

    tt_sc0 = TT.n
    vv_sc0 = VV.n
    NTT, NVV = 24, 31

    def _(t):
        t.wait_ge(VV.h, vv_kv_done)
        for tc in range(32):
            tt0 = tt_sc0 + NTT * tc
            vv0 = vv_sc0 + NVV * tc
            ksl = kfmB[0:64, tc * 128:(tc + 1) * 128]
            qsl = qfmB[0:64, tc * 128:(tc + 1) * 128]
            t.wait_ge(VV.h, vv0)
            TT.inc(t.matmul(psG, ksl, ksl, start=True, stop=True))
            for l in range(5):
                t.wait_ge(VV.h, vv0 + 2 + 3 * l)
                TT.inc(t.matmul(psD0, nLt[:], nUt[:], start=True, stop=True))
                TT.inc(t.matmul(psD1, nUt[:], nLt[:], start=True, stop=True))
            t.wait_ge(VV.h, vv0 + 17)
            TT.inc(t.matmul(psD0, nLt[:], nUt[:], start=True, stop=True))
            for idx, j in enumerate([5, 4, 3, 2, 1, 0]):
                t.wait_ge(VV.h, vv0 + 19 + idx)
                ps = psD1 if idx % 2 == 0 else psD0
                rhs = [Ta, Tb][idx % 2]
                TT.inc(t.matmul(ps, Cbuf[:, j * 128:(j + 1) * 128], rhs[:],
                                start=True, stop=True))
            for j in range(2):
                ins = t.matmul(psKM[:, j * 512:(j + 1) * 512], ksl,
                               Sbf[:, j * 512:(j + 1) * 512], start=True, stop=True)
            TT.inc(ins)
            t.wait_ge(VV.h, vv0 + 26)
            for j in range(2):
                ins = t.matmul(psU[:, j * 512:(j + 1) * 512], Rs[:],
                               brawS[:, j * 512:(j + 1) * 512], start=True, stop=True)
            TT.inc(ins)
            for j in range(2):
                ins = t.matmul(psY[:, j * 512:(j + 1) * 512], qsl,
                               Sbf[:, j * 512:(j + 1) * 512], start=True, stop=False)
            TT.inc(ins)
            t.wait_ge(VV.h, vv0 + 25)
            TT.inc(t.matmul(psG, ksl, qsl, start=True, stop=True))
            t.wait_ge(VV.h, vv0 + 28)
            for j in range(2):
                ins = t.matmul(psY[:, j * 512:(j + 1) * 512], pmS[:],
                               Utm[:, j * 512:(j + 1) * 512], start=False, stop=True)
            TT.inc(ins)
            for j in range(2):
                ins = t.matmul(psSK[:, j * 512:(j + 1) * 512],
                               ktm[:, tc * 64:(tc + 1) * 64],
                               Utm[:, j * 512:(j + 1) * 512], start=True, stop=True)
            TT.inc(ins)
    blk.tensor(_)

    def _(g):
        for tc in range(32):
            vv0 = vv_sc0 + NVV * tc
            g.wait_ge(VV.h, vv0 + 29)
            gY.put(g, [(yfull[tc * 128:(tc + 1) * 128, :], ysb[tc % 2][:])])
        gY.wait(g)
    blk.gpsimd(_)

    def _(v):
        for tc in range(32):
            tt0 = tt_sc0 + NTT * tc
            vv0 = vv_sc0 + NVV * tc
            v.wait_ge(TT.h, tt0 + 1)
            VV.inc(v.tensor_mul(nUt[:], psG, mUe[:]))
            VV.inc(v.tensor_mul(nLt[:], psG, mLe[:]))
            for l in range(5):
                v.wait_ge(VV.h, VV.n)
                if l == 0:
                    VV.inc(v.tensor_sub(Cbuf[:, 0:128], identb[:], nLt[:]))
                else:
                    VV.inc(v.tensor_add(Cbuf[:, l * 128:(l + 1) * 128], identb[:], nLt[:]))
                v.wait_ge(TT.h, tt0 + 3 + 2 * l)
                v.wait_ge(VV.h, VV.n)
                VV.inc(v.tensor_copy(nUt[:], psD0))
                VV.inc(v.tensor_copy(nLt[:], psD1))
            v.wait_ge(VV.h, VV.n)
            VV.inc(v.tensor_add(Cbuf[:, 5 * 128:6 * 128], identb[:], nLt[:]))
            v.wait_ge(TT.h, tt0 + 12)
            VV.inc(v.tensor_add(Ta[:], identb[:], psD0))
            for idx in range(6):
                v.wait_ge(TT.h, tt0 + 13 + idx)
                ps = psD1 if idx % 2 == 0 else psD0
                if idx < 5:
                    dst = [Tb, Ta][idx % 2]
                    VV.inc(v.tensor_copy(dst[:], ps))
                else:
                    with nc.allow_low_precision("lam-scaled inverse factor in bf16"):
                        VV.inc(v.tensor_scalar_mul(Rs[:], ps, TTT_LR))
            v.wait_ge(TT.h, tt0 + 19)
            VV.inc(v.tensor_sub(brawS[:], psKM, vtm[tc][:]))
            v.wait_ge(TT.h, tt0 + 20)
            VV.inc(v.tensor_copy(Utm[:], psU))
            v.wait_ge(TT.h, tt0 + 22)
            VV.inc(v.tensor_mul(pmS[:], psG, mUn[:]))
            v.wait_ge(TT.h, tt0 + 23)
            if tc >= 2:
                gY.wait(v, tc - 2)
            VV.inc(v.tensor_copy(ysb[tc % 2][:], psY))
            v.wait_ge(TT.h, tt0 + 24)
            v.wait_ge(VV.h, VV.n)
            VV.inc(v.tensor_sub(Sst[:], Sst[:], psSK))
            v.wait_ge(VV.h, VV.n)
            VV.inc(v.tensor_copy(Sbf[:], Sst[:]))
    blk.vector(_)

    blk_g.__exit__(None, None, None)
    close(len(guards))
    pg2.__exit__(None, None, None)
    pg1.__exit__(None, None, None)
    for g in reversed(keep):
        g.__exit__(None, None, None)
    return nc


def prep_ttt_inmaps(inputs, x1):
    wk = np.asarray(inputs["L1_attn_Wk"], np.float32)
    wv = np.asarray(inputs["L1_attn_Wv"], np.float32)
    wq = np.asarray(inputs["L1_attn_Wq"], np.float32)
    bk = np.asarray(inputs["L1_attn_bk"], np.float32)
    bv = np.asarray(inputs["L1_attn_bv"], np.float32)
    bq = np.asarray(inputs["L1_attn_bq"], np.float32)
    gm = np.asarray(inputs["L1_attn_gamma"], np.float32)
    bt = np.asarray(inputs["L1_attn_beta"], np.float32)
    on = np.ones((128, 128), np.float32)
    shared = dict(
        wkt=_c(wk.T), wvt=_c(wv.T), wqt=_c(wq.T),
        bk=_c(bk.reshape(64, 1)), bq=_c(bq.reshape(64, 1)),
        bkb=_c(np.tile(bk, (128, 1))), bvb=_c(np.tile(bv, (128, 1))),
        gb=_c(gm.reshape(8, 128).T), bt=_c(bt.reshape(8, 128).T),
        mUe=_c((np.triu(on, 1) * TTT_LR).astype(BF)),
        mLe=_c((np.tril(on, -1) * TTT_LR).astype(BF)),
        mUn=_c((-np.triu(on, 1)).astype(BF)),
        identb=_c(np.eye(128, dtype=np.float32).astype(BF)))
    in_maps = []
    for core in range(8):
        b = core // 2
        m = dict(shared)
        m.update(x1fm=_c(x1[b].T))
        in_maps.append(m)
    return in_maps


def run_ttt(inputs, x1):
    nc = build_ttt()
    in_maps = prep_ttt_inmaps(inputs, x1)
    res = run_bass_kernel_spmd(nc, in_maps, core_ids=list(range(8)))
    y = np.zeros((B, T, D), dtype=np.float32)
    for core in range(8):
        b, hh = core // 2, core % 2
        y[b, hh * TOWN:(hh + 1) * TOWN] = res.results[core]["yfull"][hh * TOWN:(hh + 1) * TOWN]
    return y



def build_moe():
    nc = bass.Bass("TRN2", target_bir_lowering=False)

    x1ownD = nc.dram_tensor("x1own", [D, TOWN], F32, kind="ExternalInput").ap()
    yownD = nc.dram_tensor("yown", [D, TOWN], F32, kind="ExternalInput").ap()
    wrD = nc.dram_tensor("wr", [D, 8], F32, kind="ExternalInput").ap()
    br8D = nc.dram_tensor("br8", [8, 1], F32, kind="ExternalInput").ap()
    w1rD = nc.dram_tensor("w1r", [8, 32, 128, 1024], F32, kind="ExternalInput").ap()
    w2D = nc.dram_tensor("w2m", [8, HME, D], F32, kind="ExternalInput").ap()
    b1aD = nc.dram_tensor("b1a", [128, 256], F32, kind="ExternalInput").ap()
    b2aD = nc.dram_tensor("b2a", [128, 64], F32, kind="ExternalInput").ap()
    idD = nc.dram_tensor("identb", [128, 128], BF16, kind="ExternalInput").ap()
    x2ownD = nc.dram_tensor("x2own", [D, TOWN], F32, kind="ExternalOutput").ap()

    x2acc = [nc.alloc_sbuf_tensor(f"x2a{i}", [128, TOWN], F32) for i in range(NDC)]
    xn1bf = [nc.alloc_sbuf_tensor(f"xn1b{i}", [128, TOWN], BF16) for i in range(NDC)]
    h1m = nc.alloc_sbuf_tensor("h1m", [128, 32 * 512], BF16)
    w1c = [nc.alloc_sbuf_tensor(f"w1c{i}", [128, 1024], BF16) for i in range(2)]
    w2c = [nc.alloc_sbuf_tensor(f"w2c{i}", [128, 1024], BF16) for i in range(2)]
    wbv = nc.alloc_sbuf_tensor("wbv", [128, TOWN], BF16)
    wrsb = nc.alloc_sbuf_tensor("wrsb", [128, 64], BF16)
    br8 = nc.alloc_sbuf_tensor("br8S", [8, 1], F32)
    b1a = nc.alloc_sbuf_tensor("b1aS", [128, 256], F32)
    b2a = nc.alloc_sbuf_tensor("b2aS", [128, 64], F32)
    identb = nc.alloc_sbuf_tensor("identS", [128, 128], BF16)
    ones128 = nc.alloc_sbuf_tensor("ones128", [128, 128], BF16)
    epsb = nc.alloc_sbuf_tensor("epsb", [128, 1], F32)
    scoresbf = nc.alloc_sbuf_tensor("scoresbf", [8, TOWN], BF16)
    s_tm = nc.alloc_sbuf_tensor("s_tm", [128, 128], F32)
    sA = nc.alloc_sbuf_tensor("sA", [128, 128], F32)
    sB = nc.alloc_sbuf_tensor("sB", [128, 128], F32)
    m16 = nc.alloc_sbuf_tensor("m16", [128, 64], F32)
    w_tmb = nc.alloc_sbuf_tensor("w_tmb", [128, 128], BF16)
    wrow = nc.alloc_sbuf_tensor("wrow", [1, TOWN], BF16)
    etmp = [nc.alloc_sbuf_tensor(f"etmp{i}", [128, 512], F32) for i in range(2)]

    guards = []

    def scope(name, shape, dt):
        g = nc.sbuf_tensor(name, shape, dt)
        t = g.__enter__()
        guards.append(g)
        return t

    def close(n):
        for _ in range(n):
            guards.pop().__exit__(None, None, None)

    pg1 = nc.psum_tensor("PA", [128, 2048], F32)
    PA = pg1.__enter__()
    pg2 = nc.psum_tensor("PB", [128, 2048], F32)
    PB = pg2.__enter__()

    keep = []
    sg = [nc.semaphore(n) for n in ["sTT", "sVV", "sSS"]]
    TT, VV, SS = [Sem(g.__enter__()) for g in sg]
    keep.extend(sg)
    sX1 = Stream(nc, "sX1", 1, keep)
    sY = Stream(nc, "sY", 2, keep)
    sSM = Stream(nc, "sSM", 1, keep)
    gW1L = Stream(nc, "gW1L", 2, keep)
    gW2L = Stream(nc, "gW2L", 2, keep)
    gWR = Stream(nc, "gWR", 1, keep)
    sXC = Stream(nc, "sXC", 1, keep)
    gWPP = Stream(nc, "gWPP", 1, keep)
    sW1L = Stream(nc, "sW1L", 2, keep)
    sW2L = Stream(nc, "sW2L", 2, keep)
    gXO = Stream(nc, "gXO", 2, keep)

    blk_g = nc.Block()
    blk = blk_g.__enter__()

    def _(sy):
        for dc in range(NDC):
            sX1.put(sy, [(x2acc[dc][:], x1ownD[dc * 128:(dc + 1) * 128, :])])
        sSM.put(sy, [(br8[:], br8D), (b1a[:], b1aD), (b2a[:], b2aD), (identb[:], idD)])
    blk.sync(_)

    def _(g):
        gWR.put(g, [(wrsb[:].rearrange("p (dc e) -> p dc e", dc=8),
                     wrD.rearrange("(dc p) e -> p dc e", p=128))])
    blk.gpsimd(_)

    def _(v):
        v.memset(epsb[:], EPS)
        VV.inc(v.memset(ones128[:], 1.0))
    blk.vector(_)
    vv_ones = VV.n

    def _(v):
        sX1.wait(v)
        for dc in range(NDC):
            VV.inc(v.tensor_copy(xn1bf[dc][:], x2acc[dc][:]))
    blk.vector(_)
    vv_x1b = VV.n

    fA = scope("fA", [128, 2048], F32)
    fB = scope("fB", [128, 2048], F32)
    fC = scope("fC", [128, 2048], F32)
    sqd = [scope(f"sqd{i}", [128, 2048], BF16) for i in range(2)]
    mu_bf = scope("mu_bf", [128, 2048], BF16)
    rstd_bf = scope("rstd_bf", [128, 2048], BF16)
    yld = [fC]

    tt_b = TT.n
    vv_b = VV.n

    def _(v):
        for dc in range(NDC):
            if dc >= 2:
                v.wait_ge(TT.h, tt_b + dc - 1)
            v.wait_ge(VV.h, VV.n)
            VV.inc(v.tensor_mul(sqd[dc % 2][:], xn1bf[dc][:], xn1bf[dc][:]))
    blk.vector(_)

    def _(t):
        t.wait_ge(VV.h, vv_ones)
        for dc in range(NDC):
            t.wait_ge(VV.h, vv_b + dc + 1)
            for q4 in range(4):
                t.matmul(PA[:, q4 * 512:(q4 + 1) * 512], ones128[:],
                         xn1bf[dc][:, q4 * 512:(q4 + 1) * 512],
                         start=(dc == 0), stop=(dc == NDC - 1))
            for q4 in range(4):
                ins = t.matmul(PB[:, q4 * 512:(q4 + 1) * 512], ones128[:],
                               sqd[dc % 2][:, q4 * 512:(q4 + 1) * 512],
                               start=(dc == 0), stop=(dc == NDC - 1))
            TT.inc(ins)
    blk.tensor(_)
    tt_s1 = TT.n

    def _(v):
        v.wait_ge(TT.h, tt_s1)
        VV.inc(v.tensor_scalar_mul(fA[:], PA[:], 1.0 / D))
        VV.inc(v.tensor_scalar_mul(fB[:], PB[:], 1.0 / D))
        v.wait_ge(VV.h, VV.n)
        VV.inc(v.tensor_mul(fC[:], fA[:], fA[:]))
        v.wait_ge(VV.h, VV.n)
        VV.inc(v.tensor_sub(fB[:], fB[:], fC[:]))
    blk.vector(_)
    vv_var = VV.n

    def _(s):
        s.wait_ge(VV.h, vv_var)
        SS.inc(s.activation(fC[:], fB[:], AF.Sqrt, bias=epsb[:]))
    blk.scalar(_)
    ss_sd = SS.n

    def _(v):
        v.wait_ge(SS.h, ss_sd)
        VV.inc(v.reciprocal(fB[:], fC[:]))
        VV.inc(v.tensor_copy(mu_bf[:], fA[:]))
        v.wait_ge(VV.h, VV.n)
        VV.inc(v.tensor_copy(rstd_bf[:], fB[:]))
    blk.vector(_)

    def _(sy, vv_pre=VV.n):
        sy.wait_ge(VV.h, vv_pre)
        sy.wait_ge(SS.h, SS.n)
        for dc in range(NDC):
            if dc >= 1:
                sy.wait_ge(VV.h, vv_pre + 5 * dc)
            sY.put(sy, [(yld[0][:], yownD[dc * 128:(dc + 1) * 128, :])])
    blk.sync(_)

    def _(v):
        for dc in range(NDC):
            v.wait_ge(VV.h, VV.n)
            VV.inc(v.tensor_sub(xn1bf[dc][:], xn1bf[dc][:], mu_bf[:]))
            v.wait_ge(VV.h, VV.n)
            VV.inc(v.tensor_mul(xn1bf[dc][:], xn1bf[dc][:], rstd_bf[:]))
            v.wait_ge(VV.h, VV.n)
            VV.inc(v.tensor_add(x2acc[dc][:], x2acc[dc][:], xn1bf[dc][:]))
            sY.wait(v, dc)
            v.wait_ge(VV.h, VV.n)
            VV.inc(v.tensor_add(x2acc[dc][:], x2acc[dc][:], yld[0][:]))
            VV.inc(v.engine_nop())
    blk.vector(_)
    vv_xny = VV.n

    close(7)

    def _(t):
        gWR.wait(t)
        t.wait_ge(VV.h, vv_xny)
        for q4 in range(4):
            for dc in range(NDC):
                ins = t.matmul(PA[0:8, q4 * 512:(q4 + 1) * 512],
                               wrsb[:, dc * 8:(dc + 1) * 8],
                               xn1bf[dc][:, q4 * 512:(q4 + 1) * 512],
                               start=(dc == 0), stop=(dc == NDC - 1))
        TT.inc(ins)
    blk.tensor(_)
    tt_sc = TT.n

    def _(s):
        s.wait_ge(TT.h, tt_sc)
        sSM.wait(s)
        SS.inc(s.activation(scoresbf[:], PA[0:8, :], AF.Identity, bias=br8[:]))
    blk.scalar(_)
    ss_sc = SS.n

    def _(t):
        t.wait_ge(SS.h, ss_sc)
        for tcw in range(16):
            ins = t.matmul(PB[:, tcw * 128:tcw * 128 + 8],
                           scoresbf[:, tcw * 128:(tcw + 1) * 128],
                           identb[0:8, 0:8], start=True, stop=True)
        TT.inc(ins)
    blk.tensor(_)
    tt_tr = TT.n

    def _(v):
        v.wait_ge(TT.h, tt_tr)
        for tcw in range(16):
            sl = slice(tcw * 8, (tcw + 1) * 8)
            ps = PB[:, tcw * 128:tcw * 128 + 8]
            VV.inc(v.tensor_copy(s_tm[:, sl], ps))
        for tcw in range(16):
            sl = slice(tcw * 8, (tcw + 1) * 8)
            v.wait_ge(VV.h, VV.n)
            VV.inc(v.reduce_max(m16[:, tcw * 4:tcw * 4 + 1], s_tm[:, sl], axis=AX.X))
            v.wait_ge(VV.h, VV.n)
            VV.inc(v.tensor_scalar(sA[:, sl], s_tm[:, sl], m16[:, tcw * 4:tcw * 4 + 1],
                                   None, ALU.is_ge))
            v.wait_ge(VV.h, VV.n)
            VV.inc(v.scalar_tensor_tensor(sB[:, sl], sA[:, sl], -1e30, s_tm[:, sl],
                                          ALU.mult, ALU.add))
            v.wait_ge(VV.h, VV.n)
            VV.inc(v.reduce_max(m16[:, tcw * 4 + 1:tcw * 4 + 2], sB[:, sl], axis=AX.X))
            v.wait_ge(VV.h, VV.n)
            VV.inc(v.tensor_scalar(sA[:, sl], s_tm[:, sl], m16[:, tcw * 4 + 1:tcw * 4 + 2],
                                   None, ALU.is_ge))
            v.wait_ge(VV.h, VV.n)
            VV.inc(v.tensor_scalar(sB[:, sl], s_tm[:, sl], m16[:, tcw * 4:tcw * 4 + 1],
                                   None, ALU.subtract))
    blk.vector(_)
    vv_t2 = VV.n

    def _(s):
        s.wait_ge(VV.h, vv_t2)
        SS.inc(s.activation(sB[:], sB[:], AF.Exp))
    blk.scalar(_)
    ss_exp = SS.n

    def _(v):
        v.wait_ge(SS.h, ss_exp)
        for tcw in range(16):
            sl = slice(tcw * 8, (tcw + 1) * 8)
            v.wait_ge(VV.h, VV.n)
            VV.inc(v.tensor_mul(sB[:, sl], sB[:, sl], sA[:, sl]))
            v.wait_ge(VV.h, VV.n)
            VV.inc(v.reduce_sum(m16[:, tcw * 4 + 2:tcw * 4 + 3], sB[:, sl], axis=AX.X))
            v.wait_ge(VV.h, VV.n)
            VV.inc(v.reciprocal(m16[:, tcw * 4 + 3:tcw * 4 + 4], m16[:, tcw * 4 + 2:tcw * 4 + 3]))
            v.wait_ge(VV.h, VV.n)
            VV.inc(v.tensor_scalar(w_tmb[:, sl], sB[:, sl], m16[:, tcw * 4 + 3:tcw * 4 + 4],
                                   None, ALU.mult))
    blk.vector(_)
    vv_w = VV.n

    w1r_r = w1rD
    ss_prev_evac = SS.n
    vv_prev = VV.n
    tt_prev = TT.n

    for i in range(NEXP):
        def _(t, i=i, vv_w=vv_w, vv_prev=vv_prev):
            t.wait_ge(VV.h, max(vv_w, vv_prev))
            for tcw in range(16):
                ins = t.matmul(PA[0:1, tcw * 128:(tcw + 1) * 128],
                               w_tmb[:, tcw * 8 + i:tcw * 8 + i + 1], identb[:],
                               start=True, stop=True)
            TT.inc(ins)
        blk.tensor(_)
        tt_row = TT.n

        def _(v, tt_row=tt_row):
            v.wait_ge(TT.h, tt_row)
            VV.inc(v.tensor_copy(wrow, PA[0:1, 0:2048]))
        blk.vector(_)
        vv_row = VV.n

        def _(t, vv_row=vv_row):
            t.wait_ge(VV.h, vv_row)
            for q4 in range(4):
                ins = t.matmul(PB[0:128, q4 * 512:(q4 + 1) * 512],
                               ones128[0:1, 0:128],
                               h1m[0:1, q4 * 512:(q4 + 1) * 512],
                               start=True, stop=True)
            TT.inc(ins)
        blk.tensor(_)
        tt_bc = TT.n

        def _(v, i=i, tt_bc=tt_bc):
            v.wait_ge(TT.h, tt_bc)
            VV.inc(v.tensor_copy(wbv[:], PB[:, 0:2048]))
        blk.vector(_)
        vv_bc = VV.n

        for tb in range(4):
            ts = tb * 512
            tt_f1 = TT.n
            ss_f1 = SS.n

            w1base = len(sW1L.marks)

            def _(g, i=i, tt_f1=tt_f1, vv_pre=VV.n):
                g.wait_ge(VV.h, min(vv_pre, vv_w))
                for hc in range(32):
                    if hc >= 2:
                        g.wait_ge(TT.h, tt_f1 + hc - 1)
                    gW1L.put(g, [(w1c[hc % 2][:], w1r_r[i, hc])])
            blk.gpsimd(_)

            def _(t, i=i, ts=ts, ss_f1=ss_f1, vv_bc=vv_bc, w1base=w1base,
                  ss_pe=ss_prev_evac):
                t.wait_ge(VV.h, vv_bc)
                t.wait_ge(SS.h, ss_pe)
                for hc in range(32):
                    gW1L.wait(t, w1base + hc)
                    if hc >= 1:
                        t.wait_ge(SS.h, ss_f1 + hc)
                    for dc in range(NDC):
                        ins = t.matmul(PA[:, 0:512],
                                       w1c[hc % 2][:, dc * 128:(dc + 1) * 128],
                                       xn1bf[dc][:, ts:ts + 512],
                                       start=(dc == 0), stop=(dc == NDC - 1))
                    TT.inc(ins)
            blk.tensor(_)

            def _(s, i=i, tt_f1=tt_f1):
                for hc in range(32):
                    s.wait_ge(TT.h, tt_f1 + hc + 1)
                    SS.inc(s.activation(h1m[:, hc * 512:(hc + 1) * 512], PA[:, 0:512],
                                        AF.Relu, bias=b1a[:, i * 32 + hc:i * 32 + hc + 1]))
            blk.scalar(_)
            ss_f1_done = SS.n

            tt_f2 = TT.n

            w2base = len(sW2L.marks)

            def _(g, i=i, tt_f2=tt_f2):
                for hc in range(32):
                    if hc >= 2:
                        g.wait_ge(TT.h, tt_f2 + hc - 1)
                    gW2L.put(g, [(w2c[hc % 2][:], w2D[i, hc * 128:(hc + 1) * 128, :])])
            blk.gpsimd(_)

            def _(t, i=i, ss_f1_done=ss_f1_done, ss_prev_evac=ss_prev_evac,
                  w2base=w2base):
                t.wait_ge(SS.h, max(ss_f1_done, ss_prev_evac))
                for hc in range(32):
                    gW2L.wait(t, w2base + hc)
                    for ec in range(8):
                        ps = PA if ec < 4 else PB
                        ins = t.matmul(ps[:, (ec % 4) * 512:(ec % 4 + 1) * 512],
                                       w2c[hc % 2][:, ec * 128:(ec + 1) * 128],
                                       h1m[:, hc * 512:(hc + 1) * 512],
                                       start=(hc == 0), stop=(hc == 31))
                    TT.inc(ins)
            blk.tensor(_)
            tt_f2_done = TT.n

            def _(s, i=i, tt_f2_done=tt_f2_done, vv_pre=VV.n):
                s.wait_ge(TT.h, tt_f2_done)
                s.wait_ge(VV.h, vv_pre)
                for ec in range(8):
                    if ec >= 2:
                        s.wait_ge(VV.h, vv_pre + 2 * (ec - 1))
                    ps = PA if ec < 4 else PB
                    SS.inc(s.activation(etmp[ec % 2][:],
                                        ps[:, (ec % 4) * 512:(ec % 4 + 1) * 512],
                                        AF.Identity,
                                        bias=b2a[:, i * 8 + ec:i * 8 + ec + 1]))
            blk.scalar(_)
            ss_evac0 = SS.n - 8

            def _(v, i=i, ts=ts, ss_evac0=ss_evac0):
                for ec in range(8):
                    v.wait_ge(SS.h, ss_evac0 + ec + 1)
                    v.wait_ge(VV.h, VV.n)
                    VV.inc(v.tensor_mul(etmp[ec % 2][:], etmp[ec % 2][:], wbv[:, ts:ts + 512]))
                    v.wait_ge(VV.h, VV.n)
                    VV.inc(v.tensor_add(x2acc[ec][:, ts:ts + 512],
                                        x2acc[ec][:, ts:ts + 512], etmp[ec % 2][:]))
            blk.vector(_)
            ss_prev_evac = SS.n
            vv_prev = VV.n

        tt_prev = TT.n

    vv_fin = VV.n

    def _(g):
        g.wait_ge(VV.h, vv_fin)
        for dc in range(NDC):
            gXO.put(g, [(x2ownD[dc * 128:(dc + 1) * 128, :], x2acc[dc][:])])
        gXO.wait(g)
    blk.gpsimd(_)

    blk_g.__exit__(None, None, None)
    close(len(guards))
    pg2.__exit__(None, None, None)
    pg1.__exit__(None, None, None)
    for g in reversed(keep):
        g.__exit__(None, None, None)
    return nc


def prep_moe_inmaps(inputs, x1, y):
    wr = np.asarray(inputs["L1_ffn_Wr"], np.float32)
    br = np.asarray(inputs["L1_ffn_br"], np.float32)
    w1 = np.asarray(inputs["L1_ffn_W1"], np.float32)
    b1 = np.asarray(inputs["L1_ffn_b1"], np.float32)
    w2 = np.asarray(inputs["L1_ffn_W2"], np.float32)
    b2 = np.asarray(inputs["L1_ffn_b2"], np.float32)
    w1r = _c(w1.reshape(8, 8, 128, 32, 128).transpose(0, 3, 2, 1, 4)
             .reshape(8, 32, 128, 1024))
    b1a = _c(np.concatenate([b1[i].reshape(32, 128).T for i in range(8)], axis=1))
    b2a = _c(np.concatenate([b2[i].reshape(8, 128).T for i in range(8)], axis=1))
    shared = dict(wr=_c(wr), br8=_c(br.reshape(8, 1)), w1r=w1r, w2m=_c(w2),
                  b1a=b1a, b2a=b2a,
                  identb=_c(np.eye(128, dtype=np.float32).astype(BF)))
    in_maps = []
    for core in range(8):
        b, hh = core // 2, core % 2
        own = slice(hh * TOWN, (hh + 1) * TOWN)
        m = dict(shared)
        m.update(x1own=_c(x1[b, own].T), yown=_c(y[b, own].T))
        in_maps.append(m)
    return in_maps


def run_moe(inputs, x1, y):
    nc = build_moe()
    in_maps = prep_moe_inmaps(inputs, x1, y)
    res = run_bass_kernel_spmd(nc, in_maps, core_ids=list(range(8)))
    x2 = np.zeros((B, T, D), dtype=np.float32)
    for core in range(8):
        b, hh = core // 2, core % 2
        x2[b, hh * TOWN:(hh + 1) * TOWN] = res.results[core]["x2own"].T
    return x2



def build_fused():
    nc = bass.Bass("TRN2", target_bir_lowering=False)

    x0fm = nc.dram_tensor("x0fm", [D, T], F32, kind="ExternalInput").ap()
    wqp = nc.dram_tensor("wqp", [D, D], F32, kind="ExternalInput").ap()
    wkp = nc.dram_tensor("wkp", [D, D], F32, kind="ExternalInput").ap()
    wvD = nc.dram_tensor("wv", [D, D], F32, kind="ExternalInput").ap()
    woD = nc.dram_tensor("wo", [D, D], F32, kind="ExternalInput").ap()
    w1t = nc.dram_tensor("w1t", [D, HFF], F32, kind="ExternalInput").ap()
    w1sD = nc.dram_tensor("w1s", [HFF, D], F32, kind="ExternalInput").ap()
    w2t = nc.dram_tensor("w2t", [HFF, D], F32, kind="ExternalInput").ap()
    w2sD = nc.dram_tensor("w2s", [D, HFF], F32, kind="ExternalInput").ap()
    b1tD = nc.dram_tensor("b1t", [128, 32], F32, kind="ExternalInput").ap()
    b2tD = nc.dram_tensor("b2t", [128, 8], F32, kind="ExternalInput").ap()
    cos2d = nc.dram_tensor("cos2", [128, T], BF16, kind="ExternalInput").ap()
    sin2d = nc.dram_tensor("sin2", [128, T], BF16, kind="ExternalInput").ap()
    wktD = nc.dram_tensor("wkt", [D, 64], F32, kind="ExternalInput").ap()
    wvtD = nc.dram_tensor("wvt", [D, D], F32, kind="ExternalInput").ap()
    wqtD = nc.dram_tensor("wqt", [D, 64], F32, kind="ExternalInput").ap()
    bkD = nc.dram_tensor("bk", [64, 1], F32, kind="ExternalInput").ap()
    bqD = nc.dram_tensor("bq", [64, 1], F32, kind="ExternalInput").ap()
    bkbD = nc.dram_tensor("bkb", [128, 64], F32, kind="ExternalInput").ap()
    bvbD = nc.dram_tensor("bvb", [128, D], F32, kind="ExternalInput").ap()
    gbD = nc.dram_tensor("gb", [128, 8], F32, kind="ExternalInput").ap()
    btD = nc.dram_tensor("bt", [128, 8], F32, kind="ExternalInput").ap()
    mUeD = nc.dram_tensor("mUe", [128, 128], BF16, kind="ExternalInput").ap()
    mLeD = nc.dram_tensor("mLe", [128, 128], BF16, kind="ExternalInput").ap()
    mUnD = nc.dram_tensor("mUn", [128, 128], BF16, kind="ExternalInput").ap()
    idD = nc.dram_tensor("identb", [128, 128], BF16, kind="ExternalInput").ap()
    wrD = nc.dram_tensor("wr", [D, 8], F32, kind="ExternalInput").ap()
    br8D = nc.dram_tensor("br8", [8, 1], F32, kind="ExternalInput").ap()
    w1rD = nc.dram_tensor("w1r", [8, 32, 128, 1024], F32, kind="ExternalInput").ap()
    w2D = nc.dram_tensor("w2m", [8, HME, D], F32, kind="ExternalInput").ap()
    b1aD = nc.dram_tensor("b1a", [128, 256], F32, kind="ExternalInput").ap()
    b2aD = nc.dram_tensor("b2a", [128, 64], F32, kind="ExternalInput").ap()
    sel0D = nc.dram_tensor("sel0", [128, 1], F32, kind="ExternalInput").ap()
    sel1D = nc.dram_tensor("sel1", [128, 1], F32, kind="ExternalInput").ap()
    x2ownD = nc.dram_tensor("x2own", [D, TOWN], F32, kind="ExternalOutput").ap()

    h1d = nc.dram_tensor("h1d", [32, 128, TOWN], BF16, kind="Internal").ap()
    w1sd = nc.dram_tensor("w1sd", [32, 128, 1024], BF16, kind="Internal").ap()
    w2sd = nc.dram_tensor("w2sd", [32, 128, 1024], BF16, kind="Internal").ap()
    x1d = nc.dram_tensor("x1d", [D, TOWN], F32, kind="Internal").ap()
    x1g = nc.dram_tensor("x1g", [2, D, TOWN], F32, kind="Internal").ap()
    vd = nc.dram_tensor("vd", [32, 128, D], BF16, kind="Internal").ap()
    w1bd = nc.dram_tensor("w1bd", [8, 32, 128, 1024], BF16, kind="Internal").ap()
    w2bd = nc.dram_tensor("w2bd", [8, HME, D], BF16, kind="Internal").ap()
    xn1d = nc.dram_tensor("xn1d", [NDC, 128, TOWN], BF16, kind="Internal").ap()
    wbv8d = nc.dram_tensor("wbv8d", [128, 8 * TOWN], BF16, kind="Internal").ap()
    e0d = nc.dram_tensor("e0d", [4, 8, 128, 512], BF16, kind="Internal").ap()

    ones128 = nc.alloc_sbuf_tensor("ones128", [128, 128], BF16)
    epsb = nc.alloc_sbuf_tensor("epsb", [128, 1], F32)
    identb = nc.alloc_sbuf_tensor("identS", [128, 128], BF16)

    guards = []

    def scope(name, shape, dt, side=None):
        g = nc.sbuf_tensor(name, shape, dt, side=side) if side else nc.sbuf_tensor(name, shape, dt)
        t = g.__enter__()
        guards.append(g)
        return t

    def close(n):
        for _ in range(n):
            guards.pop().__exit__(None, None, None)

    rguards = []

    def rscope(name, shape, dt):
        g = nc.sbuf_tensor(name, shape, dt, side="right")
        t = g.__enter__()
        rguards.append(g)
        return t

    def rclose(n):
        for _ in range(n):
            rguards.pop().__exit__(None, None, None)

    pg1 = nc.psum_tensor("PA", [128, 2048], F32)
    PA = pg1.__enter__()
    pg2 = nc.psum_tensor("PB", [128, 2048], F32)
    PB = pg2.__enter__()

    keep = []
    sg = [nc.semaphore(n) for n in ["sTT", "sVV", "sSS", "sCC"]]
    TT, VV, SS, CC = [Sem(g.__enter__()) for g in sg]
    keep.extend(sg)

    stB = Stream(nc, "stB", 1, keep)
    gX0 = Stream(nc, "gX0", 1, keep)
    sWT = Stream(nc, "sWT", 2, keep)
    sWS = Stream(nc, "sWS", 2, keep)
    gW1 = Stream(nc, "gW1", 2, keep)
    gW2 = Stream(nc, "gW2", 2, keep)
    sF1 = Stream(nc, "sF1", 2, keep)
    gH1 = Stream(nc, "gH1", 2, keep)
    sH1 = Stream(nc, "sH1", 1, keep)
    sF2 = Stream(nc, "sF2", 2, keep)
    sTAB = Stream(nc, "sTAB", 1, keep)
    gWG = Stream(nc, "gWG", 1, keep)
    gWOB = Stream(nc, "gWOB", 1, keep)
    sX0 = Stream(nc, "sX0", 2, keep)
    gX1D = Stream(nc, "gX1D", 2, keep)
    gX1 = Stream(nc, "gX1", 1, keep)
    gWP = Stream(nc, "gWP", 1, keep)
    sMS = Stream(nc, "sMS", 1, keep)
    gVD = Stream(nc, "gVD", 2, keep)
    sVT = Stream(nc, "sVT", 2, keep)
    gW1L = Stream(nc, "gW1L", 2, keep)
    gW2L = Stream(nc, "gW2L", 2, keep)
    gWR = Stream(nc, "gWR", 1, keep)
    sXC = Stream(nc, "sXC", 1, keep)
    gWPP = Stream(nc, "gWPP", 1, keep)
    sW1L = Stream(nc, "sW1L", 2, keep)
    sW2L = Stream(nc, "sW2L", 2, keep)
    gXO = Stream(nc, "gXO", 2, keep)
    sX1R = Stream(nc, "sX1R", 2, keep)
    sWRL = Stream(nc, "sWRL", 1, keep)
    sXNS = Stream(nc, "sXNS", 1, keep)
    sXNL = Stream(nc, "sXNL", 1, keep)
    sWBS = Stream(nc, "sWBS", 1, keep)
    sWBL = Stream(nc, "sWBL", 1, keep)
    sW1P = Stream(nc, "sW1P", 2, keep)
    sW2P = Stream(nc, "sW2P", 2, keep)
    gE0S = Stream(nc, "gE0S", 1, keep)
    sE0L = Stream(nc, "sE0L", 1, keep)

    blk_g = nc.Block()
    blk = blk_g.__enter__()

    xn = [rscope(f"xn{i}", [128, T], BF16) for i in range(NDC)]
    ffnacc = [scope(f"ffa{i}", [128, TOWN], BF16) for i in range(NDC)]
    hs_ofm = scope("hs_ofm", [128, 32 * 512], BF16)
    alpha1 = scope("alpha1", [128, 32], F32)
    alpha2 = scope("alpha2", [128, 8], F32)
    b1sb = scope("b1sb", [128, 32], F32)
    b2sb = scope("b2sb", [128, 8], F32)
    nA_base = len(guards)

    def _(sy):
        stB.put(sy, [(b1sb[:], b1tD), (b2sb[:], b2tD)])
    blk.sync(_)

    def _(g):
        for dc in range(NDC):
            gX0.put(g, [(xn[dc][:], x0fm[dc * 128:(dc + 1) * 128, :])])
    blk.gpsimd(_)

    def _(v):
        v.memset(epsb[:], EPS)
        VV.inc(v.memset(ones128[:], 1.0))
    blk.vector(_)
    vv_ones = VV.n

    wtmp = [scope(f"wtmp{i}", [128, 1024], F32) for i in range(2)]
    wstr = [scope(f"wstr{i}", [128, 4096], F32) for i in range(2)]
    wsgn = [scope(f"wsgn{i}", [128, 1024], BF16) for i in range(2)]

    w1t_r = w1t.rearrange("(dc p) h -> p dc h", p=128)

    ss0, vv0 = SS.n, VV.n

    def _(sy):
        for hc in range(32):
            if hc >= 2:
                sy.wait_ge(SS.h, ss0 + hc - 1)
                sy.wait_ge(VV.h, vv0 + hc - 1)
            sWT.put(sy, [(wtmp[hc % 2][:].rearrange("p (dc h) -> p dc h", dc=8),
                          w1t_r[:, :, hc * 128:(hc + 1) * 128])])
            sWS.put(sy, [(wstr[hc % 2][:, 0:1024], w1sD[hc * 128:(hc + 1) * 128, :])])
    blk.sync(_)

    def _(g):
        for hc in range(32):
            g.wait_ge(SS.h, ss0 + hc + 1)
            gW1.put(g, [(w1sd[hc], wsgn[hc % 2][:])])
    blk.gpsimd(_)

    def _(s):
        for hc in range(32):
            sWT.wait(s, hc)
            if hc >= 2:
                gW1.wait(s, hc - 2)
            SS.inc(s.activation(wsgn[hc % 2][:], wtmp[hc % 2][:], AF.Sign))
    blk.scalar(_)

    def _(v):
        for hc in range(32):
            sWS.wait(v, hc)
            VV.inc(v.reduce_sum(alpha1[:, hc:hc + 1], wstr[hc % 2][:, 0:1024],
                                axis=AX.X, apply_absolute_value=True))
    blk.vector(_)

    ss1, vv1 = SS.n, VV.n

    def _(sy):
        sy.wait_ge(SS.h, ss1)
        sy.wait_ge(VV.h, vv1)
        for hc in range(32):
            if hc >= 2:
                sy.wait_ge(SS.h, ss1 + hc - 1)
            sWT.put(sy, [(wtmp[hc % 2][:], w2t[hc * 128:(hc + 1) * 128, :])])
        for ec in range(8):
            if ec >= 2:
                sy.wait_ge(VV.h, vv1 + ec - 1)
            sWS.put(sy, [(wstr[ec % 2][:], w2sD[ec * 128:(ec + 1) * 128, :])])
    blk.sync(_)

    def _(g):
        for hc in range(32):
            g.wait_ge(SS.h, ss1 + hc + 1)
            gW2.put(g, [(w2sd[hc], wsgn[hc % 2][:])])
    blk.gpsimd(_)

    def _(s):
        gW1.wait(s)
        for hc in range(32):
            sWT.wait(s, 32 + hc)
            if hc >= 2:
                gW2.wait(s, hc - 2)
            SS.inc(s.activation(wsgn[hc % 2][:], wtmp[hc % 2][:], AF.Sign))
    blk.scalar(_)

    def _(v):
        for ec in range(8):
            sWS.wait(v, 32 + ec)
            VV.inc(v.reduce_sum(alpha2[:, ec:ec + 1], wstr[ec % 2][:],
                                axis=AX.X, apply_absolute_value=True))
        v.wait_ge(VV.h, VV.n)
        VV.inc(v.tensor_scalar_mul(alpha1[:], alpha1[:], 1.0 / D))
        VV.inc(v.tensor_scalar_mul(alpha2[:], alpha2[:], 1.0 / HFF))
    blk.vector(_)
    vv_alpha = VV.n

    close(len(guards) - nA_base)

    fA = scope("fA", [128, 2048], F32)
    fB = scope("fB", [128, 2048], F32)
    fC = scope("fC", [128, 2048], F32)
    sqd = [scope(f"sqd{i}", [128, 2048], BF16) for i in range(2)]
    mu_bf = scope("mu_bf", [128, T], BF16)
    rstd_bf = scope("rstd_bf", [128, T], BF16)

    for tb in range(2):
        ts = tb * 2048
        tt_b = TT.n
        vv_b = VV.n

        def _(v, ts=ts, tt_b=tt_b, ss_w=SS.n):
            gX0.wait(v)
            v.wait_ge(SS.h, ss_w)
            gW1.wait(v)
            gW2.wait(v)
            for dc in range(NDC):
                if dc >= 2:
                    v.wait_ge(TT.h, tt_b + dc - 1)
                VV.inc(v.tensor_mul(sqd[dc % 2][:], xn[dc][:, ts:ts + 2048],
                                    xn[dc][:, ts:ts + 2048]))
        blk.vector(_)

        def _(t, ts=ts, vv_b=vv_b):
            t.wait_ge(VV.h, vv_ones)
            for dc in range(NDC):
                t.wait_ge(VV.h, vv_b + dc + 1)
                for q4 in range(4):
                    t.matmul(PA[:, q4 * 512:(q4 + 1) * 512], ones128[:],
                             xn[dc][:, ts + q4 * 512:ts + (q4 + 1) * 512],
                             start=(dc == 0), stop=(dc == NDC - 1))
                for q4 in range(4):
                    ins = t.matmul(PB[:, q4 * 512:(q4 + 1) * 512], ones128[:],
                                   sqd[dc % 2][:, q4 * 512:(q4 + 1) * 512],
                                   start=(dc == 0), stop=(dc == NDC - 1))
                TT.inc(ins)
        blk.tensor(_)
        tt_s1 = TT.n

        def _(v, tt_s1=tt_s1):
            v.wait_ge(TT.h, tt_s1)
            VV.inc(v.tensor_scalar_mul(fA[:], PA[:], 1.0 / D))
            VV.inc(v.tensor_scalar_mul(fB[:], PB[:], 1.0 / D))
            v.wait_ge(VV.h, VV.n)
            VV.inc(v.tensor_mul(fC[:], fA[:], fA[:]))
            v.wait_ge(VV.h, VV.n)
            VV.inc(v.tensor_sub(fB[:], fB[:], fC[:]))
        blk.vector(_)
        vv_var = VV.n

        def _(s, vv_var=vv_var):
            s.wait_ge(VV.h, vv_var)
            SS.inc(s.activation(fC[:], fB[:], AF.Sqrt, bias=epsb[:]))
        blk.scalar(_)
        ss_sd = SS.n

        def _(v, ts=ts, ss_sd=ss_sd):
            v.wait_ge(SS.h, ss_sd)
            VV.inc(v.reciprocal(fB[:], fC[:]))
            VV.inc(v.tensor_copy(mu_bf[:, ts:ts + 2048], fA[:]))
            v.wait_ge(VV.h, VV.n)
            VV.inc(v.tensor_copy(rstd_bf[:, ts:ts + 2048], fB[:]))
        blk.vector(_)

    def _(v):
        for dc in range(NDC):
            VV.inc(v.tensor_sub(xn[dc][:], xn[dc][:], mu_bf[:]))
            v.wait_ge(VV.h, VV.n)
            VV.inc(v.tensor_mul(xn[dc][:], xn[dc][:], rstd_bf[:]))
    blk.vector(_)
    vv_xn = VV.n

    close(7)

    w1b = [scope(f"w1b{i}", [128, 1024], BF16) for i in range(2)]
    h1t = [scope(f"h1t{i}", [128, TOWN], BF16) for i in range(2)]

    ttf, ssf = TT.n, SS.n

    def _(sy, ss_pre=SS.n):
        gW1.wait(sy)
        sy.wait_ge(VV.h, vv_xn)
        sy.wait_ge(SS.h, ss_pre)
        for hc in range(32):
            if hc >= 2:
                sy.wait_ge(TT.h, ttf + hc - 1)
            sF1.put(sy, [(w1b[hc % 2][:], w1sd[hc])])
    blk.sync(_)

    def _(t):
        t.wait_ge(VV.h, vv_xn)
        for hc in range(32):
            sF1.wait(t, hc)
            if hc >= 1:
                t.wait_ge(SS.h, ssf + hc)
            for q4 in range(4):
                for dc in range(NDC):
                    ins = t.matmul(PA[:, q4 * 512:(q4 + 1) * 512],
                                   w1b[hc % 2][:, dc * 128:(dc + 1) * 128],
                                   xn[dc][:, q4 * 512:(q4 + 1) * 512],
                                   start=(dc == 0), stop=(dc == NDC - 1))
            TT.inc(ins)
    blk.tensor(_)

    def _(g):
        for hc in range(32):
            g.wait_ge(SS.h, ssf + hc + 1)
            gH1.put(g, [(h1d[hc], h1t[hc % 2][:])])
    blk.gpsimd(_)

    def _(s):
        s.wait_ge(VV.h, vv_xn)
        stB.wait(s)
        s.wait_ge(VV.h, vv_alpha)
        for hc in range(32):
            s.wait_ge(TT.h, ttf + hc + 1)
            if hc >= 2:
                gH1.wait(s, hc - 2)
            SS.inc(s.activation(h1t[hc % 2][:], PA[:], AF.Gelu,
                                bias=b1sb[:, hc:hc + 1], scale=alpha1[:, hc:hc + 1]))
    blk.scalar(_)

    close(4)

    w2b = [scope(f"w2b{i}", [128, 1024], BF16) for i in range(2)]
    h1s = hs_ofm

    tt_prev_tb = TT.n
    ss_prev_tb = SS.n
    for tb in range(4):
        ts = tb * 512
        tt2 = TT.n

        def _(sy, ts=ts, tt2=tt2, tt_prev_tb=tt_prev_tb, tb=tb):
            gH1.wait(sy)
            gW2.wait(sy)
            sy.wait_ge(TT.h, tt_prev_tb)
            sH1.put(sy, [(h1s[:, grp * 8 * 512:(grp + 1) * 8 * 512].rearrange(
                              "p (hc t) -> p hc t", hc=8),
                          h1d[grp * 8:(grp + 1) * 8, :, ts:ts + 512].rearrange(
                              "hc p t -> p hc t"))
                         for grp in range(4)])
            for hc in range(32):
                if hc >= 2:
                    sy.wait_ge(TT.h, tt2 + hc - 1)
                sF2.put(sy, [(w2b[hc % 2][:], w2sd[hc])])
        blk.sync(_)

        def _(t, ss_prev_tb=ss_prev_tb, tb=tb):
            t.wait_ge(SS.h, ss_prev_tb)
            sH1.wait(t, tb)
            for hc in range(32):
                sF2.wait(t, 32 * tb + hc)
                for ec in range(8):
                    ps = PA if ec < 4 else PB
                    ins = t.matmul(ps[:, (ec % 4) * 512:(ec % 4 + 1) * 512],
                                   w2b[hc % 2][:, ec * 128:(ec + 1) * 128],
                                   h1s[:, hc * 512:(hc + 1) * 512],
                                   start=(hc == 0), stop=(hc == 31))
                TT.inc(ins)
        blk.tensor(_)
        tt_prev_tb = TT.n

        def _(s, ts=ts, tt_done=TT.n):
            s.wait_ge(TT.h, tt_done)
            stB.wait(s)
            for ec in range(8):
                ps = PA if ec < 4 else PB
                SS.inc(s.activation(ffnacc[ec][:, ts:ts + 512],
                                    ps[:, (ec % 4) * 512:(ec % 4 + 1) * 512],
                                    AF.Identity,
                                    bias=b2sb[:, ec:ec + 1], scale=alpha2[:, ec:ec + 1]))
        blk.scalar(_)
        ss_prev_tb = SS.n

    close(2)
    ss_attn0 = SS.n

    cos2 = scope("cos2s", [128, T], BF16)
    sin2 = scope("sin2s", [128, T], BF16)
    kfm = scope("kfm", [128, T], BF16)
    qfm = scope("qfm", [128, TOWN], BF16)
    raw = scope("raw", [128, 2048], BF16)
    tscr = scope("tscr", [128, 2048], BF16)
    v65 = scope("v65", [128, 32 * 256], BF16)
    pb_sb = scope("pb_sb", [128, 2048], BF16)
    recb = scope("recb", [64, 2048], BF16)
    rec1 = scope("rec1", [1, 2048], BF16)
    wgq = scope("wgq", [128, 1024], BF16)
    wgk = scope("wgk", [128, 1024], BF16)
    wgv = scope("wgv", [128, 1024], BF16)
    ofm = hs_ofm

    def _(sy, tt_pre=TT.n, ss_pre=SS.n):
        sy.wait_ge(TT.h, tt_pre)
        sy.wait_ge(SS.h, ss_pre)
        sTAB.put(sy, [(cos2[:], cos2d), (sin2[:], sin2d)])
    blk.sync(_)

    wqp_r = wqp.rearrange("(dc p) j -> p dc j", p=128)
    wkp_r = wkp.rearrange("(dc p) j -> p dc j", p=128)
    wv_r = wvD.rearrange("(dc p) j -> p dc j", p=128)

    def emit_rope(v, dst, doff, src, tab_off, tlen):
        tsl = slice(tab_off, tab_off + tlen)
        for e in (0, 64):
            o = e + 32
            ce = cos2[e:e + 32, tsl]
            se = sin2[e:e + 32, tsl]
            co = cos2[o:o + 32, tsl]
            so = sin2[o:o + 32, tsl]
            VV.inc(v.tensor_mul(tscr[e:e + 32, 0:tlen], src[o:o + 32, 0:tlen], so))
            VV.inc(v.tensor_mul(dst[e:e + 32, doff:doff + tlen], src[e:e + 32, 0:tlen], ce))
            v.wait_ge(VV.h, VV.n)
            VV.inc(v.tensor_sub(dst[e:e + 32, doff:doff + tlen],
                                dst[e:e + 32, doff:doff + tlen], tscr[e:e + 32, 0:tlen]))
            VV.inc(v.tensor_mul(tscr[o:o + 32, 0:tlen], src[e:e + 32, 0:tlen], se))
            VV.inc(v.tensor_mul(dst[o:o + 32, doff:doff + tlen], src[o:o + 32, 0:tlen], co))
            v.wait_ge(VV.h, VV.n)
            VV.inc(v.tensor_add(dst[o:o + 32, doff:doff + tlen],
                                dst[o:o + 32, doff:doff + tlen], tscr[o:o + 32, 0:tlen]))

    tt_group_end = TT.n
    vv_group_end = VV.n

    for c in range(8):
        def _(g, tt_group_end=tt_group_end, c=c):
            g.wait_ge(TT.h, tt_group_end)
            gWG.put(g, [(wgq[:].rearrange("p (dc j) -> p dc j", dc=8),
                         wqp_r[:, :, c * 128:(c + 1) * 128]),
                        (wgk[:].rearrange("p (dc j) -> p dc j", dc=8),
                         wkp_r[:, :, c * 128:(c + 1) * 128]),
                        (wgv[:].rearrange("p (dc j) -> p dc j", dc=8),
                         wv_r[:, :, c * 128:(c + 1) * 128])])
            gWPP.put(g, [(w1bd[c], w1rD[c])])
            gWPP.put(g, [(w2bd[c], w2D[c])])
        blk.gpsimd(_)

        def _(t, vv_group_end=vv_group_end, c=c):
            gWG.wait(t, c)
            t.wait_ge(VV.h, max(vv_xn, vv_group_end))
            t.wait_ge(SS.h, ss_attn0)
            for q4 in range(4):
                for dc in range(NDC):
                    ins = t.matmul(PA[:, q4 * 512:(q4 + 1) * 512],
                                   wgq[:, dc * 128:(dc + 1) * 128],
                                   xn[dc][:, q4 * 512:(q4 + 1) * 512],
                                   start=(dc == 0), stop=(dc == NDC - 1))
            TT.inc(ins)
        blk.tensor(_)
        tt_q = TT.n

        def _(v, tt_q=tt_q):
            v.wait_ge(TT.h, tt_q)
            sTAB.wait(v)
            VV.inc(v.tensor_copy(raw[:], PA[:]))
            v.wait_ge(VV.h, VV.n)
            emit_rope(v, qfm, 0, raw, 0, 2048)
        blk.vector(_)

        for th in range(2):
            ts = th * 2048

            def _(t, ts=ts, vv_pre=VV.n):
                t.wait_ge(VV.h, vv_pre)
                for q4 in range(4):
                    for dc in range(NDC):
                        ins = t.matmul(PB[:, q4 * 512:(q4 + 1) * 512],
                                       wgk[:, dc * 128:(dc + 1) * 128],
                                       xn[dc][:, ts + q4 * 512:ts + (q4 + 1) * 512],
                                       start=(dc == 0), stop=(dc == NDC - 1))
                TT.inc(ins)
            blk.tensor(_)
            tt_k = TT.n

            def _(v, ts=ts, tt_k=tt_k):
                v.wait_ge(TT.h, tt_k)
                VV.inc(v.tensor_copy(raw[:], PB[:]))
                v.wait_ge(VV.h, VV.n)
                emit_rope(v, kfm, ts, raw, ts, 2048)
            blk.vector(_)

        def _(v, tt_group_end=tt_group_end):
            v.wait_ge(TT.h, tt_group_end)
            VV.inc(v.memset(v65[:], 1.0))
        blk.vector(_)
        vv_ms = VV.n

        tt_v0 = TT.n
        vv_v0 = VV.n

        def _(t, tt_v0=tt_v0, vv_v0=vv_v0, vv_ms=vv_ms):
            t.wait_ge(VV.h, vv_ms)
            for tc in range(32):
                if tc >= 2:
                    t.wait_ge(VV.h, vv_v0 + tc - 1)
                po = (tc % 2) * 512
                for dc in range(NDC):
                    ins = t.matmul(PA[:, po:po + 128],
                                   xn[dc][:, tc * 128:(tc + 1) * 128],
                                   wgv[:, dc * 128:(dc + 1) * 128],
                                   start=(dc == 0), stop=(dc == NDC - 1))
                TT.inc(ins)
        blk.tensor(_)

        def _(v, tt_v0=tt_v0, vv_ms=vv_ms):
            v.wait_ge(VV.h, vv_ms)
            for tc in range(32):
                v.wait_ge(TT.h, tt_v0 + tc + 1)
                po = (tc % 2) * 512
                v.tensor_copy(v65[:, tc * 256:tc * 256 + 64], PA[:, po:po + 64])
                VV.inc(v.tensor_copy(v65[:, tc * 256 + 128:tc * 256 + 192],
                                     PA[:, po + 64:po + 128]))
        blk.vector(_)

        for hh in range(2):
            hb = hh * 64
            vcol = hh * 128

            tt_h0 = TT.n
            ss_h0 = SS.n
            vv_pre_head = VV.n

            def _(t, tt_h0=tt_h0, ss_h0=ss_h0, vcol=vcol, hb=hb,
                  vv_pre_head=vv_pre_head):
                t.wait_ge(VV.h, vv_pre_head)
                t.wait_ge(SS.h, ss_h0)
                for kc in range(32):
                    for h2 in range(2):
                        if kc >= 1:
                            t.wait_ge(SS.h, ss_h0 + 2 * (kc - 1) + h2 + 1)
                        for q4 in range(2):
                            qo = h2 * 1024 + q4 * 512
                            ins = t.matmul(PA[:, qo:qo + 512],
                                           kfm[hb:hb + 64, kc * 128:(kc + 1) * 128],
                                           qfm[hb:hb + 64, qo:qo + 512],
                                           start=True, stop=True)
                        TT.inc(ins)
                    for h2 in range(2):
                        t.wait_ge(SS.h, ss_h0 + 2 * kc + h2 + 1)
                        for q4 in range(2):
                            qo = h2 * 1024 + q4 * 512
                            ins = t.matmul(PB[:, qo:qo + 512],
                                           v65[:, kc * 256 + vcol:kc * 256 + vcol + 128],
                                           pb_sb[:, qo:qo + 512],
                                           start=(kc == 0), stop=(kc == 31))
                        TT.inc(ins)
            blk.tensor(_)
            tt_av_last = TT.n

            def _(s, tt_h0=tt_h0):
                for kc in range(32):
                    for h2 in range(2):
                        s.wait_ge(TT.h, tt_h0 + 4 * kc + h2 + 1)
                        SS.inc(s.activation(pb_sb[:, h2 * 1024:(h2 + 1) * 1024],
                                            PA[:, h2 * 1024:(h2 + 1) * 1024],
                                            AF.Exp, scale=0.125))
            blk.scalar(_)

            def _(v, tt_av_last=tt_av_last, c=c, hh=hh):
                v.wait_ge(TT.h, tt_av_last)
                with nc.allow_low_precision("bf16 recip of softmax sums"):
                    VV.inc(v.reciprocal(recb[:], PB[64:128, :]))
                v.wait_ge(VV.h, VV.n)
                VV.inc(v.tensor_mul(ofm[hh * 64:(hh + 1) * 64, c * 2048:(c + 1) * 2048],
                                    PB[0:64, :], recb[:]))
            blk.vector(_)

        tt_group_end = TT.n
        vv_group_end = VV.n

    close(13)

    wob = [scope(f"wob{i}", [128, 1024], BF16) for i in range(8)]
    x0ld = [scope(f"x0ld{i}", [128, TOWN], F32) for i in range(2)]
    x1t = [scope(f"x1t{i}", [128, TOWN], F32) for i in range(2)]
    f32t = scope("f32t", [128, TOWN], F32)

    rclose(8)

    wo_r = woD.rearrange("(hc p) j -> p hc j", p=128)

    def _(g, tt_group_end=tt_group_end, vv_pre=VV.n, ss_pre=SS.n):
        g.wait_ge(TT.h, tt_group_end)
        g.wait_ge(VV.h, vv_pre)
        g.wait_ge(SS.h, ss_pre)
        gWOB.put(g, [(wob[hdc][:], wo_r[:, hdc, :]) for hdc in range(8)])
    blk.gpsimd(_)

    vv_cmb0 = VV.n
    tt_wo0 = TT.n

    def _(sy, tt_pre=TT.n, ss_pre=SS.n):
        sy.wait_ge(TT.h, tt_pre)
        sy.wait_ge(SS.h, ss_pre)
        sy.wait_ge(VV.h, vv_cmb0)
        for dc in range(NDC):
            if dc >= 2:
                sy.wait_ge(VV.h, vv_cmb0 + 3 * (dc - 1))
            sX0.put(sy, [(x0ld[dc % 2][:], x0fm[dc * 128:(dc + 1) * 128, 0:TOWN])])
    blk.sync(_)

    def _(t, vv_group_end=vv_group_end):
        gWOB.wait(t)
        t.wait_ge(VV.h, vv_group_end)
        for dco in range(NDC):
            if dco >= 2:
                t.wait_ge(VV.h, vv_cmb0 + 3 * (dco - 1))
            ps = PA if dco % 2 == 0 else PB
            for q4 in range(4):
                for hdc in range(8):
                    ins = t.matmul(ps[:, q4 * 512:(q4 + 1) * 512],
                                   wob[hdc][:, dco * 128:(dco + 1) * 128],
                                   ofm[:, hdc * 2048 + q4 * 512:hdc * 2048 + (q4 + 1) * 512],
                                   start=(hdc == 0), stop=(hdc == 7))
            TT.inc(ins)
    blk.tensor(_)

    def _(g):
        for dco in range(NDC):
            g.wait_ge(VV.h, vv_cmb0 + 3 * (dco + 1))
            gX1D.put(g, [(x1d[dco * 128:(dco + 1) * 128, :], x1t[dco % 2][:])])
        gX1D.wait(g)
        CC.inc(g.collective_compute(
            "AllGather", mybir.AluOpType.bypass,
            replica_groups=[[0, 1], [2, 3], [4, 5], [6, 7]],
            ins=[x1d.opt()], outs=[x1g.opt()]))
    blk.gpsimd(_)

    def _(v):
        for dco in range(NDC):
            v.wait_ge(TT.h, tt_wo0 + dco + 1)
            sX0.wait(v, dco)
            if dco >= 2:
                gX1D.wait(v, dco - 2)
            ps = PA if dco % 2 == 0 else PB
            v.wait_ge(VV.h, VV.n)
            VV.inc(v.tensor_copy(f32t[:], ffnacc[dco][:]))
            v.wait_ge(VV.h, VV.n)
            VV.inc(v.tensor_add(f32t[:], f32t[:], ps[:]))
            v.wait_ge(VV.h, VV.n)
            VV.inc(v.tensor_add(x1t[dco % 2][:], f32t[:], x0ld[dco % 2][:]))
    blk.vector(_)
    vv_x1done = VV.n

    close(len(guards))

    nPre = len(guards)
    xnp = [scope(f"xnp{i}", [128, TOWN], BF16) for i in range(NDC)]
    w_tmP = scope("w_tmP", [128, 128], BF16)
    wbv8P = scope("wbv8P", [128, 8 * TOWN], BF16)
    xf32 = [scope(f"xf32{i}", [128, TOWN], F32) for i in range(2)]
    wrF32 = scope("wrF32", [128, 64], F32)
    fAP = scope("fAP", [128, 2048], F32)
    fBP = scope("fBP", [128, 2048], F32)
    fCP = scope("fCP", [128, 2048], F32)
    sqdP = [scope(f"sqdP{i}", [128, 2048], BF16) for i in range(2)]
    muP = scope("muP", [128, 2048], BF16)
    rsP = scope("rsP", [128, 2048], BF16)
    wrsbP = scope("wrsbP", [128, 64], BF16)
    br8P = scope("br8P", [8, 1], F32)
    scbfP = scope("scbfP", [8, TOWN], BF16)
    s_tmP = scope("s_tmP", [128, 128], F32)
    sAP = scope("sAP", [128, 128], F32)
    sBP = scope("sBP", [128, 128], F32)
    m16P = scope("m16P", [128, 64], F32)
    wrowP = scope("wrowP", [1, TOWN], BF16)

    vv_cp0 = VV.n

    def _(sy, tt_pre=TT.n, vv_pre=VV.n, ss_pre=SS.n):
        sy.wait_ge(TT.h, tt_pre)
        sy.wait_ge(VV.h, vv_pre)
        sy.wait_ge(SS.h, ss_pre)
        gX1D.wait(sy)
        sMS.put(sy, [(br8P[:], br8D), (identb[:], idD)])
        sWRL.put(sy, [(wrF32[:].rearrange("p (dc e) -> p dc e", dc=8),
                       wrD.rearrange("(dc p) e -> p dc e", p=128))])
        for dc in range(NDC):
            if dc >= 2:
                sy.wait_ge(VV.h, vv_cp0 + dc - 1)
            sX1R.put(sy, [(xf32[dc % 2][:], x1d[dc * 128:(dc + 1) * 128, :])])
    blk.sync(_)

    def _(v):
        for dc in range(NDC):
            sX1R.wait(v, dc)
            v.wait_ge(VV.h, VV.n)
            VV.inc(v.tensor_copy(xnp[dc][:], xf32[dc % 2][:]))
        sWRL.wait(v)
        v.wait_ge(VV.h, VV.n)
        VV.inc(v.tensor_copy(wrsbP[:], wrF32[:]))
    blk.vector(_)

    tt_bP = TT.n
    vv_bP = VV.n

    def _(v, tt_bP=tt_bP):
        for dc in range(NDC):
            if dc >= 2:
                v.wait_ge(TT.h, tt_bP + dc - 1)
            v.wait_ge(VV.h, VV.n)
            VV.inc(v.tensor_mul(sqdP[dc % 2][:], xnp[dc][:], xnp[dc][:]))
    blk.vector(_)

    def _(t, vv_bP=vv_bP, ss_pre=SS.n):
        t.wait_ge(SS.h, ss_pre)
        for dc in range(NDC):
            t.wait_ge(VV.h, vv_bP + dc + 1)
            for q4 in range(4):
                t.matmul(PA[:, q4 * 512:(q4 + 1) * 512], ones128[:],
                         xnp[dc][:, q4 * 512:(q4 + 1) * 512],
                         start=(dc == 0), stop=(dc == NDC - 1))
            for q4 in range(4):
                ins = t.matmul(PB[:, q4 * 512:(q4 + 1) * 512], ones128[:],
                               sqdP[dc % 2][:, q4 * 512:(q4 + 1) * 512],
                               start=(dc == 0), stop=(dc == NDC - 1))
            TT.inc(ins)
    blk.tensor(_)
    tt_s1P = TT.n

    def _(v, tt_s1P=tt_s1P):
        v.wait_ge(TT.h, tt_s1P)
        VV.inc(v.tensor_scalar_mul(fAP[:], PA[:], 1.0 / D))
        VV.inc(v.tensor_scalar_mul(fBP[:], PB[:], 1.0 / D))
        v.wait_ge(VV.h, VV.n)
        VV.inc(v.tensor_mul(fCP[:], fAP[:], fAP[:]))
        v.wait_ge(VV.h, VV.n)
        VV.inc(v.tensor_sub(fBP[:], fBP[:], fCP[:]))
    blk.vector(_)
    vv_varP = VV.n

    def _(s, vv_varP=vv_varP):
        s.wait_ge(VV.h, vv_varP)
        SS.inc(s.activation(fCP[:], fBP[:], AF.Sqrt, bias=epsb[:]))
    blk.scalar(_)
    ss_sdP = SS.n

    def _(v, ss_sdP=ss_sdP):
        v.wait_ge(SS.h, ss_sdP)
        VV.inc(v.reciprocal(fBP[:], fCP[:]))
        VV.inc(v.tensor_copy(muP[:], fAP[:]))
        v.wait_ge(VV.h, VV.n)
        VV.inc(v.tensor_copy(rsP[:], fBP[:]))
    blk.vector(_)

    vv_apP = VV.n

    def _(v):
        for dc in range(NDC):
            v.wait_ge(VV.h, VV.n)
            VV.inc(v.tensor_sub(xnp[dc][:], xnp[dc][:], muP[:]))
            v.wait_ge(VV.h, VV.n)
            VV.inc(v.tensor_mul(xnp[dc][:], xnp[dc][:], rsP[:]))
    blk.vector(_)
    vv_xnP = VV.n

    def _(sy, vv_apP=vv_apP):
        for dc in range(NDC):
            sy.wait_ge(VV.h, vv_apP + 2 * dc + 2)
            sXNS.put(sy, [(xn1d[dc], xnp[dc][:])])
    blk.sync(_)

    def _(t, vv_xnP=vv_xnP):
        t.wait_ge(VV.h, vv_xnP)
        for q4 in range(4):
            for dc in range(NDC):
                ins = t.matmul(PA[0:8, q4 * 512:(q4 + 1) * 512],
                               wrsbP[:, dc * 8:(dc + 1) * 8],
                               xnp[dc][:, q4 * 512:(q4 + 1) * 512],
                               start=(dc == 0), stop=(dc == NDC - 1))
        TT.inc(ins)
    blk.tensor(_)
    tt_scP = TT.n

    def _(s, tt_scP=tt_scP):
        s.wait_ge(TT.h, tt_scP)
        sMS.wait(s)
        SS.inc(s.activation(scbfP[:], PA[0:8, :], AF.Identity, bias=br8P[:]))
    blk.scalar(_)
    ss_scP = SS.n

    def _(t, ss_scP=ss_scP):
        t.wait_ge(SS.h, ss_scP)
        for tcw in range(16):
            ins = t.matmul(PB[:, tcw * 128:tcw * 128 + 8],
                           scbfP[:, tcw * 128:(tcw + 1) * 128],
                           identb[0:8, 0:8], start=True, stop=True)
        TT.inc(ins)
    blk.tensor(_)
    tt_trP = TT.n

    def _(v, tt_trP=tt_trP):
        v.wait_ge(TT.h, tt_trP)
        for tcw in range(16):
            sl = slice(tcw * 8, (tcw + 1) * 8)
            VV.inc(v.tensor_copy(s_tmP[:, sl], PB[:, tcw * 128:tcw * 128 + 8]))
        for tcw in range(16):
            sl = slice(tcw * 8, (tcw + 1) * 8)
            v.wait_ge(VV.h, VV.n)
            VV.inc(v.reduce_max(m16P[:, tcw * 4:tcw * 4 + 1], s_tmP[:, sl], axis=AX.X))
            v.wait_ge(VV.h, VV.n)
            VV.inc(v.tensor_scalar(sAP[:, sl], s_tmP[:, sl], m16P[:, tcw * 4:tcw * 4 + 1],
                                   None, ALU.is_ge))
            v.wait_ge(VV.h, VV.n)
            VV.inc(v.scalar_tensor_tensor(sBP[:, sl], sAP[:, sl], -1e30, s_tmP[:, sl],
                                          ALU.mult, ALU.add))
            v.wait_ge(VV.h, VV.n)
            VV.inc(v.reduce_max(m16P[:, tcw * 4 + 1:tcw * 4 + 2], sBP[:, sl], axis=AX.X))
            v.wait_ge(VV.h, VV.n)
            VV.inc(v.tensor_scalar(sAP[:, sl], s_tmP[:, sl],
                                   m16P[:, tcw * 4 + 1:tcw * 4 + 2], None, ALU.is_ge))
            v.wait_ge(VV.h, VV.n)
            VV.inc(v.tensor_scalar(sBP[:, sl], s_tmP[:, sl], m16P[:, tcw * 4:tcw * 4 + 1],
                                   None, ALU.subtract))
    blk.vector(_)
    vv_t2P = VV.n

    def _(s, vv_t2P=vv_t2P):
        s.wait_ge(VV.h, vv_t2P)
        SS.inc(s.activation(sBP[:], sBP[:], AF.Exp))
    blk.scalar(_)
    ss_expP = SS.n

    def _(v, ss_expP=ss_expP):
        v.wait_ge(SS.h, ss_expP)
        for tcw in range(16):
            sl = slice(tcw * 8, (tcw + 1) * 8)
            v.wait_ge(VV.h, VV.n)
            VV.inc(v.tensor_mul(sBP[:, sl], sBP[:, sl], sAP[:, sl]))
            v.wait_ge(VV.h, VV.n)
            VV.inc(v.reduce_sum(m16P[:, tcw * 4 + 2:tcw * 4 + 3], sBP[:, sl], axis=AX.X))
            v.wait_ge(VV.h, VV.n)
            VV.inc(v.reciprocal(m16P[:, tcw * 4 + 3:tcw * 4 + 4],
                                m16P[:, tcw * 4 + 2:tcw * 4 + 3]))
            v.wait_ge(VV.h, VV.n)
            VV.inc(v.tensor_scalar(w_tmP[:, sl], sBP[:, sl],
                                   m16P[:, tcw * 4 + 3:tcw * 4 + 4], None, ALU.mult))
    blk.vector(_)
    vv_wP = VV.n

    tt_r0 = TT.n
    vv_r0 = VV.n

    def _(t, vv_wP=vv_wP, tt_r0=tt_r0, vv_r0=vv_r0):
        t.wait_ge(VV.h, vv_wP)
        for i in range(NEXP):
            if i >= 1:
                t.wait_ge(VV.h, vv_r0 + 2 * i - 1)
            for tcw in range(16):
                ins = t.matmul(PA[0:1, tcw * 128:(tcw + 1) * 128],
                               w_tmP[:, tcw * 8 + i:tcw * 8 + i + 1], identb[:],
                               start=True, stop=True)
            TT.inc(ins)
            t.wait_ge(VV.h, vv_r0 + 2 * i + 1)
            for q4 in range(4):
                ins = t.matmul(PB[0:128, q4 * 512:(q4 + 1) * 512],
                               ones128[0:1, 0:128],
                               wrowP[0:1, q4 * 512:(q4 + 1) * 512],
                               start=True, stop=True)
            TT.inc(ins)
    blk.tensor(_)

    def _(v, tt_r0=tt_r0):
        for i in range(NEXP):
            v.wait_ge(TT.h, tt_r0 + 2 * i + 1)
            VV.inc(v.tensor_copy(wrowP[:], PA[0:1, 0:2048]))
            v.wait_ge(TT.h, tt_r0 + 2 * i + 2)
            VV.inc(v.tensor_copy(wbv8P[:, i * TOWN:(i + 1) * TOWN], PB[:, 0:2048]))
    blk.vector(_)

    tt_rowsE = TT.n
    vv_bcP = VV.n

    def _(sy, vv_r0=vv_r0):
        for i in range(NEXP):
            sy.wait_ge(VV.h, vv_r0 + 2 * i + 2)
            sWBS.put(sy, [(wbv8d[:, i * TOWN:(i + 1) * TOWN],
                           wbv8P[:, i * TOWN:(i + 1) * TOWN])])
    blk.sync(_)

    close(18)
    h1mP = scope("h1mP", [128, 32 * 512], BF16)
    w1cP = [scope(f"w1cP{i}", [128, 4096], BF16) for i in range(2)]
    w2cP = [scope(f"w2cP{i}", [128, 4096], BF16) for i in range(2)]
    etmpP = [scope(f"etmpP{i}", [128, 512], F32) for i in range(2)]
    e0bf = scope("e0bf", [128, 4096], BF16)
    b1aP = scope("b1aP", [128, 32], F32)
    b2aP = scope("b2aP", [128, 8], F32)

    def _(sy, tt_rE=tt_rowsE, vv_bcP=vv_bcP):
        sy.wait_ge(TT.h, tt_rE)
        sy.wait_ge(VV.h, vv_bcP)
        gWPP.wait(sy)
        sMS.put(sy, [(b1aP[:], b1aD[:, 0:32]), (b2aP[:], b2aD[:, 0:8])])
    blk.sync(_)

    ss_pevP = SS.n
    for tb in range(4):
        ts = tb * 512
        tt_f1 = TT.n
        ss_f1 = SS.n
        w1base = len(sW1P.marks)

        def _(sy, tt_f1=tt_f1):
            for gr in range(8):
                if gr >= 2:
                    sy.wait_ge(TT.h, tt_f1 + 4 * (gr - 1))
                sW1P.put(sy, [(w1cP[gr % 2][:].rearrange("p (c e) -> p c e", c=4),
                               w1bd[0, 4 * gr:4 * gr + 4].rearrange("c p e -> p c e"))])
        blk.sync(_)

        def _(t, ts=ts, ss_f1=ss_f1, w1base=w1base, ss_pe=ss_pevP,
              vv_bcP=vv_bcP, tb=tb):
            if tb == 0:
                t.wait_ge(VV.h, vv_bcP)
            for hc in range(32):
                sW1P.wait(t, w1base + hc // 4)
                if hc < 2:
                    t.wait_ge(SS.h, max(0, ss_pe - 7 + hc))
                if hc >= 2:
                    t.wait_ge(SS.h, ss_f1 + hc - 1)
                po = (hc % 2) * 512
                for dc in range(NDC):
                    ins = t.matmul(PA[:, po:po + 512],
                                   w1cP[(hc // 4) % 2][:, (hc % 4) * 1024 + dc * 128:
                                                       (hc % 4) * 1024 + (dc + 1) * 128],
                                   xnp[dc][:, ts:ts + 512],
                                   start=(dc == 0), stop=(dc == NDC - 1))
                TT.inc(ins)
        blk.tensor(_)

        def _(s, tt_f1=tt_f1, ss_f1=ss_f1, tb=tb):
            if tb == 0:
                sMS.wait(s)
            for hc in range(32):
                s.wait_ge(TT.h, tt_f1 + hc + 1)
                po = (hc % 2) * 512
                SS.inc(s.activation(h1mP[:, hc * 512:(hc + 1) * 512], PA[:, po:po + 512],
                                    AF.Relu, bias=b1aP[:, hc:hc + 1]))
        blk.scalar(_)
        ss_f1_done = SS.n

        tt_f2 = TT.n
        w2base = len(sW2P.marks)

        def _(sy, tt_f2=tt_f2):
            for gr in range(8):
                if gr >= 2:
                    sy.wait_ge(TT.h, tt_f2 + 4 * (gr - 1))
                sW2P.put(sy, [(w2cP[gr % 2][:].rearrange("p (c e) -> p c e", c=4),
                               w2bd[0, 4 * gr * 128:(4 * gr + 4) * 128, :].rearrange(
                                   "(c p) e -> p c e", p=128))])
        blk.sync(_)

        def _(t, ss_f1_done=ss_f1_done, ss_pe=ss_pevP, w2base=w2base):
            t.wait_ge(SS.h, max(ss_f1_done, ss_pe))
            for hc in range(32):
                sW2P.wait(t, w2base + hc // 4)
                for ec in range(8):
                    ps = PA if ec < 4 else PB
                    ins = t.matmul(ps[:, (ec % 4) * 512:(ec % 4 + 1) * 512],
                                   w2cP[(hc // 4) % 2][:, (hc % 4) * 1024 + ec * 128:
                                                       (hc % 4) * 1024 + (ec + 1) * 128],
                                   h1mP[:, hc * 512:(hc + 1) * 512],
                                   start=(hc == 0), stop=(hc == 31))
                TT.inc(ins)
        blk.tensor(_)
        tt_f2_done = TT.n

        def _(s, tt_f2_done=tt_f2_done, vv_pre=VV.n):
            s.wait_ge(TT.h, tt_f2_done)
            s.wait_ge(VV.h, vv_pre)
            for ec in range(8):
                if ec >= 2:
                    s.wait_ge(VV.h, vv_pre + (ec - 1))
                ps = PA if ec < 4 else PB
                SS.inc(s.activation(etmpP[ec % 2][:],
                                    ps[:, (ec % 4) * 512:(ec % 4 + 1) * 512],
                                    AF.Identity,
                                    bias=b2aP[:, ec:ec + 1]))
        blk.scalar(_)
        ss_ev0 = SS.n - 8

        def _(v, ss_ev0=ss_ev0, ts=ts, tb=tb):
            if tb >= 1:
                gE0S.wait(v, tb - 1)
            for ec in range(8):
                v.wait_ge(SS.h, ss_ev0 + ec + 1)
                v.wait_ge(VV.h, VV.n)
                VV.inc(v.tensor_mul(e0bf[:, ec * 512:(ec + 1) * 512], etmpP[ec % 2][:],
                                    wbv8P[:, ts:ts + 512]))
        blk.vector(_)
        vv_e0 = VV.n

        def _(g, vv_e0=vv_e0, tb=tb):
            g.wait_ge(VV.h, vv_e0)
            gE0S.put(g, [(e0d[tb].rearrange("ec p e -> p ec e"),
                          e0bf[:].rearrange("p (ec e) -> p ec e", ec=8))])
        blk.gpsimd(_)

        ss_pevP = SS.n

    close(len(guards) - nPre)

    kfmB = scope("kfmB", [64, T], BF16)
    qfmB = scope("qfmB", [64, T], BF16)
    ktm = scope("ktm", [128, 32 * 64], BF16)
    bvb = scope("bvbS", [128, D], F32)
    bkb = scope("bkbS", [128, 64], F32)
    bk = scope("bkS", [64, 1], F32)
    bq = scope("bqS", [64, 1], F32)
    gma = scope("gma", [128, 8], F32)
    bta = scope("bta", [128, 8], F32)
    mUe = scope("mUeS", [128, 128], BF16)
    mLe = scope("mLeS", [128, 128], BF16)
    mUn = scope("mUnS", [128, 128], BF16)
    Sst = scope("Sst", [64, D], F32)
    Sbf = scope("Sbf", [64, D], BF16)
    nUt = scope("nUt", [128, 128], F32)
    nLt = scope("nLt", [128, 128], F32)
    Cbuf = scope("Cbuf", [128, 6 * 128], F32)
    Ta = scope("Ta", [128, 128], F32)
    Tb = scope("Tb", [128, 128], F32)
    Rs = scope("Rs", [128, 128], BF16)
    brawS = scope("brawS", [128, D], BF16)
    Utm = scope("Utm", [128, D], BF16)
    pmS = scope("pmS", [128, 128], BF16)
    vld = [scope(f"vld{i}", [128, D], BF16) for i in range(2)]
    vev = [scope(f"vev{i}", [128, D], BF16) for i in range(2)]
    nB_base = len(guards)

    def _(sy, tt_pre=TT.n, vv_pre=VV.n, ss_pre=SS.n):
        sy.wait_ge(TT.h, tt_pre)
        sy.wait_ge(VV.h, vv_pre)
        sy.wait_ge(SS.h, ss_pre)
        sXNS.wait(sy)
        sWBS.wait(sy)
        sW1P.wait(sy)
        sW2P.wait(sy)
        gE0S.wait(sy)
        sMS.put(sy, [(bvb[:], bvbD), (bkb[:], bkbD), (bk[:], bkD), (bq[:], bqD),
                     (gma[:], gbD), (bta[:], btD), (mUe[:], mUeD), (mLe[:], mLeD),
                     (mUn[:], mUnD), (identb[:], idD)])
    blk.sync(_)

    def _(v):
        v.wait_ge(VV.h, VV.n)
        sXNS.wait(v)
        sWBS.wait(v)
        sW1P.wait(v)
        sW2P.wait(v)
        gE0S.wait(v)
        v.memset(Sst[:], 0.0)
        VV.inc(v.memset(Sbf[:], 0.0))
    blk.vector(_)

    x1bf = [scope(f"x1b{i}", [128, T], BF16) for i in range(NDC)]
    mu_bfB = scope("mu_bfB", [128, T], BF16)
    f_bfB = scope("f_bfB", [128, T], BF16)

    def _(g, tt_pre=TT.n, vv_pre=VV.n, ss_pre=SS.n):
        g.wait_ge(CC.h, 1)
        sXNS.wait(g)
        sWBS.wait(g)
        sW1P.wait(g)
        sW2P.wait(g)
        gE0S.wait(g)
        g.wait_ge(TT.h, tt_pre)
        g.wait_ge(VV.h, vv_pre)
        g.wait_ge(SS.h, ss_pre)
        for dc in range(NDC):
            gX1.put(g, [(x1bf[dc][:, 0:TOWN], x1g[0, dc * 128:(dc + 1) * 128, :]),
                        (x1bf[dc][:, TOWN:T], x1g[1, dc * 128:(dc + 1) * 128, :])])
    blk.gpsimd(_)

    fAB = scope("fAB", [128, 2048], F32)
    fBB = scope("fBB", [128, 2048], F32)
    fCB = scope("fCB", [128, 2048], F32)
    fDB = scope("fDB", [128, 2048], F32)
    sqdB = [scope(f"sqdB{i}", [128, 2048], BF16) for i in range(2)]

    for tb in range(2):
        ts = tb * 2048
        tt_b = TT.n
        vv_b = VV.n

        def _(v, ts=ts, tt_b=tt_b):
            gX1.wait(v)
            for dc in range(NDC):
                if dc >= 2:
                    v.wait_ge(TT.h, tt_b + dc - 1)
                v.wait_ge(VV.h, VV.n)
                VV.inc(v.tensor_mul(sqdB[dc % 2][:], x1bf[dc][:, ts:ts + 2048],
                                    x1bf[dc][:, ts:ts + 2048]))
        blk.vector(_)

        def _(t, ts=ts, vv_b=vv_b, ss_pre=SS.n):
            t.wait_ge(SS.h, ss_pre)
            for dc in range(NDC):
                t.wait_ge(VV.h, vv_b + dc + 1)
                for q4 in range(4):
                    t.matmul(PA[:, q4 * 512:(q4 + 1) * 512], ones128[:],
                             x1bf[dc][:, ts + q4 * 512:ts + (q4 + 1) * 512],
                             start=(dc == 0), stop=(dc == NDC - 1))
                for q4 in range(4):
                    ins = t.matmul(PB[:, q4 * 512:(q4 + 1) * 512], ones128[:],
                                   sqdB[dc % 2][:, q4 * 512:(q4 + 1) * 512],
                                   start=(dc == 0), stop=(dc == NDC - 1))
                TT.inc(ins)
        blk.tensor(_)
        tt_s1 = TT.n

        def _(v, tt_s1=tt_s1):
            v.wait_ge(TT.h, tt_s1)
            VV.inc(v.tensor_scalar_mul(fAB[:], PA[:], 1.0 / D))
            VV.inc(v.tensor_scalar_mul(fBB[:], PB[:], 1.0 / D))
            v.wait_ge(VV.h, VV.n)
            VV.inc(v.tensor_mul(fCB[:], fAB[:], fAB[:]))
            v.wait_ge(VV.h, VV.n)
            VV.inc(v.tensor_sub(fBB[:], fBB[:], fCB[:]))
        blk.vector(_)
        vv_var = VV.n

        def _(s, vv_var=vv_var):
            s.wait_ge(VV.h, vv_var)
            SS.inc(s.activation(fCB[:], fBB[:], AF.Sqrt, bias=epsb[:]))
        blk.scalar(_)
        ss_sd = SS.n

        def _(v, ss_sd=ss_sd):
            v.wait_ge(SS.h, ss_sd)
            VV.inc(v.reciprocal(fDB[:], fCB[:]))
            v.wait_ge(VV.h, VV.n)
            VV.inc(v.tensor_mul(fCB[:], fDB[:], fDB[:]))
            v.wait_ge(VV.h, VV.n)
            VV.inc(v.tensor_mul(fCB[:], fBB[:], fCB[:]))
        blk.vector(_)
        vv_vn = VV.n

        def _(s, vv_vn=vv_vn):
            s.wait_ge(VV.h, vv_vn)
            SS.inc(s.activation(fCB[:], fCB[:], AF.Sqrt, bias=epsb[:]))
        blk.scalar(_)
        ss_sdn = SS.n

        def _(v, ts=ts, ss_sdn=ss_sdn):
            v.wait_ge(SS.h, ss_sdn)
            VV.inc(v.reciprocal(fCB[:], fCB[:]))
            v.wait_ge(VV.h, VV.n)
            VV.inc(v.tensor_mul(fCB[:], fCB[:], fDB[:]))
            v.wait_ge(VV.h, VV.n)
            VV.inc(v.tensor_copy(mu_bfB[:, ts:ts + 2048], fAB[:]))
            v.wait_ge(VV.h, VV.n)
            VV.inc(v.tensor_copy(f_bfB[:, ts:ts + 2048], fCB[:]))
        blk.vector(_)

    def _(v):
        sMS.wait(v)
        for dc in range(NDC):
            v.wait_ge(VV.h, VV.n)
            VV.inc(v.tensor_sub(x1bf[dc][:], x1bf[dc][:], mu_bfB[:]))
            v.wait_ge(VV.h, VV.n)
            VV.inc(v.tensor_mul(x1bf[dc][:], x1bf[dc][:], f_bfB[:]))
            v.wait_ge(VV.h, VV.n)
            VV.inc(v.tensor_scalar(x1bf[dc][:], x1bf[dc][:], gma[:, dc:dc + 1],
                                   bta[:, dc:dc + 1], ALU.mult, ALU.add))
    blk.vector(_)
    vv_h = VV.n

    close(6)
    h = x1bf

    wvb = scope("wvb", [128, 8 * D], BF16)
    wkb = scope("wkb", [128, 8 * 64], BF16)
    wqb = scope("wqb", [128, 8 * 64], BF16)

    def _(g, ss_pre=SS.n):
        g.wait_ge(VV.h, vv_h)
        g.wait_ge(SS.h, ss_pre)
        gWP.put(g, [(wvb[:].rearrange("p (dc e) -> p dc e", dc=8),
                     wvtD.rearrange("(dc p) e -> p dc e", p=128)),
                    (wkb[:].rearrange("p (dc e) -> p dc e", dc=8),
                     wktD.rearrange("(dc p) e -> p dc e", p=128)),
                    (wqb[:].rearrange("p (dc e) -> p dc e", dc=8),
                     wqtD.rearrange("(dc p) e -> p dc e", p=128))])
    blk.gpsimd(_)

    def _(t, ss_pre=SS.n):
        t.wait_ge(VV.h, vv_h)
        gWP.wait(t)
        for th in range(2):
            ts = th * 2048
            if th >= 1:
                t.wait_ge(SS.h, ss_pre + 2 * th - 1)
            for q4 in range(4):
                for dc in range(NDC):
                    ins = t.matmul(PA[0:64, q4 * 512:(q4 + 1) * 512],
                                   wkb[:, dc * 64:(dc + 1) * 64],
                                   h[dc][:, ts + q4 * 512:ts + (q4 + 1) * 512],
                                   start=(dc == 0), stop=(dc == NDC - 1))
            TT.inc(ins)
            if th >= 1:
                t.wait_ge(SS.h, ss_pre + 2 * th)
            for q4 in range(4):
                for dc in range(NDC):
                    ins = t.matmul(PB[0:64, q4 * 512:(q4 + 1) * 512],
                                   wqb[:, dc * 64:(dc + 1) * 64],
                                   h[dc][:, ts + q4 * 512:ts + (q4 + 1) * 512],
                                   start=(dc == 0), stop=(dc == NDC - 1))
            TT.inc(ins)
    blk.tensor(_)
    tt_kq0 = TT.n - 4

    def _(s):
        sMS.wait(s)
        for th in range(2):
            ts = th * 2048
            s.wait_ge(TT.h, tt_kq0 + 2 * th + 1)
            SS.inc(s.activation(kfmB[:, ts:ts + 2048], PA[0:64, :], AF.Identity,
                                bias=bk[:]))
            s.wait_ge(TT.h, tt_kq0 + 2 * th + 2)
            SS.inc(s.activation(qfmB[:, ts:ts + 2048], PB[0:64, :], AF.Identity,
                                bias=bq[:]))
    blk.scalar(_)
    ss_kq = SS.n

    tt_kv0 = TT.n
    vv_kv0 = VV.n

    def _(t):
        t.wait_ge(SS.h, ss_kq)
        for tc in range(32):
            if tc >= 2:
                t.wait_ge(VV.h, vv_kv0 + 2 * (tc - 1))
            pok = (tc % 2) * 512
            for dc in range(NDC):
                ins = t.matmul(PA[:, pok:pok + 64],
                               h[dc][:, tc * 128:(tc + 1) * 128],
                               wkb[:, dc * 64:(dc + 1) * 64],
                               start=(dc == 0), stop=(dc == NDC - 1))
            TT.inc(ins)
            pov = (tc % 2) * 1024
            for j in range(2):
                for dc in range(NDC):
                    ins = t.matmul(PB[:, pov + j * 512:pov + (j + 1) * 512],
                                   h[dc][:, tc * 128:(tc + 1) * 128],
                                   wvb[:, dc * 1024 + j * 512:dc * 1024 + (j + 1) * 512],
                                   start=(dc == 0), stop=(dc == NDC - 1))
            TT.inc(ins)
    blk.tensor(_)

    def _(g, vv_kv0=vv_kv0):
        for tc in range(32):
            g.wait_ge(VV.h, vv_kv0 + 2 * (tc + 1))
            gVD.put(g, [(vd[tc], vev[tc % 2][:])])
    blk.gpsimd(_)

    def _(v):
        for tc in range(32):
            v.wait_ge(TT.h, tt_kv0 + 2 * tc + 1)
            VV.inc(v.tensor_add(ktm[:, tc * 64:(tc + 1) * 64], PA[:, (tc % 2) * 512:(tc % 2) * 512 + 64], bkb[:]))
            v.wait_ge(TT.h, tt_kv0 + 2 * tc + 2)
            if tc >= 2:
                gVD.wait(v, tc - 2)
            pov = (tc % 2) * 1024
            VV.inc(v.tensor_add(vev[tc % 2][:], PB[:, pov:pov + 1024], bvb[:]))
    blk.vector(_)
    vv_kv_done = VV.n

    close(3)
    close(10)

    yfm = [rscope(f"yfm{i}", [128, T], BF16) for i in range(NDC)]

    Rs32 = rscope("Rs32", [128, 32 * 128], BF16)
    nUt2 = [scope(f"nUt2{i}", [128, 128], F32) for i in range(2)]
    nLt2 = [scope(f"nLt2{i}", [128, 128], F32) for i in range(2)]
    Cb2 = [scope(f"Cb2{i}", [128, 6 * 128], F32) for i in range(2)]
    Ta2 = [scope(f"Ta2{i}", [128, 128], F32) for i in range(2)]
    Tb2 = [scope(f"Tb2{i}", [128, 128], F32) for i in range(2)]

    pBA = [PA[:, 0:128], PA[:, 1024:1152]]
    pBB = [PA[:, 512:640], PA[:, 1536:1664]]

    tt_p1 = TT.n
    vv_p1 = VV.n
    NT1, NV1 = 18, 25

    def _(t, vv_kv_done=vv_kv_done):
        t.wait_ge(VV.h, vv_kv_done)
        for tc in range(32):
            p = tc % 2
            tt0 = tt_p1 + NT1 * tc
            vv0 = vv_p1 + NV1 * tc
            ksl = kfmB[0:64, tc * 128:(tc + 1) * 128]
            nU, nL = nUt2[p], nLt2[p]
            Cb, TA, TB = Cb2[p], Ta2[p], Tb2[p]
            if tc >= 2:
                t.wait_ge(VV.h, vv_p1 + NV1 * (tc - 1))
            TT.inc(t.matmul(pBA[p], ksl, ksl, start=True, stop=True))
            for l in range(5):
                t.wait_ge(VV.h, vv0 + 2 + 3 * l)
                TT.inc(t.matmul(pBA[p], nL[:], nU[:], start=True, stop=True))
                TT.inc(t.matmul(pBB[p], nU[:], nL[:], start=True, stop=True))
            t.wait_ge(VV.h, vv0 + 17)
            TT.inc(t.matmul(pBA[p], nL[:], nU[:], start=True, stop=True))
            for idx, j in enumerate([5, 4, 3, 2, 1, 0]):
                t.wait_ge(VV.h, vv0 + 19 + idx)
                ps = pBB[p] if idx % 2 == 0 else pBA[p]
                rhs = [TA, TB][idx % 2]
                TT.inc(t.matmul(ps, Cb[:, j * 128:(j + 1) * 128], rhs[:],
                                start=True, stop=True))
    blk.tensor(_)

    def _(v):
        for tc in range(32):
            p = tc % 2
            tt0 = tt_p1 + NT1 * tc
            vv0 = vv_p1 + NV1 * tc
            nU, nL = nUt2[p], nLt2[p]
            Cb, TA, TB = Cb2[p], Ta2[p], Tb2[p]
            v.wait_ge(TT.h, tt0 + 1)
            VV.inc(v.tensor_mul(nU[:], pBA[p], mUe[:]))
            VV.inc(v.tensor_mul(nL[:], pBA[p], mLe[:]))
            for l in range(5):
                v.wait_ge(VV.h, VV.n)
                if l == 0:
                    VV.inc(v.tensor_sub(Cb[:, 0:128], identb[:], nL[:]))
                else:
                    VV.inc(v.tensor_add(Cb[:, l * 128:(l + 1) * 128], identb[:], nL[:]))
                v.wait_ge(TT.h, tt0 + 3 + 2 * l)
                v.wait_ge(VV.h, VV.n)
                VV.inc(v.tensor_copy(nU[:], pBA[p]))
                VV.inc(v.tensor_copy(nL[:], pBB[p]))
            v.wait_ge(VV.h, VV.n)
            VV.inc(v.tensor_add(Cb[:, 5 * 128:6 * 128], identb[:], nL[:]))
            v.wait_ge(TT.h, tt0 + 12)
            VV.inc(v.tensor_add(TA[:], identb[:], pBA[p]))
            for idx in range(6):
                v.wait_ge(TT.h, tt0 + 13 + idx)
                ps = pBB[p] if idx % 2 == 0 else pBA[p]
                if idx < 5:
                    dst = [TB, TA][idx % 2]
                    VV.inc(v.tensor_copy(dst[:], ps))
                else:
                    with nc.allow_low_precision("lam-scaled inverse factor"):
                        VV.inc(v.tensor_scalar_mul(Rs32[:, tc * 128:(tc + 1) * 128],
                                                   ps, TTT_LR))
    blk.vector(_)
    vv_p1_end = VV.n

    psKM = PA[:, 0:1024]
    psU = PA[:, 1024:2048]
    psY = PB[:, 0:1024]
    psP = PB[:, 1024:1152]
    psSK = PB[0:64, 1024:2048]

    tt_p2 = TT.n
    vv_p2 = VV.n
    NT2, NV2 = 6, 6

    def _(sy):
        for tc in range(32):
            if tc >= 2:
                sy.wait_ge(VV.h, vv_p2 + NV2 * (tc - 2) + 1)
            gVD.wait(sy, tc)
            sVT.put(sy, [(vld[tc % 2][:], vd[tc])])
    blk.sync(_)

    def _(t):
        t.wait_ge(VV.h, vv_p1_end)
        for tc in range(32):
            tt0 = tt_p2 + NT2 * tc
            vv0 = vv_p2 + NV2 * tc
            ksl = kfmB[0:64, tc * 128:(tc + 1) * 128]
            qsl = qfmB[0:64, tc * 128:(tc + 1) * 128]
            t.wait_ge(VV.h, vv0)
            for j in range(2):
                ins = t.matmul(psKM[:, j * 512:(j + 1) * 512], ksl,
                               Sbf[:, j * 512:(j + 1) * 512], start=True, stop=True)
            TT.inc(ins)
            t.wait_ge(VV.h, vv0 + 1)
            for j in range(2):
                ins = t.matmul(psU[:, j * 512:(j + 1) * 512],
                               Rs32[:, tc * 128:(tc + 1) * 128],
                               brawS[:, j * 512:(j + 1) * 512], start=True, stop=True)
            TT.inc(ins)
            for dc in range(NDC):
                ins = t.matmul(psY[:, dc * 128:(dc + 1) * 128],
                               Sbf[:, dc * 128:(dc + 1) * 128], qsl,
                               start=(dc % 4 == 0), stop=False)
            TT.inc(ins)
            TT.inc(t.matmul(psP, ksl, qsl, start=True, stop=True))
            t.wait_ge(VV.h, vv0 + 3)
            for dc in range(NDC):
                ins = t.matmul(psY[:, dc * 128:(dc + 1) * 128],
                               Utm[:, dc * 128:(dc + 1) * 128], pmS[:],
                               start=False, stop=(dc % 4 == 3))
            TT.inc(ins)
            for j in range(2):
                ins = t.matmul(psSK[:, j * 512:(j + 1) * 512],
                               ktm[:, tc * 64:(tc + 1) * 64],
                               Utm[:, j * 512:(j + 1) * 512], start=True, stop=True)
            TT.inc(ins)
    blk.tensor(_)

    def _(v):
        for tc in range(32):
            tt0 = tt_p2 + NT2 * tc
            vv0 = vv_p2 + NV2 * tc
            v.wait_ge(TT.h, tt0 + 1)
            sVT.wait(v, tc)
            VV.inc(v.tensor_sub(brawS[:], psKM, vld[tc % 2][:]))
            v.wait_ge(TT.h, tt0 + 2)
            VV.inc(v.tensor_copy(Utm[:], psU))
            v.wait_ge(TT.h, tt0 + 4)
            VV.inc(v.tensor_mul(pmS[:], psP, mUn[:]))
            v.wait_ge(TT.h, tt0 + 5)
            for dc in range(NDC):
                ins = v.tensor_copy(yfm[dc][:, tc * 128:(tc + 1) * 128],
                                    psY[:, dc * 128:(dc + 1) * 128])
            VV.inc(ins)
            v.wait_ge(TT.h, tt0 + 6)
            v.wait_ge(VV.h, VV.n)
            VV.inc(v.tensor_sub(Sst[:], Sst[:], psSK))
            v.wait_ge(VV.h, VV.n)
            VV.inc(v.tensor_copy(Sbf[:], Sst[:]))
    blk.vector(_)
    vv_scan_done = VV.n

    rclose(1)

    close(len(guards) - 0)
    x2acc = [scope(f"x2a{i}", [128, TOWN], F32) for i in range(NDC)]
    xn1bf = [scope(f"xn1b{i}", [128, TOWN], BF16) for i in range(NDC)]
    sel0 = scope("sel0S", [128, 1], F32)
    sel1 = scope("sel1S", [128, 1], F32)
    w1c = [scope(f"w1c{i}", [128, 4096], BF16) for i in range(2)]
    w2c = [scope(f"w2c{i}", [128, 4096], BF16) for i in range(2)]
    b1a = scope("b1aS", [128, 256], F32)
    b2a = scope("b2aS", [128, 64], F32)
    etmp = [scope(f"etmp{i}", [128, 512], F32) for i in range(2)]

    def _(sy, tt_pre=TT.n, vv_pre=VV.n, ss_pre=SS.n):
        sy.wait_ge(TT.h, tt_pre)
        sy.wait_ge(VV.h, vv_pre)
        sy.wait_ge(SS.h, ss_pre)
        sXNS.wait(sy)
        for dc in range(NDC):
            sXNL.put(sy, [(xn1bf[dc][:], xn1d[dc])])
        sMS.put(sy, [(sel0[:], sel0D), (sel1[:], sel1D),
                     (b1a[:], b1aD), (b2a[:], b2aD)])
    blk.sync(_)

    def _(g, tt_pre=TT.n, vv_pre=VV.n, ss_pre=SS.n):
        g.wait_ge(TT.h, tt_pre)
        g.wait_ge(VV.h, vv_pre)
        g.wait_ge(SS.h, ss_pre)
        gX1D.wait(g)
        for dc in range(NDC):
            sXC.put(g, [(x2acc[dc][:], x1d[dc * 128:(dc + 1) * 128, :])])
    blk.gpsimd(_)

    def _(t):
        sXNL.wait(t)
    blk.tensor(_)

    def _(s):
        sMS.wait(s)
    blk.scalar(_)

    def _(v, vv_scan_done=vv_scan_done):
        v.wait_ge(VV.h, vv_scan_done)
        sMS.wait(v)
        sXC.wait(v)
        sXNL.wait(v)
        for dc in range(NDC):
            v.wait_ge(VV.h, VV.n)
            VV.inc(v.tensor_add(x2acc[dc][:], x2acc[dc][:], xn1bf[dc][:]))
            v.wait_ge(VV.h, VV.n)
            VV.inc(v.scalar_tensor_tensor(x2acc[dc][:], yfm[dc][:, 0:TOWN], sel0[:],
                                          x2acc[dc][:], ALU.mult, ALU.add))
            v.wait_ge(VV.h, VV.n)
            VV.inc(v.scalar_tensor_tensor(x2acc[dc][:], yfm[dc][:, TOWN:T], sel1[:],
                                          x2acc[dc][:], ALU.mult, ALU.add))
    blk.vector(_)
    vv_xny = VV.n

    rclose(8)
    h1m = scope("h1m", [128, 32 * 512], BF16)
    wbv8 = scope("wbv8", [128, 8 * TOWN], BF16)

    e0ld = scope("e0ld", [128, 4096], BF16)

    def _(g):
        g.wait_ge(VV.h, vv_xny)
        sWBS.wait(g)
        for i in range(NEXP):
            sWBL.put(g, [(wbv8[:, i * TOWN:(i + 1) * TOWN],
                          wbv8d[:, i * TOWN:(i + 1) * TOWN])])
        gE0S.wait(g)
        for tb in range(4):
            if tb >= 1:
                g.wait_ge(VV.h, vv_xny + 8 * tb)
            sE0L.put(g, [(e0ld[:].rearrange("p (ec e) -> p ec e", ec=8),
                          e0d[tb].rearrange("ec p e -> p ec e"))])
    blk.gpsimd(_)

    def _(v):
        for tb in range(4):
            sE0L.wait(v, tb)
            for ec in range(8):
                v.wait_ge(VV.h, VV.n)
                VV.inc(v.tensor_add(x2acc[ec][:, tb * 512:(tb + 1) * 512],
                                    x2acc[ec][:, tb * 512:(tb + 1) * 512],
                                    e0ld[:, ec * 512:(ec + 1) * 512]))
    blk.vector(_)

    def _(v):
        sWBL.wait(v)
    blk.vector(_)

    vv_w = vv_scan_done
    vv_bc = vv_scan_done

    ss_prev_evac = SS.n
    vv_prev = VV.n
    tt_prev = TT.n

    for i in range(1, NEXP):
        for tb in range(4):
            ts = tb * 512
            tt_f1 = TT.n
            ss_f1 = SS.n
            w1base = len(sW1L.marks)

            def _(sy, i=i, tt_f1=tt_f1, vv_pre=VV.n):
                sy.wait_ge(VV.h, min(vv_pre, vv_w))
                gWPP.wait(sy)
                for gr in range(8):
                    if gr >= 2:
                        sy.wait_ge(TT.h, tt_f1 + 4 * (gr - 1))
                    sW1L.put(sy, [(w1c[gr % 2][:].rearrange("p (c e) -> p c e", c=4),
                                   w1bd[i, 4 * gr:4 * gr + 4].rearrange("c p e -> p c e"))])
            blk.sync(_)

            def _(t, i=i, ts=ts, ss_f1=ss_f1, vv_bc=vv_bc, w1base=w1base,
                  ss_pe=ss_prev_evac):
                t.wait_ge(VV.h, vv_bc)
                for hc in range(32):
                    sW1L.wait(t, w1base + hc // 4)
                    if hc < 2:
                        t.wait_ge(SS.h, max(0, ss_pe - 7 + hc))
                    if hc >= 2:
                        t.wait_ge(SS.h, ss_f1 + hc - 1)
                    po = (hc % 2) * 512
                    for dc in range(NDC):
                        ins = t.matmul(PA[:, po:po + 512],
                                       w1c[(hc // 4) % 2][:, (hc % 4) * 1024 + dc * 128:
                                                          (hc % 4) * 1024 + (dc + 1) * 128],
                                       xn1bf[dc][:, ts:ts + 512],
                                       start=(dc == 0), stop=(dc == NDC - 1))
                    TT.inc(ins)
            blk.tensor(_)

            def _(s, i=i, tt_f1=tt_f1, first=(i == 1 and tb == 0)):
                if first:
                    s.wait_ge(VV.h, vv_xny)
                for hc in range(32):
                    s.wait_ge(TT.h, tt_f1 + hc + 1)
                    po = (hc % 2) * 512
                    SS.inc(s.activation(h1m[:, hc * 512:(hc + 1) * 512], PA[:, po:po + 512],
                                        AF.Relu, bias=b1a[:, i * 32 + hc:i * 32 + hc + 1]))
            blk.scalar(_)
            ss_f1_done = SS.n

            tt_f2 = TT.n
            w2base = len(sW2L.marks)

            def _(sy, i=i, tt_f2=tt_f2):
                gWPP.wait(sy)
                for gr in range(8):
                    if gr >= 2:
                        sy.wait_ge(TT.h, tt_f2 + 4 * (gr - 1))
                    sW2L.put(sy, [(w2c[gr % 2][:].rearrange("p (c e) -> p c e", c=4),
                                   w2bd[i, 4 * gr * 128:(4 * gr + 4) * 128, :].rearrange(
                                       "(c p) e -> p c e", p=128))])
            blk.sync(_)

            def _(t, i=i, ss_f1_done=ss_f1_done, ss_prev_evac=ss_prev_evac,
                  w2base=w2base, tt_f1=tt_f1, ss_f1=ss_f1):
                t.wait_ge(SS.h, max(ss_f1_done, ss_prev_evac))
                for hc in range(32):
                    sW2L.wait(t, w2base + hc // 4)
                    for ec in range(8):
                        ps = PA if ec < 4 else PB
                        ins = t.matmul(ps[:, (ec % 4) * 512:(ec % 4 + 1) * 512],
                                       w2c[(hc // 4) % 2][:, (hc % 4) * 1024 + ec * 128:
                                                          (hc % 4) * 1024 + (ec + 1) * 128],
                                       h1m[:, hc * 512:(hc + 1) * 512],
                                       start=(hc == 0), stop=(hc == 31))
                    TT.inc(ins)
            blk.tensor(_)
            tt_f2_done = TT.n

            def _(s, i=i, tt_f2_done=tt_f2_done, vv_pre=VV.n):
                s.wait_ge(TT.h, tt_f2_done)
                s.wait_ge(VV.h, vv_pre)
                for ec in range(8):
                    if ec >= 2:
                        s.wait_ge(VV.h, vv_pre + 2 * (ec - 1))
                    ps = PA if ec < 4 else PB
                    SS.inc(s.activation(etmp[ec % 2][:],
                                        ps[:, (ec % 4) * 512:(ec % 4 + 1) * 512],
                                        AF.Identity,
                                        bias=b2a[:, i * 8 + ec:i * 8 + ec + 1]))
            blk.scalar(_)
            ss_evac0 = SS.n - 8

            def _(v, i=i, ts=ts, ss_evac0=ss_evac0):
                for ec in range(8):
                    v.wait_ge(SS.h, ss_evac0 + ec + 1)
                    v.wait_ge(VV.h, VV.n)
                    VV.inc(v.tensor_mul(etmp[ec % 2][:], etmp[ec % 2][:],
                                        wbv8[:, i * TOWN + ts:i * TOWN + ts + 512]))
                    v.wait_ge(VV.h, VV.n)
                    VV.inc(v.tensor_add(x2acc[ec][:, ts:ts + 512],
                                        x2acc[ec][:, ts:ts + 512], etmp[ec % 2][:]))
            blk.vector(_)
            ss_prev_evac = SS.n
            vv_prev = VV.n

        tt_prev = TT.n

    vv_fin = VV.n

    def _(g):
        g.wait_ge(VV.h, vv_fin)
        for dc in range(NDC):
            gXO.put(g, [(x2ownD[dc * 128:(dc + 1) * 128, :], x2acc[dc][:])])
        gXO.wait(g)
    blk.gpsimd(_)

    blk_g.__exit__(None, None, None)
    close(len(guards))
    rclose(len(rguards))
    pg2.__exit__(None, None, None)
    pg1.__exit__(None, None, None)
    for g in reversed(keep):
        g.__exit__(None, None, None)
    return nc


def prep_fused_inmaps(inputs):
    a_maps = prep_block0_inmaps(inputs)
    wk = np.asarray(inputs["L1_attn_Wk"], np.float32)
    wv = np.asarray(inputs["L1_attn_Wv"], np.float32)
    wq = np.asarray(inputs["L1_attn_Wq"], np.float32)
    bk = np.asarray(inputs["L1_attn_bk"], np.float32)
    bv = np.asarray(inputs["L1_attn_bv"], np.float32)
    bq = np.asarray(inputs["L1_attn_bq"], np.float32)
    gm = np.asarray(inputs["L1_attn_gamma"], np.float32)
    bt = np.asarray(inputs["L1_attn_beta"], np.float32)
    wr = np.asarray(inputs["L1_ffn_Wr"], np.float32)
    br = np.asarray(inputs["L1_ffn_br"], np.float32)
    w1 = np.asarray(inputs["L1_ffn_W1"], np.float32)
    b1 = np.asarray(inputs["L1_ffn_b1"], np.float32)
    w2 = np.asarray(inputs["L1_ffn_W2"], np.float32)
    b2 = np.asarray(inputs["L1_ffn_b2"], np.float32)
    on = np.ones((128, 128), np.float32)
    shared = dict(
        wkt=_c(wk.T), wvt=_c(wv.T), wqt=_c(wq.T),
        bk=_c(bk.reshape(64, 1)), bq=_c(bq.reshape(64, 1)),
        bkb=_c(np.tile(bk, (128, 1))), bvb=_c(np.tile(bv, (128, 1))),
        gb=_c(gm.reshape(8, 128).T), bt=_c(bt.reshape(8, 128).T),
        mUe=_c((np.triu(on, 1) * TTT_LR).astype(BF)),
        mLe=_c((np.tril(on, -1) * TTT_LR).astype(BF)),
        mUn=_c((-np.triu(on, 1)).astype(BF)),
        identb=_c(np.eye(128, dtype=np.float32).astype(BF)),
        wr=_c(wr), br8=_c(br.reshape(8, 1)),
        w1r=_c(w1.reshape(8, 8, 128, 32, 128).transpose(0, 3, 2, 1, 4)
               .reshape(8, 32, 128, 1024)),
        w2m=_c(w2),
        b1a=_c(np.concatenate([b1[i].reshape(32, 128).T for i in range(8)], axis=1)),
        b2a=_c(np.concatenate([b2[i].reshape(8, 128).T for i in range(8)], axis=1)))
    in_maps = []
    for core in range(8):
        hh = core % 2
        m = dict(a_maps[core])
        m.update(shared)
        m.update(sel0=_c(np.full((128, 1), 1.0 - hh, np.float32)),
                 sel1=_c(np.full((128, 1), float(hh), np.float32)))
        in_maps.append(m)
    return in_maps


def run_fused(inputs):
    nc = build_fused()
    in_maps = prep_fused_inmaps(inputs)
    res = run_bass_kernel_spmd(nc, in_maps, core_ids=list(range(8)))
    x2 = np.zeros((B, T, D), dtype=np.float32)
    for core in range(8):
        b, hh = core // 2, core % 2
        x2[b, hh * TOWN:(hh + 1) * TOWN] = res.results[core]["x2own"].T
    return x2


def prep_block0_inmaps(inputs):
    x0 = np.asarray(inputs["L0_attn_x"], dtype=np.float32)
    wq = np.asarray(inputs["L0_attn_Wq"], dtype=np.float32)
    wk = np.asarray(inputs["L0_attn_Wk"], dtype=np.float32)
    wv = np.asarray(inputs["L0_attn_Wv"], dtype=np.float32)
    wo = np.asarray(inputs["L0_attn_Wo"], dtype=np.float32)
    w1 = np.asarray(inputs["L0_ffn_w1"], dtype=np.float32)
    b1 = np.asarray(inputs["L0_ffn_b1"], dtype=np.float32)
    w2 = np.asarray(inputs["L0_ffn_w2"], dtype=np.float32)
    b2 = np.asarray(inputs["L0_ffn_b2"], dtype=np.float32)

    perm = _rope_perm()
    cos2, sin2 = _rope_tables()
    shared = dict(wqp=_c(wq[:, perm]), wkp=_c(wk[:, perm]), wv=_c(wv), wo=_c(wo),
                  w1t=_c(w1.T), w1s=_c(w1), w2t=_c(w2.T), w2s=_c(w2),
                  b1t=_c(b1.reshape(32, 128).T), b2t=_c(b2.reshape(8, 128).T))
    in_maps = []
    for core in range(8):
        b, h = core // 2, core % 2
        own = slice(h * TOWN, (h + 1) * TOWN)
        oth = slice((1 - h) * TOWN, (2 - h) * TOWN)
        m = dict(shared)
        m.update(x0fm=_c(np.concatenate([x0[b, own], x0[b, oth]], axis=0).T),
                 cos2=_c(np.concatenate([cos2[:, own], cos2[:, oth]], axis=1)),
                 sin2=_c(np.concatenate([sin2[:, own], sin2[:, oth]], axis=1)))
        in_maps.append(m)
    return in_maps


def run_block0(inputs):
    nc = build_block0()
    in_maps = prep_block0_inmaps(inputs)
    res = run_bass_kernel_spmd(nc, in_maps, core_ids=list(range(8)))
    x1 = np.zeros((B, T, D), dtype=np.float32)
    for core in range(8):
        b, h = core // 2, core % 2
        x1[b, h * TOWN:(h + 1) * TOWN] = res.results[core]["x1own"].T
    return x1


def kernel(**inputs):
    return run_fused(inputs)
